# version 02
# baseline (speedup 1.0000x reference)
import numpy as np
from contextlib import ExitStack

import concourse.bass as bass
import concourse.mybir as mybir
from concourse.alu_op_type import AluOpType as OP
from concourse.bass_utils import run_bass_kernel_spmd

F32 = mybir.dt.float32
BF16 = mybir.dt.bfloat16
AX = mybir.AxisListType.X
AF = mybir.ActivationFunctionType

import os
DBG = os.environ.get("MAMBA_DBG", "") == "1"

NCORES = 8
T = 4096
D = 1024
NT = T // 128
ND = D // 128


class Ch:
    def __init__(self, h):
        self.h = h
        self.n = 0

class ChView:
    def __init__(self, ch):
        self._c = ch
        self.base = ch.n
    @property
    def h(self):
        return self._c.h
    @property
    def n(self):
        return self._c.n
    @n.setter
    def n(self, v):
        self._c.n = v

class Prog:
    ENGS = ("sync", "scalar", "vector", "tensor", "gpsimd")

    def __init__(self, nc, stack):
        self.nc = nc
        self.stack = stack
        self.q = {e: [] for e in self.ENGS}
        self.dirty = {e: False for e in self.ENGS}
        self._nsem = 0
        self._bar = None
        self._dma_chs = set()

    def ch(self, name=None):
        h = self.stack.enter_context(self.nc.semaphore(name or f"ch{self._nsem}"))
        self._nsem += 1
        return Ch(h)

    def emit(self, eng, f, inc=None, k=1):
        incs = []
        if inc is not None:
            inc.n += k
            incs.append((inc, k))
        ent = [f, incs, 1]
        self.q[eng].append(ent)
        self.dirty[eng] = True
        return ent

    def chv(self, ch):
        return ChView(ch)

    def dump(self, name, tile_ap, shape, dt=F32):
        if not DBG:
            return
        self.barrier()
        dram = self.nc.dram_tensor("dbg_" + name, list(shape), dt, kind="ExternalOutput").ap()
        ch = self.chp(19)
        self.dma("sync", dram, tile_ap, inc=ch)
        self.wait("sync", ch)
        self.barrier()

    def drain(self, eng):
        self.q[eng].append([("drain",), None, 0])

    def chp(self, i):
        if not hasattr(self, "_pool"):
            self._pool = []
        while len(self._pool) <= i:
            self._pool.append(self.ch(f"pool{len(self._pool)}"))
        return ChView(self._pool[i])

    def wait(self, eng, ch, n=None):
        if n is None:
            n = ch.n
        elif isinstance(ch, ChView):
            n = ch.base + n
        self.q[eng].append([("wait", ch, n), None, 0])

    def barrier(self, bar=None, engines=None):
        targets = [(c, c.n) for c in self._dma_chs]
        self._dma_chs = set()
        for e in self.ENGS:
            if not self.dirty[e]:
                continue
            for ent in reversed(self.q[e]):
                if ent[1] is not None:
                    if ent[1]:
                        c, _k = ent[1][0]
                        targets.append((c, c.n))
                    else:
                        if bar is None:
                            if self._bar is None:
                                self._bar = self.ch("bar")
                            bar = self._bar
                        ent[1].append((bar, ent[2]))
                        bar.n += ent[2]
                        targets.append((bar, bar.n))
                    break
            else:
                raise RuntimeError(f"engine {e} dirty but has no real instruction")
            self.dirty[e] = False
        seen = {}
        for c, n in targets:
            seen[id(c)] = (c, max(n, seen.get(id(c), (c, 0))[1]))
        for e in (engines or self.ENGS):
            for c, n in seen.values():
                self.q[e].append([("wait", c, n), None, 0])
        return bar

    def dma(self, eng, out, in_, inc=None, **kw):
        assert inc is not None
        incs = []
        if inc is not None:
            inc.n += 16
            incs.append((inc, 16))
            base = inc._c if isinstance(inc, ChView) else inc
            self._dma_chs.add(base)
        ent = [lambda e, out=out, in_=in_, kw=kw: e.dma_start(out=out, in_=in_, **kw), incs, 16]
        self.q[eng].append(ent)
        self.dirty[eng] = True
        return ent

    def finalize(self, block):
        for e in self.ENGS:
            ops = self.q[e]
            if not ops:
                continue
            def section(engine, ops=ops):
                for ent in ops:
                    f = ent[0]
                    if isinstance(f, tuple) and f[0] == "wait":
                        engine.wait_ge(f[1].h, f[2])
                        continue
                    if isinstance(f, tuple) and f[0] == "drain":
                        engine.drain()
                        continue
                    instr = f(engine)
                    if ent[1]:
                        for (c, k) in ent[1]:
                            instr = instr.then_inc(c.h, k)
            getattr(block, e)(section)


_NAME_N = [0]

def _uniq(name):
    _NAME_N[0] += 1
    return f"{name}_{_NAME_N[0]}"

def sbuf(nc, stack, name, shape, dt=F32):
    return stack.enter_context(nc.sbuf_tensor(_uniq(name), shape, dt))

def psum(nc, stack, name, shape, dt=F32):
    return stack.enter_context(nc.psum_tensor(_uniq(name), shape, dt))



def host_constants():
    import ml_dtypes
    c = {}
    c["id128"] = np.eye(128, dtype=np.float32)
    ut = np.triu(np.ones((128, 128), np.float32))
    c["utones"] = ut
    c["ltstrict"] = np.tril(np.ones((128, 128), np.float32), -1)
    c["neglt"] = (-1e9 * np.tril(np.ones((128, 128), np.float32), -1))
    c["ones128"] = np.ones((128, 128), np.float32)
    blk = np.zeros((32, 32 * 128), np.float32)
    for h in range(32):
        blk[h, h * 128:(h + 1) * 128] = 1.0
    c["blk"] = blk
    c["neglt_t"] = np.tile(-1e9 * np.tril(np.ones((128, 128), np.float32), -1), (1, 32))
    erep = np.zeros((32, 16 * 128), np.float32)
    for i in range(16):
        erep[2 * i, i * 128:i * 128 + 64] = 1.0
        erep[2 * i + 1, i * 128 + 64:(i + 1) * 128] = 1.0
    c["erep"] = erep
    blkp = np.zeros((32, 32 * 64), np.float32)
    for h in range(32):
        blkp[h, h * 64:(h + 1) * 64] = 1.0
    c["blkp"] = blkp
    c["negblk"] = -blk
    c["neglt4"] = np.tile(-1e9 * np.tril(np.ones((128, 128), np.float32), -1), (1, 4))
    bd2 = np.zeros((128, 2), np.float32)
    bd2[0:64, 0] = 1.0
    bd2[64:128, 1] = 1.0
    c["bd2"] = bd2
    return c



def emit_ln_feat(P, stack, x_dram, feats, eps=1e-5):
    nc = P.nc
    nf = len(feats)
    assert nf <= 2
    xb = sbuf(nc, stack, "lnf_x", [128, 2 * D])
    tokb = [sbuf(nc, stack, f"lnf_tok{i}", [128, 2 * D], BF16) for i in range(nf)]
    trash = sbuf(nc, stack, "lnf_trash", [128, D], BF16)
    trash2x = sbuf(nc, stack, "lnf_trash2", [128, D], BF16)
    tmp1 = sbuf(nc, stack, "lnf_tmp1", [128, D])
    tmp2 = sbuf(nc, stack, "lnf_tmp2", [128, D])
    st = sbuf(nc, stack, "lnf_st", [128, 16])
    pt = psum(nc, stack, "lnf_pt", [128, nf * 2 * D], BF16)
    idb = P.cid128b
    ch_ld = [P.chp(0), P.chp(7)]; ch_stat = P.chp(1); ch_var = P.chp(2); ch_rstd = P.chp(3)
    ch_norm = P.chp(4); ch_tp = P.chp(5); ch_evac = P.chp(6)

    for t in range(NT):
        par = t % 2
        x_t = xb[:, par * D:(par + 1) * D]
        c = lambda i, par=par: st[:, par * 8 + i:par * 8 + i + 1]
        if t >= 2:
            P.wait("sync", ch_norm, nf * (t - 1))
        P.dma("sync", x_t, x_dram[t * 128:(t + 1) * 128, :], inc=ch_ld[par])
        P.wait("scalar", ch_ld[par], 16 * (t // 2 + 1))
        P.emit("scalar", lambda e, x=x_t, c=c: e.activation(out=trash[:], in_=x, func=AF.Copy,
                                                            accum_out=c(0)))
        P.drain("scalar")
        P.emit("scalar", lambda e, x=x_t, c=c: e.activation(out=trash2x[:], in_=x, func=AF.Square,
                                                            accum_out=c(1)), inc=ch_stat)
        P.drain("scalar")
        P.emit("scalar", lambda e, c=c: e.activation(out=c(2), in_=c(0), func=AF.Copy, scale=1.0 / D))
        P.emit("scalar", lambda e, c=c: e.activation(out=c(3), in_=c(0), func=AF.Square, scale=1.0 / D))
        P.emit("scalar", lambda e, c=c: e.activation(out=c(4), in_=c(1), func=AF.Copy, scale=1.0 / D),
               inc=ch_stat)
        P.wait("vector", ch_stat, 2 * (t + 1))
        P.emit("vector", lambda e, c=c: e.tensor_tensor(out=c(5), in0=c(4), in1=c(3), op=OP.subtract),
               inc=ch_var)
        P.wait("scalar", ch_var, t + 1)
        P.emit("scalar", lambda e, c=c: e.activation(out=c(6), in_=c(5), func=AF.Ln, bias=P.scal("eps_ln")))
        P.drain("scalar")
        P.emit("scalar", lambda e, c=c: e.activation(out=c(7), in_=c(6), func=AF.Exp, scale=-0.5),
               inc=ch_rstd)
        P.wait("vector", ch_rstd, t + 1)
        if t >= 2:
            P.wait("vector", ch_tp, nf * (t - 1))
        outs_tok = []
        for fi, (gam, bet, feat_sb, also_tok) in enumerate(feats):
            tok_t = tokb[fi][:, par * D:(par + 1) * D]
            if gam is None:
                P.emit("vector", lambda e, x=x_t, o=tok_t, c=c: e.tensor_scalar(
                    out=o, in0=x, scalar1=c(2), scalar2=c(7),
                    op0=OP.subtract, op1=OP.mult), inc=ch_norm)
            else:
                P.emit("vector", lambda e, x=x_t, c=c: e.tensor_scalar(
                    out=tmp1[:], in0=x, scalar1=c(2), scalar2=c(7),
                    op0=OP.subtract, op1=OP.mult))
                P.drain("vector")
                P.emit("vector", lambda e, g=gam: e.tensor_tensor(
                    out=tmp2[:], in0=tmp1[:], in1=g, op=OP.mult))
                P.drain("vector")
                P.emit("vector", lambda e, o=tok_t, b=bet: e.tensor_tensor(
                    out=o, in0=tmp2[:], in1=b, op=OP.add), inc=ch_norm)
            if also_tok is not None:
                P.wait("scalar", ch_norm, nf * t + fi + 1)
                P.emit("scalar", lambda e, s=tok_t, d=also_tok, t=t: e.copy(
                    out=d[:, t * D:(t + 1) * D], in_=s))
            outs_tok.append((tok_t, feat_sb))
        P.wait("tensor", ch_norm, nf * (t + 1))
        if t >= 2:
            P.wait("tensor", ch_evac, nf * (t - 1))
        for fi, (tok_t, feat_sb) in enumerate(outs_tok):
            bank = (par * nf + fi) * D
            for d in range(ND):
                P.emit("tensor", lambda e, s=tok_t, d=d, bank=bank: e.transpose(
                    pt[:, bank + d * 128:bank + (d + 1) * 128],
                    in_=s[:, d * 128:(d + 1) * 128], identity=idb[:]),
                    inc=ch_tp if d == ND - 1 else None)
            P.wait("scalar", ch_tp, nf * t + fi + 1)
            for d in range(ND):
                P.emit("scalar", lambda e, f=feat_sb, d=d, t=t, bank=bank: e.copy(
                    out=f[:, d * T + t * 128:d * T + (t + 1) * 128],
                    in_=pt[:, bank + d * 128:bank + (d + 1) * 128]),
                    inc=ch_evac if d == ND - 1 else None)
    P.barrier()



_CONST_CACHE = None

def const_inputs():
    global _CONST_CACHE
    if _CONST_CACHE is None:
        import ml_dtypes
        c = host_constants()
        bf = lambda x: np.asarray(x, dtype=ml_dtypes.bfloat16)
        scal = np.zeros((128, 16), np.float32)
        for i, v in enumerate(_SCAL_VALS):
            scal[:, i] = v
        _CONST_CACHE = {
            "c_scal": scal,
            "c_id128b": bf(c["id128"]),
            "c_id128f": c["id128"],
            "c_utonesb": bf(c["utones"]),
            "c_ltstrictb": bf(c["ltstrict"]),
            "c_negltb": bf(c["neglt"]),
            "c_ones128b": bf(c["ones128"]),
            "c_ones128f": c["ones128"],
            "c_blkf": c["blk"],
            "c_negblkf": c["negblk"],
            "c_neglt4f": c["neglt4"],
            "c_blkpb": bf(c["blkp"]),
            "c_bd2b": bf(c["bd2"]),
            "c_bd2tb": bf(c["bd2"].T.copy()),
        }
    return _CONST_CACHE

_SCAL_NAMES = ["eps_ln", "one", "eps_l2", "eps_moe", "f32eps", "half", "neg1e9", "zero"]
_SCAL_VALS = [1e-5, 1.0, 1e-12, 1e-6, float(np.finfo(np.float32).eps), 0.5, -1e9, 0.0]

_CONST_SHAPES = {
    "c_scal": (128, 16),
    "c_id128b": (128, 128), "c_id128f": (128, 128), "c_utonesb": (128, 128),
    "c_ltstrictb": (128, 128), "c_negltb": (128, 128), "c_ones128b": (128, 128),
    "c_ones128f": (128, 128), "c_blkf": (32, 32 * 128), "c_negblkf": (32, 32 * 128),
    "c_neglt4f": (128, 4 * 128), "c_blkpb": (32, 32 * 64),
    "c_bd2b": (128, 2), "c_bd2tb": (2, 128),
}
_CONST_F32 = {"c_scal", "c_id128f", "c_ones128f", "c_blkf", "c_negblkf", "c_neglt4f"}
_CONST_ATTR = {
    "c_scal": "cscal",
    "c_id128b": "cid128b", "c_id128f": "cid128f", "c_utonesb": "cutonesb",
    "c_ltstrictb": "cltstrictb", "c_negltb": "cnegltb", "c_ones128b": "cones128b",
    "c_ones128f": "cones128f", "c_blkf": "cblkf", "c_negblkf": "cnegblkf",
    "c_neglt4f": "cneglt4f", "c_blkpb": "cblkpb",
    "c_bd2b": "cbd2b", "c_bd2tb": "cbd2tb",
}

def load_consts(P, stack, names):
    nc = P.nc
    ch = P.ch("consts")
    names = ["c_scal"] + [n for n in names if n != "c_scal"]
    for name in names:
        shape = _CONST_SHAPES[name]
        dt = F32 if name in _CONST_F32 else BF16
        dram = nc.dram_tensor(name, list(shape), dt, kind="ExternalInput").ap()
        tile = sbuf(nc, stack, "s" + name, list(shape), dt)
        P.dma("sync", tile[:], dram, inc=ch)
        setattr(P, _CONST_ATTR[name], tile)
    P.scal = lambda n: P.cscal[:, _SCAL_NAMES.index(n):_SCAL_NAMES.index(n) + 1]
    P.scal32 = lambda n: P.cscal[0:32, _SCAL_NAMES.index(n):_SCAL_NAMES.index(n) + 1]
    for e in Prog.ENGS:
        P.wait(e, ch)
    return ch


def repl_rows(v, dtype=np.float32):
    return np.ascontiguousarray(np.broadcast_to(np.asarray(v, dtype).reshape(1, -1), (128, len(v))))



def build_stage_a(with_h=True):
    nc = bass.Bass("TRN2", target_bir_lowering=False)
    x = nc.dram_tensor("x", [T, D], F32, kind="ExternalInput").ap()
    o_xn = nc.dram_tensor("xn_feat", [D, T], BF16, kind="ExternalOutput").ap()
    o_h = nc.dram_tensor("h_feat", [D, T], BF16, kind="ExternalOutput").ap() if with_h else None
    g_r = nc.dram_tensor("ln_g_r", [128, D], F32, kind="ExternalInput").ap() if with_h else None
    b_r = nc.dram_tensor("ln_b_r", [128, D], F32, kind="ExternalInput").ap() if with_h else None

    with ExitStack() as stack:
        P = Prog(nc, stack)
        load_consts(P, stack, ["c_id128b"])
        xn_feat = sbuf(nc, stack, "xn_feat_sb", [128, ND * T], BF16)
        feats = [(None, None, xn_feat, None)]
        if with_h:
            sg = sbuf(nc, stack, "sg", [128, D]); sb_ = sbuf(nc, stack, "sbeta", [128, D])
            h_feat = sbuf(nc, stack, "h_feat_sb", [128, ND * T], BF16)
            ch0 = P.ch()
            P.dma("sync", sg[:], g_r, inc=ch0)
            P.dma("sync", sb_[:], b_r, inc=ch0)
            for e in Prog.ENGS:
                P.wait(e, ch0)
            feats.append((sg[:], sb_[:], h_feat, None))
        with ExitStack() as s2:
            emit_ln_feat(P, s2, x, feats)
        ch_st = P.ch("st")
        for d in range(ND):
            P.dma("sync", o_xn[d * 128:(d + 1) * 128, :], xn_feat[:, d * T:(d + 1) * T], inc=ch_st)
            if with_h:
                P.dma("sync", o_h[d * 128:(d + 1) * 128, :], h_feat[:, d * T:(d + 1) * T], inc=ch_st)
        P.wait("sync", ch_st)
        block = stack.enter_context(nc.Block())
        P.finalize(block)
    return nc



def emit_cast_f32_to_bf16_dram(P, stack, src, dst, rows, cols):
    nc = P.nc
    nr = rows // 128
    fb = sbuf(nc, stack, f"wc_f_{src.tensor.name}", [128, 2 * cols])
    bb = sbuf(nc, stack, f"wc_b_{src.tensor.name}", [128, 2 * cols], BF16)
    ch_ld = [P.chp(0), P.chp(4)]; ch_cv = P.chp(1); ch_st = [P.chp(2), P.chp(3)]
    for r in range(nr):
        par = r % 2
        if r >= 2:
            P.wait("sync", ch_cv, r - 1)
        P.dma("sync", fb[:, par * cols:(par + 1) * cols], src[r * 128:(r + 1) * 128, :],
              inc=ch_ld[par])
        P.wait("scalar", ch_ld[par], 16 * (r // 2 + 1))
        if r >= 2:
            P.wait("scalar", ch_st[par], 16 * (r // 2))
        P.emit("scalar", lambda e, par=par: e.copy(out=bb[:, par * cols:(par + 1) * cols],
                                                   in_=fb[:, par * cols:(par + 1) * cols]), inc=ch_cv)
        P.wait("sync", ch_cv, r + 1)
        P.dma("sync", dst[r * 128:(r + 1) * 128, :], bb[:, par * cols:(par + 1) * cols], inc=ch_st[par])
    P.barrier()



D_IN = 2048
NH = 32
HP = 64
DST = 128
CONV_DIM = 2304
NXT = 16
SEG = 1024
NSEG = T // SEG
SC = SEG // 128

def emit_mamba(P, stack, xn_feat, win, convw_t, convb_t, dtb, alog, normw_r, wout, d_zs, d_s):
    nc = P.nc

    winb = nc.dram_tensor("winb", [D, 4384], BF16, kind="Internal").ap()
    woutb = nc.dram_tensor("woutb", [D_IN, D], BF16, kind="Internal").ap()
    with ExitStack() as s0:
        emit_cast_f32_to_bf16_dram(P, s0, win, winb, D, 4384)
        emit_cast_f32_to_bf16_dram(P, s0, wout, woutb, D_IN, D)

    sconvw = sbuf(nc, stack, "sconvw", [128, 18 * 4])
    sconvb = sbuf(nc, stack, "sconvb", [128, 18])
    sdtb = sbuf(nc, stack, "sdtb", [32, 4])
    salog = sbuf(nc, stack, "salog", [32, 4])
    snormw = sbuf(nc, stack, "snormw", [128, D_IN])
    chp = P.chp(0)
    P.dma("sync", sconvw[:], convw_t, inc=chp)
    P.dma("sync", sconvb[:], convb_t, inc=chp)
    P.dma("sync", sdtb[:, 0:1], dtb, inc=chp)
    P.dma("sync", salog[:, 0:1], alog, inc=chp)
    P.dma("sync", snormw[:], normw_r, inc=chp)
    P.wait("scalar", chp)
    P.emit("scalar", lambda e: e.activation(out=salog[:, 1:2], in_=salog[:, 0:1], func=AF.Exp, scale=1.0))
    P.drain("scalar")
    P.emit("scalar", lambda e: e.activation(out=salog[:, 2:3], in_=salog[:, 1:2], func=AF.Copy, scale=-1.0))
    P.barrier()
    import os
    if os.environ.get("MAMBA_TRUNC") == "params":
        return

    with ExitStack() as s1:
        wz = sbuf(nc, s1, "wz", [128, 8 * D_IN], BF16)
        ch_wl = P.chp(0)
        for k in range(ND):
            P.dma("sync", wz[:, k * D_IN:(k + 1) * D_IN], winb[k * 128:(k + 1) * 128, 0:D_IN], inc=ch_wl)
        P.wait("tensor", ch_wl)
        pz = psum(nc, s1, "pz", [128, 2 * D_IN])
        zt = sbuf(nc, s1, "zt", [128, 2 * D_IN], BF16)
        ch_mm = P.chp(1); ch_ev = P.chp(2); ch_st = [P.chp(3), P.chp(4)]
        for t in range(NT):
            par = t % 2
            if t >= 2:
                P.wait("tensor", ch_ev, t - 1)
            for k in range(ND):
                for j in range(4):
                    P.emit("tensor", lambda e, par=par, t=t, k=k, j=j: e.matmul(
                        pz[:, par * D_IN + j * 512:par * D_IN + (j + 1) * 512],
                        lhsT=xn_feat[:, k * T + t * 128:k * T + (t + 1) * 128],
                        rhs=wz[:, k * D_IN + j * 512:k * D_IN + (j + 1) * 512],
                        start=(k == 0), stop=(k == ND - 1)),
                        inc=ch_mm if (k == ND - 1 and j == 3) else None)
            P.wait("scalar", ch_mm, t + 1)
            if t >= 2:
                P.wait("scalar", ch_st[par], 16 * (t // 2))
            P.emit("scalar", lambda e, par=par: e.activation(
                out=zt[:, par * D_IN:(par + 1) * D_IN], in_=pz[:, par * D_IN:(par + 1) * D_IN],
                func=AF.Silu), inc=ch_ev)
            P.wait("sync", ch_ev, t + 1)
            P.dma("sync", d_zs[t * 128:(t + 1) * 128, :], zt[:, par * D_IN:(par + 1) * D_IN],
                  inc=ch_st[par])
        P.barrier()

    sB45 = stack.enter_context(ExitStack())
    dtfb = sbuf(nc, sB45, "dtfb", [32, T], BF16)
    abf = sbuf(nc, sB45, "abf", [128, 32])
    ealc = sbuf(nc, sB45, "ealc", [32, 96])
    dtf_holder = []
    with ExitStack() as s2:
        wdt = sbuf(nc, s2, "wdt", [128, 8 * 32], BF16)
        ch_wl = P.chp(0)
        for k in range(ND):
            P.dma("sync", wdt[:, k * 32:(k + 1) * 32], winb[k * 128:(k + 1) * 128, 4352:4384], inc=ch_wl)
        P.wait("tensor", ch_wl)
        pdt = psum(nc, s2, "pdt", [32, 512])
        raw = sbuf(nc, s2, "dtraw", [32, T])
        tmp = sbuf(nc, s2, "dttmp", [32, 2 * T])
        dtf = sbuf(nc, s2, "dtf", [32, T])
        ch_mm = P.chp(1); ch_ev = P.chp(2)
        for j in range(8):
            if j >= 1:
                P.wait("tensor", ch_ev, j)
            for k in range(ND):
                P.emit("tensor", lambda e, j=j, k=k: e.matmul(
                    pdt[:, 0:512],
                    lhsT=wdt[:, k * 32:(k + 1) * 32],
                    rhs=xn_feat[:, k * T + j * 512:k * T + (j + 1) * 512],
                    start=(k == 0), stop=(k == ND - 1)),
                    inc=ch_mm if k == ND - 1 else None)
            P.wait("scalar", ch_mm, j + 1)
            P.emit("scalar", lambda e, j=j: e.copy(out=raw[:, j * 512:(j + 1) * 512], in_=pdt[:, 0:512]),
                   inc=ch_ev)
        P.wait("scalar", ch_ev, 8)
        P.barrier()
        ch_b2a = P.chp(3); ch_b2b = P.chp(4); ch_b2c = P.chp(5); ch_b2d = P.chp(6); ch_b2e = P.chp(7)
        ch_b2g = P.chp(9); ch_b2h = P.chp(10); ch_b2i = P.chp(11)
        u_t = tmp[:, 0:T]; a_t = tmp[:, T:2 * T]
        P.emit("vector", lambda e: e.tensor_scalar(out=u_t, in0=raw[:], scalar1=sdtb[:, 0:1],
                                                   scalar2=None, op0=OP.add), inc=ch_b2a)
        P.wait("scalar", ch_b2a)
        P.emit("scalar", lambda e: e.activation(out=raw[:], in_=u_t, func=AF.Relu))
        P.emit("scalar", lambda e: e.activation(out=a_t, in_=u_t, func=AF.Abs))
        P.drain("scalar")
        P.emit("scalar", lambda e: e.activation(out=a_t, in_=a_t, func=AF.Exp, scale=-1.0))
        P.drain("scalar")
        P.emit("scalar", lambda e: e.activation(out=a_t, in_=a_t, func=AF.Ln,
                                                bias=P.scal32("one")), inc=ch_b2b)
        P.wait("vector", ch_b2b)
        P.emit("vector", lambda e: e.tensor_tensor(out=dtf[:], in0=raw[:], in1=a_t, op=OP.add),
               inc=ch_b2c)
        dtf_holder.append(dtf)
        P.wait("scalar", ch_b2c)
        P.emit("scalar", lambda e: e.copy(out=dtfb[:], in_=dtf[:]))
        P.wait("vector", ch_b2c)
        for c in range(NT):
            P.emit("vector", lambda e, c=c: e.tensor_reduce(
                out=ealc[:, 64 + c:65 + c], in_=dtf[:, c * 128:(c + 1) * 128], axis=AX, op=OP.add),
                inc=ch_b2d if c == NT - 1 else None)
        P.wait("vector", ch_b2d)
        P.emit("vector", lambda e: e.tensor_scalar(out=ealc[:, 32:64], in0=ealc[:, 64:96],
                                                   scalar1=salog[:, 2:3], scalar2=None,
                                                   op0=OP.mult), inc=ch_b2e)
        P.wait("scalar", ch_b2e)
        P.emit("scalar", lambda e: e.activation(out=ealc[:, 0:32], in_=ealc[:, 32:64], func=AF.Exp))
        d_arow = nc.dram_tensor("d_arow", [32, 1], F32, kind="Internal").ap()
        ch_b2f = P.chp(8)
        P.dma("sync", d_arow, salog[:, 2:3], inc=ch_b2f)
        P.wait("sync", ch_b2f, 16)
        P.dma("sync", abf[:], d_arow.squeeze(1).unsqueeze(0).partition_broadcast(128).squeeze(1),
              inc=ch_b2f)
        P.barrier()

    P.dump("dtf", dtf_holder[0][:], [32, T])
    p_sb = sbuf(nc, sB45, "p_sb", [128, NH * HP])
    psb_bf = sbuf(nc, sB45, "psb_bf", [128, NH * HP], BF16)
    xbc = sbuf(nc, sB45, "xbc", [128, 18 * (SEG + 3)], BF16)
    tails = sbuf(nc, sB45, "tails", [128, 18 * 4], BF16)
    ch_pi = P.chp(0)
    P.emit("vector", lambda e: e.memset(p_sb[:], 0.0), inc=ch_pi)
    P.emit("gpsimd", lambda e: e.memset(xbc[:], 0.0))
    P.barrier()

    d_ygf = nc.dram_tensor("d_ygf", [NT, 128, D_IN], BF16, kind="Internal").ap()

    adtT = sbuf(nc, sB45, "adtT", [128, 32], BF16)
    dtT = sbuf(nc, sB45, "dtT", [128, 32], BF16)
    acst = sbuf(nc, sB45, "acst", [128, 32])
    acsf = sbuf(nc, sB45, "acsf", [32, 128])
    nacsf = sbuf(nc, sB45, "nacsf", [32, 128])
    odt = sbuf(nc, sB45, "odt", [128, 32], BF16)
    dect = sbuf(nc, sB45, "dect", [128, 32], BF16)
    fexp_hg = sbuf(nc, sB45, "fexp_hg", [32, 512])
    fz = sbuf(nc, sB45, "fz", [32, NH * HP], BF16)
    eg = sbuf(nc, sB45, "eg", [128, 4 * 512], BF16)
    scb = sbuf(nc, sB45, "scb", [128, 128], BF16)
    xtok = sbuf(nc, sB45, "xtok", [128, NXT * 128], BF16)
    xdt = sbuf(nc, sB45, "xdt", [128, NXT * 128], BF16)
    xdd = sbuf(nc, sB45, "xdd", [128, NXT * 128], BF16)
    ysc = sbuf(nc, sB45, "ysc", [128, D_IN])
    yf = sbuf(nc, sB45, "yf", [128, D_IN], BF16)
    ygb = sbuf(nc, sB45, "ygb", [128, D_IN], BF16)
    ygf = sbuf(nc, sB45, "ygf", [128, D_IN], BF16)
    zbuf = sbuf(nc, sB45, "zbuf", [128, D_IN], BF16)
    gst = sbuf(nc, sB45, "gst", [128, 8])
    btok = sbuf(nc, sB45, "btok", [128, 128], BF16)
    wcol = sbuf(nc, sB45, "wcol", [128, 8 * 128], BF16)
    ctmp = [sbuf(nc, sB45, f"ctmp{i}", [128, 512]) for i in range(2)]

    for half in range(NSEG):
        t0 = half * SEG
        if half >= 1:
            for m in range(18):
                P.emit("scalar", lambda e, m=m: e.copy(
                    out=xbc[:, m * (SEG + 3):m * (SEG + 3) + 3], in_=tails[:, m * 4:m * 4 + 3]))
            P.barrier()
        ch_w = P.chp(0); ch_mm = P.chp(1); ch_ev = P.chp(2)
        sxa = ExitStack()
        pxa = psum(nc, sxa, "pxa", [128, 1024])
        ntc = SEG // 512
        for m in range(18):
            if m >= 1:
                P.wait("sync", ch_mm, m * ntc)
            for k in range(ND):
                P.dma("sync", wcol[:, k * 128:(k + 1) * 128],
                      winb[k * 128:(k + 1) * 128, D_IN + m * 128:D_IN + (m + 1) * 128], inc=ch_w)
            P.wait("tensor", ch_w, 16 * 8 * (m + 1))
            if half == 0 and m == 0:
                P.dump("wcol0", wcol[:], [128, 8 * 128], BF16)
                P.dump("xnslice0", xn_feat[:, 0:512], [128, 512], BF16)
            for tc in range(ntc):
                ppar = tc % 2
                if m * ntc + tc >= 2:
                    P.wait("tensor", ch_ev, m * ntc + tc - 1)
                for k in range(ND):
                    P.emit("tensor", lambda e, m=m, tc=tc, k=k, ppar=ppar, pxa=pxa, t0=t0: e.matmul(
                        pxa[:, ppar * 512:(ppar + 1) * 512],
                        lhsT=wcol[:, k * 128:(k + 1) * 128],
                        rhs=xn_feat[:, k * T + t0 + tc * 512:k * T + t0 + (tc + 1) * 512],
                        start=(k == 0), stop=(k == ND - 1)),
                        inc=ch_mm if k == ND - 1 else None)
                P.wait("scalar", ch_mm, m * ntc + tc + 1)
                P.emit("scalar", lambda e, m=m, tc=tc, ppar=ppar, pxa=pxa: e.copy(
                    out=xbc[:, m * (SEG + 3) + 3 + tc * 512:m * (SEG + 3) + 3 + (tc + 1) * 512],
                    in_=pxa[:, ppar * 512:(ppar + 1) * 512]), inc=ch_ev)
                if half == 0 and m == 0 and tc == 0:
                    P.dump("xbc_t0", xbc[:, 0:1027], [128, 1027], BF16)
        P.barrier()
        sxa.close()
        if half == 0:
            P.dump("xbc_raw0", xbc[:], [128, 18 * (SEG + 3)], BF16)
        if half < NSEG - 1:
            for m in range(18):
                P.emit("scalar", lambda e, m=m: e.copy(
                    out=tails[:, m * 4:m * 4 + 3],
                    in_=xbc[:, m * (SEG + 3) + SEG:m * (SEG + 3) + SEG + 3]))
            P.barrier()
        ch_cv = P.chp(3); ch_cs = P.chp(4)
        npc = SEG // 512
        for m in range(18):
            for ci_idx, pc in enumerate(reversed(range(npc))):
                ci = m * npc + ci_idx
                w4 = lambda j, m=m: sconvw[:, m * 4 + j:m * 4 + j + 1]
                src = lambda j, m=m, pc=pc: xbc[:, m * (SEG + 3) + pc * 512 + j:
                                                m * (SEG + 3) + pc * 512 + j + 512]
                P.emit("vector", lambda e, w4=w4, src=src: e.tensor_scalar(
                    out=ctmp[0][:], in0=src(0), scalar1=w4(0), scalar2=None, op0=OP.mult))
                P.drain("vector")
                P.emit("vector", lambda e, w4=w4, src=src: e.scalar_tensor_tensor(
                    out=ctmp[1][:], in0=src(1), scalar=w4(1), in1=ctmp[0][:], op0=OP.mult, op1=OP.add))
                P.drain("vector")
                P.emit("vector", lambda e, w4=w4, src=src: e.scalar_tensor_tensor(
                    out=ctmp[0][:], in0=src(2), scalar=w4(2), in1=ctmp[1][:], op0=OP.mult, op1=OP.add))
                P.drain("vector")
                P.emit("vector", lambda e, w4=w4, src=src: e.scalar_tensor_tensor(
                    out=ctmp[1][:], in0=src(3), scalar=w4(3), in1=ctmp[0][:], op0=OP.mult, op1=OP.add),
                    inc=ch_cv)
                P.wait("scalar", ch_cv, ci + 1)
                P.emit("scalar", lambda e, m=m, pc=pc: e.activation(
                    out=xbc[:, m * (SEG + 3) + 3 + pc * 512:m * (SEG + 3) + 3 + (pc + 1) * 512],
                    in_=ctmp[1][:], func=AF.Silu, bias=sconvb[:, m:m + 1]), inc=ch_cs)
                P.wait("vector", ch_cs, ci + 1)
        P.barrier()
        if half == 0:
            P.dump("xbc0", xbc[:], [128, 18 * (SEG + 3)], BF16)
        for cc in range(SC):
            cg = half * SC + cc
            xo = 3 + cc * 128
            Bsl = xbc[:, 16 * (SEG + 3) + xo:16 * (SEG + 3) + xo + 128]
            Csl = xbc[:, 17 * (SEG + 3) + xo:17 * (SEG + 3) + xo + 128]
            with ExitStack() as ps1:
                pT = psum(nc, ps1, "pT", [128, 2048], BF16)
                pTf = psum(nc, ps1, "pTf", [128, 128])
                pA = psum(nc, ps1, "pA", [128, 64])
                ch1 = P.chp(5); ch1e = P.chp(6); ch1v = P.chp(7)
                P.emit("tensor", lambda e, cg=cg: e.transpose(
                    pT[:, 32:64], in_=dtfb[:, cg * 128:(cg + 1) * 128], identity=P.cid128b[0:32, 0:32]))
                P.emit("tensor", lambda e, Bsl=Bsl: e.transpose(
                    pT[:, 64:192], in_=Bsl, identity=P.cid128b[:]), inc=ch1)
                P.wait("scalar", ch1, 1)
                P.emit("scalar", lambda e: e.copy(out=dtT[:], in_=pT[:, 32:64]))
                P.emit("scalar", lambda e: e.copy(out=btok[:], in_=pT[:, 64:192]), inc=ch1e)
                P.wait("vector", ch1e, 1)
                P.emit("vector", lambda e: e.tensor_tensor(out=adtT[:], in0=dtT[:], in1=abf[:],
                                                           op=OP.mult), inc=ch1v)
                P.wait("tensor", ch1v, 1)
                P.emit("tensor", lambda e: e.matmul(pA[:, 0:32], lhsT=P.cutonesb[:], rhs=adtT[:],
                                                    start=True, stop=True))
                P.emit("tensor", lambda e: e.matmul(pA[:, 32:64], lhsT=P.cltstrictb[:],
                                                    rhs=adtT[:], start=True, stop=True), inc=ch1)
                P.wait("scalar", ch1, 2)
                P.emit("scalar", lambda e: e.activation(out=odt[:], in_=pA[:, 0:32], func=AF.Exp))
                P.emit("scalar", lambda e: e.activation(out=dect[:], in_=pA[:, 32:64], func=AF.Exp))
                P.emit("scalar", lambda e: e.copy(out=acst[:], in_=pA[:, 0:32]), inc=ch1e)
                P.wait("tensor", ch1e, 2)
                P.emit("tensor", lambda e: e.transpose(pTf[0:32, 0:128], in_=acst[:],
                                                       identity=P.cid128f[:]), inc=ch1)
                P.wait("scalar", ch1, 3)
                if cg == 0:
                    P.dump("acst0", acst[:], [128, 32])
                P.emit("scalar", lambda e: e.copy(out=acsf[:], in_=pTf[0:32, 0:128]))
                P.emit("scalar", lambda e: e.activation(out=nacsf[:], in_=pTf[0:32, 0:128],
                                                        func=AF.Copy, scale=-1.0), inc=ch1e)
                for b2 in range(2):
                    base = 1024 * (1 - b2)
                    for i in range(8):
                        xi = b2 * 8 + i
                        P.emit("tensor", lambda e, xi=xi, i=i, base=base, xo=xo: e.transpose(
                            pT[:, base + i * 128:base + (i + 1) * 128],
                            in_=xbc[:, xi * (SEG + 3) + xo:xi * (SEG + 3) + xo + 128],
                            identity=P.cid128b[:]),
                            inc=ch1 if i == 7 else None)
                    P.wait("scalar", ch1, 4 + b2)
                    for i in range(8):
                        xi = b2 * 8 + i
                        P.emit("scalar", lambda e, xi=xi, i=i, base=base: e.copy(
                            out=xtok[:, xi * 128:(xi + 1) * 128],
                            in_=pT[:, base + i * 128:base + (i + 1) * 128]),
                            inc=ch1e if i == 7 else None)
                P.barrier()
            with ExitStack() as ps2:
                pSc = psum(nc, ps2, "pSc", [128, 128])
                ch2 = P.chp(8)
                P.emit("vector", lambda e: e.tensor_tensor(
                    out=xdt[:].rearrange("p (h q) -> p h q", h=NH),
                    in0=xtok[:].rearrange("p (h q) -> p h q", h=NH),
                    in1=dtT[:].unsqueeze(2).broadcast_to((128, NH, HP)), op=OP.mult))
                P.drain("vector")
                P.emit("vector", lambda e: e.tensor_tensor(
                    out=xdd[:].rearrange("p (h q) -> p h q", h=NH),
                    in0=xdt[:].rearrange("p (h q) -> p h q", h=NH),
                    in1=dect[:].unsqueeze(2).broadcast_to((128, NH, HP)), op=OP.mult))
                P.emit("vector", lambda e, cg=cg: e.tensor_scalar(
                    out=fz[:], in0=P.cblkpb[:], scalar1=ealc[:, cg:cg + 1], scalar2=None,
                    op0=OP.mult), inc=ch2)
                P.emit("tensor", lambda e, Bsl=Bsl, Csl=Csl: e.matmul(
                    pSc[:], lhsT=Bsl, rhs=Csl, start=True, stop=True), inc=ch2)
                P.wait("scalar", ch2, 2)
                P.emit("scalar", lambda e: e.copy(out=scb[:], in_=pSc[:]))
                P.emit("scalar", lambda e: e.copy(out=psb_bf[:], in_=p_sb[:]), inc=ch2)
                P.barrier()
            with ExitStack() as ps3:
                pYo = psum(nc, ps3, "pYo", [128, D_IN])
                ch3 = P.chp(9)
                for j in range(4):
                    P.emit("tensor", lambda e, j=j, Csl=Csl: e.matmul(
                        pYo[:, j * 512:(j + 1) * 512], lhsT=Csl,
                        rhs=psb_bf[:, j * 512:(j + 1) * 512],
                        start=True, stop=True), inc=ch3 if j == 3 else None)
                P.wait("vector", ch3, 1)
                P.emit("vector", lambda e: e.tensor_tensor(
                    out=ysc[:].rearrange("p (h q) -> p h q", h=NH),
                    in0=pYo[:].rearrange("p (h q) -> p h q", h=NH),
                    in1=odt[:].unsqueeze(2).broadcast_to((128, NH, HP)), op=OP.mult), inc=ch3)
                P.barrier()
            with ExitStack() as ps4:
                pD = psum(nc, ps4, "pD", [128, 1024])
                pY = psum(nc, ps4, "pY", [128, D_IN])
                ch4m = P.chp(10); ch4x = P.chp(11); ch4g = P.chp(12); ch4d = P.chp(13); ch4y = P.chp(14)
                ch4f = P.chp(15)
                for hg in range(8):
                    dpar = hg % 2
                    if hg >= 1:
                        P.wait("vector", ch4m, hg)
                    P.emit("vector", lambda e, hg=hg: e.tensor_tensor(
                        out=fexp_hg[:].rearrange("k (h l) -> k h l", h=4),
                        in0=acsf[:].unsqueeze(1).broadcast_to((32, 4, 128)),
                        in1=P.cblkf[:, hg * 512:(hg + 1) * 512].rearrange("k (h l) -> k h l", h=4),
                        op=OP.mult), inc=ch4f)
                    P.wait("tensor", ch4f, hg + 1)
                    if hg >= 2:
                        P.wait("tensor", ch4x, hg - 1)
                    P.emit("tensor", lambda e, hg=hg, dpar=dpar: e.matmul(
                        pD[:, dpar * 512:(dpar + 1) * 512], lhsT=P.cones128f[0:32, :],
                        rhs=fexp_hg[:], start=True, stop=False), inc=ch4m)
                    P.emit("tensor", lambda e, hg=hg, dpar=dpar: e.matmul(
                        pD[:, dpar * 512:(dpar + 1) * 512], lhsT=nacsf[:],
                        rhs=P.cblkf[:, hg * 512:(hg + 1) * 512], start=False, stop=False))
                    P.emit("tensor", lambda e, hg=hg, dpar=dpar: e.matmul(
                        pD[:, dpar * 512:(dpar + 1) * 512], lhsT=P.cid128f[:],
                        rhs=P.cneglt4f[:], start=False, stop=True), inc=ch4m)
                    P.wait("scalar", ch4m, 2 * (hg + 1))
                    if hg >= 2:
                        P.wait("scalar", ch4g, hg - 1)
                    P.emit("scalar", lambda e, dpar=dpar: e.activation(
                        out=eg[:, dpar * 512:(dpar + 1) * 512],
                        in_=pD[:, dpar * 512:(dpar + 1) * 512], func=AF.Exp), inc=ch4x)
                    P.wait("vector", ch4x, hg + 1)
                    if hg >= 2:
                        P.wait("vector", ch4d, hg - 1)
                    P.emit("vector", lambda e, dpar=dpar: e.tensor_tensor(
                        out=eg[:, (2 + dpar) * 512:(3 + dpar) * 512].rearrange("p (h l) -> p h l", h=4),
                        in0=eg[:, dpar * 512:(dpar + 1) * 512].rearrange("p (h l) -> p h l", h=4),
                        in1=scb[:].unsqueeze(1).broadcast_to((128, 4, 128)),
                        op=OP.mult), inc=ch4g)
                    P.wait("tensor", ch4g, hg + 1)
                    for hh in range(4):
                        h = hg * 4 + hh
                        P.emit("tensor", lambda e, h=h, hh=hh, dpar=dpar: e.matmul(
                            pY[:, h * 64:(h + 1) * 64],
                            lhsT=eg[:, (2 + dpar) * 512 + hh * 128:(2 + dpar) * 512 + (hh + 1) * 128],
                            rhs=xdt[:, h * 64:(h + 1) * 64], start=True, stop=True),
                            inc=ch4d if hh == 3 else None)
                P.wait("vector", ch4d, 8)
                P.emit("vector", lambda e: e.tensor_tensor(out=yf[:], in0=ysc[:], in1=pY[:],
                                                           op=OP.add), inc=ch4y)
                P.barrier()
                if cg in (0, 1):
                    P.dump(f"yf{cg}", yf[:], [128, D_IN], BF16)
            with ExitStack() as ps5:
                pDz = psum(nc, ps5, "pDz", [128, D_IN])
                pS = psum(nc, ps5, "pS", [128, D_IN])
                ch5 = P.chp(16)
                for j in range(4):
                    P.emit("tensor", lambda e, j=j: e.matmul(
                        pS[:, j * 512:(j + 1) * 512], lhsT=btok[:],
                        rhs=xdd[:, j * 512:(j + 1) * 512], start=True, stop=True))
                for j in range(4):
                    P.emit("tensor", lambda e, j=j: e.matmul(
                        pDz[:, j * 512:(j + 1) * 512], lhsT=P.cones128b[0:32, :],
                        rhs=fz[:, j * 512:(j + 1) * 512], start=True, stop=True),
                        inc=ch5 if j == 3 else None)
                P.wait("vector", ch5, 1)
                P.emit("vector", lambda e: e.tensor_tensor(out=ysc[:], in0=p_sb[:], in1=pDz[:],
                                                           op=OP.mult))
                P.drain("vector")
                P.emit("vector", lambda e: e.tensor_tensor(out=p_sb[:], in0=ysc[:], in1=pS[:],
                                                           op=OP.add), inc=ch5)
                P.barrier()
                if cg == 0:
                    P.dump("p0", p_sb[:], [128, NH * HP])
            with ExitStack() as ps6:
                pTg = psum(nc, ps6, "pTg", [128, D_IN], BF16)
                ch6 = P.chp(17); ch6e = P.chp(18)
                P.dma("sync", zbuf[:], d_zs[cg * 128:(cg + 1) * 128, :], inc=ch6)
                P.emit("scalar", lambda e: e.activation(out=ygb[:], in_=yf[:], func=AF.Copy,
                                                        accum_out=gst[:, 0:1]))
                P.drain("scalar")
                P.emit("scalar", lambda e: e.activation(out=ygf[:], in_=yf[:], func=AF.Square,
                                                        accum_out=gst[:, 1:2]))
                P.drain("scalar")
                P.emit("scalar", lambda e: e.activation(out=gst[:, 2:3], in_=gst[:, 0:1],
                                                        func=AF.Copy, scale=1.0 / D_IN))
                P.emit("scalar", lambda e: e.activation(out=gst[:, 3:4], in_=gst[:, 0:1],
                                                        func=AF.Square, scale=1.0 / D_IN))
                P.emit("scalar", lambda e: e.activation(out=gst[:, 4:5], in_=gst[:, 1:2],
                                                        func=AF.Copy, scale=1.0 / D_IN), inc=ch6e)
                P.wait("vector", ch6e, 1)
                P.emit("vector", lambda e: e.tensor_tensor(out=gst[:, 5:6], in0=gst[:, 4:5],
                                                           in1=gst[:, 3:4], op=OP.subtract), inc=ch6e)
                P.wait("scalar", ch6e, 2)
                P.emit("scalar", lambda e: e.activation(out=gst[:, 6:7], in_=gst[:, 5:6],
                                                        func=AF.Ln, bias=P.scal("eps_ln")))
                P.drain("scalar")
                P.emit("scalar", lambda e: e.activation(out=gst[:, 7:8], in_=gst[:, 6:7],
                                                        func=AF.Exp, scale=-0.5), inc=ch6e)
                P.wait("vector", ch6e, 3)
                P.wait("vector", ch6, 16)
                P.emit("vector", lambda e: e.tensor_scalar(out=ysc[:], in0=yf[:],
                                                           scalar1=gst[:, 2:3], scalar2=gst[:, 7:8],
                                                           op0=OP.subtract, op1=OP.mult))
                P.drain("vector")
                P.emit("vector", lambda e: e.tensor_tensor(out=ygf[:], in0=ysc[:], in1=snormw[:],
                                                           op=OP.mult))
                P.drain("vector")
                P.emit("vector", lambda e: e.tensor_tensor(out=ygb[:], in0=ygf[:], in1=zbuf[:],
                                                           op=OP.mult), inc=ch6)
                P.wait("tensor", ch6, 17)
                for i in range(16):
                    P.emit("tensor", lambda e, i=i: e.transpose(
                        pTg[:, i * 128:(i + 1) * 128], in_=ygb[:, i * 128:(i + 1) * 128],
                        identity=P.cid128b[:]), inc=ch6 if i == 15 else None)
                P.wait("scalar", ch6, 18)
                P.emit("scalar", lambda e: e.copy(out=ygf[:], in_=pTg[:]), inc=ch6e)
                P.wait("sync", ch6e, 4)
                P.dma("sync", d_ygf[cg], ygf[:], inc=ch6)
                P.barrier()

    sB45.close()
    with ExitStack() as s6:
        wo_sb = sbuf(nc, s6, "wo_sb", [128, 16 * D], BF16)
        ch_w = P.chp(0)
        for k in range(16):
            P.dma("sync", wo_sb[:, k * D:(k + 1) * D], woutb[k * 128:(k + 1) * 128, :], inc=ch_w)
        ygl = sbuf(nc, s6, "ygl", [128, 2 * D_IN], BF16)
        sout = sbuf(nc, s6, "sout", [128, 2 * D], BF16)
        pO = psum(nc, s6, "pO", [128, 2 * D])
        P.wait("tensor", ch_w)
        ch_l = [P.chp(1), P.chp(6)]; ch_m = P.chp(2); ch_e = P.chp(3); ch_s = [P.chp(4), P.chp(5)]
        for t in range(NT):
            par = t % 2
            if t >= 2:
                P.wait("sync", ch_m, t - 1)
            P.dma("sync", ygl[:, par * D_IN:(par + 1) * D_IN], d_ygf[t], inc=ch_l[par])
            P.wait("tensor", ch_l[par], 16 * (t // 2 + 1))
            if t >= 2:
                P.wait("tensor", ch_e, t - 1)
            for k in range(16):
                for j in range(2):
                    P.emit("tensor", lambda e, t=t, par=par, k=k, j=j: e.matmul(
                        pO[:, par * D + j * 512:par * D + (j + 1) * 512],
                        lhsT=ygl[:, par * D_IN + k * 128:par * D_IN + (k + 1) * 128],
                        rhs=wo_sb[:, k * D + j * 512:k * D + (j + 1) * 512],
                        start=(k == 0), stop=(k == 15)),
                        inc=ch_m if (k == 15 and j == 1) else None)
            P.wait("scalar", ch_m, t + 1)
            if t >= 2:
                P.wait("scalar", ch_s[par], 16 * (t // 2))
            P.emit("scalar", lambda e, par=par: e.copy(out=sout[:, par * D:(par + 1) * D],
                                                       in_=pO[:, par * D:(par + 1) * D]), inc=ch_e)
            P.wait("sync", ch_e, t + 1)
            P.dma("sync", d_s[t * 128:(t + 1) * 128, :], sout[:, par * D:(par + 1) * D],
                  inc=ch_s[par])
        P.barrier()


def mamba_host_prep(ins, prefix="L0_attn_"):
    return {
        "win": np.ascontiguousarray(ins[prefix + "W_in"]),
        "wout": np.ascontiguousarray(ins[prefix + "W_out"]),
        "convw_t": np.ascontiguousarray(ins[prefix + "conv_w"].reshape(18, 128, 4)
                                        .transpose(1, 0, 2).reshape(128, 72)),
        "convb_t": np.ascontiguousarray(ins[prefix + "conv_b"].reshape(18, 128).T),
        "dtb": np.ascontiguousarray(ins[prefix + "dt_bias"].reshape(32, 1)),
        "alog": np.ascontiguousarray(ins[prefix + "A_log"].reshape(32, 1)),
        "normw_r": repl_rows(ins[prefix + "norm_w"]),
    }

MAMBA_CONSTS = ["c_id128b", "c_id128f", "c_utonesb", "c_ltstrictb", "c_ones128b",
                "c_ones128f", "c_blkf", "c_neglt4f", "c_blkpb"]

def build_stage_b():
    nc = bass.Bass("TRN2", target_bir_lowering=False)
    xnf = nc.dram_tensor("xn_feat", [D, T], BF16, kind="ExternalInput").ap()
    win = nc.dram_tensor("win", [D, 4384], F32, kind="ExternalInput").ap()
    wout = nc.dram_tensor("wout", [D_IN, D], F32, kind="ExternalInput").ap()
    convw_t = nc.dram_tensor("convw_t", [128, 72], F32, kind="ExternalInput").ap()
    convb_t = nc.dram_tensor("convb_t", [128, 18], F32, kind="ExternalInput").ap()
    dtb = nc.dram_tensor("dtb", [32, 1], F32, kind="ExternalInput").ap()
    alog = nc.dram_tensor("alog", [32, 1], F32, kind="ExternalInput").ap()
    normw_r = nc.dram_tensor("normw_r", [128, D_IN], F32, kind="ExternalInput").ap()
    d_s = nc.dram_tensor("s_out", [T, D], BF16, kind="ExternalOutput").ap()
    d_zs = nc.dram_tensor("d_zs", [T, D_IN], BF16, kind="Internal").ap()

    with ExitStack() as stack:
        P = Prog(nc, stack)
        load_consts(P, stack, MAMBA_CONSTS)
        xn_feat = sbuf(nc, stack, "xn_feat_sb", [128, ND * T], BF16)
        ch0 = P.ch()
        for dd in range(ND):
            P.dma("sync", xn_feat[:, dd * T:(dd + 1) * T], xnf[dd * 128:(dd + 1) * 128, :], inc=ch0)
        for e in Prog.ENGS:
            P.wait(e, ch0)
        emit_mamba(P, stack, xn_feat, win, convw_t, convb_t, dtb, alog, normw_r, wout,
                   d_zs, d_s)
        block = stack.enter_context(nc.Block())
        P.finalize(block)
    return nc



def emit_attn_l0_flash(P, stack, h_feat, out_feat, wqb, wkb, wvb, qs_col, ks_col):
    nc = P.nc
    with ExitStack() as sA:
        qa = [sbuf(nc, sA, f"qa{i}", [65, T], BF16) for i in range(2)]
        ka = [sbuf(nc, sA, f"ka{i}", [65, T], BF16) for i in range(2)]
        vtok = sbuf(nc, sA, "vtok", [128, NT * 130], BF16)
        qf = sbuf(nc, sA, "qf", [128, T], BF16)
        vst = sbuf(nc, sA, "vst", [128, 1024], BF16)
        sqc = sbuf(nc, sA, "sqc", [128, 512], BF16)
        wcol = sbuf(nc, sA, "wcol3", [128, 3 * 8 * 128], BF16)
        sqs = sbuf(nc, sA, "sqs", [128, 2])
        nmall = sbuf(nc, sA, "nmall", [128, 64])
        nmt = sbuf(nc, sA, "nmt", [32, 128], BF16)
        d_nm = nc.dram_tensor(_uniq("d_nm"), [32, 128], BF16, kind="Internal").ap()
        rp = sbuf(nc, sA, "rp", [65, 1024])
        rpb = sbuf(nc, sA, "rpb", [65, 1024], BF16)
        osb = sbuf(nc, sA, "osb", [64, 1024], BF16)
        ptb = [sbuf(nc, sA, f"ptb{i}", [128, 1024], BF16) for i in range(2)]
        ptd = sbuf(nc, sA, "ptd", [128, 128], BF16)

        for dt_i in range(ND):
            chw = P.chp(1); chm = P.chp(2); che = P.chp(3); chv = P.chp(4)
            for wi, w in enumerate((wqb, wkb, wvb)):
                for k in range(ND):
                    P.dma("sync", wcol[:, (wi * 8 + k) * 128:(wi * 8 + k + 1) * 128],
                          w[k * 128:(k + 1) * 128, dt_i * 128:(dt_i + 1) * 128], inc=chw)
            P.dma("sync", sqs[:, 0:1], qs_col[dt_i * 128:(dt_i + 1) * 128, :], inc=chw)
            P.dma("sync", sqs[:, 1:2], ks_col[dt_i * 128:(dt_i + 1) * 128, :], inc=chw)
            P.emit("vector", lambda e: e.memset(vtok[:], 1.0))
            P.emit("vector", lambda e, h2i=0: e.memset(ka[0][64:65, :], 1.0))
            P.emit("vector", lambda e, h2i=1: e.memset(ka[1][64:65, :], 1.0), inc=chv)
            P.barrier()
            for pss in range(2):
                with ExitStack() as sP:
                    pqkv = psum(nc, sP, "pqkv", [128, 2 * 512])
                    pT2 = psum(nc, sP, "pT2", [128, 1024], BF16)
                    nwi = 2 if pss == 0 else 1
                    wis = (0, 2) if pss == 0 else (1,)
                    for tc in range(8):
                        if tc >= 1:
                            P.wait("tensor", che, 2 * tc - 1 if pss == 0 else 16 + tc)
                        for wj, wi in enumerate(wis):
                            for k in range(ND):
                                P.emit("tensor", lambda e, tc=tc, wi=wi, wj=wj, k=k: e.matmul(
                                    pqkv[:, wj * 512:(wj + 1) * 512],
                                    lhsT=wcol[:, (wi * 8 + k) * 128:(wi * 8 + k + 1) * 128],
                                    rhs=h_feat[:, k * T + tc * 512:k * T + (tc + 1) * 512],
                                    start=(k == 0), stop=(k == ND - 1)),
                                    inc=chm if k == ND - 1 else None)
                        nmm = (3 * tc + 2) if pss == 0 else (24 + tc + 1)
                        P.wait("scalar", chm, nmm)
                        P.emit("scalar", lambda e, tc=tc: e.copy(
                            out=qf[:, tc * 512:(tc + 1) * 512], in_=pqkv[:, 0:512]),
                            inc=che if pss == 1 else None)
                        if pss == 0:
                            P.emit("scalar", lambda e, tc=tc, par=tc % 2: e.copy(
                                out=vst[:, par * 512:(par + 1) * 512], in_=pqkv[:, 512:1024]),
                                inc=che)
                            P.wait("tensor", che, 2 * tc + 1)
                            for j in range(4):
                                P.emit("tensor", lambda e, tc=tc, j=j, par=tc % 2: e.transpose(
                                    pT2[:, par * 512 + j * 128:par * 512 + (j + 1) * 128],
                                    in_=vst[:, par * 512 + j * 128:par * 512 + (j + 1) * 128],
                                    identity=P.cid128b[:]),
                                    inc=chm if j == 3 else None)
                            P.wait("scalar", chm, nmm + 1)
                            for j in range(4):
                                kj = tc * 4 + j
                                P.emit("scalar", lambda e, kj=kj, j=j, par=tc % 2: e.copy(
                                    out=vtok[:, kj * 130:kj * 130 + 64],
                                    in_=pT2[:, par * 512 + j * 128:par * 512 + j * 128 + 64]))
                                P.emit("scalar", lambda e, kj=kj, j=j, par=tc % 2: e.copy(
                                    out=vtok[:, kj * 130 + 65:kj * 130 + 129],
                                    in_=pT2[:, par * 512 + j * 128 + 64:par * 512 + (j + 1) * 128]),
                                    inc=che if j == 3 else None)
                    P.barrier()
                with ExitStack() as sL:
                    pR = psum(nc, sL, "pR", [2, 512])
                    pB = psum(nc, sL, "pB", [128, 2 * 512])
                    rrc = sbuf(nc, sL, "rrc", [2, 512])
                    rrc2 = sbuf(nc, sL, "rrc2", [2, 512])
                    rrcb = sbuf(nc, sL, "rrcb", [2, 512], BF16)
                    chl = P.chp(5); chn = P.chp(6)
                    nl = 0
                    nn_ = 0
                    dsts = qa if pss == 0 else ka
                    for tc in range(8):
                        par = tc % 2
                        P.emit("scalar", lambda e, tc=tc: e.activation(
                            out=sqc[:], in_=qf[:, tc * 512:(tc + 1) * 512], func=AF.Square),
                            inc=chl)
                        nl += 1
                        P.wait("tensor", chl, nl)
                        P.emit("tensor", lambda e: e.matmul(pR[:], lhsT=P.cbd2b[:], rhs=sqc[:],
                                                            start=True, stop=True), inc=chl)
                        nl += 1
                        P.wait("scalar", chl, nl)
                        P.emit("scalar", lambda e: e.copy(out=rrc[:], in_=pR[:]))
                        P.drain("scalar")
                        P.emit("scalar", lambda e: e.activation(out=rrc2[:], in_=rrc[:],
                                                                func=AF.Ln,
                                                                bias=P.scal("eps_l2")[0:2, :]))
                        P.drain("scalar")
                        P.emit("scalar", lambda e: e.activation(out=rrcb[:], in_=rrc2[:],
                                                                func=AF.Exp, scale=-0.5), inc=chl)
                        nl += 1
                        P.wait("tensor", chl, nl)
                        if nn_ >= 2:
                            P.wait("tensor", chn, nn_ - 1)
                        P.emit("tensor", lambda e, par=par: e.matmul(
                            pB[:, par * 512:(par + 1) * 512], lhsT=P.cbd2tb[:], rhs=rrcb[:],
                            start=True, stop=True), inc=chn)
                        nn_ += 1
                        P.wait("vector", chn, nn_)
                        sc_col = sqs[:, pss:pss + 1]
                        for hf in range(2):
                            P.emit("vector", lambda e, tc=tc, par=par, hf=hf,
                                   dsts=dsts, sc_col=sc_col: e.scalar_tensor_tensor(
                                out=dsts[hf][0:64, tc * 512:(tc + 1) * 512],
                                in0=qf[64 * hf:64 * hf + 64, tc * 512:(tc + 1) * 512],
                                scalar=sc_col[64 * hf:64 * hf + 64, :],
                                in1=pB[64 * hf:64 * hf + 64, par * 512:(par + 1) * 512],
                                op0=OP.mult, op1=OP.mult),
                                inc=chn if hf == 1 else None)
                        nn_ += 1
                    P.barrier()
            with ExitStack() as s1:
                pS = psum(nc, s1, "pS", [128, T])
                ch1m = P.chp(5); ch1v = P.chp(6)
                for h2i in range(2):
                    for qi in range(NT):
                        if h2i * NT + qi >= 1:
                            P.wait("tensor", ch1v, h2i * NT + qi)
                        nch = (qi * 128 + 128 + 511) // 512
                        for j in range(nch):
                            n0 = j * 512
                            n1 = min((qi + 1) * 128, n0 + 512)
                            P.emit("tensor", lambda e, h2i=h2i, qi=qi, n0=n0, n1=n1: e.matmul(
                                pS[:, n0:n1], lhsT=qa[h2i][0:64, qi * 128:(qi + 1) * 128],
                                rhs=ka[h2i][0:64, n0:n1], start=True, stop=True),
                                inc=ch1m if j == nch - 1 else None)
                        P.wait("vector", ch1m, h2i * NT + qi + 1)
                        P.emit("vector", lambda e, h2i=h2i, qi=qi: e.reduce_max(
                            out=nmall[:, h2i * 32 + qi:h2i * 32 + qi + 1],
                            in_=pS[:, 0:(qi + 1) * 128], axis=AX, negate=True), inc=ch1v)
                P.barrier()
            with ExitStack() as s1b:
                pTf = psum(nc, s1b, "pTf2", [128, 128])
                ch1b = P.chp(5)
                n1b = 0
                for h2i in range(2):
                    P.emit("tensor", lambda e, h2i=h2i: e.transpose(
                        pTf[0:32, 0:128], in_=nmall[:, h2i * 32:(h2i + 1) * 32],
                        identity=P.cid128f[:]), inc=ch1b)
                    n1b += 1
                    P.wait("scalar", ch1b, n1b)
                    P.emit("scalar", lambda e: e.copy(out=nmt[:], in_=pTf[0:32, 0:128]), inc=ch1b)
                    n1b += 1
                    P.wait("sync", ch1b, n1b)
                    P.dma("sync", d_nm, nmt[:], inc=ch1b)
                    n1b += 16
                    P.wait("sync", ch1b, n1b)
                    P.dma("sync", qa[h2i][64:65, :],
                          d_nm.rearrange("a b -> (a b)").unsqueeze(0), inc=ch1b)
                    n1b += 16
                    P.wait("tensor", ch1b, n1b)
                    P.wait("scalar", ch1b, n1b)
                P.barrier()
            for h2i in range(2):
                for ic in range(4):
                    kjmax = (ic + 1) * 8 - 1
                    with ExitStack() as s2o:
                        pOut = psum(nc, s2o, "pOut", [65, 1024])
                        with ExitStack() as s2:
                            pST = psum(nc, s2, "pST", [128, 2 * 1024])
                            chP = P.chp(5); chPV = P.chp(6); chE = P.chp(7)
                            for kj in range(kjmax + 1):
                                par = kj % 2
                                if kj >= 2:
                                    P.wait("tensor", chE, kj - 1)
                                for j2 in range(2):
                                    P.emit("tensor", lambda e, h2i=h2i, ic=ic, kj=kj, par=par, j2=j2: e.matmul(
                                        pST[:, par * 1024 + j2 * 512:par * 1024 + (j2 + 1) * 512],
                                        lhsT=ka[h2i][:, kj * 128:(kj + 1) * 128],
                                        rhs=qa[h2i][:, ic * 1024 + j2 * 512:ic * 1024 + (j2 + 1) * 512],
                                        start=True, stop=True),
                                        inc=chP if j2 == 1 else None)
                                P.wait("scalar", chP, 3 * kj + 1)
                                if kj >= 2:
                                    P.wait("scalar", chPV, kj - 1)
                                offs = (kj - ic * 8) * 128
                                if offs < 0:
                                    P.emit("scalar", lambda e, par=par: e.activation(
                                        out=ptb[par][:], in_=pST[:, par * 1024:(par + 1) * 1024],
                                        func=AF.Exp, scale=10.0), inc=chP, k=2)
                                else:
                                    if offs > 0:
                                        P.emit("vector", lambda e, par=par, offs=offs: e.memset(
                                            ptb[par][:, 0:offs], 0.0))
                                    if offs + 128 < 1024:
                                        P.emit("scalar", lambda e, par=par, offs=offs: e.activation(
                                            out=ptb[par][:, offs + 128:1024],
                                            in_=pST[:, par * 1024 + offs + 128:par * 1024 + 1024],
                                            func=AF.Exp, scale=10.0))
                                    P.emit("scalar", lambda e, par=par, offs=offs: e.activation(
                                        out=ptd[:], in_=pST[:, par * 1024 + offs:par * 1024 + offs + 128],
                                        func=AF.Exp, scale=10.0), inc=chP)
                                    P.wait("vector", chP, 3 * kj + 2)
                                    P.emit("vector", lambda e, par=par, offs=offs: e.tensor_tensor(
                                        out=ptb[par][:, offs:offs + 128], in0=ptd[:],
                                        in1=P.cutonesb[:], op=OP.mult), inc=chP)
                                P.wait("tensor", chP, 3 * kj + 3)
                                for j2 in range(2):
                                    P.emit("tensor", lambda e, h2i=h2i, kj=kj, par=par, j2=j2,
                                           pOut=pOut, kjmax=kjmax: e.matmul(
                                        pOut[:, j2 * 512:(j2 + 1) * 512],
                                        lhsT=vtok[:, kj * 130 + h2i * 65:kj * 130 + (h2i + 1) * 65],
                                        rhs=ptb[par][:, j2 * 512:(j2 + 1) * 512],
                                        start=(kj == 0), stop=(kj == kjmax)),
                                        inc=(chPV if j2 == 0 else chE) if True else None)
                            P.barrier()
                        with ExitStack() as s3:
                            pB2 = psum(nc, s3, "pB2", [64, 1024])
                            chN = P.chp(5)
                            P.emit("vector", lambda e, pOut=pOut: e.reciprocal(
                                out=rp[64:65, 0:1024], in_=pOut[64:65, :]), inc=chN)
                            P.wait("scalar", chN, 1)
                            P.emit("scalar", lambda e: e.copy(out=rpb[64:65, 0:1024],
                                                              in_=rp[64:65, 0:1024]))
                            P.emit("scalar", lambda e, pOut=pOut: e.copy(out=osb[:],
                                                                         in_=pOut[0:64, :]), inc=chN)
                            P.wait("tensor", chN, 2)
                            for j2 in range(2):
                                P.emit("tensor", lambda e, j2=j2: e.matmul(
                                    pB2[:, j2 * 512:(j2 + 1) * 512],
                                    lhsT=P.cones128b[64:65, 0:64],
                                    rhs=rpb[64:65, j2 * 512:(j2 + 1) * 512],
                                    start=True, stop=True), inc=chN if j2 == 1 else None)
                            P.wait("vector", chN, 3)
                            P.emit("vector", lambda e, dt_i=dt_i, h2i=h2i, ic=ic: e.tensor_tensor(
                                out=out_feat[64 * h2i:64 * (h2i + 1),
                                             dt_i * T + ic * 1024:dt_i * T + (ic + 1) * 1024],
                                in0=osb[:], in1=pB2[:], op=OP.mult), inc=chN)
                            P.barrier()
    P.barrier()

def emit_attn_l0_tail(P, stack, out_feat, wob, lngc, lnbc, gwbd, gb, d_h2, d_gates):
    nc = P.nc
    with ExitStack() as sW:
        af = sbuf(nc, sW, "af", [128, ND * T], BF16)
        wocol = sbuf(nc, sW, "wocol", [128, 8 * 128], BF16)
        sqc2 = sbuf(nc, sW, "sqc2", [128, 512], BF16)
        murow = sbuf(nc, sW, "murow", [1, T], BF16)
        rstrow = sbuf(nc, sW, "rstrow", [1, T], BF16)
        rc1 = sbuf(nc, sW, "rc1", [1, 512])
        rc2 = sbuf(nc, sW, "rc2", [1, 512])
        rc3 = sbuf(nc, sW, "rc3", [1, 512])
        slng = sbuf(nc, sW, "slng", [128, 8])
        slnb = sbuf(nc, sW, "slnb", [128, 8])
        gwb = sbuf(nc, sW, "gwb", [128, 8 * 8], BF16)
        gbb = sbuf(nc, sW, "gbb", [1, 8], BF16)
        h2tc = sbuf(nc, sW, "h2tc", [128, 8 * 512], BF16)
        tsc1 = sbuf(nc, sW, "tsc1", [128, 512])
        tsc2 = sbuf(nc, sW, "tsc2", [128, 512])
        gmx = sbuf(nc, sW, "gmx", [128, 4])
        gts = sbuf(nc, sW, "gts", [128, 32])
        gsm = sbuf(nc, sW, "gsm", [128, 4])
        gr = sbuf(nc, sW, "gr", [128, 4])
        gout = sbuf(nc, sW, "gout", [128, 32])
        gcols = sbuf(nc, sW, "gcols", [128, 4])
        chW = P.chp(1)
        P.dma("sync", slng[:], lngc, inc=chW)
        P.dma("sync", slnb[:], lnbc, inc=chW)
        gbf = sbuf(nc, sW, "gbf", [1, 8])
        for k in range(ND):
            P.dma("sync", gwb[:, k * 8:(k + 1) * 8], gwbd[k * 128:(k + 1) * 128, :], inc=chW)
        P.dma("sync", gbf[:], gb, inc=chW)
        P.wait("scalar", chW)
        P.emit("scalar", lambda e: e.copy(out=gbb[:], in_=gbf[:]))
        P.barrier()
        with ExitStack() as sW1:
            pW = psum(nc, sW1, "pW", [128, 2 * 512])
            chm = P.chp(2); che = P.chp(3); chw2 = P.chp(4)
            for m in range(ND):
                if m >= 1:
                    P.wait("sync", chm, m * 8)
                for k in range(ND):
                    P.dma("sync", wocol[:, k * 128:(k + 1) * 128],
                          wob[k * 128:(k + 1) * 128, m * 128:(m + 1) * 128], inc=chw2)
                P.wait("tensor", chw2, 16 * 8 * (m + 1))
                for tc in range(8):
                    par = tc % 2
                    if m * 8 + tc >= 2:
                        P.wait("tensor", che, m * 8 + tc - 1)
                    for k in range(ND):
                        P.emit("tensor", lambda e, m=m, tc=tc, k=k, par=par: e.matmul(
                            pW[:, par * 512:(par + 1) * 512],
                            lhsT=wocol[:, k * 128:(k + 1) * 128],
                            rhs=out_feat[:, k * T + tc * 512:k * T + (tc + 1) * 512],
                            start=(k == 0), stop=(k == ND - 1)),
                            inc=chm if k == ND - 1 else None)
                    P.wait("scalar", chm, m * 8 + tc + 1)
                    P.emit("scalar", lambda e, m=m, tc=tc, par=par: e.copy(
                        out=af[:, m * T + tc * 512:m * T + (tc + 1) * 512],
                        in_=pW[:, par * 512:(par + 1) * 512]), inc=che)
            P.barrier()
        with ExitStack() as sW2:
            pS0 = psum(nc, sW2, "pS0", [1, 512])
            pS1 = psum(nc, sW2, "pS1", [1, 512])
            chs = P.chp(2)
            ns = 0
            for tc in range(8):
                for k in range(ND):
                    P.emit("tensor", lambda e, k=k, tc=tc: e.matmul(
                        pS0[:], lhsT=P.cones128b[:, 0:1],
                        rhs=af[:, k * T + tc * 512:k * T + (tc + 1) * 512],
                        start=(k == 0), stop=(k == ND - 1)))
                for k in range(ND):
                    if k >= 1:
                        P.wait("scalar", chs, ns)
                    P.emit("scalar", lambda e, k=k, tc=tc: e.activation(
                        out=sqc2[:], in_=af[:, k * T + tc * 512:k * T + (tc + 1) * 512],
                        func=AF.Square), inc=chs)
                    ns += 1
                    P.wait("tensor", chs, ns)
                    P.emit("tensor", lambda e, k=k: e.matmul(
                        pS1[:], lhsT=P.cones128b[:, 1:2], rhs=sqc2[:],
                        start=(k == 0), stop=(k == ND - 1)), inc=chs)
                    ns += 1
                P.wait("scalar", chs, ns)
                P.emit("scalar", lambda e, tc=tc: e.activation(out=murow[:, tc * 512:(tc + 1) * 512],
                                                               in_=pS0[:], func=AF.Copy,
                                                               scale=1.0 / D))
                P.emit("scalar", lambda e: e.activation(out=rc1[:], in_=pS0[:], func=AF.Square,
                                                        scale=1.0 / D))
                P.emit("scalar", lambda e: e.activation(out=rc2[:], in_=pS1[:], func=AF.Copy,
                                                        scale=1.0 / D), inc=chs)
                ns += 1
                P.wait("vector", chs, ns)
                P.emit("vector", lambda e: e.tensor_tensor(out=rc3[:], in0=rc2[:], in1=rc1[:],
                                                           op=OP.subtract), inc=chs)
                ns += 1
                P.wait("scalar", chs, ns)
                P.emit("scalar", lambda e: e.activation(out=rc1[:], in_=rc3[:], func=AF.Ln,
                                                        bias=P.scal("eps_ln")[0:1, :]))
                P.drain("scalar")
                P.emit("scalar", lambda e, tc=tc: e.activation(
                    out=rstrow[:, tc * 512:(tc + 1) * 512], in_=rc1[:], func=AF.Exp,
                    scale=-0.5), inc=chs)
                ns += 1
                P.wait("tensor", chs, ns)
            P.barrier()
        with ExitStack() as sW3:
            pMu = psum(nc, sW3, "pMu", [128, 512])
            pRs = psum(nc, sW3, "pRs", [128, 512])
            pG = psum(nc, sW3, "pG", [128, 32])
            chh = P.chp(2); chd = P.chp(4)
            nh = 0
            for tc in range(8):
                P.emit("tensor", lambda e, tc=tc: e.matmul(
                    pMu[:], lhsT=P.cones128b[0:1, :], rhs=murow[:, tc * 512:(tc + 1) * 512],
                    start=True, stop=True))
                P.emit("tensor", lambda e, tc=tc: e.matmul(
                    pRs[:], lhsT=P.cones128b[0:1, :], rhs=rstrow[:, tc * 512:(tc + 1) * 512],
                    start=True, stop=True), inc=chh)
                nh += 1
                P.wait("vector", chh, nh)
                for m in range(ND):
                    P.emit("vector", lambda e, m=m, tc=tc: e.tensor_tensor(
                        out=tsc1[:], in0=af[:, m * T + tc * 512:m * T + (tc + 1) * 512],
                        in1=pMu[:], op=OP.subtract))
                    P.drain("vector")
                    P.emit("vector", lambda e, m=m: e.tensor_tensor(
                        out=tsc2[:], in0=tsc1[:], in1=pRs[:], op=OP.mult))
                    P.drain("vector")
                    P.emit("vector", lambda e, m=m: e.tensor_scalar(
                        out=h2tc[:, m * 512:(m + 1) * 512], in0=tsc2[:],
                        scalar1=slng[:, m:m + 1], scalar2=slnb[:, m:m + 1],
                        op0=OP.mult, op1=OP.add), inc=chh)
                    nh += 1
                P.wait("sync", chh, nh)
                for m in range(ND):
                    P.dma("sync", d_h2[m * 128:(m + 1) * 128, tc * 512:(tc + 1) * 512],
                          h2tc[:, m * 512:(m + 1) * 512], inc=chd)
                P.barrier()
                for tb in range(4):
                    for m in range(ND):
                        P.emit("tensor", lambda e, tb=tb, m=m: e.matmul(
                            pG[:, tb * 8:(tb + 1) * 8],
                            lhsT=h2tc[:, m * 512 + tb * 128:m * 512 + (tb + 1) * 128],
                            rhs=gwb[:, m * 8:(m + 1) * 8],
                            start=(m == 0), stop=False))
                    P.emit("tensor", lambda e, tb=tb: e.matmul(
                        pG[:, tb * 8:(tb + 1) * 8], lhsT=P.cones128b[0:1, :],
                        rhs=gbb[:], start=False, stop=True))
                P.barrier()
                for tb in range(4):
                    P.emit("vector", lambda e, tb=tb: e.reduce_max(
                        out=gmx[:, tb:tb + 1], in_=pG[:, tb * 8:(tb + 1) * 8], axis=AX,
                        negate=True))
                    P.drain("vector")
                P.barrier()
                for tb in range(4):
                    P.emit("scalar", lambda e, tb=tb: e.activation(
                        out=gts[:, tb * 8:(tb + 1) * 8], in_=pG[:, tb * 8:(tb + 1) * 8],
                        func=AF.Exp, bias=gmx[:, tb:tb + 1]))
                    P.drain("scalar")
                P.barrier()
                for tb in range(4):
                    P.emit("vector", lambda e, tb=tb: e.tensor_reduce(
                        out=gsm[:, tb:tb + 1], in_=gts[:, tb * 8:(tb + 1) * 8], axis=AX,
                        op=OP.add))
                    P.drain("vector")
                for tb in range(4):
                    P.emit("vector", lambda e, tb=tb: e.reciprocal(
                        out=gr[:, tb:tb + 1], in_=gsm[:, tb:tb + 1]))
                    P.drain("vector")
                for tb in range(4):
                    P.emit("vector", lambda e, tb=tb: e.tensor_scalar(
                        out=gout[:, tb * 8:(tb + 1) * 8], in0=gts[:, tb * 8:(tb + 1) * 8],
                        scalar1=gr[:, tb:tb + 1], scalar2=None, op0=OP.mult))
                    P.drain("vector")
                P.barrier()
                chgd = P.chp(5)
                for tb in range(4):
                    P.dma("sync", d_gates[tc * 512 + tb * 128:tc * 512 + (tb + 1) * 128, :],
                          gout[:, tb * 8:(tb + 1) * 8], inc=chgd)
                P.barrier()
    P.barrier()

ATTN_CONSTS = ["c_id128b", "c_id128f", "c_utonesb", "c_ones128b", "c_bd2b", "c_bd2tb"]

def attn_l0_host_prep(ins):
    return {
        "wq": np.ascontiguousarray(ins["L0_ffn_Wq"]),
        "wk": np.ascontiguousarray(ins["L0_ffn_Wk"]),
        "wv": np.ascontiguousarray(ins["L0_ffn_Wv"]),
        "wo": np.ascontiguousarray(ins["L0_ffn_Wo"]),
        "qs_col": np.ascontiguousarray(ins["L0_ffn_q_scale"].reshape(1024, 1)),
        "ks_col": np.ascontiguousarray(ins["L0_ffn_k_scale"].reshape(1024, 1)),
        "lngc": np.ascontiguousarray(ins["L0_ffn_ln_g"].reshape(8, 128).T),
        "lnbc": np.ascontiguousarray(ins["L0_ffn_ln_b"].reshape(8, 128).T),
        "gw": np.ascontiguousarray(ins["L0_ffn_gate_w"]),
        "gb": np.ascontiguousarray(ins["L0_ffn_gate_b"].reshape(1, 8)),
    }

def build_stage_c():
    nc = bass.Bass("TRN2", target_bir_lowering=False)
    hf = nc.dram_tensor("h_feat", [D, T], BF16, kind="ExternalInput").ap()
    wq = nc.dram_tensor("wq", [D, D], F32, kind="ExternalInput").ap()
    wk = nc.dram_tensor("wk", [D, D], F32, kind="ExternalInput").ap()
    wv = nc.dram_tensor("wv", [D, D], F32, kind="ExternalInput").ap()
    wo = nc.dram_tensor("wo", [D, D], F32, kind="ExternalInput").ap()
    qs = nc.dram_tensor("qs_col", [D, 1], F32, kind="ExternalInput").ap()
    ks = nc.dram_tensor("ks_col", [D, 1], F32, kind="ExternalInput").ap()
    lngc = nc.dram_tensor("lngc", [128, 8], F32, kind="ExternalInput").ap()
    lnbc = nc.dram_tensor("lnbc", [128, 8], F32, kind="ExternalInput").ap()
    gw = nc.dram_tensor("gw", [D, 8], F32, kind="ExternalInput").ap()
    gb = nc.dram_tensor("gb", [1, 8], F32, kind="ExternalInput").ap()
    d_h2 = nc.dram_tensor("h2_feat", [D, T], BF16, kind="ExternalOutput").ap()
    d_gates = nc.dram_tensor("gates", [T, 8], F32, kind="ExternalOutput").ap()
    with ExitStack() as stack:
        P = Prog(nc, stack)
        load_consts(P, stack, ATTN_CONSTS)
        wqb = nc.dram_tensor("wqb", [D, D], BF16, kind="Internal").ap()
        wkb = nc.dram_tensor("wkb", [D, D], BF16, kind="Internal").ap()
        wvb = nc.dram_tensor("wvb", [D, D], BF16, kind="Internal").ap()
        wob = nc.dram_tensor("wob", [D, D], BF16, kind="Internal").ap()
        gwbd = nc.dram_tensor("gwbd", [D, 8], BF16, kind="Internal").ap()
        with ExitStack() as s0:
            emit_cast_f32_to_bf16_dram(P, s0, wq, wqb, D, D)
            emit_cast_f32_to_bf16_dram(P, s0, wk, wkb, D, D)
            emit_cast_f32_to_bf16_dram(P, s0, wv, wvb, D, D)
            emit_cast_f32_to_bf16_dram(P, s0, wo, wob, D, D)
            emit_cast_f32_to_bf16_dram(P, s0, gw, gwbd, D, 8)
        out_feat = sbuf(nc, stack, "at_out", [128, ND * T], BF16)
        with ExitStack() as sh:
            h_feat = sbuf(nc, sh, "h_feat_sb", [128, ND * T], BF16)
            ch0 = P.ch()
            for dd in range(ND):
                P.dma("sync", h_feat[:, dd * T:(dd + 1) * T], hf[dd * 128:(dd + 1) * 128, :], inc=ch0)
            for e in Prog.ENGS:
                P.wait(e, ch0)
            emit_attn_l0_flash(P, stack, h_feat, out_feat, wqb, wkb, wvb, qs, ks)
        emit_attn_l0_tail(P, stack, out_feat, wob, lngc, lnbc, gwbd, gb, d_h2, d_gates)
        block = stack.enter_context(nc.Block())
        P.finalize(block)
    return nc



def emit_moe_l0(P, stack, d_h2, w1b, w2b, b1_t, b2, b2r_all, gates_d, lng_r, lnb_r, d_g, d_moe_ext=None):
    nc = P.nc
    d_moe = d_moe_ext if d_moe_ext is not None else nc.dram_tensor(_uniq("d_moe"), [T, D], F32, kind="Internal").ap()
    gs = sbuf(nc, stack, "gs", [128, NT * 8])
    sb1 = sbuf(nc, stack, "sb1", [128, 256])
    sb2 = sbuf(nc, stack, "sb2", [8, 1024])
    sb2b = sbuf(nc, stack, "sb2b", [8, 1024], BF16)

    ch0 = P.chp(0)
    for tt in range(NT):
        P.dma("sync", gs[:, tt * 8:(tt + 1) * 8], gates_d[tt * 128:(tt + 1) * 128, :], inc=ch0)
    P.dma("sync", sb1[:], b1_t, inc=ch0)
    P.dma("sync", sb2[:], b2, inc=ch0)
    P.wait("scalar", ch0)
    P.emit("scalar", lambda e: e.copy(out=sb2b[:], in_=sb2[:]))
    P.barrier()

    sE = ExitStack()
    if os.environ.get("DTRUNC") == "pre":
        sE = ExitStack()
        return
    wsb1 = sbuf(nc, sE, "wsb1", [128, 8 * T], BF16)
    wsb2 = sbuf(nc, sE, "wsb2", [128, 32 * D], BF16)
    h2c = sbuf(nc, sE, "h2c", [128, 2 * 8 * 512], BF16)
    hid = sbuf(nc, sE, "hid", [128, 32 * 512], BF16)
    mout = sbuf(nc, sE, "mout", [128, 1024])
    macc = sbuf(nc, sE, "macc", [128, 1024])
    nexp = int(os.environ.get("DNEXP", "8"))
    for e_i in range(nexp):
        chwl = P.chp(1)
        for k in range(ND):
            P.dma("sync", wsb1[:, k * T:(k + 1) * T],
                  w1b[e_i * D + k * 128:e_i * D + (k + 1) * 128, :], inc=chwl)
        for m in range(32):
            P.dma("sync", wsb2[:, m * D:(m + 1) * D],
                  w2b[e_i * 4096 + m * 128:e_i * 4096 + (m + 1) * 128, :], inc=chwl)
        for en in Prog.ENGS:
            P.wait(en, chwl)
        chh = P.chp(2); chm = P.chp(3); che = P.chp(4); chd = P.chp(5)
        nm = 0; ne = 0
        for tc in range(8):
            hpar = tc % 2
            P.dma("sync", h2c[:, hpar * 4096:hpar * 4096 + 4096],
                  d_h2.rearrange("(k p) t -> p (k t)", p=128)[:, [slice(0, 1)]] if False else
                  d_h2[0:128, tc * 512:(tc + 1) * 512], inc=chh) if False else None
            for k in range(ND):
                P.dma("sync", h2c[:, hpar * 4096 + k * 512:hpar * 4096 + (k + 1) * 512],
                      d_h2[k * 128:(k + 1) * 128, tc * 512:(tc + 1) * 512], inc=chh)
            for en in ("tensor",):
                P.wait(en, chh)
            with ExitStack() as s1:
                ph = psum(nc, s1, "ph", [128, 2 * 512])
                for m in range(32):
                    par = m % 2
                    if m >= 2:
                        P.wait("tensor", che, ne - 1)
                    for k in range(ND):
                        P.emit("tensor", lambda e, m=m, k=k, par=par, hpar=hpar: e.matmul(
                            ph[:, par * 512:(par + 1) * 512],
                            lhsT=wsb1[:, k * T + m * 128:k * T + (m + 1) * 128],
                            rhs=h2c[:, hpar * 4096 + k * 512:hpar * 4096 + (k + 1) * 512],
                            start=(k == 0), stop=(k == ND - 1)),
                            inc=chm if k == ND - 1 else None)
                    nm += 1
                    P.wait("scalar", chm, nm)
                    P.emit("scalar", lambda e, m=m, e_i=e_i, par=par: e.activation(
                        out=hid[:, m * 512:(m + 1) * 512], in_=ph[:, par * 512:(par + 1) * 512],
                        func=AF.Gelu_apprx_tanh, bias=sb1[:, e_i * 32 + m:e_i * 32 + m + 1]),
                        inc=che)
                    ne += 1
                P.barrier()
            with ExitStack() as s2:
                pO = psum(nc, s2, "pO", [128, 4 * 1024])
                chw2 = P.chp(6); chw3 = P.chp(7)
                chdl = P.chp(5); chds = P.chp(8)
                for t128 in range(4):
                    for m in range(32):
                        for j in range(2):
                            P.emit("tensor", lambda e, t128=t128, m=m, j=j: e.matmul(
                                pO[:, t128 * 1024 + j * 512:t128 * 1024 + (j + 1) * 512],
                                lhsT=hid[:, m * 512 + t128 * 128:m * 512 + (t128 + 1) * 128],
                                rhs=wsb2[:, m * D + j * 512:m * D + (j + 1) * 512],
                                start=(m == 0), stop=(m == 31)),
                                inc=chw2 if (m == 31 and j == 1) else None)
                    rows = slice(tc * 512 + t128 * 128, tc * 512 + (t128 + 1) * 128)
                    if e_i > 0:
                        P.dma("sync", macc[:], d_moe[rows, :], inc=chdl)
                        P.wait("vector", chdl)
                    P.wait("vector", chds)
                    P.wait("vector", chw2, t128 + 1)
                    gcol = gs[:, (tc * 4 + t128) * 8 + e_i:(tc * 4 + t128) * 8 + e_i + 1]
                    if e_i == 0:
                        P.emit("vector", lambda e, t128=t128, gcol=gcol: e.tensor_scalar(
                            out=mout[:], in0=pO[:, t128 * 1024:(t128 + 1) * 1024],
                            scalar1=gcol, scalar2=None, op0=OP.mult), inc=chw3)
                    else:
                        P.emit("vector", lambda e, t128=t128, gcol=gcol: e.scalar_tensor_tensor(
                            out=mout[:], in0=pO[:, t128 * 1024:(t128 + 1) * 1024],
                            scalar=gcol, in1=macc[:], op0=OP.mult, op1=OP.add), inc=chw3)
                    P.wait("sync", chw3, t128 + 1)
                    P.dma("sync", d_moe[rows, :], mout[:], inc=chds)
                P.barrier()
            P.barrier()
    sE.close()
    if os.environ.get("DTRUNC") == "experts":
        return
    with ExitStack() as sF:
        mo = sbuf(nc, sF, "mo", [128, 2 * 1024])
        mo2 = sbuf(nc, sF, "mo2", [128, 1024])
        wsum = sbuf(nc, sF, "wsum", [128, 1024])
        b2r = sbuf(nc, sF, "b2r", [128, 8 * 1024])
        gcl = sbuf(nc, sF, "gcl", [128, 8])
        slg = sbuf(nc, sF, "slg", [128, D])
        slb = sbuf(nc, sF, "slb", [128, D])
        gt = sbuf(nc, sF, "gt", [128, 3 * 1024], BF16)
        chf = P.chp(1)
        P.dma("sync", slg[:], lng_r, inc=chf)
        P.dma("sync", slb[:], lnb_r, inc=chf)
        for e8 in range(8):
            P.dma("sync", b2r[:, e8 * 1024:(e8 + 1) * 1024],
                  b2r_all[:, e8 * 1024:(e8 + 1) * 1024], inc=chf)
        P.barrier()
        chl = P.chp(1); chs = [P.chp(2), P.chp(5)]; chst = [P.chp(3), P.chp(4)]
        nl = 0
        for tt in range(NT):
            par = tt % 2
            if tt >= 2:
                P.wait("sync", chl, nl)
            P.dma("sync", mo[:, par * 1024:(par + 1) * 1024],
                  d_moe[tt * 128:(tt + 1) * 128, :], inc=chs[par])
            P.wait("vector", chs[par], 16 * (tt // 2 + 1))
            P.emit("vector", lambda e, tt=tt: e.tensor_scalar(
                out=wsum[:], in0=b2r[:, 0:1024], scalar1=gs[:, tt * 8:tt * 8 + 1],
                scalar2=None, op0=OP.mult))
            for e8 in range(1, 8):
                P.drain("vector")
                P.emit("vector", lambda e, tt=tt, e8=e8: e.scalar_tensor_tensor(
                    out=wsum[:], in0=b2r[:, e8 * 1024:(e8 + 1) * 1024],
                    scalar=gs[:, tt * 8 + e8:tt * 8 + e8 + 1], in1=wsum[:],
                    op0=OP.mult, op1=OP.add))
            P.drain("vector")
            P.emit("vector", lambda e, par=par: e.tensor_tensor(
                out=mo2[:], in0=mo[:, par * 1024:(par + 1) * 1024], in1=wsum[:], op=OP.add),
                inc=chl)
            nl += 1
            P.wait("scalar", chl, nl)
            P.emit("scalar", lambda e: e.activation(out=gt[:, 2048:3072], in_=mo2[:], func=AF.Copy,
                                                    accum_out=gcl[:, 0:1]))
            P.drain("scalar")
            P.emit("scalar", lambda e: e.activation(out=gt[:, 2048:3072], in_=mo2[:],
                                                    func=AF.Square, accum_out=gcl[:, 1:2]))
            P.drain("scalar")
            P.emit("scalar", lambda e: e.activation(out=gcl[:, 2:3], in_=gcl[:, 0:1],
                                                    func=AF.Copy, scale=1.0 / D))
            P.emit("scalar", lambda e: e.activation(out=gcl[:, 3:4], in_=gcl[:, 0:1],
                                                    func=AF.Square, scale=1.0 / D))
            P.emit("scalar", lambda e: e.activation(out=gcl[:, 4:5], in_=gcl[:, 1:2],
                                                    func=AF.Copy, scale=1.0 / D), inc=chl)
            nl += 1
            P.wait("vector", chl, nl)
            P.emit("vector", lambda e: e.tensor_tensor(out=gcl[:, 5:6], in0=gcl[:, 4:5],
                                                       in1=gcl[:, 3:4], op=OP.subtract), inc=chl)
            nl += 1
            P.wait("scalar", chl, nl)
            P.emit("scalar", lambda e: e.activation(out=gcl[:, 6:7], in_=gcl[:, 5:6],
                                                    func=AF.Ln, bias=P.scal("eps_ln")))
            P.drain("scalar")
            P.emit("scalar", lambda e: e.activation(out=gcl[:, 7:8], in_=gcl[:, 6:7],
                                                    func=AF.Exp, scale=-0.5), inc=chl)
            nl += 1
            P.wait("vector", chl, nl)
            P.emit("vector", lambda e, par=par: e.tensor_scalar(
                out=mo[:, par * 1024:(par + 1) * 1024], in0=mo2[:],
                scalar1=gcl[:, 2:3], scalar2=gcl[:, 7:8], op0=OP.subtract, op1=OP.mult))
            P.drain("vector")
            P.emit("vector", lambda e, par=par: e.tensor_tensor(
                out=mo2[:], in0=mo[:, par * 1024:(par + 1) * 1024], in1=slg[:], op=OP.mult))
            P.drain("vector")
            P.emit("vector", lambda e, par=par: e.tensor_tensor(
                out=mo[:, par * 1024:(par + 1) * 1024], in0=mo2[:], in1=slb[:], op=OP.add), inc=chl)
            nl += 1
            P.wait("scalar", chl, nl)
            if tt >= 2:
                P.wait("scalar", chst[par], 16 * (tt // 2))
            P.emit("scalar", lambda e, par=par: e.activation(
                out=gt[:, par * 1024:(par + 1) * 1024],
                in_=mo[:, par * 1024:(par + 1) * 1024], func=AF.Silu), inc=chl)
            nl += 1
            P.wait("sync", chl, nl)
            P.dma("sync", d_g[tt * 128:(tt + 1) * 128, :], gt[:, par * 1024:(par + 1) * 1024],
                  inc=chst[par])
        P.barrier()


def moe_l0_host_prep(ins):
    b1 = ins["L0_ffn_b1"]
    b1_t = b1.reshape(8, 32, 128).transpose(2, 0, 1).reshape(128, 256)
    return {
        "w1": np.ascontiguousarray(ins["L0_ffn_w1"].reshape(8 * 1024, 4096)),
        "w2": np.ascontiguousarray(ins["L0_ffn_w2"].reshape(8 * 4096, 1024)),
        "b1_t": np.ascontiguousarray(b1_t),
        "b2": np.ascontiguousarray(ins["L0_ffn_b2"]),
        "b2r_all": np.ascontiguousarray(np.broadcast_to(ins["L0_ffn_b2"].reshape(1, 8 * 1024),
                                                        (128, 8 * 1024)).copy()),
        "lng_r": repl_rows(ins["L0_ffn_ln_g"]),
        "lnb_r": repl_rows(ins["L0_ffn_ln_b"]),
    }

MOE_CONSTS = ["c_ones128b", "c_id128b"]

def build_stage_d():
    nc = bass.Bass("TRN2", target_bir_lowering=False)
    d_h2 = nc.dram_tensor("h2_feat", [D, T], BF16, kind="ExternalInput").ap()
    w1 = nc.dram_tensor("w1", [8 * 1024, 4096], F32, kind="ExternalInput").ap()
    w2 = nc.dram_tensor("w2", [8 * 4096, 1024], F32, kind="ExternalInput").ap()
    b1_t = nc.dram_tensor("b1_t", [128, 256], F32, kind="ExternalInput").ap()
    b2 = nc.dram_tensor("b2", [8, 1024], F32, kind="ExternalInput").ap()
    b2r_all = nc.dram_tensor("b2r_all", [128, 8 * 1024], F32, kind="ExternalInput").ap()
    gates_d = nc.dram_tensor("gates", [T, 8], F32, kind="ExternalInput").ap()
    lng_r = nc.dram_tensor("lng_r", [128, D], F32, kind="ExternalInput").ap()
    lnb_r = nc.dram_tensor("lnb_r", [128, D], F32, kind="ExternalInput").ap()
    d_g = nc.dram_tensor("g_out", [T, D], BF16, kind="ExternalOutput").ap()
    d_moe_ext = nc.dram_tensor("moe_dbg", [T, D], F32, kind="ExternalOutput").ap() if DBG else None
    with ExitStack() as stack:
        P = Prog(nc, stack)
        load_consts(P, stack, MOE_CONSTS)
        w1b = nc.dram_tensor("w1b", [8 * 1024, 4096], BF16, kind="Internal").ap()
        w2b = nc.dram_tensor("w2b", [8 * 4096, 1024], BF16, kind="Internal").ap()
        dt_ = os.environ.get("DTRUNC", "")
        if dt_ != "nocast1":
            with ExitStack() as s0:
                emit_cast_f32_to_bf16_dram(P, s0, w1, w1b, 8 * 1024, 4096)
        if dt_ not in ("cast1", "nocast2"):
            with ExitStack() as s0:
                emit_cast_f32_to_bf16_dram(P, s0, w2, w2b, 8 * 4096, 1024)
        if dt_ not in ("cast1", "cast2"):
            emit_moe_l0(P, stack, d_h2, w1b, w2b, b1_t, b2, b2r_all, gates_d, lng_r, lnb_r, d_g, d_moe_ext)
        else:
            ch_f = P.ch("fill")
            gtmp = sbuf(nc, stack, "gtmp", [128, 1024], BF16)
            P.emit("vector", lambda e: e.memset(gtmp[:], 0.0), inc=ch_f)
            P.wait("sync", ch_f)
            for tt in range(NT):
                P.dma("sync", d_g[tt * 128:(tt + 1) * 128, :], gtmp[:], inc=ch_f)
            P.wait("sync", ch_f)
        block = stack.enter_context(nc.Block())
        P.finalize(block)
    return nc



def emit_combine(P, stack, x_dram, s_dram, g_dram, xout_dram, out_f32=True):
    nc = P.nc
    with ExitStack() as sC:
        xb = sbuf(nc, sC, "cmb_x", [128, 2 * D])
        sb_ = sbuf(nc, sC, "cmb_s", [128, 2 * D], BF16)
        gb_ = sbuf(nc, sC, "cmb_g", [128, 2 * D], BF16)
        ob = sbuf(nc, sC, "cmb_o", [128, 2 * D])
        tb_ = sbuf(nc, sC, "cmb_t", [128, D])
        chl = [P.chp(0), P.chp(4)]; chv = P.chp(1); chst = [P.chp(2), P.chp(3)]
        for tt in range(NT):
            par = tt % 2
            if tt >= 2:
                P.wait("sync", chv, 2 * (tt - 1))
            P.dma("sync", xb[:, par * D:(par + 1) * D], x_dram[tt * 128:(tt + 1) * 128, :],
                  inc=chl[par])
            P.dma("sync", sb_[:, par * D:(par + 1) * D], s_dram[tt * 128:(tt + 1) * 128, :],
                  inc=chl[par])
            P.dma("sync", gb_[:, par * D:(par + 1) * D], g_dram[tt * 128:(tt + 1) * 128, :],
                  inc=chl[par])
            P.wait("vector", chl[par], 48 * (tt // 2 + 1))
            P.emit("vector", lambda e, par=par: e.tensor_tensor(
                out=tb_[:], in0=sb_[:, par * D:(par + 1) * D], in1=gb_[:, par * D:(par + 1) * D],
                op=OP.mult), inc=chv)
            P.drain("vector")
            if tt >= 2:
                P.wait("vector", chst[par], 16 * (tt // 2))
            P.emit("vector", lambda e, par=par: e.tensor_tensor(
                out=ob[:, par * D:(par + 1) * D], in0=xb[:, par * D:(par + 1) * D], in1=tb_[:],
                op=OP.add), inc=chv)
            P.wait("sync", chv, 2 * (tt + 1))
            P.dma("sync", xout_dram[tt * 128:(tt + 1) * 128, :], ob[:, par * D:(par + 1) * D],
                  inc=chst[par])
        P.barrier()



def emit_attn_l1(P, stack, xf, wqb, wkb, wvb, wob, bq_c, bkv_c, bo_r, d_s1):
    nc = P.nc
    out_feat = sbuf(nc, stack, "l1_out", [128, ND * T], BF16)
    with ExitStack() as sA:
        kvw = sbuf(nc, sA, "kvw", [128, 2 * 8 * 64], BF16)
        sbkv = sbuf(nc, sA, "sbkv", [64, 2])
        sbq = sbuf(nc, sA, "sbq", [128, 8])
        ka = sbuf(nc, sA, "ka1", [65, T], BF16)
        qa = sbuf(nc, sA, "qa1", [65, T], BF16)
        qf = sbuf(nc, sA, "qf1", [128, T], BF16)
        vtok = sbuf(nc, sA, "vtok1", [128, NT * 65], BF16)
        vst = sbuf(nc, sA, "vst1", [64, 2 * 512], BF16)
        wcol = sbuf(nc, sA, "wcol1", [128, 8 * 128], BF16)
        nmall = sbuf(nc, sA, "nmall1", [128, 64])
        nmt = sbuf(nc, sA, "nmt1", [32, 128], BF16)
        d_nm = nc.dram_tensor(_uniq("d_nm1"), [32, 128], BF16, kind="Internal").ap()
        rp = sbuf(nc, sA, "rp1", [65, 1024])
        rpb = sbuf(nc, sA, "rpb1", [65, 1024], BF16)
        osb = sbuf(nc, sA, "osb1", [64, 1024], BF16)
        ptb = [sbuf(nc, sA, f"ptb1{i}", [128, 1024], BF16) for i in range(2)]
        ch0 = P.chp(0)
        for k in range(ND):
            P.dma("sync", kvw[:, k * 64:(k + 1) * 64], wkb[k * 128:(k + 1) * 128, :], inc=ch0)
            P.dma("sync", kvw[:, 512 + k * 64:512 + (k + 1) * 64],
                  wvb[k * 128:(k + 1) * 128, :], inc=ch0)
        P.dma("sync", sbkv[:], bkv_c, inc=ch0)
        P.dma("sync", sbq[:, 0:8], bq_c, inc=ch0)
        P.emit("vector", lambda e: e.memset(vtok[:], 1.0))
        P.emit("vector", lambda e: e.memset(ka[64:65, :], 1.0))
        P.barrier()
        with ExitStack() as sP:
            pkv = psum(nc, sP, "pkv", [64, 2 * 512])
            pT2 = psum(nc, sP, "pT21", [128, 1024], BF16)
            chm = P.chp(1); che = P.chp(2)
            nm = 0; ne = 0
            for tc in range(8):
                par = tc % 2
                if tc >= 2:
                    P.wait("tensor", che, ne - 1)
                for wi in range(2):
                    for k in range(ND):
                        P.emit("tensor", lambda e, tc=tc, wi=wi, k=k, par=par: e.matmul(
                            pkv[:, wi * 512:(wi + 1) * 512],
                            lhsT=kvw[:, wi * 512 + k * 64:wi * 512 + (k + 1) * 64],
                            rhs=xf[:, k * T + tc * 512:k * T + (tc + 1) * 512],
                            start=(k == 0), stop=(k == ND - 1)),
                            inc=chm if k == ND - 1 else None)
                    nm += 1
                P.wait("scalar", chm, nm)
                P.emit("scalar", lambda e, tc=tc: e.activation(
                    out=ka[0:64, tc * 512:(tc + 1) * 512], in_=pkv[:, 0:512],
                    func=AF.Identity, bias=sbkv[:, 0:1]))
                P.emit("scalar", lambda e, tc=tc, par=par: e.activation(
                    out=vst[:, par * 512:(par + 1) * 512], in_=pkv[:, 512:1024],
                    func=AF.Identity, bias=sbkv[:, 1:2]), inc=che)
                ne += 1
                P.wait("tensor", che, ne)
                for j in range(4):
                    P.emit("tensor", lambda e, j=j, par=par: e.transpose(
                        pT2[:, par * 512 + j * 128:par * 512 + j * 128 + 64],
                        in_=vst[:, par * 512 + j * 128:par * 512 + (j + 1) * 128],
                        identity=P.cid128b[0:64, 0:64]), inc=chm if j == 3 else None)
                nm += 1
                P.wait("scalar", chm, nm)
                for j in range(4):
                    kj = tc * 4 + j
                    P.emit("scalar", lambda e, kj=kj, j=j, par=par: e.copy(
                        out=vtok[:, kj * 65:kj * 65 + 64],
                        in_=pT2[:, par * 512 + j * 128:par * 512 + j * 128 + 64]),
                        inc=che if j == 3 else None)
                ne += 1
            P.barrier()
        for dt_i in range(ND):
            chw = P.chp(1); chm = P.chp(2); che = P.chp(3); chv = P.chp(4)
            for k in range(ND):
                P.dma("sync", wcol[:, k * 128:(k + 1) * 128],
                      wqb[k * 128:(k + 1) * 128, dt_i * 128:(dt_i + 1) * 128], inc=chw)
            P.wait("tensor", chw)
            with ExitStack() as sQ:
                pq = psum(nc, sQ, "pq", [128, 2 * 512])
                for tc in range(8):
                    par = tc % 2
                    if tc >= 2:
                        P.wait("tensor", che, tc - 1)
                    for k in range(ND):
                        P.emit("tensor", lambda e, tc=tc, k=k, par=par: e.matmul(
                            pq[:, par * 512:(par + 1) * 512],
                            lhsT=wcol[:, k * 128:(k + 1) * 128],
                            rhs=xf[:, k * T + tc * 512:k * T + (tc + 1) * 512],
                            start=(k == 0), stop=(k == ND - 1)),
                            inc=chm if k == ND - 1 else None)
                    P.wait("scalar", chm, tc + 1)
                    P.emit("scalar", lambda e, tc=tc, par=par, dt_i=dt_i: e.activation(
                        out=qf[:, tc * 512:(tc + 1) * 512], in_=pq[:, par * 512:(par + 1) * 512],
                        func=AF.Identity, bias=sbq[:, dt_i:dt_i + 1]), inc=che)
                P.barrier()
            for h2i in range(2):
                chq = P.chp(5)
                P.emit("scalar", lambda e, h2i=h2i: e.copy(
                    out=qa[0:64, :], in_=qf[64 * h2i:64 * h2i + 64, :]), inc=chq)
                for en in Prog.ENGS:
                    P.wait(en, chq)
                with ExitStack() as s1:
                    pS = psum(nc, s1, "pS1", [128, T])
                    ch1m = P.chp(1); ch1v = P.chp(2)
                    for qi in range(NT):
                        if qi >= 1:
                            P.wait("tensor", ch1v, qi)
                        for j in range(8):
                            P.emit("tensor", lambda e, qi=qi, j=j: e.matmul(
                                pS[:, j * 512:(j + 1) * 512],
                                lhsT=qa[0:64, qi * 128:(qi + 1) * 128],
                                rhs=ka[0:64, j * 512:(j + 1) * 512],
                                start=True, stop=True),
                                inc=ch1m if j == 7 else None)
                        P.wait("vector", ch1m, qi + 1)
                        P.emit("vector", lambda e, h2i=h2i, qi=qi: e.reduce_max(
                            out=nmall[:, h2i * 32 + qi:h2i * 32 + qi + 1],
                            in_=pS[:], axis=AX, negate=True), inc=ch1v)
                    P.barrier()
                with ExitStack() as s1b:
                    pTf = psum(nc, s1b, "pTf1", [128, 128])
                    ch1b = P.chp(1)
                    P.emit("tensor", lambda e, h2i=h2i: e.transpose(
                        pTf[0:32, 0:128], in_=nmall[:, h2i * 32:(h2i + 1) * 32],
                        identity=P.cid128f[:]), inc=ch1b)
                    P.wait("scalar", ch1b, 1)
                    P.emit("scalar", lambda e: e.copy(out=nmt[:], in_=pTf[0:32, 0:128]), inc=ch1b)
                    P.wait("sync", ch1b, 2)
                    P.dma("sync", d_nm, nmt[:], inc=ch1b)
                    P.wait("sync", ch1b, 18)
                    P.dma("sync", qa[64:65, :], d_nm.rearrange("a b -> (a b)").unsqueeze(0),
                          inc=ch1b)
                    P.barrier()
                for ic in range(4):
                    with ExitStack() as s2o:
                        pOut = psum(nc, s2o, "pOut1", [65, 1024])
                        with ExitStack() as s2:
                            pST = psum(nc, s2, "pST1", [128, 2 * 1024])
                            chP = P.chp(1); chPV = P.chp(2); chE = P.chp(3)
                            for kj in range(NT):
                                par = kj % 2
                                if kj >= 2:
                                    P.wait("tensor", chE, kj - 1)
                                for j2 in range(2):
                                    P.emit("tensor", lambda e, ic=ic, kj=kj, par=par, j2=j2: e.matmul(
                                        pST[:, par * 1024 + j2 * 512:par * 1024 + (j2 + 1) * 512],
                                        lhsT=ka[:, kj * 128:(kj + 1) * 128],
                                        rhs=qa[:, ic * 1024 + j2 * 512:ic * 1024 + (j2 + 1) * 512],
                                        start=True, stop=True),
                                        inc=chP if j2 == 1 else None)
                                P.wait("scalar", chP, 2 * kj + 1)
                                if kj >= 2:
                                    P.wait("scalar", chPV, kj - 1)
                                P.emit("scalar", lambda e, par=par: e.activation(
                                    out=ptb[par][:], in_=pST[:, par * 1024:(par + 1) * 1024],
                                    func=AF.Exp, scale=0.125), inc=chP)
                                P.wait("tensor", chP, 2 * kj + 2)
                                for j2 in range(2):
                                    P.emit("tensor", lambda e, kj=kj, par=par, j2=j2,
                                           pOut=pOut: e.matmul(
                                        pOut[:, j2 * 512:(j2 + 1) * 512],
                                        lhsT=vtok[:, kj * 65:(kj + 1) * 65],
                                        rhs=ptb[par][:, j2 * 512:(j2 + 1) * 512],
                                        start=(kj == 0), stop=(kj == NT - 1)),
                                        inc=(chPV if j2 == 0 else chE))
                            P.barrier()
                        with ExitStack() as s3:
                            pB2 = psum(nc, s3, "pB21", [64, 1024])
                            chN = P.chp(1)
                            P.emit("vector", lambda e, pOut=pOut: e.reciprocal(
                                out=rp[64:65, 0:1024], in_=pOut[64:65, :]), inc=chN)
                            P.wait("scalar", chN, 1)
                            P.emit("scalar", lambda e: e.copy(out=rpb[64:65, 0:1024],
                                                              in_=rp[64:65, 0:1024]))
                            P.emit("scalar", lambda e, pOut=pOut: e.copy(out=osb[:],
                                                                         in_=pOut[0:64, :]),
                                   inc=chN)
                            P.wait("tensor", chN, 2)
                            for j2 in range(2):
                                P.emit("tensor", lambda e, j2=j2: e.matmul(
                                    pB2[:, j2 * 512:(j2 + 1) * 512],
                                    lhsT=P.cones128b[64:65, 0:64],
                                    rhs=rpb[64:65, j2 * 512:(j2 + 1) * 512],
                                    start=True, stop=True), inc=chN if j2 == 1 else None)
                            P.wait("vector", chN, 3)
                            P.emit("vector", lambda e, dt_i=dt_i, h2i=h2i, ic=ic: e.tensor_tensor(
                                out=out_feat[64 * h2i:64 * (h2i + 1),
                                             dt_i * T + ic * 1024:dt_i * T + (ic + 1) * 1024],
                                in0=osb[:], in1=pB2[:], op=OP.mult), inc=chN)
                            P.barrier()
    with ExitStack() as sW:
        wo_sb = sbuf(nc, sW, "wo1_sb", [128, 8 * D], BF16)
        sbo = sbuf(nc, sW, "sbo", [128, D])
        so1 = sbuf(nc, sW, "so1", [128, 1024])
        so2 = sbuf(nc, sW, "so2", [128, 2 * 1024], BF16)
        chw = P.chp(0)
        for k in range(ND):
            P.dma("sync", wo_sb[:, k * D:(k + 1) * D], wob[k * 128:(k + 1) * 128, :], inc=chw)
        P.dma("sync", sbo[:], bo_r, inc=chw)
        P.barrier()
        with ExitStack() as sW1:
            pW = psum(nc, sW1, "pW1", [128, 2 * 1024])
            chm = P.chp(1); chv = P.chp(2); chst = [P.chp(3), P.chp(4)]
            for tt in range(NT):
                par = tt % 2
                if tt >= 2:
                    P.wait("tensor", chv, 2 * tt - 3)
                for k in range(ND):
                    for j in range(2):
                        P.emit("tensor", lambda e, tt=tt, k=k, j=j, par=par: e.matmul(
                            pW[:, par * 1024 + j * 512:par * 1024 + (j + 1) * 512],
                            lhsT=out_feat[:, k * T + tt * 128:k * T + (tt + 1) * 128],
                            rhs=wo_sb[:, k * D + j * 512:k * D + (j + 1) * 512],
                            start=(k == 0), stop=(k == ND - 1)),
                            inc=chm if (k == ND - 1 and j == 1) else None)
                P.wait("vector", chm, tt + 1)
                P.emit("vector", lambda e, par=par: e.tensor_tensor(
                    out=so1[:], in0=pW[:, par * 1024:(par + 1) * 1024], in1=sbo[:],
                    op=OP.add), inc=chv)
                P.drain("vector")
                if tt >= 2:
                    P.wait("vector", chst[par], 16 * (tt // 2))
                P.emit("vector", lambda e, par=par: e.tensor_copy(
                    so2[:, par * 1024:(par + 1) * 1024], so1[:]), inc=chv)
                P.drain("vector")
                P.wait("sync", chv, 2 * (tt + 1))
                P.dma("sync", d_s1[tt * 128:(tt + 1) * 128, :],
                      so2[:, par * 1024:(par + 1) * 1024], inc=chst[par])
            P.barrier()



def emit_moe_l1(P, stack, xf, w1b, w2b, b1_t, b2, gwT_b, gb, rmsw_r, d_g1):
    nc = P.nc
    d_ys = nc.dram_tensor(_uniq("d_ys"), [T, D], F32, kind="Internal").ap()
    ngs = sbuf(nc, stack, "ngs", [128, NT])
    sb1 = sbuf(nc, stack, "sb1_1", [128, 128])
    sb2 = sbuf(nc, stack, "sb2_1", [8, 1024])
    sb2b = sbuf(nc, stack, "sb2b_1", [8, 1024], BF16)
    gwb = sbuf(nc, stack, "gwb1", [128, 8 * 8], BF16)
    gbf = sbuf(nc, stack, "gbf1", [1, 8])
    gbb = sbuf(nc, stack, "gbb1", [1, 8], BF16)
    srms = sbuf(nc, stack, "srms", [128, 8])
    ch0 = P.chp(0)
    P.dma("sync", sb1[:], b1_t, inc=ch0)
    P.dma("sync", sb2[:], b2, inc=ch0)
    P.dma("sync", gbf[:], gb, inc=ch0)
    P.dma("sync", srms[:], rmsw_r, inc=ch0)
    for k in range(ND):
        P.dma("sync", gwb[:, k * 8:(k + 1) * 8], gwT_b[k * 128:(k + 1) * 128, :], inc=ch0)
    P.wait("scalar", ch0)
    P.emit("scalar", lambda e: e.copy(out=sb2b[:], in_=sb2[:]))
    P.drain("scalar")
    P.emit("scalar", lambda e: e.copy(out=gbb[:], in_=gbf[:]))
    P.barrier()
    with ExitStack() as sG:
        pG = psum(nc, sG, "pG1", [128, 32])
        sq8 = sbuf(nc, sG, "sq8", [128, 32])
        gcl = sbuf(nc, sG, "gcl1", [128, 8])
        lt8 = sbuf(nc, sG, "lt8", [128, 32])
        for tq in range(8):
            P.barrier()
            for tb in range(4):
                tt = tq * 4 + tb
                for k in range(ND):
                    P.emit("tensor", lambda e, tb=tb, tt=tt, k=k: e.matmul(
                        pG[:, tb * 8:(tb + 1) * 8],
                        lhsT=xf[:, k * T + tt * 128:k * T + (tt + 1) * 128],
                        rhs=gwb[:, k * 8:(k + 1) * 8],
                        start=(k == 0), stop=False))
                P.emit("tensor", lambda e, tb=tb: e.matmul(
                    pG[:, tb * 8:(tb + 1) * 8], lhsT=P.cones128b[0:1, :],
                    rhs=gbb[:], start=False, stop=True))
            P.barrier()
            for tb in range(4):
                P.emit("scalar", lambda e, tb=tb: e.activation(
                    out=sq8[:, tb * 8:(tb + 1) * 8], in_=pG[:, tb * 8:(tb + 1) * 8],
                    func=AF.Square))
                P.drain("scalar")
            P.barrier()
            for tb in range(4):
                P.emit("vector", lambda e, tb=tb: e.tensor_reduce(
                    out=gcl[:, 0:1], in_=sq8[:, tb * 8:(tb + 1) * 8], axis=AX, op=OP.add))
                P.drain("vector")
                P.barrier()
                P.emit("scalar", lambda e: e.activation(out=gcl[:, 1:2], in_=gcl[:, 0:1],
                                                        func=AF.Ln, scale=1.0 / 8,
                                                        bias=P.scal("f32eps")))
                P.drain("scalar")
                P.emit("scalar", lambda e: e.activation(out=gcl[:, 2:3], in_=gcl[:, 1:2],
                                                        func=AF.Exp, scale=-0.5))
                P.barrier()
                P.emit("vector", lambda e, tb=tb: e.tensor_scalar(
                    out=lt8[:, tb * 8:(tb + 1) * 8], in0=pG[:, tb * 8:(tb + 1) * 8],
                    scalar1=gcl[:, 2:3], scalar2=None, op0=OP.mult))
                P.drain("vector")
                P.emit("vector", lambda e, tb=tb: e.tensor_tensor(
                    out=lt8[:, tb * 8:(tb + 1) * 8], in0=lt8[:, tb * 8:(tb + 1) * 8],
                    in1=srms[:], op=OP.mult))
                P.drain("vector")
                P.emit("vector", lambda e, tb=tb: e.reduce_max(
                    out=gcl[:, 3:4], in_=lt8[:, tb * 8:(tb + 1) * 8], axis=AX, negate=True))
                P.drain("vector")
                P.emit("vector", lambda e: e.tensor_scalar(
                    out=gcl[:, 4:5], in0=gcl[:, 3:4], scalar1=2.0, scalar2=None, op0=OP.mult))
                P.barrier()
                P.emit("scalar", lambda e, tb=tb: e.activation(
                    out=sq8[:, tb * 8:(tb + 1) * 8], in_=lt8[:, tb * 8:(tb + 1) * 8],
                    func=AF.Exp, scale=2.0, bias=gcl[:, 4:5]))
                P.barrier()
                P.emit("vector", lambda e, tb=tb: e.tensor_reduce(
                    out=gcl[:, 5:6], in_=sq8[:, tb * 8:(tb + 1) * 8], axis=AX, op=OP.add))
                P.drain("vector")
                P.emit("vector", lambda e: e.tensor_scalar(
                    out=gcl[:, 6:7], in0=gcl[:, 5:6], scalar1=1e-6, scalar2=1.0,
                    op0=OP.mult, op1=OP.add))
                P.drain("vector")
                P.emit("vector", lambda e: e.reciprocal(out=gcl[:, 7:8], in_=gcl[:, 6:7]))
                P.drain("vector")
                P.emit("vector", lambda e, tb=tb, tq=tq: e.tensor_copy(
                    ngs[:, tq * 4 + tb:tq * 4 + tb + 1], gcl[:, 7:8]))
                P.drain("vector")
            P.barrier()
    sE = ExitStack()
    wsb1 = sbuf(nc, sE, "wsb1_1", [128, 8 * 2048], BF16)
    wsb2 = sbuf(nc, sE, "wsb2_1", [128, 16 * D], BF16)
    hid = sbuf(nc, sE, "hid_1", [128, 16 * 512], BF16)
    mout = sbuf(nc, sE, "mout_1", [128, 1024])
    macc = sbuf(nc, sE, "macc_1", [128, 1024])
    for e_i in range(8):
        chwl = P.chp(1)
        for k in range(ND):
            P.dma("sync", wsb1[:, k * 2048:(k + 1) * 2048],
                  w1b[e_i * D + k * 128:e_i * D + (k + 1) * 128, :], inc=chwl)
        for m in range(16):
            P.dma("sync", wsb2[:, m * D:(m + 1) * D],
                  w2b[e_i * 2048 + m * 128:e_i * 2048 + (m + 1) * 128, :], inc=chwl)
        for en in Prog.ENGS:
            P.wait(en, chwl)
        chh = P.chp(2); chm = P.chp(3); che = P.chp(4)
        nm = 0; ne = 0
        for tc in range(8):
            with ExitStack() as s1:
                ph = psum(nc, s1, "ph1", [128, 2 * 512])
                for m in range(16):
                    par = m % 2
                    if m >= 2:
                        P.wait("tensor", che, ne - 1)
                    for k in range(ND):
                        P.emit("tensor", lambda e, m=m, k=k, par=par, tc=tc: e.matmul(
                            ph[:, par * 512:(par + 1) * 512],
                            lhsT=wsb1[:, k * 2048 + m * 128:k * 2048 + (m + 1) * 128],
                            rhs=xf[:, k * T + tc * 512:k * T + (tc + 1) * 512],
                            start=(k == 0), stop=(k == ND - 1)),
                            inc=chm if k == ND - 1 else None)
                    nm += 1
                    P.wait("scalar", chm, nm)
                    P.emit("scalar", lambda e, m=m, e_i=e_i, par=par: e.activation(
                        out=hid[:, m * 512:(m + 1) * 512], in_=ph[:, par * 512:(par + 1) * 512],
                        func=AF.Gelu, bias=sb1[:, e_i * 16 + m:e_i * 16 + m + 1]),
                        inc=che)
                    ne += 1
                P.barrier()
            with ExitStack() as s2:
                pO = psum(nc, s2, "pO1", [128, 4 * 1024])
                chw2 = P.chp(6); chw3 = P.chp(7)
                chdl = P.chp(5); chds = P.chp(8)
                for t128 in range(4):
                    for m in range(16):
                        for j in range(2):
                            P.emit("tensor", lambda e, t128=t128, m=m, j=j: e.matmul(
                                pO[:, t128 * 1024 + j * 512:t128 * 1024 + (j + 1) * 512],
                                lhsT=hid[:, m * 512 + t128 * 128:m * 512 + (t128 + 1) * 128],
                                rhs=wsb2[:, m * D + j * 512:m * D + (j + 1) * 512],
                                start=(m == 0), stop=(m == 15)),
                                inc=chw2 if (m == 15 and j == 1) else None)
                    rows = slice(tc * 512 + t128 * 128, tc * 512 + (t128 + 1) * 128)
                    if e_i > 0:
                        P.dma("sync", macc[:], d_ys[rows, :], inc=chdl)
                        P.wait("vector", chdl)
                    P.wait("vector", chds)
                    P.wait("vector", chw2, t128 + 1)
                    if e_i == 0:
                        P.emit("vector", lambda e, t128=t128: e.tensor_scalar(
                            out=mout[:], in0=pO[:, t128 * 1024:(t128 + 1) * 1024],
                            scalar1=1.0, scalar2=None, op0=OP.mult), inc=chw3)
                    else:
                        P.emit("vector", lambda e, t128=t128: e.tensor_tensor(
                            out=mout[:], in0=pO[:, t128 * 1024:(t128 + 1) * 1024],
                            in1=macc[:], op=OP.add), inc=chw3)
                    P.wait("sync", chw3, t128 + 1)
                    P.dma("sync", d_ys[rows, :], mout[:], inc=chds)
                P.barrier()
            P.barrier()
    sE.close()
    with ExitStack() as sF:
        pb2r = psum(nc, sF, "pb2r1", [1, 1024])
        pb = psum(nc, sF, "pbf1", [128, 1024])
        b2row = sbuf(nc, sF, "b2row1", [1, 1024], BF16)
        mo = sbuf(nc, sF, "mo1", [128, 2 * 1024])
        mo2 = sbuf(nc, sF, "mo21", [128, 1024])
        gt = sbuf(nc, sF, "gt1", [128, 2 * 1024], BF16)
        chb = P.chp(1)
        P.emit("tensor", lambda e: e.matmul(pb2r[:, 0:512], lhsT=P.cones128b[0:8, 0:1],
                                            rhs=sb2b[:, 0:512], start=True, stop=True))
        P.emit("tensor", lambda e: e.matmul(pb2r[:, 512:1024], lhsT=P.cones128b[0:8, 0:1],
                                            rhs=sb2b[:, 512:1024], start=True, stop=True), inc=chb)
        P.wait("scalar", chb, 1)
        P.emit("scalar", lambda e: e.copy(out=b2row[:], in_=pb2r[:]), inc=chb)
        P.wait("tensor", chb, 2)
        P.emit("tensor", lambda e: e.matmul(pb[:, 0:512], lhsT=P.cones128b[0:1, :],
                                            rhs=b2row[:, 0:512], start=True, stop=True))
        P.emit("tensor", lambda e: e.matmul(pb[:, 512:1024], lhsT=P.cones128b[0:1, :],
                                            rhs=b2row[:, 512:1024], start=True, stop=True))
        P.barrier()
        chl = P.chp(1); chs = [P.chp(2), P.chp(5)]; chst = [P.chp(3), P.chp(4)]
        nl = 0
        for tt in range(NT):
            par = tt % 2
            if tt >= 2:
                P.wait("sync", chl, nl)
            P.dma("sync", mo[:, par * 1024:(par + 1) * 1024],
                  d_ys[tt * 128:(tt + 1) * 128, :], inc=chs[par])
            P.wait("vector", chs[par], 16 * (tt // 2 + 1))
            P.emit("vector", lambda e, par=par: e.tensor_tensor(
                out=mo2[:], in0=mo[:, par * 1024:(par + 1) * 1024], in1=pb[:], op=OP.add))
            P.drain("vector")
            P.emit("vector", lambda e, tt=tt, par=par: e.tensor_scalar(
                out=mo[:, par * 1024:(par + 1) * 1024], in0=mo2[:],
                scalar1=ngs[:, tt:tt + 1], scalar2=None, op0=OP.mult), inc=chl)
            nl += 1
            P.wait("scalar", chl, nl)
            if tt >= 2:
                P.wait("scalar", chst[par], 16 * (tt // 2))
            P.emit("scalar", lambda e, par=par: e.activation(
                out=gt[:, par * 1024:(par + 1) * 1024],
                in_=mo[:, par * 1024:(par + 1) * 1024], func=AF.Silu), inc=chl)
            nl += 1
            P.wait("sync", chl, nl)
            P.dma("sync", d_g1[tt * 128:(tt + 1) * 128, :], gt[:, par * 1024:(par + 1) * 1024],
                  inc=chst[par])
        P.barrier()



def attn_l1_host_prep(ins):
    return {
        "l1_wq": np.ascontiguousarray(ins["L1_attn_Wq"]),
        "l1_wk": np.ascontiguousarray(ins["L1_attn_Wk"]),
        "l1_wv": np.ascontiguousarray(ins["L1_attn_Wv"]),
        "l1_wo": np.ascontiguousarray(ins["L1_attn_Wo"]),
        "l1_bq": np.ascontiguousarray(ins["L1_attn_bq"].reshape(8, 128).T),
        "l1_bkv": np.ascontiguousarray(np.stack([ins["L1_attn_bk"], ins["L1_attn_bv"]], axis=1)),
        "l1_bo": repl_rows(ins["L1_attn_bo"]),
    }

def moe_l1_host_prep(ins):
    b1 = ins["L1_ffn_b1"]
    return {
        "l1_w1": np.ascontiguousarray(ins["L1_ffn_w1"].reshape(8 * 1024, 2048)),
        "l1_w2": np.ascontiguousarray(ins["L1_ffn_w2"].reshape(8 * 2048, 1024)),
        "l1_b1t": np.ascontiguousarray(b1.reshape(8, 16, 128).transpose(2, 0, 1).reshape(128, 128)),
        "l1_b2": np.ascontiguousarray(ins["L1_ffn_b2"]),
        "l1_gwT": np.ascontiguousarray(ins["L1_ffn_gate_w"].T),
        "l1_gb": np.ascontiguousarray(ins["L1_ffn_gate_b"].reshape(1, 8)),
        "l1_rmsw": np.ascontiguousarray(np.broadcast_to(ins["L1_ffn_rms_w"].reshape(1, 8), (128, 8)).copy()),
    }

def build_stage_f():
    nc = bass.Bass("TRN2", target_bir_lowering=False)
    xnf = nc.dram_tensor("x1n_feat", [D, T], BF16, kind="ExternalInput").ap()
    wq = nc.dram_tensor("l1_wq", [D, D], F32, kind="ExternalInput").ap()
    wk = nc.dram_tensor("l1_wk", [D, 64], F32, kind="ExternalInput").ap()
    wv = nc.dram_tensor("l1_wv", [D, 64], F32, kind="ExternalInput").ap()
    wo = nc.dram_tensor("l1_wo", [D, D], F32, kind="ExternalInput").ap()
    bq = nc.dram_tensor("l1_bq", [128, 8], F32, kind="ExternalInput").ap()
    bkv = nc.dram_tensor("l1_bkv", [64, 2], F32, kind="ExternalInput").ap()
    bo = nc.dram_tensor("l1_bo", [128, D], F32, kind="ExternalInput").ap()
    d_s1 = nc.dram_tensor("s1_out", [T, D], BF16, kind="ExternalOutput").ap()
    with ExitStack() as stack:
        P = Prog(nc, stack)
        load_consts(P, stack, ATTN_CONSTS)
        wqb = nc.dram_tensor("l1wqb", [D, D], BF16, kind="Internal").ap()
        wkb = nc.dram_tensor("l1wkb", [D, 64], BF16, kind="Internal").ap()
        wvb = nc.dram_tensor("l1wvb", [D, 64], BF16, kind="Internal").ap()
        wob = nc.dram_tensor("l1wob", [D, D], BF16, kind="Internal").ap()
        with ExitStack() as s0:
            emit_cast_f32_to_bf16_dram(P, s0, wq, wqb, D, D)
            emit_cast_f32_to_bf16_dram(P, s0, wk, wkb, D, 64)
            emit_cast_f32_to_bf16_dram(P, s0, wv, wvb, D, 64)
            emit_cast_f32_to_bf16_dram(P, s0, wo, wob, D, D)
        xf = sbuf(nc, stack, "x1nf_sb", [128, ND * T], BF16)
        ch0 = P.ch()
        for dd in range(ND):
            P.dma("sync", xf[:, dd * T:(dd + 1) * T], xnf[dd * 128:(dd + 1) * 128, :], inc=ch0)
        for e in Prog.ENGS:
            P.wait(e, ch0)
        emit_attn_l1(P, stack, xf, wqb, wkb, wvb, wob, bq, bkv, bo, d_s1)
        block = stack.enter_context(nc.Block())
        P.finalize(block)
    return nc

def build_stage_g():
    nc = bass.Bass("TRN2", target_bir_lowering=False)
    xnf = nc.dram_tensor("x1n_feat", [D, T], BF16, kind="ExternalInput").ap()
    w1 = nc.dram_tensor("l1_w1", [8 * 1024, 2048], F32, kind="ExternalInput").ap()
    w2 = nc.dram_tensor("l1_w2", [8 * 2048, 1024], F32, kind="ExternalInput").ap()
    b1t = nc.dram_tensor("l1_b1t", [128, 128], F32, kind="ExternalInput").ap()
    b2 = nc.dram_tensor("l1_b2", [8, 1024], F32, kind="ExternalInput").ap()
    gwT = nc.dram_tensor("l1_gwT", [D, 8], F32, kind="ExternalInput").ap()
    gb = nc.dram_tensor("l1_gb", [1, 8], F32, kind="ExternalInput").ap()
    rmsw = nc.dram_tensor("l1_rmsw", [128, 8], F32, kind="ExternalInput").ap()
    d_g1 = nc.dram_tensor("g1_out", [T, D], BF16, kind="ExternalOutput").ap()
    with ExitStack() as stack:
        P = Prog(nc, stack)
        load_consts(P, stack, MOE_CONSTS)
        w1b = nc.dram_tensor("l1w1b", [8 * 1024, 2048], BF16, kind="Internal").ap()
        w2b = nc.dram_tensor("l1w2b", [8 * 2048, 1024], BF16, kind="Internal").ap()
        gwTb = nc.dram_tensor("l1gwTb", [D, 8], BF16, kind="Internal").ap()
        with ExitStack() as s0:
            emit_cast_f32_to_bf16_dram(P, s0, w1, w1b, 8 * 1024, 2048)
        with ExitStack() as s0:
            emit_cast_f32_to_bf16_dram(P, s0, w2, w2b, 8 * 2048, 1024)
        with ExitStack() as s0:
            emit_cast_f32_to_bf16_dram(P, s0, gwT, gwTb, D, 8)
        xf = sbuf(nc, stack, "x1nf_sb2", [128, ND * T], BF16)
        ch0 = P.ch()
        for dd in range(ND):
            P.dma("sync", xf[:, dd * T:(dd + 1) * T], xnf[dd * 128:(dd + 1) * 128, :], inc=ch0)
        for e in Prog.ENGS:
            P.wait(e, ch0)
        emit_moe_l1(P, stack, xf, w1b, w2b, b1t, b2, gwTb, gb, rmsw, d_g1)
        block = stack.enter_context(nc.Block())
        P.finalize(block)
    return nc

def build_stage_e():
    nc = bass.Bass("TRN2", target_bir_lowering=False)
    x = nc.dram_tensor("x", [T, D], F32, kind="ExternalInput").ap()
    s_ = nc.dram_tensor("s_in", [T, D], BF16, kind="ExternalInput").ap()
    g_ = nc.dram_tensor("g_in", [T, D], BF16, kind="ExternalInput").ap()
    x1 = nc.dram_tensor("x1_out", [T, D], F32, kind="ExternalOutput").ap()
    xnf = nc.dram_tensor("x1n_feat", [D, T], BF16, kind="ExternalOutput").ap()
    with ExitStack() as stack:
        P = Prog(nc, stack)
        load_consts(P, stack, ["c_id128b"])
        emit_combine(P, stack, x, s_, g_, x1)
        xn_feat = sbuf(nc, stack, "x1n_sb", [128, ND * T], BF16)
        with ExitStack() as s2:
            emit_ln_feat(P, s2, x1, [(None, None, xn_feat, None)])
        ch_st = P.ch("ste")
        for dd in range(ND):
            P.dma("sync", xnf[dd * 128:(dd + 1) * 128, :], xn_feat[:, dd * T:(dd + 1) * T], inc=ch_st)
        P.wait("sync", ch_st)
        block = stack.enter_context(nc.Block())
        P.finalize(block)
    return nc

def build_stage_h():
    nc = bass.Bass("TRN2", target_bir_lowering=False)
    x = nc.dram_tensor("x1", [T, D], F32, kind="ExternalInput").ap()
    s_ = nc.dram_tensor("s1_in", [T, D], BF16, kind="ExternalInput").ap()
    g_ = nc.dram_tensor("g1_in", [T, D], BF16, kind="ExternalInput").ap()
    out = nc.dram_tensor("out", [T, D], F32, kind="ExternalOutput").ap()
    with ExitStack() as stack:
        P = Prog(nc, stack)
        load_consts(P, stack, [])
        emit_combine(P, stack, x, s_, g_, out)
        block = stack.enter_context(nc.Block())
        P.finalize(block)
    return nc



_NC_CACHE = {}

def _get_nc(name):
    if name not in _NC_CACHE:
        _NC_CACHE[name] = {
            "a": lambda: build_stage_a(True),
            "b": build_stage_b,
            "c": build_stage_c,
            "d": build_stage_d,
            "e": build_stage_e,
            "f": build_stage_f,
            "g": build_stage_g,
            "h": build_stage_h,
        }[name]()
    return _NC_CACHE[name]

def _run(nc, maps, ncores):
    res = run_bass_kernel_spmd(nc, maps, core_ids=list(range(ncores)))
    return res.results

def kernel(**ins):
    import ml_dtypes
    ci = const_inputs()
    NB = ins["L0_attn_u"].shape[0]
    xs = [np.ascontiguousarray(ins["L0_attn_u"][b]) for b in range(NB)]

    def with_consts(m, names):
        m = dict(m)
        m["c_scal"] = ci["c_scal"]
        for n in names:
            m[n] = ci[n]
        return m

    lng_r = repl_rows(ins["L0_ffn_ln_g"]); lnb_r = repl_rows(ins["L0_ffn_ln_b"])
    mapsA = [with_consts({"x": xs[b], "ln_g_r": lng_r, "ln_b_r": lnb_r}, ["c_id128b"])
             for b in range(NB)]
    rA = _run(_get_nc("a"), mapsA, NB)
    mb = mamba_host_prep(ins)
    mapsB = [with_consts({**mb, "xn_feat": np.asarray(rA[b]["xn_feat"])}, MAMBA_CONSTS)
             for b in range(NB)]
    rB = _run(_get_nc("b"), mapsB, NB)
    mc = attn_l0_host_prep(ins)
    mapsC = [with_consts({**mc, "h_feat": np.asarray(rA[b]["h_feat"])}, ATTN_CONSTS)
             for b in range(NB)]
    rC = _run(_get_nc("c"), mapsC, NB)
    md = moe_l0_host_prep(ins)
    mapsD = [with_consts({**md, "h2_feat": np.asarray(rC[b]["h2_feat"]),
                          "gates": np.asarray(rC[b]["gates"])}, MOE_CONSTS)
             for b in range(NB)]
    rD = _run(_get_nc("d"), mapsD, NB)
    mapsE = [with_consts({"x": xs[b], "s_in": np.asarray(rB[b]["s_out"]),
                          "g_in": np.asarray(rD[b]["g_out"])}, ["c_id128b"])
             for b in range(NB)]
    rE = _run(_get_nc("e"), mapsE, NB)
    mf = attn_l1_host_prep(ins)
    mapsF = [with_consts({**mf, "x1n_feat": np.asarray(rE[b]["x1n_feat"])}, ATTN_CONSTS)
             for b in range(NB)]
    rF = _run(_get_nc("f"), mapsF, NB)
    mg = moe_l1_host_prep(ins)
    mapsG = [with_consts({**mg, "x1n_feat": np.asarray(rE[b]["x1n_feat"])}, MOE_CONSTS)
             for b in range(NB)]
    rG = _run(_get_nc("g"), mapsG, NB)
    mapsH = [with_consts({"x1": np.asarray(rE[b]["x1_out"]), "s1_in": np.asarray(rF[b]["s1_out"]),
                          "g1_in": np.asarray(rG[b]["g1_out"])}, [])
             for b in range(NB)]
    rH = _run(_get_nc("h"), mapsH, NB)
    out = np.stack([np.asarray(rH[b]["out"], dtype=np.float32) for b in range(NB)], axis=0)
    return out


# version 03
# speedup vs baseline: 1.0334x; 1.0334x over previous
import numpy as np
from contextlib import ExitStack

import concourse.bass as bass
import concourse.mybir as mybir
from concourse.alu_op_type import AluOpType as OP
from concourse.bass_utils import run_bass_kernel_spmd

F32 = mybir.dt.float32
BF16 = mybir.dt.bfloat16
AX = mybir.AxisListType.X
AF = mybir.ActivationFunctionType

import os
DBG = os.environ.get("MAMBA_DBG", "") == "1"

NCORES = 8
T = 4096
D = 1024
NT = T // 128
ND = D // 128


class Ch:
    def __init__(self, h):
        self.h = h
        self.n = 0

class ChView:
    def __init__(self, ch):
        self._c = ch
        self.base = ch.n
    @property
    def h(self):
        return self._c.h
    @property
    def n(self):
        return self._c.n
    @n.setter
    def n(self, v):
        self._c.n = v

class Prog:
    ENGS = ("sync", "scalar", "vector", "tensor", "gpsimd")

    def __init__(self, nc, stack):
        self.nc = nc
        self.stack = stack
        self.q = {e: [] for e in self.ENGS}
        self.dirty = {e: False for e in self.ENGS}
        self._nsem = 0
        self._bar = None
        self._dma_chs = set()

    def ch(self, name=None):
        h = self.stack.enter_context(self.nc.semaphore(name or f"ch{self._nsem}"))
        self._nsem += 1
        return Ch(h)

    def emit(self, eng, f, inc=None, k=1):
        incs = []
        if inc is not None:
            inc.n += k
            incs.append((inc, k))
        ent = [f, incs, 1]
        self.q[eng].append(ent)
        self.dirty[eng] = True
        return ent

    def chv(self, ch):
        return ChView(ch)

    def dump(self, name, tile_ap, shape, dt=F32):
        if not DBG:
            return
        self.barrier()
        dram = self.nc.dram_tensor("dbg_" + name, list(shape), dt, kind="ExternalOutput").ap()
        ch = self.chp(19)
        self.dma("sync", dram, tile_ap, inc=ch)
        self.wait("sync", ch)
        self.barrier()

    def drain(self, eng):
        self.q[eng].append([("drain",), None, 0])

    def chp(self, i):
        if not hasattr(self, "_pool"):
            self._pool = []
        while len(self._pool) <= i:
            self._pool.append(self.ch(f"pool{len(self._pool)}"))
        return ChView(self._pool[i])

    def wait(self, eng, ch, n=None):
        if n is None:
            n = ch.n
        elif isinstance(ch, ChView):
            n = ch.base + n
        self.q[eng].append([("wait", ch, n), None, 0])

    def barrier(self, bar=None, engines=None):
        targets = [(c, c.n) for c in self._dma_chs]
        self._dma_chs = set()
        for e in self.ENGS:
            if not self.dirty[e]:
                continue
            for ent in reversed(self.q[e]):
                if ent[1] is not None:
                    if ent[1]:
                        c, _k = ent[1][0]
                        targets.append((c, c.n))
                    else:
                        if bar is None:
                            if self._bar is None:
                                self._bar = self.ch("bar")
                            bar = self._bar
                        ent[1].append((bar, ent[2]))
                        bar.n += ent[2]
                        targets.append((bar, bar.n))
                    break
            else:
                raise RuntimeError(f"engine {e} dirty but has no real instruction")
            self.dirty[e] = False
        seen = {}
        for c, n in targets:
            seen[id(c)] = (c, max(n, seen.get(id(c), (c, 0))[1]))
        for e in (engines or self.ENGS):
            for c, n in seen.values():
                self.q[e].append([("wait", c, n), None, 0])
        return bar

    def dma(self, eng, out, in_, inc=None, **kw):
        assert inc is not None
        incs = []
        if inc is not None:
            inc.n += 16
            incs.append((inc, 16))
            base = inc._c if isinstance(inc, ChView) else inc
            self._dma_chs.add(base)
        ent = [lambda e, out=out, in_=in_, kw=kw: e.dma_start(out=out, in_=in_, **kw), incs, 16]
        self.q[eng].append(ent)
        self.dirty[eng] = True
        return ent

    def finalize(self, block):
        for e in self.ENGS:
            ops = self.q[e]
            if not ops:
                continue
            def section(engine, ops=ops):
                for ent in ops:
                    f = ent[0]
                    if isinstance(f, tuple) and f[0] == "wait":
                        engine.wait_ge(f[1].h, f[2])
                        continue
                    if isinstance(f, tuple) and f[0] == "drain":
                        engine.drain()
                        continue
                    instr = f(engine)
                    if ent[1]:
                        for (c, k) in ent[1]:
                            instr = instr.then_inc(c.h, k)
            getattr(block, e)(section)


_NAME_N = [0]

def _uniq(name):
    _NAME_N[0] += 1
    return f"{name}_{_NAME_N[0]}"

def sbuf(nc, stack, name, shape, dt=F32):
    return stack.enter_context(nc.sbuf_tensor(_uniq(name), shape, dt))

def psum(nc, stack, name, shape, dt=F32):
    return stack.enter_context(nc.psum_tensor(_uniq(name), shape, dt))



def host_constants():
    import ml_dtypes
    c = {}
    c["id128"] = np.eye(128, dtype=np.float32)
    ut = np.triu(np.ones((128, 128), np.float32))
    c["utones"] = ut
    c["ltstrict"] = np.tril(np.ones((128, 128), np.float32), -1)
    c["neglt"] = (-1e9 * np.tril(np.ones((128, 128), np.float32), -1))
    c["ones128"] = np.ones((128, 128), np.float32)
    blk = np.zeros((32, 32 * 128), np.float32)
    for h in range(32):
        blk[h, h * 128:(h + 1) * 128] = 1.0
    c["blk"] = blk
    c["neglt_t"] = np.tile(-1e9 * np.tril(np.ones((128, 128), np.float32), -1), (1, 32))
    erep = np.zeros((32, 16 * 128), np.float32)
    for i in range(16):
        erep[2 * i, i * 128:i * 128 + 64] = 1.0
        erep[2 * i + 1, i * 128 + 64:(i + 1) * 128] = 1.0
    c["erep"] = erep
    blkp = np.zeros((32, 32 * 64), np.float32)
    for h in range(32):
        blkp[h, h * 64:(h + 1) * 64] = 1.0
    c["blkp"] = blkp
    c["negblk"] = -blk
    c["neglt4"] = np.tile(-1e9 * np.tril(np.ones((128, 128), np.float32), -1), (1, 4))
    bd2 = np.zeros((128, 2), np.float32)
    bd2[0:64, 0] = 1.0
    bd2[64:128, 1] = 1.0
    c["bd2"] = bd2
    return c



def emit_ln_feat(P, stack, x_dram, feats, eps=1e-5):
    nc = P.nc
    nf = len(feats)
    assert nf <= 2
    xb = sbuf(nc, stack, "lnf_x", [128, 2 * D])
    tokb = [sbuf(nc, stack, f"lnf_tok{i}", [128, 2 * D], BF16) for i in range(nf)]
    trash = sbuf(nc, stack, "lnf_trash", [128, D], BF16)
    trash2x = sbuf(nc, stack, "lnf_trash2", [128, D], BF16)
    tmp1 = sbuf(nc, stack, "lnf_tmp1", [128, D])
    tmp2 = sbuf(nc, stack, "lnf_tmp2", [128, D])
    st = sbuf(nc, stack, "lnf_st", [128, 16])
    pt = psum(nc, stack, "lnf_pt", [128, nf * 2 * D], BF16)
    idb = P.cid128b
    ch_ld = [P.chp(0), P.chp(7)]; ch_stat = P.chp(1); ch_var = P.chp(2); ch_rstd = P.chp(3)
    ch_norm = P.chp(4); ch_tp = P.chp(5); ch_evac = P.chp(6)

    for t in range(NT):
        par = t % 2
        x_t = xb[:, par * D:(par + 1) * D]
        c = lambda i, par=par: st[:, par * 8 + i:par * 8 + i + 1]
        if t >= 2:
            P.wait("sync", ch_norm, nf * (t - 1))
        P.dma("sync", x_t, x_dram[t * 128:(t + 1) * 128, :], inc=ch_ld[par])
        P.wait("scalar", ch_ld[par], 16 * (t // 2 + 1))
        P.emit("scalar", lambda e, x=x_t, c=c: e.activation(out=trash[:], in_=x, func=AF.Copy,
                                                            accum_out=c(0)))
        P.drain("scalar")
        P.emit("scalar", lambda e, x=x_t, c=c: e.activation(out=trash2x[:], in_=x, func=AF.Square,
                                                            accum_out=c(1)), inc=ch_stat)
        P.drain("scalar")
        P.emit("scalar", lambda e, c=c: e.activation(out=c(2), in_=c(0), func=AF.Copy, scale=1.0 / D))
        P.emit("scalar", lambda e, c=c: e.activation(out=c(3), in_=c(0), func=AF.Square, scale=1.0 / D))
        P.emit("scalar", lambda e, c=c: e.activation(out=c(4), in_=c(1), func=AF.Copy, scale=1.0 / D),
               inc=ch_stat)
        P.wait("vector", ch_stat, 2 * (t + 1))
        P.emit("vector", lambda e, c=c: e.tensor_tensor(out=c(5), in0=c(4), in1=c(3), op=OP.subtract),
               inc=ch_var)
        P.wait("scalar", ch_var, t + 1)
        P.emit("scalar", lambda e, c=c: e.activation(out=c(6), in_=c(5), func=AF.Ln, bias=P.scal("eps_ln")))
        P.drain("scalar")
        P.emit("scalar", lambda e, c=c: e.activation(out=c(7), in_=c(6), func=AF.Exp, scale=-0.5),
               inc=ch_rstd)
        P.wait("vector", ch_rstd, t + 1)
        if t >= 2:
            P.wait("vector", ch_tp, nf * (t - 1))
        outs_tok = []
        for fi, (gam, bet, feat_sb, also_tok) in enumerate(feats):
            tok_t = tokb[fi][:, par * D:(par + 1) * D]
            if gam is None:
                P.emit("vector", lambda e, x=x_t, o=tok_t, c=c: e.tensor_scalar(
                    out=o, in0=x, scalar1=c(2), scalar2=c(7),
                    op0=OP.subtract, op1=OP.mult), inc=ch_norm)
            else:
                P.emit("vector", lambda e, x=x_t, c=c: e.tensor_scalar(
                    out=tmp1[:], in0=x, scalar1=c(2), scalar2=c(7),
                    op0=OP.subtract, op1=OP.mult))
                P.drain("vector")
                P.emit("vector", lambda e, g=gam: e.tensor_tensor(
                    out=tmp2[:], in0=tmp1[:], in1=g, op=OP.mult))
                P.drain("vector")
                P.emit("vector", lambda e, o=tok_t, b=bet: e.tensor_tensor(
                    out=o, in0=tmp2[:], in1=b, op=OP.add), inc=ch_norm)
            if also_tok is not None:
                P.wait("scalar", ch_norm, nf * t + fi + 1)
                P.emit("scalar", lambda e, s=tok_t, d=also_tok, t=t: e.copy(
                    out=d[:, t * D:(t + 1) * D], in_=s))
            outs_tok.append((tok_t, feat_sb))
        P.wait("tensor", ch_norm, nf * (t + 1))
        if t >= 2:
            P.wait("tensor", ch_evac, nf * (t - 1))
        for fi, (tok_t, feat_sb) in enumerate(outs_tok):
            bank = (par * nf + fi) * D
            for d in range(ND):
                P.emit("tensor", lambda e, s=tok_t, d=d, bank=bank: e.transpose(
                    pt[:, bank + d * 128:bank + (d + 1) * 128],
                    in_=s[:, d * 128:(d + 1) * 128], identity=idb[:]),
                    inc=ch_tp if d == ND - 1 else None)
            P.wait("scalar", ch_tp, nf * t + fi + 1)
            for d in range(ND):
                P.emit("scalar", lambda e, f=feat_sb, d=d, t=t, bank=bank: e.copy(
                    out=f[:, d * T + t * 128:d * T + (t + 1) * 128],
                    in_=pt[:, bank + d * 128:bank + (d + 1) * 128]),
                    inc=ch_evac if d == ND - 1 else None)
    P.barrier()



_CONST_CACHE = None

def const_inputs():
    global _CONST_CACHE
    if _CONST_CACHE is None:
        import ml_dtypes
        c = host_constants()
        bf = lambda x: np.asarray(x, dtype=ml_dtypes.bfloat16)
        scal = np.zeros((128, 16), np.float32)
        for i, v in enumerate(_SCAL_VALS):
            scal[:, i] = v
        _CONST_CACHE = {
            "c_scal": scal,
            "c_id128b": bf(c["id128"]),
            "c_id128f": c["id128"],
            "c_utonesb": bf(c["utones"]),
            "c_ltstrictb": bf(c["ltstrict"]),
            "c_negltb": bf(c["neglt"]),
            "c_ones128b": bf(c["ones128"]),
            "c_ones128f": c["ones128"],
            "c_blkf": c["blk"],
            "c_negblkf": c["negblk"],
            "c_neglt4f": c["neglt4"],
            "c_blkpb": bf(c["blkp"]),
            "c_bd2b": bf(c["bd2"]),
            "c_bd2tb": bf(c["bd2"].T.copy()),
        }
    return _CONST_CACHE

_SCAL_NAMES = ["eps_ln", "one", "eps_l2", "eps_moe", "f32eps", "half", "neg1e9", "zero"]
_SCAL_VALS = [1e-5, 1.0, 1e-12, 1e-6, float(np.finfo(np.float32).eps), 0.5, -1e9, 0.0]

_CONST_SHAPES = {
    "c_scal": (128, 16),
    "c_id128b": (128, 128), "c_id128f": (128, 128), "c_utonesb": (128, 128),
    "c_ltstrictb": (128, 128), "c_negltb": (128, 128), "c_ones128b": (128, 128),
    "c_ones128f": (128, 128), "c_blkf": (32, 32 * 128), "c_negblkf": (32, 32 * 128),
    "c_neglt4f": (128, 4 * 128), "c_blkpb": (32, 32 * 64),
    "c_bd2b": (128, 2), "c_bd2tb": (2, 128),
}
_CONST_F32 = {"c_scal", "c_id128f", "c_ones128f", "c_blkf", "c_negblkf", "c_neglt4f"}
_CONST_ATTR = {
    "c_scal": "cscal",
    "c_id128b": "cid128b", "c_id128f": "cid128f", "c_utonesb": "cutonesb",
    "c_ltstrictb": "cltstrictb", "c_negltb": "cnegltb", "c_ones128b": "cones128b",
    "c_ones128f": "cones128f", "c_blkf": "cblkf", "c_negblkf": "cnegblkf",
    "c_neglt4f": "cneglt4f", "c_blkpb": "cblkpb",
    "c_bd2b": "cbd2b", "c_bd2tb": "cbd2tb",
}

def load_consts(P, stack, names):
    nc = P.nc
    ch = P.ch(_uniq("consts"))
    if not hasattr(P, "cscal"):
        names = ["c_scal"] + [n for n in names if n != "c_scal"]
    for name in names:
        shape = _CONST_SHAPES[name]
        dt = F32 if name in _CONST_F32 else BF16
        dram = nc.dram_tensor(name, list(shape), dt, kind="ExternalInput").ap()
        tile = sbuf(nc, stack, "s" + name, list(shape), dt)
        P.dma("sync", tile[:], dram, inc=ch)
        setattr(P, _CONST_ATTR[name], tile)
    P.scal = lambda n: P.cscal[:, _SCAL_NAMES.index(n):_SCAL_NAMES.index(n) + 1]
    P.scal32 = lambda n: P.cscal[0:32, _SCAL_NAMES.index(n):_SCAL_NAMES.index(n) + 1]
    for e in Prog.ENGS:
        P.wait(e, ch)
    return ch


def repl_rows(v, dtype=np.float32):
    return np.ascontiguousarray(np.broadcast_to(np.asarray(v, dtype).reshape(1, -1), (128, len(v))))



def build_stage_a(with_h=True):
    nc = bass.Bass("TRN2", target_bir_lowering=False)
    x = nc.dram_tensor("x", [T, D], F32, kind="ExternalInput").ap()
    o_xn = nc.dram_tensor("xn_feat", [D, T], BF16, kind="ExternalOutput").ap()
    o_h = nc.dram_tensor("h_feat", [D, T], BF16, kind="ExternalOutput").ap() if with_h else None
    g_r = nc.dram_tensor("ln_g_r", [128, D], F32, kind="ExternalInput").ap() if with_h else None
    b_r = nc.dram_tensor("ln_b_r", [128, D], F32, kind="ExternalInput").ap() if with_h else None

    with ExitStack() as stack:
        P = Prog(nc, stack)
        load_consts(P, stack, ["c_id128b"])
        xn_feat = sbuf(nc, stack, "xn_feat_sb", [128, ND * T], BF16)
        feats = [(None, None, xn_feat, None)]
        if with_h:
            sg = sbuf(nc, stack, "sg", [128, D]); sb_ = sbuf(nc, stack, "sbeta", [128, D])
            h_feat = sbuf(nc, stack, "h_feat_sb", [128, ND * T], BF16)
            ch0 = P.ch()
            P.dma("sync", sg[:], g_r, inc=ch0)
            P.dma("sync", sb_[:], b_r, inc=ch0)
            for e in Prog.ENGS:
                P.wait(e, ch0)
            feats.append((sg[:], sb_[:], h_feat, None))
        with ExitStack() as s2:
            emit_ln_feat(P, s2, x, feats)
        ch_st = P.ch("st")
        for d in range(ND):
            P.dma("sync", o_xn[d * 128:(d + 1) * 128, :], xn_feat[:, d * T:(d + 1) * T], inc=ch_st)
            if with_h:
                P.dma("sync", o_h[d * 128:(d + 1) * 128, :], h_feat[:, d * T:(d + 1) * T], inc=ch_st)
        P.wait("sync", ch_st)
        block = stack.enter_context(nc.Block())
        P.finalize(block)
    return nc



def emit_cast_f32_to_bf16_dram(P, stack, src, dst, rows, cols):
    nc = P.nc
    nr = rows // 128
    fb = sbuf(nc, stack, f"wc_f_{src.tensor.name}", [128, 2 * cols])
    bb = sbuf(nc, stack, f"wc_b_{src.tensor.name}", [128, 2 * cols], BF16)
    ch_ld = [P.chp(0), P.chp(4)]; ch_cv = P.chp(1); ch_st = [P.chp(2), P.chp(3)]
    for r in range(nr):
        par = r % 2
        if r >= 2:
            P.wait("sync", ch_cv, r - 1)
        P.dma("sync", fb[:, par * cols:(par + 1) * cols], src[r * 128:(r + 1) * 128, :],
              inc=ch_ld[par])
        P.wait("scalar", ch_ld[par], 16 * (r // 2 + 1))
        if r >= 2:
            P.wait("scalar", ch_st[par], 16 * (r // 2))
        P.emit("scalar", lambda e, par=par: e.copy(out=bb[:, par * cols:(par + 1) * cols],
                                                   in_=fb[:, par * cols:(par + 1) * cols]), inc=ch_cv)
        P.wait("sync", ch_cv, r + 1)
        P.dma("sync", dst[r * 128:(r + 1) * 128, :], bb[:, par * cols:(par + 1) * cols], inc=ch_st[par])
    P.barrier()



D_IN = 2048
NH = 32
HP = 64
DST = 128
CONV_DIM = 2304
NXT = 16
SEG = 1024
NSEG = T // SEG
SC = SEG // 128

def emit_mamba(P, stack, xn_feat, win, convw_t, convb_t, dtb, alog, normw_r, wout, d_zs, d_s):
    nc = P.nc

    winb = nc.dram_tensor("winb", [D, 4384], BF16, kind="Internal").ap()
    woutb = nc.dram_tensor("woutb", [D_IN, D], BF16, kind="Internal").ap()
    with ExitStack() as s0:
        emit_cast_f32_to_bf16_dram(P, s0, win, winb, D, 4384)
        emit_cast_f32_to_bf16_dram(P, s0, wout, woutb, D_IN, D)

    sconvw = sbuf(nc, stack, "sconvw", [128, 18 * 4])
    sconvb = sbuf(nc, stack, "sconvb", [128, 18])
    sdtb = sbuf(nc, stack, "sdtb", [32, 4])
    salog = sbuf(nc, stack, "salog", [32, 4])
    snormw = sbuf(nc, stack, "snormw", [128, D_IN])
    chp = P.chp(0)
    P.dma("sync", sconvw[:], convw_t, inc=chp)
    P.dma("sync", sconvb[:], convb_t, inc=chp)
    P.dma("sync", sdtb[:, 0:1], dtb, inc=chp)
    P.dma("sync", salog[:, 0:1], alog, inc=chp)
    P.dma("sync", snormw[:], normw_r, inc=chp)
    P.wait("scalar", chp)
    P.emit("scalar", lambda e: e.activation(out=salog[:, 1:2], in_=salog[:, 0:1], func=AF.Exp, scale=1.0))
    P.drain("scalar")
    P.emit("scalar", lambda e: e.activation(out=salog[:, 2:3], in_=salog[:, 1:2], func=AF.Copy, scale=-1.0))
    P.barrier()
    import os
    if os.environ.get("MAMBA_TRUNC") == "params":
        return

    with ExitStack() as s1:
        wz = sbuf(nc, s1, "wz", [128, 8 * D_IN], BF16)
        ch_wl = P.chp(0)
        for k in range(ND):
            P.dma("sync", wz[:, k * D_IN:(k + 1) * D_IN], winb[k * 128:(k + 1) * 128, 0:D_IN], inc=ch_wl)
        P.wait("tensor", ch_wl)
        pz = psum(nc, s1, "pz", [128, 2 * D_IN])
        zt = sbuf(nc, s1, "zt", [128, 2 * D_IN], BF16)
        ch_mm = P.chp(1); ch_ev = P.chp(2); ch_st = [P.chp(3), P.chp(4)]
        for t in range(NT):
            par = t % 2
            if t >= 2:
                P.wait("tensor", ch_ev, t - 1)
            for k in range(ND):
                for j in range(4):
                    P.emit("tensor", lambda e, par=par, t=t, k=k, j=j: e.matmul(
                        pz[:, par * D_IN + j * 512:par * D_IN + (j + 1) * 512],
                        lhsT=xn_feat[:, k * T + t * 128:k * T + (t + 1) * 128],
                        rhs=wz[:, k * D_IN + j * 512:k * D_IN + (j + 1) * 512],
                        start=(k == 0), stop=(k == ND - 1)),
                        inc=ch_mm if (k == ND - 1 and j == 3) else None)
            P.wait("scalar", ch_mm, t + 1)
            if t >= 2:
                P.wait("scalar", ch_st[par], 16 * (t // 2))
            P.emit("scalar", lambda e, par=par: e.activation(
                out=zt[:, par * D_IN:(par + 1) * D_IN], in_=pz[:, par * D_IN:(par + 1) * D_IN],
                func=AF.Silu), inc=ch_ev)
            P.wait("sync", ch_ev, t + 1)
            P.dma("sync", d_zs[t * 128:(t + 1) * 128, :], zt[:, par * D_IN:(par + 1) * D_IN],
                  inc=ch_st[par])
        P.barrier()

    sB45 = stack.enter_context(ExitStack())
    dtfb = sbuf(nc, sB45, "dtfb", [32, T], BF16)
    abf = sbuf(nc, sB45, "abf", [128, 32])
    ealc = sbuf(nc, sB45, "ealc", [32, 96])
    dtf_holder = []
    with ExitStack() as s2:
        wdt = sbuf(nc, s2, "wdt", [128, 8 * 32], BF16)
        ch_wl = P.chp(0)
        for k in range(ND):
            P.dma("sync", wdt[:, k * 32:(k + 1) * 32], winb[k * 128:(k + 1) * 128, 4352:4384], inc=ch_wl)
        P.wait("tensor", ch_wl)
        pdt = psum(nc, s2, "pdt", [32, 512])
        raw = sbuf(nc, s2, "dtraw", [32, T])
        tmp = sbuf(nc, s2, "dttmp", [32, 2 * T])
        dtf = sbuf(nc, s2, "dtf", [32, T])
        ch_mm = P.chp(1); ch_ev = P.chp(2)
        for j in range(8):
            if j >= 1:
                P.wait("tensor", ch_ev, j)
            for k in range(ND):
                P.emit("tensor", lambda e, j=j, k=k: e.matmul(
                    pdt[:, 0:512],
                    lhsT=wdt[:, k * 32:(k + 1) * 32],
                    rhs=xn_feat[:, k * T + j * 512:k * T + (j + 1) * 512],
                    start=(k == 0), stop=(k == ND - 1)),
                    inc=ch_mm if k == ND - 1 else None)
            P.wait("scalar", ch_mm, j + 1)
            P.emit("scalar", lambda e, j=j: e.copy(out=raw[:, j * 512:(j + 1) * 512], in_=pdt[:, 0:512]),
                   inc=ch_ev)
        P.wait("scalar", ch_ev, 8)
        P.barrier()
        ch_b2a = P.chp(3); ch_b2b = P.chp(4); ch_b2c = P.chp(5); ch_b2d = P.chp(6); ch_b2e = P.chp(7)
        ch_b2g = P.chp(9); ch_b2h = P.chp(10); ch_b2i = P.chp(11)
        u_t = tmp[:, 0:T]; a_t = tmp[:, T:2 * T]
        P.emit("vector", lambda e: e.tensor_scalar(out=u_t, in0=raw[:], scalar1=sdtb[:, 0:1],
                                                   scalar2=None, op0=OP.add), inc=ch_b2a)
        P.wait("scalar", ch_b2a)
        P.emit("scalar", lambda e: e.activation(out=raw[:], in_=u_t, func=AF.Relu))
        P.emit("scalar", lambda e: e.activation(out=a_t, in_=u_t, func=AF.Abs))
        P.drain("scalar")
        P.emit("scalar", lambda e: e.activation(out=a_t, in_=a_t, func=AF.Exp, scale=-1.0))
        P.drain("scalar")
        P.emit("scalar", lambda e: e.activation(out=a_t, in_=a_t, func=AF.Ln,
                                                bias=P.scal32("one")), inc=ch_b2b)
        P.wait("vector", ch_b2b)
        P.emit("vector", lambda e: e.tensor_tensor(out=dtf[:], in0=raw[:], in1=a_t, op=OP.add),
               inc=ch_b2c)
        dtf_holder.append(dtf)
        P.wait("scalar", ch_b2c)
        P.emit("scalar", lambda e: e.copy(out=dtfb[:], in_=dtf[:]))
        P.wait("vector", ch_b2c)
        for c in range(NT):
            P.emit("vector", lambda e, c=c: e.tensor_reduce(
                out=ealc[:, 64 + c:65 + c], in_=dtf[:, c * 128:(c + 1) * 128], axis=AX, op=OP.add),
                inc=ch_b2d if c == NT - 1 else None)
        P.wait("vector", ch_b2d)
        P.emit("vector", lambda e: e.tensor_scalar(out=ealc[:, 32:64], in0=ealc[:, 64:96],
                                                   scalar1=salog[:, 2:3], scalar2=None,
                                                   op0=OP.mult), inc=ch_b2e)
        P.wait("scalar", ch_b2e)
        P.emit("scalar", lambda e: e.activation(out=ealc[:, 0:32], in_=ealc[:, 32:64], func=AF.Exp))
        d_arow = nc.dram_tensor("d_arow", [32, 1], F32, kind="Internal").ap()
        ch_b2f = P.chp(8)
        P.dma("sync", d_arow, salog[:, 2:3], inc=ch_b2f)
        P.wait("sync", ch_b2f, 16)
        P.dma("sync", abf[:], d_arow.squeeze(1).unsqueeze(0).partition_broadcast(128).squeeze(1),
              inc=ch_b2f)
        P.barrier()

    P.dump("dtf", dtf_holder[0][:], [32, T])
    p_sb = sbuf(nc, sB45, "p_sb", [128, NH * HP])
    psb_bf = sbuf(nc, sB45, "psb_bf", [128, NH * HP], BF16)
    xbc = sbuf(nc, sB45, "xbc", [128, 18 * (SEG + 3)], BF16)
    tails = sbuf(nc, sB45, "tails", [128, 18 * 4], BF16)
    ch_pi = P.chp(0)
    P.emit("vector", lambda e: e.memset(p_sb[:], 0.0), inc=ch_pi)
    P.emit("gpsimd", lambda e: e.memset(xbc[:], 0.0))
    P.barrier()

    d_ygf = nc.dram_tensor("d_ygf", [NT, 128, D_IN], BF16, kind="Internal").ap()

    adtT = sbuf(nc, sB45, "adtT", [128, 32], BF16)
    dtT = sbuf(nc, sB45, "dtT", [128, 32], BF16)
    acst = sbuf(nc, sB45, "acst", [128, 32])
    acsf = sbuf(nc, sB45, "acsf", [32, 128])
    nacsf = sbuf(nc, sB45, "nacsf", [32, 128])
    odt = sbuf(nc, sB45, "odt", [128, 32], BF16)
    dect = sbuf(nc, sB45, "dect", [128, 32], BF16)
    fexp_hg = sbuf(nc, sB45, "fexp_hg", [32, 512])
    fz = sbuf(nc, sB45, "fz", [32, NH * HP], BF16)
    eg = sbuf(nc, sB45, "eg", [128, 4 * 512], BF16)
    scb = sbuf(nc, sB45, "scb", [128, 128], BF16)
    xtok = sbuf(nc, sB45, "xtok", [128, NXT * 128], BF16)
    xdt = sbuf(nc, sB45, "xdt", [128, NXT * 128], BF16)
    xdd = sbuf(nc, sB45, "xdd", [128, NXT * 128], BF16)
    ysc = sbuf(nc, sB45, "ysc", [128, D_IN])
    yf = sbuf(nc, sB45, "yf", [128, D_IN], BF16)
    ygb = sbuf(nc, sB45, "ygb", [128, D_IN], BF16)
    ygf = sbuf(nc, sB45, "ygf", [128, D_IN], BF16)
    zbuf = sbuf(nc, sB45, "zbuf", [128, D_IN], BF16)
    gst = sbuf(nc, sB45, "gst", [128, 8])
    btok = sbuf(nc, sB45, "btok", [128, 128], BF16)
    wcol = sbuf(nc, sB45, "wcol", [128, 8 * 128], BF16)
    ctmp = [sbuf(nc, sB45, f"ctmp{i}", [128, 512]) for i in range(2)]

    for half in range(NSEG):
        t0 = half * SEG
        if half >= 1:
            for m in range(18):
                P.emit("scalar", lambda e, m=m: e.copy(
                    out=xbc[:, m * (SEG + 3):m * (SEG + 3) + 3], in_=tails[:, m * 4:m * 4 + 3]))
            P.barrier()
        ch_w = P.chp(0); ch_mm = P.chp(1); ch_ev = P.chp(2)
        sxa = ExitStack()
        pxa = psum(nc, sxa, "pxa", [128, 1024])
        ntc = SEG // 512
        for m in range(18):
            if m >= 1:
                P.wait("sync", ch_mm, m * ntc)
            for k in range(ND):
                P.dma("sync", wcol[:, k * 128:(k + 1) * 128],
                      winb[k * 128:(k + 1) * 128, D_IN + m * 128:D_IN + (m + 1) * 128], inc=ch_w)
            P.wait("tensor", ch_w, 16 * 8 * (m + 1))
            if half == 0 and m == 0:
                P.dump("wcol0", wcol[:], [128, 8 * 128], BF16)
                P.dump("xnslice0", xn_feat[:, 0:512], [128, 512], BF16)
            for tc in range(ntc):
                ppar = tc % 2
                if m * ntc + tc >= 2:
                    P.wait("tensor", ch_ev, m * ntc + tc - 1)
                for k in range(ND):
                    P.emit("tensor", lambda e, m=m, tc=tc, k=k, ppar=ppar, pxa=pxa, t0=t0: e.matmul(
                        pxa[:, ppar * 512:(ppar + 1) * 512],
                        lhsT=wcol[:, k * 128:(k + 1) * 128],
                        rhs=xn_feat[:, k * T + t0 + tc * 512:k * T + t0 + (tc + 1) * 512],
                        start=(k == 0), stop=(k == ND - 1)),
                        inc=ch_mm if k == ND - 1 else None)
                P.wait("scalar", ch_mm, m * ntc + tc + 1)
                P.emit("scalar", lambda e, m=m, tc=tc, ppar=ppar, pxa=pxa: e.copy(
                    out=xbc[:, m * (SEG + 3) + 3 + tc * 512:m * (SEG + 3) + 3 + (tc + 1) * 512],
                    in_=pxa[:, ppar * 512:(ppar + 1) * 512]), inc=ch_ev)
                if half == 0 and m == 0 and tc == 0:
                    P.dump("xbc_t0", xbc[:, 0:1027], [128, 1027], BF16)
        P.barrier()
        sxa.close()
        if half == 0:
            P.dump("xbc_raw0", xbc[:], [128, 18 * (SEG + 3)], BF16)
        if half < NSEG - 1:
            for m in range(18):
                P.emit("scalar", lambda e, m=m: e.copy(
                    out=tails[:, m * 4:m * 4 + 3],
                    in_=xbc[:, m * (SEG + 3) + SEG:m * (SEG + 3) + SEG + 3]))
            P.barrier()
        ch_cv = P.chp(3); ch_cs = P.chp(4)
        npc = SEG // 512
        for m in range(18):
            for ci_idx, pc in enumerate(reversed(range(npc))):
                ci = m * npc + ci_idx
                w4 = lambda j, m=m: sconvw[:, m * 4 + j:m * 4 + j + 1]
                src = lambda j, m=m, pc=pc: xbc[:, m * (SEG + 3) + pc * 512 + j:
                                                m * (SEG + 3) + pc * 512 + j + 512]
                P.emit("vector", lambda e, w4=w4, src=src: e.tensor_scalar(
                    out=ctmp[0][:], in0=src(0), scalar1=w4(0), scalar2=None, op0=OP.mult))
                P.drain("vector")
                P.emit("vector", lambda e, w4=w4, src=src: e.scalar_tensor_tensor(
                    out=ctmp[1][:], in0=src(1), scalar=w4(1), in1=ctmp[0][:], op0=OP.mult, op1=OP.add))
                P.drain("vector")
                P.emit("vector", lambda e, w4=w4, src=src: e.scalar_tensor_tensor(
                    out=ctmp[0][:], in0=src(2), scalar=w4(2), in1=ctmp[1][:], op0=OP.mult, op1=OP.add))
                P.drain("vector")
                P.emit("vector", lambda e, w4=w4, src=src: e.scalar_tensor_tensor(
                    out=ctmp[1][:], in0=src(3), scalar=w4(3), in1=ctmp[0][:], op0=OP.mult, op1=OP.add),
                    inc=ch_cv)
                P.wait("scalar", ch_cv, ci + 1)
                P.emit("scalar", lambda e, m=m, pc=pc: e.activation(
                    out=xbc[:, m * (SEG + 3) + 3 + pc * 512:m * (SEG + 3) + 3 + (pc + 1) * 512],
                    in_=ctmp[1][:], func=AF.Silu, bias=sconvb[:, m:m + 1]), inc=ch_cs)
                P.wait("vector", ch_cs, ci + 1)
        P.barrier()
        if half == 0:
            P.dump("xbc0", xbc[:], [128, 18 * (SEG + 3)], BF16)
        for cc in range(SC):
            cg = half * SC + cc
            xo = 3 + cc * 128
            Bsl = xbc[:, 16 * (SEG + 3) + xo:16 * (SEG + 3) + xo + 128]
            Csl = xbc[:, 17 * (SEG + 3) + xo:17 * (SEG + 3) + xo + 128]
            with ExitStack() as ps1:
                pT = psum(nc, ps1, "pT", [128, 2048], BF16)
                pTf = psum(nc, ps1, "pTf", [128, 128])
                pA = psum(nc, ps1, "pA", [128, 64])
                ch1 = P.chp(5); ch1e = P.chp(6); ch1v = P.chp(7)
                P.emit("tensor", lambda e, cg=cg: e.transpose(
                    pT[:, 32:64], in_=dtfb[:, cg * 128:(cg + 1) * 128], identity=P.cid128b[0:32, 0:32]))
                P.emit("tensor", lambda e, Bsl=Bsl: e.transpose(
                    pT[:, 64:192], in_=Bsl, identity=P.cid128b[:]), inc=ch1)
                P.wait("scalar", ch1, 1)
                P.emit("scalar", lambda e: e.copy(out=dtT[:], in_=pT[:, 32:64]))
                P.emit("scalar", lambda e: e.copy(out=btok[:], in_=pT[:, 64:192]), inc=ch1e)
                P.wait("vector", ch1e, 1)
                P.emit("vector", lambda e: e.tensor_tensor(out=adtT[:], in0=dtT[:], in1=abf[:],
                                                           op=OP.mult), inc=ch1v)
                P.wait("tensor", ch1v, 1)
                P.emit("tensor", lambda e: e.matmul(pA[:, 0:32], lhsT=P.cutonesb[:], rhs=adtT[:],
                                                    start=True, stop=True))
                P.emit("tensor", lambda e: e.matmul(pA[:, 32:64], lhsT=P.cltstrictb[:],
                                                    rhs=adtT[:], start=True, stop=True), inc=ch1)
                P.wait("scalar", ch1, 2)
                P.emit("scalar", lambda e: e.activation(out=odt[:], in_=pA[:, 0:32], func=AF.Exp))
                P.emit("scalar", lambda e: e.activation(out=dect[:], in_=pA[:, 32:64], func=AF.Exp))
                P.emit("scalar", lambda e: e.copy(out=acst[:], in_=pA[:, 0:32]), inc=ch1e)
                P.wait("tensor", ch1e, 2)
                P.emit("tensor", lambda e: e.transpose(pTf[0:32, 0:128], in_=acst[:],
                                                       identity=P.cid128f[:]), inc=ch1)
                P.wait("scalar", ch1, 3)
                if cg == 0:
                    P.dump("acst0", acst[:], [128, 32])
                P.emit("scalar", lambda e: e.copy(out=acsf[:], in_=pTf[0:32, 0:128]))
                P.emit("scalar", lambda e: e.activation(out=nacsf[:], in_=pTf[0:32, 0:128],
                                                        func=AF.Copy, scale=-1.0), inc=ch1e)
                for b2 in range(2):
                    base = 1024 * (1 - b2)
                    for i in range(8):
                        xi = b2 * 8 + i
                        P.emit("tensor", lambda e, xi=xi, i=i, base=base, xo=xo: e.transpose(
                            pT[:, base + i * 128:base + (i + 1) * 128],
                            in_=xbc[:, xi * (SEG + 3) + xo:xi * (SEG + 3) + xo + 128],
                            identity=P.cid128b[:]),
                            inc=ch1 if i == 7 else None)
                    P.wait("scalar", ch1, 4 + b2)
                    for i in range(8):
                        xi = b2 * 8 + i
                        P.emit("scalar", lambda e, xi=xi, i=i, base=base: e.copy(
                            out=xtok[:, xi * 128:(xi + 1) * 128],
                            in_=pT[:, base + i * 128:base + (i + 1) * 128]),
                            inc=ch1e if i == 7 else None)
                P.barrier()
            with ExitStack() as ps2:
                pSc = psum(nc, ps2, "pSc", [128, 128])
                ch2 = P.chp(8)
                P.emit("vector", lambda e: e.tensor_tensor(
                    out=xdt[:].rearrange("p (h q) -> p h q", h=NH),
                    in0=xtok[:].rearrange("p (h q) -> p h q", h=NH),
                    in1=dtT[:].unsqueeze(2).broadcast_to((128, NH, HP)), op=OP.mult))
                P.drain("vector")
                P.emit("vector", lambda e: e.tensor_tensor(
                    out=xdd[:].rearrange("p (h q) -> p h q", h=NH),
                    in0=xdt[:].rearrange("p (h q) -> p h q", h=NH),
                    in1=dect[:].unsqueeze(2).broadcast_to((128, NH, HP)), op=OP.mult))
                P.emit("vector", lambda e, cg=cg: e.tensor_scalar(
                    out=fz[:], in0=P.cblkpb[:], scalar1=ealc[:, cg:cg + 1], scalar2=None,
                    op0=OP.mult), inc=ch2)
                P.emit("tensor", lambda e, Bsl=Bsl, Csl=Csl: e.matmul(
                    pSc[:], lhsT=Bsl, rhs=Csl, start=True, stop=True), inc=ch2)
                P.wait("scalar", ch2, 2)
                P.emit("scalar", lambda e: e.copy(out=scb[:], in_=pSc[:]))
                P.emit("scalar", lambda e: e.copy(out=psb_bf[:], in_=p_sb[:]), inc=ch2)
                P.barrier()
            with ExitStack() as ps3:
                pYo = psum(nc, ps3, "pYo", [128, D_IN])
                ch3 = P.chp(9)
                for j in range(4):
                    P.emit("tensor", lambda e, j=j, Csl=Csl: e.matmul(
                        pYo[:, j * 512:(j + 1) * 512], lhsT=Csl,
                        rhs=psb_bf[:, j * 512:(j + 1) * 512],
                        start=True, stop=True), inc=ch3 if j == 3 else None)
                P.wait("vector", ch3, 1)
                P.emit("vector", lambda e: e.tensor_tensor(
                    out=ysc[:].rearrange("p (h q) -> p h q", h=NH),
                    in0=pYo[:].rearrange("p (h q) -> p h q", h=NH),
                    in1=odt[:].unsqueeze(2).broadcast_to((128, NH, HP)), op=OP.mult), inc=ch3)
                P.barrier()
            with ExitStack() as ps4:
                pD = psum(nc, ps4, "pD", [128, 1024])
                pY = psum(nc, ps4, "pY", [128, D_IN])
                ch4m = P.chp(10); ch4x = P.chp(11); ch4g = P.chp(12); ch4d = P.chp(13); ch4y = P.chp(14)
                ch4f = P.chp(15)
                for hg in range(8):
                    dpar = hg % 2
                    if hg >= 1:
                        P.wait("vector", ch4m, hg)
                    P.emit("vector", lambda e, hg=hg: e.tensor_tensor(
                        out=fexp_hg[:].rearrange("k (h l) -> k h l", h=4),
                        in0=acsf[:].unsqueeze(1).broadcast_to((32, 4, 128)),
                        in1=P.cblkf[:, hg * 512:(hg + 1) * 512].rearrange("k (h l) -> k h l", h=4),
                        op=OP.mult), inc=ch4f)
                    P.wait("tensor", ch4f, hg + 1)
                    if hg >= 2:
                        P.wait("tensor", ch4x, hg - 1)
                    P.emit("tensor", lambda e, hg=hg, dpar=dpar: e.matmul(
                        pD[:, dpar * 512:(dpar + 1) * 512], lhsT=P.cones128f[0:32, :],
                        rhs=fexp_hg[:], start=True, stop=False), inc=ch4m)
                    P.emit("tensor", lambda e, hg=hg, dpar=dpar: e.matmul(
                        pD[:, dpar * 512:(dpar + 1) * 512], lhsT=nacsf[:],
                        rhs=P.cblkf[:, hg * 512:(hg + 1) * 512], start=False, stop=False))
                    P.emit("tensor", lambda e, hg=hg, dpar=dpar: e.matmul(
                        pD[:, dpar * 512:(dpar + 1) * 512], lhsT=P.cid128f[:],
                        rhs=P.cneglt4f[:], start=False, stop=True), inc=ch4m)
                    P.wait("scalar", ch4m, 2 * (hg + 1))
                    if hg >= 2:
                        P.wait("scalar", ch4g, hg - 1)
                    P.emit("scalar", lambda e, dpar=dpar: e.activation(
                        out=eg[:, dpar * 512:(dpar + 1) * 512],
                        in_=pD[:, dpar * 512:(dpar + 1) * 512], func=AF.Exp), inc=ch4x)
                    P.wait("vector", ch4x, hg + 1)
                    if hg >= 2:
                        P.wait("vector", ch4d, hg - 1)
                    P.emit("vector", lambda e, dpar=dpar: e.tensor_tensor(
                        out=eg[:, (2 + dpar) * 512:(3 + dpar) * 512].rearrange("p (h l) -> p h l", h=4),
                        in0=eg[:, dpar * 512:(dpar + 1) * 512].rearrange("p (h l) -> p h l", h=4),
                        in1=scb[:].unsqueeze(1).broadcast_to((128, 4, 128)),
                        op=OP.mult), inc=ch4g)
                    P.wait("tensor", ch4g, hg + 1)
                    for hh in range(4):
                        h = hg * 4 + hh
                        P.emit("tensor", lambda e, h=h, hh=hh, dpar=dpar: e.matmul(
                            pY[:, h * 64:(h + 1) * 64],
                            lhsT=eg[:, (2 + dpar) * 512 + hh * 128:(2 + dpar) * 512 + (hh + 1) * 128],
                            rhs=xdt[:, h * 64:(h + 1) * 64], start=True, stop=True),
                            inc=ch4d if hh == 3 else None)
                P.wait("vector", ch4d, 8)
                P.emit("vector", lambda e: e.tensor_tensor(out=yf[:], in0=ysc[:], in1=pY[:],
                                                           op=OP.add), inc=ch4y)
                P.barrier()
                if cg in (0, 1):
                    P.dump(f"yf{cg}", yf[:], [128, D_IN], BF16)
            with ExitStack() as ps5:
                pDz = psum(nc, ps5, "pDz", [128, D_IN])
                pS = psum(nc, ps5, "pS", [128, D_IN])
                ch5 = P.chp(16)
                for j in range(4):
                    P.emit("tensor", lambda e, j=j: e.matmul(
                        pS[:, j * 512:(j + 1) * 512], lhsT=btok[:],
                        rhs=xdd[:, j * 512:(j + 1) * 512], start=True, stop=True))
                for j in range(4):
                    P.emit("tensor", lambda e, j=j: e.matmul(
                        pDz[:, j * 512:(j + 1) * 512], lhsT=P.cones128b[0:32, :],
                        rhs=fz[:, j * 512:(j + 1) * 512], start=True, stop=True),
                        inc=ch5 if j == 3 else None)
                P.wait("vector", ch5, 1)
                P.emit("vector", lambda e: e.tensor_tensor(out=ysc[:], in0=p_sb[:], in1=pDz[:],
                                                           op=OP.mult))
                P.drain("vector")
                P.emit("vector", lambda e: e.tensor_tensor(out=p_sb[:], in0=ysc[:], in1=pS[:],
                                                           op=OP.add), inc=ch5)
                P.barrier()
                if cg == 0:
                    P.dump("p0", p_sb[:], [128, NH * HP])
            with ExitStack() as ps6:
                pTg = psum(nc, ps6, "pTg", [128, D_IN], BF16)
                ch6 = P.chp(17); ch6e = P.chp(18)
                P.dma("sync", zbuf[:], d_zs[cg * 128:(cg + 1) * 128, :], inc=ch6)
                P.emit("scalar", lambda e: e.activation(out=ygb[:], in_=yf[:], func=AF.Copy,
                                                        accum_out=gst[:, 0:1]))
                P.drain("scalar")
                P.emit("scalar", lambda e: e.activation(out=ygf[:], in_=yf[:], func=AF.Square,
                                                        accum_out=gst[:, 1:2]))
                P.drain("scalar")
                P.emit("scalar", lambda e: e.activation(out=gst[:, 2:3], in_=gst[:, 0:1],
                                                        func=AF.Copy, scale=1.0 / D_IN))
                P.emit("scalar", lambda e: e.activation(out=gst[:, 3:4], in_=gst[:, 0:1],
                                                        func=AF.Square, scale=1.0 / D_IN))
                P.emit("scalar", lambda e: e.activation(out=gst[:, 4:5], in_=gst[:, 1:2],
                                                        func=AF.Copy, scale=1.0 / D_IN), inc=ch6e)
                P.wait("vector", ch6e, 1)
                P.emit("vector", lambda e: e.tensor_tensor(out=gst[:, 5:6], in0=gst[:, 4:5],
                                                           in1=gst[:, 3:4], op=OP.subtract), inc=ch6e)
                P.wait("scalar", ch6e, 2)
                P.emit("scalar", lambda e: e.activation(out=gst[:, 6:7], in_=gst[:, 5:6],
                                                        func=AF.Ln, bias=P.scal("eps_ln")))
                P.drain("scalar")
                P.emit("scalar", lambda e: e.activation(out=gst[:, 7:8], in_=gst[:, 6:7],
                                                        func=AF.Exp, scale=-0.5), inc=ch6e)
                P.wait("vector", ch6e, 3)
                P.wait("vector", ch6, 16)
                P.emit("vector", lambda e: e.tensor_scalar(out=ysc[:], in0=yf[:],
                                                           scalar1=gst[:, 2:3], scalar2=gst[:, 7:8],
                                                           op0=OP.subtract, op1=OP.mult))
                P.drain("vector")
                P.emit("vector", lambda e: e.tensor_tensor(out=ygf[:], in0=ysc[:], in1=snormw[:],
                                                           op=OP.mult))
                P.drain("vector")
                P.emit("vector", lambda e: e.tensor_tensor(out=ygb[:], in0=ygf[:], in1=zbuf[:],
                                                           op=OP.mult), inc=ch6)
                P.wait("tensor", ch6, 17)
                for i in range(16):
                    P.emit("tensor", lambda e, i=i: e.transpose(
                        pTg[:, i * 128:(i + 1) * 128], in_=ygb[:, i * 128:(i + 1) * 128],
                        identity=P.cid128b[:]), inc=ch6 if i == 15 else None)
                P.wait("scalar", ch6, 18)
                P.emit("scalar", lambda e: e.copy(out=ygf[:], in_=pTg[:]), inc=ch6e)
                P.wait("sync", ch6e, 4)
                P.dma("sync", d_ygf[cg], ygf[:], inc=ch6)
                P.barrier()

    sB45.close()
    with ExitStack() as s6:
        wo_sb = sbuf(nc, s6, "wo_sb", [128, 16 * D], BF16)
        ch_w = P.chp(0)
        for k in range(16):
            P.dma("sync", wo_sb[:, k * D:(k + 1) * D], woutb[k * 128:(k + 1) * 128, :], inc=ch_w)
        ygl = sbuf(nc, s6, "ygl", [128, 2 * D_IN], BF16)
        sout = sbuf(nc, s6, "sout", [128, 2 * D], BF16)
        pO = psum(nc, s6, "pO", [128, 2 * D])
        P.wait("tensor", ch_w)
        ch_l = [P.chp(1), P.chp(6)]; ch_m = P.chp(2); ch_e = P.chp(3); ch_s = [P.chp(4), P.chp(5)]
        for t in range(NT):
            par = t % 2
            if t >= 2:
                P.wait("sync", ch_m, t - 1)
            P.dma("sync", ygl[:, par * D_IN:(par + 1) * D_IN], d_ygf[t], inc=ch_l[par])
            P.wait("tensor", ch_l[par], 16 * (t // 2 + 1))
            if t >= 2:
                P.wait("tensor", ch_e, t - 1)
            for k in range(16):
                for j in range(2):
                    P.emit("tensor", lambda e, t=t, par=par, k=k, j=j: e.matmul(
                        pO[:, par * D + j * 512:par * D + (j + 1) * 512],
                        lhsT=ygl[:, par * D_IN + k * 128:par * D_IN + (k + 1) * 128],
                        rhs=wo_sb[:, k * D + j * 512:k * D + (j + 1) * 512],
                        start=(k == 0), stop=(k == 15)),
                        inc=ch_m if (k == 15 and j == 1) else None)
            P.wait("scalar", ch_m, t + 1)
            if t >= 2:
                P.wait("scalar", ch_s[par], 16 * (t // 2))
            P.emit("scalar", lambda e, par=par: e.copy(out=sout[:, par * D:(par + 1) * D],
                                                       in_=pO[:, par * D:(par + 1) * D]), inc=ch_e)
            P.wait("sync", ch_e, t + 1)
            P.dma("sync", d_s[t * 128:(t + 1) * 128, :], sout[:, par * D:(par + 1) * D],
                  inc=ch_s[par])
        P.barrier()


def mamba_host_prep(ins, prefix="L0_attn_"):
    return {
        "win": np.ascontiguousarray(ins[prefix + "W_in"]),
        "wout": np.ascontiguousarray(ins[prefix + "W_out"]),
        "convw_t": np.ascontiguousarray(ins[prefix + "conv_w"].reshape(18, 128, 4)
                                        .transpose(1, 0, 2).reshape(128, 72)),
        "convb_t": np.ascontiguousarray(ins[prefix + "conv_b"].reshape(18, 128).T),
        "dtb": np.ascontiguousarray(ins[prefix + "dt_bias"].reshape(32, 1)),
        "alog": np.ascontiguousarray(ins[prefix + "A_log"].reshape(32, 1)),
        "normw_r": repl_rows(ins[prefix + "norm_w"]),
    }

MAMBA_CONSTS = ["c_id128b", "c_id128f", "c_utonesb", "c_ltstrictb", "c_ones128b",
                "c_ones128f", "c_blkf", "c_neglt4f", "c_blkpb"]

def build_stage_b():
    nc = bass.Bass("TRN2", target_bir_lowering=False)
    xnf = nc.dram_tensor("xn_feat", [D, T], BF16, kind="ExternalInput").ap()
    win = nc.dram_tensor("win", [D, 4384], F32, kind="ExternalInput").ap()
    wout = nc.dram_tensor("wout", [D_IN, D], F32, kind="ExternalInput").ap()
    convw_t = nc.dram_tensor("convw_t", [128, 72], F32, kind="ExternalInput").ap()
    convb_t = nc.dram_tensor("convb_t", [128, 18], F32, kind="ExternalInput").ap()
    dtb = nc.dram_tensor("dtb", [32, 1], F32, kind="ExternalInput").ap()
    alog = nc.dram_tensor("alog", [32, 1], F32, kind="ExternalInput").ap()
    normw_r = nc.dram_tensor("normw_r", [128, D_IN], F32, kind="ExternalInput").ap()
    d_s = nc.dram_tensor("s_out", [T, D], BF16, kind="ExternalOutput").ap()
    d_zs = nc.dram_tensor("d_zs", [T, D_IN], BF16, kind="Internal").ap()

    with ExitStack() as stack:
        P = Prog(nc, stack)
        load_consts(P, stack, MAMBA_CONSTS)
        xn_feat = sbuf(nc, stack, "xn_feat_sb", [128, ND * T], BF16)
        ch0 = P.ch()
        for dd in range(ND):
            P.dma("sync", xn_feat[:, dd * T:(dd + 1) * T], xnf[dd * 128:(dd + 1) * 128, :], inc=ch0)
        for e in Prog.ENGS:
            P.wait(e, ch0)
        emit_mamba(P, stack, xn_feat, win, convw_t, convb_t, dtb, alog, normw_r, wout,
                   d_zs, d_s)
        block = stack.enter_context(nc.Block())
        P.finalize(block)
    return nc



def emit_attn_l0_flash(P, stack, h_feat, out_feat, wqb, wkb, wvb, qs_col, ks_col):
    nc = P.nc
    with ExitStack() as sA:
        qa = [sbuf(nc, sA, f"qa{i}", [65, T], BF16) for i in range(2)]
        ka = [sbuf(nc, sA, f"ka{i}", [65, T], BF16) for i in range(2)]
        vtok = sbuf(nc, sA, "vtok", [128, NT * 130], BF16)
        qf = sbuf(nc, sA, "qf", [128, T], BF16)
        vst = sbuf(nc, sA, "vst", [128, 1024], BF16)
        sqc = sbuf(nc, sA, "sqc", [128, 512], BF16)
        wcol = sbuf(nc, sA, "wcol3", [128, 3 * 8 * 128], BF16)
        sqs = sbuf(nc, sA, "sqs", [128, 2])
        nmall = sbuf(nc, sA, "nmall", [128, 64])
        nmt = sbuf(nc, sA, "nmt", [32, 128], BF16)
        d_nm = nc.dram_tensor(_uniq("d_nm"), [32, 128], BF16, kind="Internal").ap()
        rp = sbuf(nc, sA, "rp", [65, 1024])
        rpb = sbuf(nc, sA, "rpb", [65, 1024], BF16)
        osb = sbuf(nc, sA, "osb", [64, 1024], BF16)
        ptb = [sbuf(nc, sA, f"ptb{i}", [128, 1024], BF16) for i in range(2)]
        ptd = sbuf(nc, sA, "ptd", [128, 128], BF16)

        for dt_i in range(ND):
            chw = P.chp(1); chm = P.chp(2); che = P.chp(3); chv = P.chp(4)
            for wi, w in enumerate((wqb, wkb, wvb)):
                for k in range(ND):
                    P.dma("sync", wcol[:, (wi * 8 + k) * 128:(wi * 8 + k + 1) * 128],
                          w[k * 128:(k + 1) * 128, dt_i * 128:(dt_i + 1) * 128], inc=chw)
            P.dma("sync", sqs[:, 0:1], qs_col[dt_i * 128:(dt_i + 1) * 128, :], inc=chw)
            P.dma("sync", sqs[:, 1:2], ks_col[dt_i * 128:(dt_i + 1) * 128, :], inc=chw)
            P.emit("vector", lambda e: e.memset(vtok[:], 1.0))
            P.emit("vector", lambda e, h2i=0: e.memset(ka[0][64:65, :], 1.0))
            P.emit("vector", lambda e, h2i=1: e.memset(ka[1][64:65, :], 1.0), inc=chv)
            P.barrier()
            for pss in range(2):
                with ExitStack() as sP:
                    pqkv = psum(nc, sP, "pqkv", [128, 2 * 512])
                    pT2 = psum(nc, sP, "pT2", [128, 1024], BF16)
                    nwi = 2 if pss == 0 else 1
                    wis = (0, 2) if pss == 0 else (1,)
                    for tc in range(8):
                        if tc >= 1:
                            P.wait("tensor", che, 2 * tc - 1 if pss == 0 else 16 + tc)
                        for wj, wi in enumerate(wis):
                            for k in range(ND):
                                P.emit("tensor", lambda e, tc=tc, wi=wi, wj=wj, k=k: e.matmul(
                                    pqkv[:, wj * 512:(wj + 1) * 512],
                                    lhsT=wcol[:, (wi * 8 + k) * 128:(wi * 8 + k + 1) * 128],
                                    rhs=h_feat[:, k * T + tc * 512:k * T + (tc + 1) * 512],
                                    start=(k == 0), stop=(k == ND - 1)),
                                    inc=chm if k == ND - 1 else None)
                        nmm = (3 * tc + 2) if pss == 0 else (24 + tc + 1)
                        P.wait("scalar", chm, nmm)
                        P.emit("scalar", lambda e, tc=tc: e.copy(
                            out=qf[:, tc * 512:(tc + 1) * 512], in_=pqkv[:, 0:512]),
                            inc=che if pss == 1 else None)
                        if pss == 0:
                            P.emit("scalar", lambda e, tc=tc, par=tc % 2: e.copy(
                                out=vst[:, par * 512:(par + 1) * 512], in_=pqkv[:, 512:1024]),
                                inc=che)
                            P.wait("tensor", che, 2 * tc + 1)
                            for j in range(4):
                                P.emit("tensor", lambda e, tc=tc, j=j, par=tc % 2: e.transpose(
                                    pT2[:, par * 512 + j * 128:par * 512 + (j + 1) * 128],
                                    in_=vst[:, par * 512 + j * 128:par * 512 + (j + 1) * 128],
                                    identity=P.cid128b[:]),
                                    inc=chm if j == 3 else None)
                            P.wait("scalar", chm, nmm + 1)
                            for j in range(4):
                                kj = tc * 4 + j
                                P.emit("scalar", lambda e, kj=kj, j=j, par=tc % 2: e.copy(
                                    out=vtok[:, kj * 130:kj * 130 + 64],
                                    in_=pT2[:, par * 512 + j * 128:par * 512 + j * 128 + 64]))
                                P.emit("scalar", lambda e, kj=kj, j=j, par=tc % 2: e.copy(
                                    out=vtok[:, kj * 130 + 65:kj * 130 + 129],
                                    in_=pT2[:, par * 512 + j * 128 + 64:par * 512 + (j + 1) * 128]),
                                    inc=che if j == 3 else None)
                    P.barrier()
                with ExitStack() as sL:
                    pR = psum(nc, sL, "pR", [2, 512])
                    pB = psum(nc, sL, "pB", [128, 2 * 512])
                    rrc = sbuf(nc, sL, "rrc", [2, 512])
                    rrc2 = sbuf(nc, sL, "rrc2", [2, 512])
                    rrcb = sbuf(nc, sL, "rrcb", [2, 512], BF16)
                    chl = P.chp(5); chn = P.chp(6)
                    nl = 0
                    nn_ = 0
                    dsts = qa if pss == 0 else ka
                    for tc in range(8):
                        par = tc % 2
                        P.emit("scalar", lambda e, tc=tc: e.activation(
                            out=sqc[:], in_=qf[:, tc * 512:(tc + 1) * 512], func=AF.Square),
                            inc=chl)
                        nl += 1
                        P.wait("tensor", chl, nl)
                        P.emit("tensor", lambda e: e.matmul(pR[:], lhsT=P.cbd2b[:], rhs=sqc[:],
                                                            start=True, stop=True), inc=chl)
                        nl += 1
                        P.wait("scalar", chl, nl)
                        P.emit("scalar", lambda e: e.copy(out=rrc[:], in_=pR[:]))
                        P.drain("scalar")
                        P.emit("scalar", lambda e: e.activation(out=rrc2[:], in_=rrc[:],
                                                                func=AF.Ln,
                                                                bias=P.scal("eps_l2")[0:2, :]))
                        P.drain("scalar")
                        P.emit("scalar", lambda e: e.activation(out=rrcb[:], in_=rrc2[:],
                                                                func=AF.Exp, scale=-0.5), inc=chl)
                        nl += 1
                        P.wait("tensor", chl, nl)
                        if nn_ >= 2:
                            P.wait("tensor", chn, nn_ - 1)
                        P.emit("tensor", lambda e, par=par: e.matmul(
                            pB[:, par * 512:(par + 1) * 512], lhsT=P.cbd2tb[:], rhs=rrcb[:],
                            start=True, stop=True), inc=chn)
                        nn_ += 1
                        P.wait("vector", chn, nn_)
                        sc_col = sqs[:, pss:pss + 1]
                        for hf in range(2):
                            P.emit("vector", lambda e, tc=tc, par=par, hf=hf,
                                   dsts=dsts, sc_col=sc_col: e.scalar_tensor_tensor(
                                out=dsts[hf][0:64, tc * 512:(tc + 1) * 512],
                                in0=qf[64 * hf:64 * hf + 64, tc * 512:(tc + 1) * 512],
                                scalar=sc_col[64 * hf:64 * hf + 64, :],
                                in1=pB[64 * hf:64 * hf + 64, par * 512:(par + 1) * 512],
                                op0=OP.mult, op1=OP.mult),
                                inc=chn if hf == 1 else None)
                        nn_ += 1
                    P.barrier()
            with ExitStack() as s1:
                pS = psum(nc, s1, "pS", [128, T])
                ch1m = P.chp(5); ch1v = P.chp(6)
                for h2i in range(2):
                    for qi in range(NT):
                        if h2i * NT + qi >= 1:
                            P.wait("tensor", ch1v, h2i * NT + qi)
                        nch = (qi * 128 + 128 + 511) // 512
                        for j in range(nch):
                            n0 = j * 512
                            n1 = min((qi + 1) * 128, n0 + 512)
                            P.emit("tensor", lambda e, h2i=h2i, qi=qi, n0=n0, n1=n1: e.matmul(
                                pS[:, n0:n1], lhsT=qa[h2i][0:64, qi * 128:(qi + 1) * 128],
                                rhs=ka[h2i][0:64, n0:n1], start=True, stop=True),
                                inc=ch1m if j == nch - 1 else None)
                        P.wait("vector", ch1m, h2i * NT + qi + 1)
                        P.emit("vector", lambda e, h2i=h2i, qi=qi: e.reduce_max(
                            out=nmall[:, h2i * 32 + qi:h2i * 32 + qi + 1],
                            in_=pS[:, 0:(qi + 1) * 128], axis=AX, negate=True), inc=ch1v)
                P.barrier()
            with ExitStack() as s1b:
                pTf = psum(nc, s1b, "pTf2", [128, 128])
                ch1b = P.chp(5)
                n1b = 0
                for h2i in range(2):
                    P.emit("tensor", lambda e, h2i=h2i: e.transpose(
                        pTf[0:32, 0:128], in_=nmall[:, h2i * 32:(h2i + 1) * 32],
                        identity=P.cid128f[:]), inc=ch1b)
                    n1b += 1
                    P.wait("scalar", ch1b, n1b)
                    P.emit("scalar", lambda e: e.copy(out=nmt[:], in_=pTf[0:32, 0:128]), inc=ch1b)
                    n1b += 1
                    P.wait("sync", ch1b, n1b)
                    P.dma("sync", d_nm, nmt[:], inc=ch1b)
                    n1b += 16
                    P.wait("sync", ch1b, n1b)
                    P.dma("sync", qa[h2i][64:65, :],
                          d_nm.rearrange("a b -> (a b)").unsqueeze(0), inc=ch1b)
                    n1b += 16
                    P.wait("tensor", ch1b, n1b)
                    P.wait("scalar", ch1b, n1b)
                P.barrier()
            for h2i in range(2):
                for ic in range(4):
                    kjmax = (ic + 1) * 8 - 1
                    with ExitStack() as s2o:
                        pOut = psum(nc, s2o, "pOut", [65, 1024])
                        with ExitStack() as s2:
                            pST = psum(nc, s2, "pST", [128, 2 * 1024])
                            chP = P.chp(5); chPV = P.chp(6); chE = P.chp(7)
                            for kj in range(kjmax + 1):
                                par = kj % 2
                                if kj >= 2:
                                    P.wait("tensor", chE, kj - 1)
                                for j2 in range(2):
                                    P.emit("tensor", lambda e, h2i=h2i, ic=ic, kj=kj, par=par, j2=j2: e.matmul(
                                        pST[:, par * 1024 + j2 * 512:par * 1024 + (j2 + 1) * 512],
                                        lhsT=ka[h2i][:, kj * 128:(kj + 1) * 128],
                                        rhs=qa[h2i][:, ic * 1024 + j2 * 512:ic * 1024 + (j2 + 1) * 512],
                                        start=True, stop=True),
                                        inc=chP if j2 == 1 else None)
                                P.wait("scalar", chP, 3 * kj + 1)
                                if kj >= 2:
                                    P.wait("scalar", chPV, kj - 1)
                                offs = (kj - ic * 8) * 128
                                if offs < 0:
                                    P.emit("scalar", lambda e, par=par: e.activation(
                                        out=ptb[par][:], in_=pST[:, par * 1024:(par + 1) * 1024],
                                        func=AF.Exp, scale=10.0), inc=chP, k=2)
                                else:
                                    if offs > 0:
                                        P.emit("vector", lambda e, par=par, offs=offs: e.memset(
                                            ptb[par][:, 0:offs], 0.0))
                                    if offs + 128 < 1024:
                                        P.emit("scalar", lambda e, par=par, offs=offs: e.activation(
                                            out=ptb[par][:, offs + 128:1024],
                                            in_=pST[:, par * 1024 + offs + 128:par * 1024 + 1024],
                                            func=AF.Exp, scale=10.0))
                                    P.emit("scalar", lambda e, par=par, offs=offs: e.activation(
                                        out=ptd[:], in_=pST[:, par * 1024 + offs:par * 1024 + offs + 128],
                                        func=AF.Exp, scale=10.0), inc=chP)
                                    P.wait("vector", chP, 3 * kj + 2)
                                    P.emit("vector", lambda e, par=par, offs=offs: e.tensor_tensor(
                                        out=ptb[par][:, offs:offs + 128], in0=ptd[:],
                                        in1=P.cutonesb[:], op=OP.mult), inc=chP)
                                P.wait("tensor", chP, 3 * kj + 3)
                                for j2 in range(2):
                                    P.emit("tensor", lambda e, h2i=h2i, kj=kj, par=par, j2=j2,
                                           pOut=pOut, kjmax=kjmax: e.matmul(
                                        pOut[:, j2 * 512:(j2 + 1) * 512],
                                        lhsT=vtok[:, kj * 130 + h2i * 65:kj * 130 + (h2i + 1) * 65],
                                        rhs=ptb[par][:, j2 * 512:(j2 + 1) * 512],
                                        start=(kj == 0), stop=(kj == kjmax)),
                                        inc=(chPV if j2 == 0 else chE) if True else None)
                            P.barrier()
                        with ExitStack() as s3:
                            pB2 = psum(nc, s3, "pB2", [64, 1024])
                            chN = P.chp(5)
                            P.emit("vector", lambda e, pOut=pOut: e.reciprocal(
                                out=rp[64:65, 0:1024], in_=pOut[64:65, :]), inc=chN)
                            P.wait("scalar", chN, 1)
                            P.emit("scalar", lambda e: e.copy(out=rpb[64:65, 0:1024],
                                                              in_=rp[64:65, 0:1024]))
                            P.emit("scalar", lambda e, pOut=pOut: e.copy(out=osb[:],
                                                                         in_=pOut[0:64, :]), inc=chN)
                            P.wait("tensor", chN, 2)
                            for j2 in range(2):
                                P.emit("tensor", lambda e, j2=j2: e.matmul(
                                    pB2[:, j2 * 512:(j2 + 1) * 512],
                                    lhsT=P.cones128b[64:65, 0:64],
                                    rhs=rpb[64:65, j2 * 512:(j2 + 1) * 512],
                                    start=True, stop=True), inc=chN if j2 == 1 else None)
                            P.wait("vector", chN, 3)
                            P.emit("vector", lambda e, dt_i=dt_i, h2i=h2i, ic=ic: e.tensor_tensor(
                                out=out_feat[64 * h2i:64 * (h2i + 1),
                                             dt_i * T + ic * 1024:dt_i * T + (ic + 1) * 1024],
                                in0=osb[:], in1=pB2[:], op=OP.mult), inc=chN)
                            P.barrier()
    P.barrier()

def emit_attn_l0_tail(P, stack, out_feat, wob, lngc, lnbc, gwbd, gb, d_h2, d_gates):
    nc = P.nc
    with ExitStack() as sW:
        af = sbuf(nc, sW, "af", [128, ND * T], BF16)
        wocol = sbuf(nc, sW, "wocol", [128, 8 * 128], BF16)
        sqc2 = sbuf(nc, sW, "sqc2", [128, 512], BF16)
        murow = sbuf(nc, sW, "murow", [1, T], BF16)
        rstrow = sbuf(nc, sW, "rstrow", [1, T], BF16)
        rc1 = sbuf(nc, sW, "rc1", [1, 512])
        rc2 = sbuf(nc, sW, "rc2", [1, 512])
        rc3 = sbuf(nc, sW, "rc3", [1, 512])
        slng = sbuf(nc, sW, "slng", [128, 8])
        slnb = sbuf(nc, sW, "slnb", [128, 8])
        gwb = sbuf(nc, sW, "gwb", [128, 8 * 8], BF16)
        gbb = sbuf(nc, sW, "gbb", [1, 8], BF16)
        h2tc = sbuf(nc, sW, "h2tc", [128, 8 * 512], BF16)
        tsc1 = sbuf(nc, sW, "tsc1", [128, 512])
        tsc2 = sbuf(nc, sW, "tsc2", [128, 512])
        gmx = sbuf(nc, sW, "gmx", [128, 4])
        gts = sbuf(nc, sW, "gts", [128, 32])
        gsm = sbuf(nc, sW, "gsm", [128, 4])
        gr = sbuf(nc, sW, "gr", [128, 4])
        gout = sbuf(nc, sW, "gout", [128, 32])
        gcols = sbuf(nc, sW, "gcols", [128, 4])
        chW = P.chp(1)
        P.dma("sync", slng[:], lngc, inc=chW)
        P.dma("sync", slnb[:], lnbc, inc=chW)
        gbf = sbuf(nc, sW, "gbf", [1, 8])
        for k in range(ND):
            P.dma("sync", gwb[:, k * 8:(k + 1) * 8], gwbd[k * 128:(k + 1) * 128, :], inc=chW)
        P.dma("sync", gbf[:], gb, inc=chW)
        P.wait("scalar", chW)
        P.emit("scalar", lambda e: e.copy(out=gbb[:], in_=gbf[:]))
        P.barrier()
        with ExitStack() as sW1:
            pW = psum(nc, sW1, "pW", [128, 2 * 512])
            chm = P.chp(2); che = P.chp(3); chw2 = P.chp(4)
            for m in range(ND):
                if m >= 1:
                    P.wait("sync", chm, m * 8)
                for k in range(ND):
                    P.dma("sync", wocol[:, k * 128:(k + 1) * 128],
                          wob[k * 128:(k + 1) * 128, m * 128:(m + 1) * 128], inc=chw2)
                P.wait("tensor", chw2, 16 * 8 * (m + 1))
                for tc in range(8):
                    par = tc % 2
                    if m * 8 + tc >= 2:
                        P.wait("tensor", che, m * 8 + tc - 1)
                    for k in range(ND):
                        P.emit("tensor", lambda e, m=m, tc=tc, k=k, par=par: e.matmul(
                            pW[:, par * 512:(par + 1) * 512],
                            lhsT=wocol[:, k * 128:(k + 1) * 128],
                            rhs=out_feat[:, k * T + tc * 512:k * T + (tc + 1) * 512],
                            start=(k == 0), stop=(k == ND - 1)),
                            inc=chm if k == ND - 1 else None)
                    P.wait("scalar", chm, m * 8 + tc + 1)
                    P.emit("scalar", lambda e, m=m, tc=tc, par=par: e.copy(
                        out=af[:, m * T + tc * 512:m * T + (tc + 1) * 512],
                        in_=pW[:, par * 512:(par + 1) * 512]), inc=che)
            P.barrier()
        with ExitStack() as sW2:
            pS0 = psum(nc, sW2, "pS0", [1, 512])
            pS1 = psum(nc, sW2, "pS1", [1, 512])
            chs = P.chp(2)
            ns = 0
            for tc in range(8):
                for k in range(ND):
                    P.emit("tensor", lambda e, k=k, tc=tc: e.matmul(
                        pS0[:], lhsT=P.cones128b[:, 0:1],
                        rhs=af[:, k * T + tc * 512:k * T + (tc + 1) * 512],
                        start=(k == 0), stop=(k == ND - 1)))
                for k in range(ND):
                    if k >= 1:
                        P.wait("scalar", chs, ns)
                    P.emit("scalar", lambda e, k=k, tc=tc: e.activation(
                        out=sqc2[:], in_=af[:, k * T + tc * 512:k * T + (tc + 1) * 512],
                        func=AF.Square), inc=chs)
                    ns += 1
                    P.wait("tensor", chs, ns)
                    P.emit("tensor", lambda e, k=k: e.matmul(
                        pS1[:], lhsT=P.cones128b[:, 1:2], rhs=sqc2[:],
                        start=(k == 0), stop=(k == ND - 1)), inc=chs)
                    ns += 1
                P.wait("scalar", chs, ns)
                P.emit("scalar", lambda e, tc=tc: e.activation(out=murow[:, tc * 512:(tc + 1) * 512],
                                                               in_=pS0[:], func=AF.Copy,
                                                               scale=1.0 / D))
                P.emit("scalar", lambda e: e.activation(out=rc1[:], in_=pS0[:], func=AF.Square,
                                                        scale=1.0 / D))
                P.emit("scalar", lambda e: e.activation(out=rc2[:], in_=pS1[:], func=AF.Copy,
                                                        scale=1.0 / D), inc=chs)
                ns += 1
                P.wait("vector", chs, ns)
                P.emit("vector", lambda e: e.tensor_tensor(out=rc3[:], in0=rc2[:], in1=rc1[:],
                                                           op=OP.subtract), inc=chs)
                ns += 1
                P.wait("scalar", chs, ns)
                P.emit("scalar", lambda e: e.activation(out=rc1[:], in_=rc3[:], func=AF.Ln,
                                                        bias=P.scal("eps_ln")[0:1, :]))
                P.drain("scalar")
                P.emit("scalar", lambda e, tc=tc: e.activation(
                    out=rstrow[:, tc * 512:(tc + 1) * 512], in_=rc1[:], func=AF.Exp,
                    scale=-0.5), inc=chs)
                ns += 1
                P.wait("tensor", chs, ns)
            P.barrier()
        with ExitStack() as sW3:
            pMu = psum(nc, sW3, "pMu", [128, 512])
            pRs = psum(nc, sW3, "pRs", [128, 512])
            pG = psum(nc, sW3, "pG", [128, 32])
            chh = P.chp(2); chd = P.chp(4)
            nh = 0
            for tc in range(8):
                P.emit("tensor", lambda e, tc=tc: e.matmul(
                    pMu[:], lhsT=P.cones128b[0:1, :], rhs=murow[:, tc * 512:(tc + 1) * 512],
                    start=True, stop=True))
                P.emit("tensor", lambda e, tc=tc: e.matmul(
                    pRs[:], lhsT=P.cones128b[0:1, :], rhs=rstrow[:, tc * 512:(tc + 1) * 512],
                    start=True, stop=True), inc=chh)
                nh += 1
                P.wait("vector", chh, nh)
                for m in range(ND):
                    P.emit("vector", lambda e, m=m, tc=tc: e.tensor_tensor(
                        out=tsc1[:], in0=af[:, m * T + tc * 512:m * T + (tc + 1) * 512],
                        in1=pMu[:], op=OP.subtract))
                    P.drain("vector")
                    P.emit("vector", lambda e, m=m: e.tensor_tensor(
                        out=tsc2[:], in0=tsc1[:], in1=pRs[:], op=OP.mult))
                    P.drain("vector")
                    P.emit("vector", lambda e, m=m: e.tensor_scalar(
                        out=h2tc[:, m * 512:(m + 1) * 512], in0=tsc2[:],
                        scalar1=slng[:, m:m + 1], scalar2=slnb[:, m:m + 1],
                        op0=OP.mult, op1=OP.add), inc=chh)
                    nh += 1
                P.wait("sync", chh, nh)
                for m in range(ND):
                    P.dma("sync", d_h2[m * 128:(m + 1) * 128, tc * 512:(tc + 1) * 512],
                          h2tc[:, m * 512:(m + 1) * 512], inc=chd)
                P.barrier()
                for tb in range(4):
                    for m in range(ND):
                        P.emit("tensor", lambda e, tb=tb, m=m: e.matmul(
                            pG[:, tb * 8:(tb + 1) * 8],
                            lhsT=h2tc[:, m * 512 + tb * 128:m * 512 + (tb + 1) * 128],
                            rhs=gwb[:, m * 8:(m + 1) * 8],
                            start=(m == 0), stop=False))
                    P.emit("tensor", lambda e, tb=tb: e.matmul(
                        pG[:, tb * 8:(tb + 1) * 8], lhsT=P.cones128b[0:1, :],
                        rhs=gbb[:], start=False, stop=True))
                P.barrier()
                for tb in range(4):
                    P.emit("vector", lambda e, tb=tb: e.reduce_max(
                        out=gmx[:, tb:tb + 1], in_=pG[:, tb * 8:(tb + 1) * 8], axis=AX,
                        negate=True))
                    P.drain("vector")
                P.barrier()
                for tb in range(4):
                    P.emit("scalar", lambda e, tb=tb: e.activation(
                        out=gts[:, tb * 8:(tb + 1) * 8], in_=pG[:, tb * 8:(tb + 1) * 8],
                        func=AF.Exp, bias=gmx[:, tb:tb + 1]))
                    P.drain("scalar")
                P.barrier()
                for tb in range(4):
                    P.emit("vector", lambda e, tb=tb: e.tensor_reduce(
                        out=gsm[:, tb:tb + 1], in_=gts[:, tb * 8:(tb + 1) * 8], axis=AX,
                        op=OP.add))
                    P.drain("vector")
                for tb in range(4):
                    P.emit("vector", lambda e, tb=tb: e.reciprocal(
                        out=gr[:, tb:tb + 1], in_=gsm[:, tb:tb + 1]))
                    P.drain("vector")
                for tb in range(4):
                    P.emit("vector", lambda e, tb=tb: e.tensor_scalar(
                        out=gout[:, tb * 8:(tb + 1) * 8], in0=gts[:, tb * 8:(tb + 1) * 8],
                        scalar1=gr[:, tb:tb + 1], scalar2=None, op0=OP.mult))
                    P.drain("vector")
                P.barrier()
                chgd = P.chp(5)
                for tb in range(4):
                    P.dma("sync", d_gates[tc * 512 + tb * 128:tc * 512 + (tb + 1) * 128, :],
                          gout[:, tb * 8:(tb + 1) * 8], inc=chgd)
                P.barrier()
    P.barrier()

ATTN_CONSTS = ["c_id128b", "c_id128f", "c_utonesb", "c_ones128b", "c_bd2b", "c_bd2tb"]

def attn_l0_host_prep(ins):
    return {
        "wq": np.ascontiguousarray(ins["L0_ffn_Wq"]),
        "wk": np.ascontiguousarray(ins["L0_ffn_Wk"]),
        "wv": np.ascontiguousarray(ins["L0_ffn_Wv"]),
        "wo": np.ascontiguousarray(ins["L0_ffn_Wo"]),
        "qs_col": np.ascontiguousarray(ins["L0_ffn_q_scale"].reshape(1024, 1)),
        "ks_col": np.ascontiguousarray(ins["L0_ffn_k_scale"].reshape(1024, 1)),
        "lngc": np.ascontiguousarray(ins["L0_ffn_ln_g"].reshape(8, 128).T),
        "lnbc": np.ascontiguousarray(ins["L0_ffn_ln_b"].reshape(8, 128).T),
        "gw": np.ascontiguousarray(ins["L0_ffn_gate_w"]),
        "gb": np.ascontiguousarray(ins["L0_ffn_gate_b"].reshape(1, 8)),
    }

def build_stage_c():
    nc = bass.Bass("TRN2", target_bir_lowering=False)
    hf = nc.dram_tensor("h_feat", [D, T], BF16, kind="ExternalInput").ap()
    wq = nc.dram_tensor("wq", [D, D], F32, kind="ExternalInput").ap()
    wk = nc.dram_tensor("wk", [D, D], F32, kind="ExternalInput").ap()
    wv = nc.dram_tensor("wv", [D, D], F32, kind="ExternalInput").ap()
    wo = nc.dram_tensor("wo", [D, D], F32, kind="ExternalInput").ap()
    qs = nc.dram_tensor("qs_col", [D, 1], F32, kind="ExternalInput").ap()
    ks = nc.dram_tensor("ks_col", [D, 1], F32, kind="ExternalInput").ap()
    lngc = nc.dram_tensor("lngc", [128, 8], F32, kind="ExternalInput").ap()
    lnbc = nc.dram_tensor("lnbc", [128, 8], F32, kind="ExternalInput").ap()
    gw = nc.dram_tensor("gw", [D, 8], F32, kind="ExternalInput").ap()
    gb = nc.dram_tensor("gb", [1, 8], F32, kind="ExternalInput").ap()
    d_h2 = nc.dram_tensor("h2_feat", [D, T], BF16, kind="ExternalOutput").ap()
    d_gates = nc.dram_tensor("gates", [T, 8], F32, kind="ExternalOutput").ap()
    with ExitStack() as stack:
        P = Prog(nc, stack)
        load_consts(P, stack, ATTN_CONSTS)
        wqb = nc.dram_tensor("wqb", [D, D], BF16, kind="Internal").ap()
        wkb = nc.dram_tensor("wkb", [D, D], BF16, kind="Internal").ap()
        wvb = nc.dram_tensor("wvb", [D, D], BF16, kind="Internal").ap()
        wob = nc.dram_tensor("wob", [D, D], BF16, kind="Internal").ap()
        gwbd = nc.dram_tensor("gwbd", [D, 8], BF16, kind="Internal").ap()
        with ExitStack() as s0:
            emit_cast_f32_to_bf16_dram(P, s0, wq, wqb, D, D)
            emit_cast_f32_to_bf16_dram(P, s0, wk, wkb, D, D)
            emit_cast_f32_to_bf16_dram(P, s0, wv, wvb, D, D)
            emit_cast_f32_to_bf16_dram(P, s0, wo, wob, D, D)
            emit_cast_f32_to_bf16_dram(P, s0, gw, gwbd, D, 8)
        out_feat = sbuf(nc, stack, "at_out", [128, ND * T], BF16)
        with ExitStack() as sh:
            h_feat = sbuf(nc, sh, "h_feat_sb", [128, ND * T], BF16)
            ch0 = P.ch()
            for dd in range(ND):
                P.dma("sync", h_feat[:, dd * T:(dd + 1) * T], hf[dd * 128:(dd + 1) * 128, :], inc=ch0)
            for e in Prog.ENGS:
                P.wait(e, ch0)
            emit_attn_l0_flash(P, stack, h_feat, out_feat, wqb, wkb, wvb, qs, ks)
        emit_attn_l0_tail(P, stack, out_feat, wob, lngc, lnbc, gwbd, gb, d_h2, d_gates)
        block = stack.enter_context(nc.Block())
        P.finalize(block)
    return nc



def emit_moe_l0(P, stack, d_h2, w1b, w2b, b1_t, b2, b2r_all, gates_d, lng_r, lnb_r, d_g, d_moe_ext=None):
    nc = P.nc
    d_moe = d_moe_ext if d_moe_ext is not None else nc.dram_tensor(_uniq("d_moe"), [T, D], F32, kind="Internal").ap()
    gs = sbuf(nc, stack, "gs", [128, NT * 8])
    sb1 = sbuf(nc, stack, "sb1", [128, 256])
    sb2 = sbuf(nc, stack, "sb2", [8, 1024])
    sb2b = sbuf(nc, stack, "sb2b", [8, 1024], BF16)

    ch0 = P.chp(0)
    for tt in range(NT):
        P.dma("sync", gs[:, tt * 8:(tt + 1) * 8], gates_d[tt * 128:(tt + 1) * 128, :], inc=ch0)
    P.dma("sync", sb1[:], b1_t, inc=ch0)
    P.dma("sync", sb2[:], b2, inc=ch0)
    P.wait("scalar", ch0)
    P.emit("scalar", lambda e: e.copy(out=sb2b[:], in_=sb2[:]))
    P.barrier()

    sE = ExitStack()
    if os.environ.get("DTRUNC") == "pre":
        sE = ExitStack()
        return
    wsb1 = sbuf(nc, sE, "wsb1", [128, 8 * T], BF16)
    wsb2 = sbuf(nc, sE, "wsb2", [128, 32 * D], BF16)
    h2c = sbuf(nc, sE, "h2c", [128, 2 * 8 * 512], BF16)
    hid = sbuf(nc, sE, "hid", [128, 32 * 512], BF16)
    mout = sbuf(nc, sE, "mout", [128, 1024])
    macc = sbuf(nc, sE, "macc", [128, 1024])
    nexp = int(os.environ.get("DNEXP", "8"))
    for e_i in range(nexp):
        chwl = P.chp(1)
        for k in range(ND):
            P.dma("sync", wsb1[:, k * T:(k + 1) * T],
                  w1b[e_i * D + k * 128:e_i * D + (k + 1) * 128, :], inc=chwl)
        for m in range(32):
            P.dma("sync", wsb2[:, m * D:(m + 1) * D],
                  w2b[e_i * 4096 + m * 128:e_i * 4096 + (m + 1) * 128, :], inc=chwl)
        for en in Prog.ENGS:
            P.wait(en, chwl)
        chh = P.chp(2); chm = P.chp(3); che = P.chp(4); chd = P.chp(5)
        nm = 0; ne = 0
        for tc in range(8):
            hpar = tc % 2
            P.dma("sync", h2c[:, hpar * 4096:hpar * 4096 + 4096],
                  d_h2.rearrange("(k p) t -> p (k t)", p=128)[:, [slice(0, 1)]] if False else
                  d_h2[0:128, tc * 512:(tc + 1) * 512], inc=chh) if False else None
            for k in range(ND):
                P.dma("sync", h2c[:, hpar * 4096 + k * 512:hpar * 4096 + (k + 1) * 512],
                      d_h2[k * 128:(k + 1) * 128, tc * 512:(tc + 1) * 512], inc=chh)
            for en in ("tensor",):
                P.wait(en, chh)
            with ExitStack() as s1:
                ph = psum(nc, s1, "ph", [128, 2 * 512])
                for m in range(32):
                    par = m % 2
                    if m >= 2:
                        P.wait("tensor", che, ne - 1)
                    for k in range(ND):
                        P.emit("tensor", lambda e, m=m, k=k, par=par, hpar=hpar: e.matmul(
                            ph[:, par * 512:(par + 1) * 512],
                            lhsT=wsb1[:, k * T + m * 128:k * T + (m + 1) * 128],
                            rhs=h2c[:, hpar * 4096 + k * 512:hpar * 4096 + (k + 1) * 512],
                            start=(k == 0), stop=(k == ND - 1)),
                            inc=chm if k == ND - 1 else None)
                    nm += 1
                    P.wait("scalar", chm, nm)
                    P.emit("scalar", lambda e, m=m, e_i=e_i, par=par: e.activation(
                        out=hid[:, m * 512:(m + 1) * 512], in_=ph[:, par * 512:(par + 1) * 512],
                        func=AF.Gelu_apprx_tanh, bias=sb1[:, e_i * 32 + m:e_i * 32 + m + 1]),
                        inc=che)
                    ne += 1
                P.barrier()
            with ExitStack() as s2:
                pO = psum(nc, s2, "pO", [128, 4 * 1024])
                chw2 = P.chp(6); chw3 = P.chp(7)
                chdl = P.chp(5); chds = P.chp(8)
                for t128 in range(4):
                    for m in range(32):
                        for j in range(2):
                            P.emit("tensor", lambda e, t128=t128, m=m, j=j: e.matmul(
                                pO[:, t128 * 1024 + j * 512:t128 * 1024 + (j + 1) * 512],
                                lhsT=hid[:, m * 512 + t128 * 128:m * 512 + (t128 + 1) * 128],
                                rhs=wsb2[:, m * D + j * 512:m * D + (j + 1) * 512],
                                start=(m == 0), stop=(m == 31)),
                                inc=chw2 if (m == 31 and j == 1) else None)
                    rows = slice(tc * 512 + t128 * 128, tc * 512 + (t128 + 1) * 128)
                    if e_i > 0:
                        P.dma("sync", macc[:], d_moe[rows, :], inc=chdl)
                        P.wait("vector", chdl)
                    P.wait("vector", chds)
                    P.wait("vector", chw2, t128 + 1)
                    gcol = gs[:, (tc * 4 + t128) * 8 + e_i:(tc * 4 + t128) * 8 + e_i + 1]
                    if e_i == 0:
                        P.emit("vector", lambda e, t128=t128, gcol=gcol: e.tensor_scalar(
                            out=mout[:], in0=pO[:, t128 * 1024:(t128 + 1) * 1024],
                            scalar1=gcol, scalar2=None, op0=OP.mult), inc=chw3)
                    else:
                        P.emit("vector", lambda e, t128=t128, gcol=gcol: e.scalar_tensor_tensor(
                            out=mout[:], in0=pO[:, t128 * 1024:(t128 + 1) * 1024],
                            scalar=gcol, in1=macc[:], op0=OP.mult, op1=OP.add), inc=chw3)
                    P.wait("sync", chw3, t128 + 1)
                    P.dma("sync", d_moe[rows, :], mout[:], inc=chds)
                P.barrier()
            P.barrier()
    sE.close()
    if os.environ.get("DTRUNC") == "experts":
        return
    with ExitStack() as sF:
        mo = sbuf(nc, sF, "mo", [128, 2 * 1024])
        mo2 = sbuf(nc, sF, "mo2", [128, 1024])
        wsum = sbuf(nc, sF, "wsum", [128, 1024])
        b2r = sbuf(nc, sF, "b2r", [128, 8 * 1024])
        gcl = sbuf(nc, sF, "gcl", [128, 8])
        slg = sbuf(nc, sF, "slg", [128, D])
        slb = sbuf(nc, sF, "slb", [128, D])
        gt = sbuf(nc, sF, "gt", [128, 3 * 1024], BF16)
        chf = P.chp(1)
        P.dma("sync", slg[:], lng_r, inc=chf)
        P.dma("sync", slb[:], lnb_r, inc=chf)
        for e8 in range(8):
            P.dma("sync", b2r[:, e8 * 1024:(e8 + 1) * 1024],
                  b2r_all[:, e8 * 1024:(e8 + 1) * 1024], inc=chf)
        P.barrier()
        chl = P.chp(1); chs = [P.chp(2), P.chp(5)]; chst = [P.chp(3), P.chp(4)]
        nl = 0
        for tt in range(NT):
            par = tt % 2
            if tt >= 2:
                P.wait("sync", chl, nl)
            P.dma("sync", mo[:, par * 1024:(par + 1) * 1024],
                  d_moe[tt * 128:(tt + 1) * 128, :], inc=chs[par])
            P.wait("vector", chs[par], 16 * (tt // 2 + 1))
            P.emit("vector", lambda e, tt=tt: e.tensor_scalar(
                out=wsum[:], in0=b2r[:, 0:1024], scalar1=gs[:, tt * 8:tt * 8 + 1],
                scalar2=None, op0=OP.mult))
            for e8 in range(1, 8):
                P.drain("vector")
                P.emit("vector", lambda e, tt=tt, e8=e8: e.scalar_tensor_tensor(
                    out=wsum[:], in0=b2r[:, e8 * 1024:(e8 + 1) * 1024],
                    scalar=gs[:, tt * 8 + e8:tt * 8 + e8 + 1], in1=wsum[:],
                    op0=OP.mult, op1=OP.add))
            P.drain("vector")
            P.emit("vector", lambda e, par=par: e.tensor_tensor(
                out=mo2[:], in0=mo[:, par * 1024:(par + 1) * 1024], in1=wsum[:], op=OP.add),
                inc=chl)
            nl += 1
            P.wait("scalar", chl, nl)
            P.emit("scalar", lambda e: e.activation(out=gt[:, 2048:3072], in_=mo2[:], func=AF.Copy,
                                                    accum_out=gcl[:, 0:1]))
            P.drain("scalar")
            P.emit("scalar", lambda e: e.activation(out=gt[:, 2048:3072], in_=mo2[:],
                                                    func=AF.Square, accum_out=gcl[:, 1:2]))
            P.drain("scalar")
            P.emit("scalar", lambda e: e.activation(out=gcl[:, 2:3], in_=gcl[:, 0:1],
                                                    func=AF.Copy, scale=1.0 / D))
            P.emit("scalar", lambda e: e.activation(out=gcl[:, 3:4], in_=gcl[:, 0:1],
                                                    func=AF.Square, scale=1.0 / D))
            P.emit("scalar", lambda e: e.activation(out=gcl[:, 4:5], in_=gcl[:, 1:2],
                                                    func=AF.Copy, scale=1.0 / D), inc=chl)
            nl += 1
            P.wait("vector", chl, nl)
            P.emit("vector", lambda e: e.tensor_tensor(out=gcl[:, 5:6], in0=gcl[:, 4:5],
                                                       in1=gcl[:, 3:4], op=OP.subtract), inc=chl)
            nl += 1
            P.wait("scalar", chl, nl)
            P.emit("scalar", lambda e: e.activation(out=gcl[:, 6:7], in_=gcl[:, 5:6],
                                                    func=AF.Ln, bias=P.scal("eps_ln")))
            P.drain("scalar")
            P.emit("scalar", lambda e: e.activation(out=gcl[:, 7:8], in_=gcl[:, 6:7],
                                                    func=AF.Exp, scale=-0.5), inc=chl)
            nl += 1
            P.wait("vector", chl, nl)
            P.emit("vector", lambda e, par=par: e.tensor_scalar(
                out=mo[:, par * 1024:(par + 1) * 1024], in0=mo2[:],
                scalar1=gcl[:, 2:3], scalar2=gcl[:, 7:8], op0=OP.subtract, op1=OP.mult))
            P.drain("vector")
            P.emit("vector", lambda e, par=par: e.tensor_tensor(
                out=mo2[:], in0=mo[:, par * 1024:(par + 1) * 1024], in1=slg[:], op=OP.mult))
            P.drain("vector")
            P.emit("vector", lambda e, par=par: e.tensor_tensor(
                out=mo[:, par * 1024:(par + 1) * 1024], in0=mo2[:], in1=slb[:], op=OP.add), inc=chl)
            nl += 1
            P.wait("scalar", chl, nl)
            if tt >= 2:
                P.wait("scalar", chst[par], 16 * (tt // 2))
            P.emit("scalar", lambda e, par=par: e.activation(
                out=gt[:, par * 1024:(par + 1) * 1024],
                in_=mo[:, par * 1024:(par + 1) * 1024], func=AF.Silu), inc=chl)
            nl += 1
            P.wait("sync", chl, nl)
            P.dma("sync", d_g[tt * 128:(tt + 1) * 128, :], gt[:, par * 1024:(par + 1) * 1024],
                  inc=chst[par])
        P.barrier()


def moe_l0_host_prep(ins):
    b1 = ins["L0_ffn_b1"]
    b1_t = b1.reshape(8, 32, 128).transpose(2, 0, 1).reshape(128, 256)
    return {
        "w1": np.ascontiguousarray(ins["L0_ffn_w1"].reshape(8 * 1024, 4096)),
        "w2": np.ascontiguousarray(ins["L0_ffn_w2"].reshape(8 * 4096, 1024)),
        "b1_t": np.ascontiguousarray(b1_t),
        "b2": np.ascontiguousarray(ins["L0_ffn_b2"]),
        "b2r_all": np.ascontiguousarray(np.broadcast_to(ins["L0_ffn_b2"].reshape(1, 8 * 1024),
                                                        (128, 8 * 1024)).copy()),
        "lng_r": repl_rows(ins["L0_ffn_ln_g"]),
        "lnb_r": repl_rows(ins["L0_ffn_ln_b"]),
    }

MOE_CONSTS = ["c_ones128b", "c_id128b"]

def build_stage_d():
    nc = bass.Bass("TRN2", target_bir_lowering=False)
    d_h2 = nc.dram_tensor("h2_feat", [D, T], BF16, kind="ExternalInput").ap()
    w1 = nc.dram_tensor("w1", [8 * 1024, 4096], F32, kind="ExternalInput").ap()
    w2 = nc.dram_tensor("w2", [8 * 4096, 1024], F32, kind="ExternalInput").ap()
    b1_t = nc.dram_tensor("b1_t", [128, 256], F32, kind="ExternalInput").ap()
    b2 = nc.dram_tensor("b2", [8, 1024], F32, kind="ExternalInput").ap()
    b2r_all = nc.dram_tensor("b2r_all", [128, 8 * 1024], F32, kind="ExternalInput").ap()
    gates_d = nc.dram_tensor("gates", [T, 8], F32, kind="ExternalInput").ap()
    lng_r = nc.dram_tensor("lng_r", [128, D], F32, kind="ExternalInput").ap()
    lnb_r = nc.dram_tensor("lnb_r", [128, D], F32, kind="ExternalInput").ap()
    d_g = nc.dram_tensor("g_out", [T, D], BF16, kind="ExternalOutput").ap()
    d_moe_ext = nc.dram_tensor("moe_dbg", [T, D], F32, kind="ExternalOutput").ap() if DBG else None
    with ExitStack() as stack:
        P = Prog(nc, stack)
        load_consts(P, stack, MOE_CONSTS)
        w1b = nc.dram_tensor("w1b", [8 * 1024, 4096], BF16, kind="Internal").ap()
        w2b = nc.dram_tensor("w2b", [8 * 4096, 1024], BF16, kind="Internal").ap()
        dt_ = os.environ.get("DTRUNC", "")
        if dt_ != "nocast1":
            with ExitStack() as s0:
                emit_cast_f32_to_bf16_dram(P, s0, w1, w1b, 8 * 1024, 4096)
        if dt_ not in ("cast1", "nocast2"):
            with ExitStack() as s0:
                emit_cast_f32_to_bf16_dram(P, s0, w2, w2b, 8 * 4096, 1024)
        if dt_ not in ("cast1", "cast2"):
            emit_moe_l0(P, stack, d_h2, w1b, w2b, b1_t, b2, b2r_all, gates_d, lng_r, lnb_r, d_g, d_moe_ext)
        else:
            ch_f = P.ch("fill")
            gtmp = sbuf(nc, stack, "gtmp", [128, 1024], BF16)
            P.emit("vector", lambda e: e.memset(gtmp[:], 0.0), inc=ch_f)
            P.wait("sync", ch_f)
            for tt in range(NT):
                P.dma("sync", d_g[tt * 128:(tt + 1) * 128, :], gtmp[:], inc=ch_f)
            P.wait("sync", ch_f)
        block = stack.enter_context(nc.Block())
        P.finalize(block)
    return nc



def emit_combine(P, stack, x_dram, s_dram, g_dram, xout_dram, out_f32=True):
    nc = P.nc
    with ExitStack() as sC:
        xb = sbuf(nc, sC, "cmb_x", [128, 2 * D])
        sb_ = sbuf(nc, sC, "cmb_s", [128, 2 * D], BF16)
        gb_ = sbuf(nc, sC, "cmb_g", [128, 2 * D], BF16)
        ob = sbuf(nc, sC, "cmb_o", [128, 2 * D])
        tb_ = sbuf(nc, sC, "cmb_t", [128, D])
        chl = [P.chp(0), P.chp(4)]; chv = P.chp(1); chst = [P.chp(2), P.chp(3)]
        for tt in range(NT):
            par = tt % 2
            if tt >= 2:
                P.wait("sync", chv, 2 * (tt - 1))
            P.dma("sync", xb[:, par * D:(par + 1) * D], x_dram[tt * 128:(tt + 1) * 128, :],
                  inc=chl[par])
            P.dma("sync", sb_[:, par * D:(par + 1) * D], s_dram[tt * 128:(tt + 1) * 128, :],
                  inc=chl[par])
            P.dma("sync", gb_[:, par * D:(par + 1) * D], g_dram[tt * 128:(tt + 1) * 128, :],
                  inc=chl[par])
            P.wait("vector", chl[par], 48 * (tt // 2 + 1))
            P.emit("vector", lambda e, par=par: e.tensor_tensor(
                out=tb_[:], in0=sb_[:, par * D:(par + 1) * D], in1=gb_[:, par * D:(par + 1) * D],
                op=OP.mult), inc=chv)
            P.drain("vector")
            if tt >= 2:
                P.wait("vector", chst[par], 16 * (tt // 2))
            P.emit("vector", lambda e, par=par: e.tensor_tensor(
                out=ob[:, par * D:(par + 1) * D], in0=xb[:, par * D:(par + 1) * D], in1=tb_[:],
                op=OP.add), inc=chv)
            P.wait("sync", chv, 2 * (tt + 1))
            P.dma("sync", xout_dram[tt * 128:(tt + 1) * 128, :], ob[:, par * D:(par + 1) * D],
                  inc=chst[par])
        P.barrier()



def emit_attn_l1(P, stack, xf, wqb, wkb, wvb, wob, bq_c, bkv_c, bo_r, d_s1):
    nc = P.nc
    out_feat = sbuf(nc, stack, "l1_out", [128, ND * T], BF16)
    with ExitStack() as sA:
        kvw = sbuf(nc, sA, "kvw", [128, 2 * 8 * 64], BF16)
        sbkv = sbuf(nc, sA, "sbkv", [64, 2])
        sbq = sbuf(nc, sA, "sbq", [128, 8])
        ka = sbuf(nc, sA, "ka1", [65, T], BF16)
        qa = sbuf(nc, sA, "qa1", [65, T], BF16)
        qf = sbuf(nc, sA, "qf1", [128, T], BF16)
        vtok = sbuf(nc, sA, "vtok1", [128, NT * 65], BF16)
        vst = sbuf(nc, sA, "vst1", [64, 2 * 512], BF16)
        wcol = sbuf(nc, sA, "wcol1", [128, 8 * 128], BF16)
        nmall = sbuf(nc, sA, "nmall1", [128, 64])
        nmt = sbuf(nc, sA, "nmt1", [32, 128], BF16)
        d_nm = nc.dram_tensor(_uniq("d_nm1"), [32, 128], BF16, kind="Internal").ap()
        rp = sbuf(nc, sA, "rp1", [65, 1024])
        rpb = sbuf(nc, sA, "rpb1", [65, 1024], BF16)
        osb = sbuf(nc, sA, "osb1", [64, 1024], BF16)
        ptb = [sbuf(nc, sA, f"ptb1{i}", [128, 1024], BF16) for i in range(2)]
        ch0 = P.chp(0)
        for k in range(ND):
            P.dma("sync", kvw[:, k * 64:(k + 1) * 64], wkb[k * 128:(k + 1) * 128, :], inc=ch0)
            P.dma("sync", kvw[:, 512 + k * 64:512 + (k + 1) * 64],
                  wvb[k * 128:(k + 1) * 128, :], inc=ch0)
        P.dma("sync", sbkv[:], bkv_c, inc=ch0)
        P.dma("sync", sbq[:, 0:8], bq_c, inc=ch0)
        P.emit("vector", lambda e: e.memset(vtok[:], 1.0))
        P.emit("vector", lambda e: e.memset(ka[64:65, :], 1.0))
        P.barrier()
        with ExitStack() as sP:
            pkv = psum(nc, sP, "pkv", [64, 2 * 512])
            pT2 = psum(nc, sP, "pT21", [128, 1024], BF16)
            chm = P.chp(1); che = P.chp(2)
            nm = 0; ne = 0
            for tc in range(8):
                par = tc % 2
                if tc >= 2:
                    P.wait("tensor", che, ne - 1)
                for wi in range(2):
                    for k in range(ND):
                        P.emit("tensor", lambda e, tc=tc, wi=wi, k=k, par=par: e.matmul(
                            pkv[:, wi * 512:(wi + 1) * 512],
                            lhsT=kvw[:, wi * 512 + k * 64:wi * 512 + (k + 1) * 64],
                            rhs=xf[:, k * T + tc * 512:k * T + (tc + 1) * 512],
                            start=(k == 0), stop=(k == ND - 1)),
                            inc=chm if k == ND - 1 else None)
                    nm += 1
                P.wait("scalar", chm, nm)
                P.emit("scalar", lambda e, tc=tc: e.activation(
                    out=ka[0:64, tc * 512:(tc + 1) * 512], in_=pkv[:, 0:512],
                    func=AF.Identity, bias=sbkv[:, 0:1]))
                P.emit("scalar", lambda e, tc=tc, par=par: e.activation(
                    out=vst[:, par * 512:(par + 1) * 512], in_=pkv[:, 512:1024],
                    func=AF.Identity, bias=sbkv[:, 1:2]), inc=che)
                ne += 1
                P.wait("tensor", che, ne)
                for j in range(4):
                    P.emit("tensor", lambda e, j=j, par=par: e.transpose(
                        pT2[:, par * 512 + j * 128:par * 512 + j * 128 + 64],
                        in_=vst[:, par * 512 + j * 128:par * 512 + (j + 1) * 128],
                        identity=P.cid128b[0:64, 0:64]), inc=chm if j == 3 else None)
                nm += 1
                P.wait("scalar", chm, nm)
                for j in range(4):
                    kj = tc * 4 + j
                    P.emit("scalar", lambda e, kj=kj, j=j, par=par: e.copy(
                        out=vtok[:, kj * 65:kj * 65 + 64],
                        in_=pT2[:, par * 512 + j * 128:par * 512 + j * 128 + 64]),
                        inc=che if j == 3 else None)
                ne += 1
            P.barrier()
        for dt_i in range(ND):
            chw = P.chp(1); chm = P.chp(2); che = P.chp(3); chv = P.chp(4)
            for k in range(ND):
                P.dma("sync", wcol[:, k * 128:(k + 1) * 128],
                      wqb[k * 128:(k + 1) * 128, dt_i * 128:(dt_i + 1) * 128], inc=chw)
            P.wait("tensor", chw)
            with ExitStack() as sQ:
                pq = psum(nc, sQ, "pq", [128, 2 * 512])
                for tc in range(8):
                    par = tc % 2
                    if tc >= 2:
                        P.wait("tensor", che, tc - 1)
                    for k in range(ND):
                        P.emit("tensor", lambda e, tc=tc, k=k, par=par: e.matmul(
                            pq[:, par * 512:(par + 1) * 512],
                            lhsT=wcol[:, k * 128:(k + 1) * 128],
                            rhs=xf[:, k * T + tc * 512:k * T + (tc + 1) * 512],
                            start=(k == 0), stop=(k == ND - 1)),
                            inc=chm if k == ND - 1 else None)
                    P.wait("scalar", chm, tc + 1)
                    P.emit("scalar", lambda e, tc=tc, par=par, dt_i=dt_i: e.activation(
                        out=qf[:, tc * 512:(tc + 1) * 512], in_=pq[:, par * 512:(par + 1) * 512],
                        func=AF.Identity, bias=sbq[:, dt_i:dt_i + 1]), inc=che)
                P.barrier()
            for h2i in range(2):
                chq = P.chp(5)
                P.emit("scalar", lambda e, h2i=h2i: e.copy(
                    out=qa[0:64, :], in_=qf[64 * h2i:64 * h2i + 64, :]), inc=chq)
                for en in Prog.ENGS:
                    P.wait(en, chq)
                with ExitStack() as s1:
                    pS = psum(nc, s1, "pS1", [128, T])
                    ch1m = P.chp(1); ch1v = P.chp(2)
                    for qi in range(NT):
                        if qi >= 1:
                            P.wait("tensor", ch1v, qi)
                        for j in range(8):
                            P.emit("tensor", lambda e, qi=qi, j=j: e.matmul(
                                pS[:, j * 512:(j + 1) * 512],
                                lhsT=qa[0:64, qi * 128:(qi + 1) * 128],
                                rhs=ka[0:64, j * 512:(j + 1) * 512],
                                start=True, stop=True),
                                inc=ch1m if j == 7 else None)
                        P.wait("vector", ch1m, qi + 1)
                        P.emit("vector", lambda e, h2i=h2i, qi=qi: e.reduce_max(
                            out=nmall[:, h2i * 32 + qi:h2i * 32 + qi + 1],
                            in_=pS[:], axis=AX, negate=True), inc=ch1v)
                    P.barrier()
                with ExitStack() as s1b:
                    pTf = psum(nc, s1b, "pTf1", [128, 128])
                    ch1b = P.chp(1)
                    P.emit("tensor", lambda e, h2i=h2i: e.transpose(
                        pTf[0:32, 0:128], in_=nmall[:, h2i * 32:(h2i + 1) * 32],
                        identity=P.cid128f[:]), inc=ch1b)
                    P.wait("scalar", ch1b, 1)
                    P.emit("scalar", lambda e: e.copy(out=nmt[:], in_=pTf[0:32, 0:128]), inc=ch1b)
                    P.wait("sync", ch1b, 2)
                    P.dma("sync", d_nm, nmt[:], inc=ch1b)
                    P.wait("sync", ch1b, 18)
                    P.dma("sync", qa[64:65, :], d_nm.rearrange("a b -> (a b)").unsqueeze(0),
                          inc=ch1b)
                    P.barrier()
                for ic in range(4):
                    with ExitStack() as s2o:
                        pOut = psum(nc, s2o, "pOut1", [65, 1024])
                        with ExitStack() as s2:
                            pST = psum(nc, s2, "pST1", [128, 2 * 1024])
                            chP = P.chp(1); chPV = P.chp(2); chE = P.chp(3)
                            for kj in range(NT):
                                par = kj % 2
                                if kj >= 2:
                                    P.wait("tensor", chE, kj - 1)
                                for j2 in range(2):
                                    P.emit("tensor", lambda e, ic=ic, kj=kj, par=par, j2=j2: e.matmul(
                                        pST[:, par * 1024 + j2 * 512:par * 1024 + (j2 + 1) * 512],
                                        lhsT=ka[:, kj * 128:(kj + 1) * 128],
                                        rhs=qa[:, ic * 1024 + j2 * 512:ic * 1024 + (j2 + 1) * 512],
                                        start=True, stop=True),
                                        inc=chP if j2 == 1 else None)
                                P.wait("scalar", chP, 2 * kj + 1)
                                if kj >= 2:
                                    P.wait("scalar", chPV, kj - 1)
                                P.emit("scalar", lambda e, par=par: e.activation(
                                    out=ptb[par][:], in_=pST[:, par * 1024:(par + 1) * 1024],
                                    func=AF.Exp, scale=0.125), inc=chP)
                                P.wait("tensor", chP, 2 * kj + 2)
                                for j2 in range(2):
                                    P.emit("tensor", lambda e, kj=kj, par=par, j2=j2,
                                           pOut=pOut: e.matmul(
                                        pOut[:, j2 * 512:(j2 + 1) * 512],
                                        lhsT=vtok[:, kj * 65:(kj + 1) * 65],
                                        rhs=ptb[par][:, j2 * 512:(j2 + 1) * 512],
                                        start=(kj == 0), stop=(kj == NT - 1)),
                                        inc=(chPV if j2 == 0 else chE))
                            P.barrier()
                        with ExitStack() as s3:
                            pB2 = psum(nc, s3, "pB21", [64, 1024])
                            chN = P.chp(1)
                            P.emit("vector", lambda e, pOut=pOut: e.reciprocal(
                                out=rp[64:65, 0:1024], in_=pOut[64:65, :]), inc=chN)
                            P.wait("scalar", chN, 1)
                            P.emit("scalar", lambda e: e.copy(out=rpb[64:65, 0:1024],
                                                              in_=rp[64:65, 0:1024]))
                            P.emit("scalar", lambda e, pOut=pOut: e.copy(out=osb[:],
                                                                         in_=pOut[0:64, :]),
                                   inc=chN)
                            P.wait("tensor", chN, 2)
                            for j2 in range(2):
                                P.emit("tensor", lambda e, j2=j2: e.matmul(
                                    pB2[:, j2 * 512:(j2 + 1) * 512],
                                    lhsT=P.cones128b[64:65, 0:64],
                                    rhs=rpb[64:65, j2 * 512:(j2 + 1) * 512],
                                    start=True, stop=True), inc=chN if j2 == 1 else None)
                            P.wait("vector", chN, 3)
                            P.emit("vector", lambda e, dt_i=dt_i, h2i=h2i, ic=ic: e.tensor_tensor(
                                out=out_feat[64 * h2i:64 * (h2i + 1),
                                             dt_i * T + ic * 1024:dt_i * T + (ic + 1) * 1024],
                                in0=osb[:], in1=pB2[:], op=OP.mult), inc=chN)
                            P.barrier()
    with ExitStack() as sW:
        wo_sb = sbuf(nc, sW, "wo1_sb", [128, 8 * D], BF16)
        sbo = sbuf(nc, sW, "sbo", [128, D])
        so1 = sbuf(nc, sW, "so1", [128, 1024])
        so2 = sbuf(nc, sW, "so2", [128, 2 * 1024], BF16)
        chw = P.chp(0)
        for k in range(ND):
            P.dma("sync", wo_sb[:, k * D:(k + 1) * D], wob[k * 128:(k + 1) * 128, :], inc=chw)
        P.dma("sync", sbo[:], bo_r, inc=chw)
        P.barrier()
        with ExitStack() as sW1:
            pW = psum(nc, sW1, "pW1", [128, 2 * 1024])
            chm = P.chp(1); chv = P.chp(2); chst = [P.chp(3), P.chp(4)]
            for tt in range(NT):
                par = tt % 2
                if tt >= 2:
                    P.wait("tensor", chv, 2 * tt - 3)
                for k in range(ND):
                    for j in range(2):
                        P.emit("tensor", lambda e, tt=tt, k=k, j=j, par=par: e.matmul(
                            pW[:, par * 1024 + j * 512:par * 1024 + (j + 1) * 512],
                            lhsT=out_feat[:, k * T + tt * 128:k * T + (tt + 1) * 128],
                            rhs=wo_sb[:, k * D + j * 512:k * D + (j + 1) * 512],
                            start=(k == 0), stop=(k == ND - 1)),
                            inc=chm if (k == ND - 1 and j == 1) else None)
                P.wait("vector", chm, tt + 1)
                P.emit("vector", lambda e, par=par: e.tensor_tensor(
                    out=so1[:], in0=pW[:, par * 1024:(par + 1) * 1024], in1=sbo[:],
                    op=OP.add), inc=chv)
                P.drain("vector")
                if tt >= 2:
                    P.wait("vector", chst[par], 16 * (tt // 2))
                P.emit("vector", lambda e, par=par: e.tensor_copy(
                    so2[:, par * 1024:(par + 1) * 1024], so1[:]), inc=chv)
                P.drain("vector")
                P.wait("sync", chv, 2 * (tt + 1))
                P.dma("sync", d_s1[tt * 128:(tt + 1) * 128, :],
                      so2[:, par * 1024:(par + 1) * 1024], inc=chst[par])
            P.barrier()



def emit_moe_l1(P, stack, xf, w1b, w2b, b1_t, b2, gwT_b, gb, rmsw_r, d_g1):
    nc = P.nc
    d_ys = nc.dram_tensor(_uniq("d_ys"), [T, D], F32, kind="Internal").ap()
    ngs = sbuf(nc, stack, "ngs", [128, NT])
    sb1 = sbuf(nc, stack, "sb1_1", [128, 128])
    sb2 = sbuf(nc, stack, "sb2_1", [8, 1024])
    sb2b = sbuf(nc, stack, "sb2b_1", [8, 1024], BF16)
    gwb = sbuf(nc, stack, "gwb1", [128, 8 * 8], BF16)
    gbf = sbuf(nc, stack, "gbf1", [1, 8])
    gbb = sbuf(nc, stack, "gbb1", [1, 8], BF16)
    srms = sbuf(nc, stack, "srms", [128, 8])
    ch0 = P.chp(0)
    P.dma("sync", sb1[:], b1_t, inc=ch0)
    P.dma("sync", sb2[:], b2, inc=ch0)
    P.dma("sync", gbf[:], gb, inc=ch0)
    P.dma("sync", srms[:], rmsw_r, inc=ch0)
    for k in range(ND):
        P.dma("sync", gwb[:, k * 8:(k + 1) * 8], gwT_b[k * 128:(k + 1) * 128, :], inc=ch0)
    P.wait("scalar", ch0)
    P.emit("scalar", lambda e: e.copy(out=sb2b[:], in_=sb2[:]))
    P.drain("scalar")
    P.emit("scalar", lambda e: e.copy(out=gbb[:], in_=gbf[:]))
    P.barrier()
    with ExitStack() as sG:
        pG = psum(nc, sG, "pG1", [128, 32])
        sq8 = sbuf(nc, sG, "sq8", [128, 32])
        gcl = sbuf(nc, sG, "gcl1", [128, 8])
        lt8 = sbuf(nc, sG, "lt8", [128, 32])
        for tq in range(8):
            P.barrier()
            for tb in range(4):
                tt = tq * 4 + tb
                for k in range(ND):
                    P.emit("tensor", lambda e, tb=tb, tt=tt, k=k: e.matmul(
                        pG[:, tb * 8:(tb + 1) * 8],
                        lhsT=xf[:, k * T + tt * 128:k * T + (tt + 1) * 128],
                        rhs=gwb[:, k * 8:(k + 1) * 8],
                        start=(k == 0), stop=False))
                P.emit("tensor", lambda e, tb=tb: e.matmul(
                    pG[:, tb * 8:(tb + 1) * 8], lhsT=P.cones128b[0:1, :],
                    rhs=gbb[:], start=False, stop=True))
            P.barrier()
            for tb in range(4):
                P.emit("scalar", lambda e, tb=tb: e.activation(
                    out=sq8[:, tb * 8:(tb + 1) * 8], in_=pG[:, tb * 8:(tb + 1) * 8],
                    func=AF.Square))
                P.drain("scalar")
            P.barrier()
            for tb in range(4):
                P.emit("vector", lambda e, tb=tb: e.tensor_reduce(
                    out=gcl[:, 0:1], in_=sq8[:, tb * 8:(tb + 1) * 8], axis=AX, op=OP.add))
                P.drain("vector")
                P.barrier()
                P.emit("scalar", lambda e: e.activation(out=gcl[:, 1:2], in_=gcl[:, 0:1],
                                                        func=AF.Ln, scale=1.0 / 8,
                                                        bias=P.scal("f32eps")))
                P.drain("scalar")
                P.emit("scalar", lambda e: e.activation(out=gcl[:, 2:3], in_=gcl[:, 1:2],
                                                        func=AF.Exp, scale=-0.5))
                P.barrier()
                P.emit("vector", lambda e, tb=tb: e.tensor_scalar(
                    out=lt8[:, tb * 8:(tb + 1) * 8], in0=pG[:, tb * 8:(tb + 1) * 8],
                    scalar1=gcl[:, 2:3], scalar2=None, op0=OP.mult))
                P.drain("vector")
                P.emit("vector", lambda e, tb=tb: e.tensor_tensor(
                    out=lt8[:, tb * 8:(tb + 1) * 8], in0=lt8[:, tb * 8:(tb + 1) * 8],
                    in1=srms[:], op=OP.mult))
                P.drain("vector")
                P.emit("vector", lambda e, tb=tb: e.reduce_max(
                    out=gcl[:, 3:4], in_=lt8[:, tb * 8:(tb + 1) * 8], axis=AX, negate=True))
                P.drain("vector")
                P.emit("vector", lambda e: e.tensor_scalar(
                    out=gcl[:, 4:5], in0=gcl[:, 3:4], scalar1=2.0, scalar2=None, op0=OP.mult))
                P.barrier()
                P.emit("scalar", lambda e, tb=tb: e.activation(
                    out=sq8[:, tb * 8:(tb + 1) * 8], in_=lt8[:, tb * 8:(tb + 1) * 8],
                    func=AF.Exp, scale=2.0, bias=gcl[:, 4:5]))
                P.barrier()
                P.emit("vector", lambda e, tb=tb: e.tensor_reduce(
                    out=gcl[:, 5:6], in_=sq8[:, tb * 8:(tb + 1) * 8], axis=AX, op=OP.add))
                P.drain("vector")
                P.emit("vector", lambda e: e.tensor_scalar(
                    out=gcl[:, 6:7], in0=gcl[:, 5:6], scalar1=1e-6, scalar2=1.0,
                    op0=OP.mult, op1=OP.add))
                P.drain("vector")
                P.emit("vector", lambda e: e.reciprocal(out=gcl[:, 7:8], in_=gcl[:, 6:7]))
                P.drain("vector")
                P.emit("vector", lambda e, tb=tb, tq=tq: e.tensor_copy(
                    ngs[:, tq * 4 + tb:tq * 4 + tb + 1], gcl[:, 7:8]))
                P.drain("vector")
            P.barrier()
    sE = ExitStack()
    wsb1 = sbuf(nc, sE, "wsb1_1", [128, 8 * 2048], BF16)
    wsb2 = sbuf(nc, sE, "wsb2_1", [128, 16 * D], BF16)
    hid = sbuf(nc, sE, "hid_1", [128, 16 * 512], BF16)
    mout = sbuf(nc, sE, "mout_1", [128, 1024])
    macc = sbuf(nc, sE, "macc_1", [128, 1024])
    for e_i in range(8):
        chwl = P.chp(1)
        for k in range(ND):
            P.dma("sync", wsb1[:, k * 2048:(k + 1) * 2048],
                  w1b[e_i * D + k * 128:e_i * D + (k + 1) * 128, :], inc=chwl)
        for m in range(16):
            P.dma("sync", wsb2[:, m * D:(m + 1) * D],
                  w2b[e_i * 2048 + m * 128:e_i * 2048 + (m + 1) * 128, :], inc=chwl)
        for en in Prog.ENGS:
            P.wait(en, chwl)
        chh = P.chp(2); chm = P.chp(3); che = P.chp(4)
        nm = 0; ne = 0
        for tc in range(8):
            with ExitStack() as s1:
                ph = psum(nc, s1, "ph1", [128, 2 * 512])
                for m in range(16):
                    par = m % 2
                    if m >= 2:
                        P.wait("tensor", che, ne - 1)
                    for k in range(ND):
                        P.emit("tensor", lambda e, m=m, k=k, par=par, tc=tc: e.matmul(
                            ph[:, par * 512:(par + 1) * 512],
                            lhsT=wsb1[:, k * 2048 + m * 128:k * 2048 + (m + 1) * 128],
                            rhs=xf[:, k * T + tc * 512:k * T + (tc + 1) * 512],
                            start=(k == 0), stop=(k == ND - 1)),
                            inc=chm if k == ND - 1 else None)
                    nm += 1
                    P.wait("scalar", chm, nm)
                    P.emit("scalar", lambda e, m=m, e_i=e_i, par=par: e.activation(
                        out=hid[:, m * 512:(m + 1) * 512], in_=ph[:, par * 512:(par + 1) * 512],
                        func=AF.Gelu, bias=sb1[:, e_i * 16 + m:e_i * 16 + m + 1]),
                        inc=che)
                    ne += 1
                P.barrier()
            with ExitStack() as s2:
                pO = psum(nc, s2, "pO1", [128, 4 * 1024])
                chw2 = P.chp(6); chw3 = P.chp(7)
                chdl = P.chp(5); chds = P.chp(8)
                for t128 in range(4):
                    for m in range(16):
                        for j in range(2):
                            P.emit("tensor", lambda e, t128=t128, m=m, j=j: e.matmul(
                                pO[:, t128 * 1024 + j * 512:t128 * 1024 + (j + 1) * 512],
                                lhsT=hid[:, m * 512 + t128 * 128:m * 512 + (t128 + 1) * 128],
                                rhs=wsb2[:, m * D + j * 512:m * D + (j + 1) * 512],
                                start=(m == 0), stop=(m == 15)),
                                inc=chw2 if (m == 15 and j == 1) else None)
                    rows = slice(tc * 512 + t128 * 128, tc * 512 + (t128 + 1) * 128)
                    if e_i > 0:
                        P.dma("sync", macc[:], d_ys[rows, :], inc=chdl)
                        P.wait("vector", chdl)
                    P.wait("vector", chds)
                    P.wait("vector", chw2, t128 + 1)
                    if e_i == 0:
                        P.emit("vector", lambda e, t128=t128: e.tensor_scalar(
                            out=mout[:], in0=pO[:, t128 * 1024:(t128 + 1) * 1024],
                            scalar1=1.0, scalar2=None, op0=OP.mult), inc=chw3)
                    else:
                        P.emit("vector", lambda e, t128=t128: e.tensor_tensor(
                            out=mout[:], in0=pO[:, t128 * 1024:(t128 + 1) * 1024],
                            in1=macc[:], op=OP.add), inc=chw3)
                    P.wait("sync", chw3, t128 + 1)
                    P.dma("sync", d_ys[rows, :], mout[:], inc=chds)
                P.barrier()
            P.barrier()
    sE.close()
    with ExitStack() as sF:
        pb2r = psum(nc, sF, "pb2r1", [1, 1024])
        pb = psum(nc, sF, "pbf1", [128, 1024])
        b2row = sbuf(nc, sF, "b2row1", [1, 1024], BF16)
        mo = sbuf(nc, sF, "mo1", [128, 2 * 1024])
        mo2 = sbuf(nc, sF, "mo21", [128, 1024])
        gt = sbuf(nc, sF, "gt1", [128, 2 * 1024], BF16)
        chb = P.chp(1)
        P.emit("tensor", lambda e: e.matmul(pb2r[:, 0:512], lhsT=P.cones128b[0:8, 0:1],
                                            rhs=sb2b[:, 0:512], start=True, stop=True))
        P.emit("tensor", lambda e: e.matmul(pb2r[:, 512:1024], lhsT=P.cones128b[0:8, 0:1],
                                            rhs=sb2b[:, 512:1024], start=True, stop=True), inc=chb)
        P.wait("scalar", chb, 1)
        P.emit("scalar", lambda e: e.copy(out=b2row[:], in_=pb2r[:]), inc=chb)
        P.wait("tensor", chb, 2)
        P.emit("tensor", lambda e: e.matmul(pb[:, 0:512], lhsT=P.cones128b[0:1, :],
                                            rhs=b2row[:, 0:512], start=True, stop=True))
        P.emit("tensor", lambda e: e.matmul(pb[:, 512:1024], lhsT=P.cones128b[0:1, :],
                                            rhs=b2row[:, 512:1024], start=True, stop=True))
        P.barrier()
        chl = P.chp(1); chs = [P.chp(2), P.chp(5)]; chst = [P.chp(3), P.chp(4)]
        nl = 0
        for tt in range(NT):
            par = tt % 2
            if tt >= 2:
                P.wait("sync", chl, nl)
            P.dma("sync", mo[:, par * 1024:(par + 1) * 1024],
                  d_ys[tt * 128:(tt + 1) * 128, :], inc=chs[par])
            P.wait("vector", chs[par], 16 * (tt // 2 + 1))
            P.emit("vector", lambda e, par=par: e.tensor_tensor(
                out=mo2[:], in0=mo[:, par * 1024:(par + 1) * 1024], in1=pb[:], op=OP.add))
            P.drain("vector")
            P.emit("vector", lambda e, tt=tt, par=par: e.tensor_scalar(
                out=mo[:, par * 1024:(par + 1) * 1024], in0=mo2[:],
                scalar1=ngs[:, tt:tt + 1], scalar2=None, op0=OP.mult), inc=chl)
            nl += 1
            P.wait("scalar", chl, nl)
            if tt >= 2:
                P.wait("scalar", chst[par], 16 * (tt // 2))
            P.emit("scalar", lambda e, par=par: e.activation(
                out=gt[:, par * 1024:(par + 1) * 1024],
                in_=mo[:, par * 1024:(par + 1) * 1024], func=AF.Silu), inc=chl)
            nl += 1
            P.wait("sync", chl, nl)
            P.dma("sync", d_g1[tt * 128:(tt + 1) * 128, :], gt[:, par * 1024:(par + 1) * 1024],
                  inc=chst[par])
        P.barrier()



def attn_l1_host_prep(ins):
    return {
        "l1_wq": np.ascontiguousarray(ins["L1_attn_Wq"]),
        "l1_wk": np.ascontiguousarray(ins["L1_attn_Wk"]),
        "l1_wv": np.ascontiguousarray(ins["L1_attn_Wv"]),
        "l1_wo": np.ascontiguousarray(ins["L1_attn_Wo"]),
        "l1_bq": np.ascontiguousarray(ins["L1_attn_bq"].reshape(8, 128).T),
        "l1_bkv": np.ascontiguousarray(np.stack([ins["L1_attn_bk"], ins["L1_attn_bv"]], axis=1)),
        "l1_bo": repl_rows(ins["L1_attn_bo"]),
    }

def moe_l1_host_prep(ins):
    b1 = ins["L1_ffn_b1"]
    return {
        "l1_w1": np.ascontiguousarray(ins["L1_ffn_w1"].reshape(8 * 1024, 2048)),
        "l1_w2": np.ascontiguousarray(ins["L1_ffn_w2"].reshape(8 * 2048, 1024)),
        "l1_b1t": np.ascontiguousarray(b1.reshape(8, 16, 128).transpose(2, 0, 1).reshape(128, 128)),
        "l1_b2": np.ascontiguousarray(ins["L1_ffn_b2"]),
        "l1_gwT": np.ascontiguousarray(ins["L1_ffn_gate_w"].T),
        "l1_gb": np.ascontiguousarray(ins["L1_ffn_gate_b"].reshape(1, 8)),
        "l1_rmsw": np.ascontiguousarray(np.broadcast_to(ins["L1_ffn_rms_w"].reshape(1, 8), (128, 8)).copy()),
    }

def build_stage_f():
    nc = bass.Bass("TRN2", target_bir_lowering=False)
    xnf = nc.dram_tensor("x1n_feat", [D, T], BF16, kind="ExternalInput").ap()
    wq = nc.dram_tensor("l1_wq", [D, D], F32, kind="ExternalInput").ap()
    wk = nc.dram_tensor("l1_wk", [D, 64], F32, kind="ExternalInput").ap()
    wv = nc.dram_tensor("l1_wv", [D, 64], F32, kind="ExternalInput").ap()
    wo = nc.dram_tensor("l1_wo", [D, D], F32, kind="ExternalInput").ap()
    bq = nc.dram_tensor("l1_bq", [128, 8], F32, kind="ExternalInput").ap()
    bkv = nc.dram_tensor("l1_bkv", [64, 2], F32, kind="ExternalInput").ap()
    bo = nc.dram_tensor("l1_bo", [128, D], F32, kind="ExternalInput").ap()
    d_s1 = nc.dram_tensor("s1_out", [T, D], BF16, kind="ExternalOutput").ap()
    with ExitStack() as stack:
        P = Prog(nc, stack)
        load_consts(P, stack, ATTN_CONSTS)
        wqb = nc.dram_tensor("l1wqb", [D, D], BF16, kind="Internal").ap()
        wkb = nc.dram_tensor("l1wkb", [D, 64], BF16, kind="Internal").ap()
        wvb = nc.dram_tensor("l1wvb", [D, 64], BF16, kind="Internal").ap()
        wob = nc.dram_tensor("l1wob", [D, D], BF16, kind="Internal").ap()
        with ExitStack() as s0:
            emit_cast_f32_to_bf16_dram(P, s0, wq, wqb, D, D)
            emit_cast_f32_to_bf16_dram(P, s0, wk, wkb, D, 64)
            emit_cast_f32_to_bf16_dram(P, s0, wv, wvb, D, 64)
            emit_cast_f32_to_bf16_dram(P, s0, wo, wob, D, D)
        xf = sbuf(nc, stack, "x1nf_sb", [128, ND * T], BF16)
        ch0 = P.ch()
        for dd in range(ND):
            P.dma("sync", xf[:, dd * T:(dd + 1) * T], xnf[dd * 128:(dd + 1) * 128, :], inc=ch0)
        for e in Prog.ENGS:
            P.wait(e, ch0)
        emit_attn_l1(P, stack, xf, wqb, wkb, wvb, wob, bq, bkv, bo, d_s1)
        block = stack.enter_context(nc.Block())
        P.finalize(block)
    return nc

def build_stage_g():
    nc = bass.Bass("TRN2", target_bir_lowering=False)
    xnf = nc.dram_tensor("x1n_feat", [D, T], BF16, kind="ExternalInput").ap()
    w1 = nc.dram_tensor("l1_w1", [8 * 1024, 2048], F32, kind="ExternalInput").ap()
    w2 = nc.dram_tensor("l1_w2", [8 * 2048, 1024], F32, kind="ExternalInput").ap()
    b1t = nc.dram_tensor("l1_b1t", [128, 128], F32, kind="ExternalInput").ap()
    b2 = nc.dram_tensor("l1_b2", [8, 1024], F32, kind="ExternalInput").ap()
    gwT = nc.dram_tensor("l1_gwT", [D, 8], F32, kind="ExternalInput").ap()
    gb = nc.dram_tensor("l1_gb", [1, 8], F32, kind="ExternalInput").ap()
    rmsw = nc.dram_tensor("l1_rmsw", [128, 8], F32, kind="ExternalInput").ap()
    d_g1 = nc.dram_tensor("g1_out", [T, D], BF16, kind="ExternalOutput").ap()
    with ExitStack() as stack:
        P = Prog(nc, stack)
        load_consts(P, stack, MOE_CONSTS)
        w1b = nc.dram_tensor("l1w1b", [8 * 1024, 2048], BF16, kind="Internal").ap()
        w2b = nc.dram_tensor("l1w2b", [8 * 2048, 1024], BF16, kind="Internal").ap()
        gwTb = nc.dram_tensor("l1gwTb", [D, 8], BF16, kind="Internal").ap()
        with ExitStack() as s0:
            emit_cast_f32_to_bf16_dram(P, s0, w1, w1b, 8 * 1024, 2048)
        with ExitStack() as s0:
            emit_cast_f32_to_bf16_dram(P, s0, w2, w2b, 8 * 2048, 1024)
        with ExitStack() as s0:
            emit_cast_f32_to_bf16_dram(P, s0, gwT, gwTb, D, 8)
        xf = sbuf(nc, stack, "x1nf_sb2", [128, ND * T], BF16)
        ch0 = P.ch()
        for dd in range(ND):
            P.dma("sync", xf[:, dd * T:(dd + 1) * T], xnf[dd * 128:(dd + 1) * 128, :], inc=ch0)
        for e in Prog.ENGS:
            P.wait(e, ch0)
        emit_moe_l1(P, stack, xf, w1b, w2b, b1t, b2, gwTb, gb, rmsw, d_g1)
        block = stack.enter_context(nc.Block())
        P.finalize(block)
    return nc

def build_stage_e():
    nc = bass.Bass("TRN2", target_bir_lowering=False)
    x = nc.dram_tensor("x", [T, D], F32, kind="ExternalInput").ap()
    s_ = nc.dram_tensor("s_in", [T, D], BF16, kind="ExternalInput").ap()
    g_ = nc.dram_tensor("g_in", [T, D], BF16, kind="ExternalInput").ap()
    x1 = nc.dram_tensor("x1_out", [T, D], F32, kind="ExternalOutput").ap()
    xnf = nc.dram_tensor("x1n_feat", [D, T], BF16, kind="ExternalOutput").ap()
    with ExitStack() as stack:
        P = Prog(nc, stack)
        load_consts(P, stack, ["c_id128b"])
        emit_combine(P, stack, x, s_, g_, x1)
        xn_feat = sbuf(nc, stack, "x1n_sb", [128, ND * T], BF16)
        with ExitStack() as s2:
            emit_ln_feat(P, s2, x1, [(None, None, xn_feat, None)])
        ch_st = P.ch("ste")
        for dd in range(ND):
            P.dma("sync", xnf[dd * 128:(dd + 1) * 128, :], xn_feat[:, dd * T:(dd + 1) * T], inc=ch_st)
        P.wait("sync", ch_st)
        block = stack.enter_context(nc.Block())
        P.finalize(block)
    return nc

def build_stage_h():
    nc = bass.Bass("TRN2", target_bir_lowering=False)
    x = nc.dram_tensor("x1", [T, D], F32, kind="ExternalInput").ap()
    s_ = nc.dram_tensor("s1_in", [T, D], BF16, kind="ExternalInput").ap()
    g_ = nc.dram_tensor("g1_in", [T, D], BF16, kind="ExternalInput").ap()
    out = nc.dram_tensor("out", [T, D], F32, kind="ExternalOutput").ap()
    with ExitStack() as stack:
        P = Prog(nc, stack)
        load_consts(P, stack, [])
        emit_combine(P, stack, x, s_, g_, out)
        block = stack.enter_context(nc.Block())
        P.finalize(block)
    return nc



_NC_CACHE = {}

def _get_nc(name):
    if name not in _NC_CACHE:
        _NC_CACHE[name] = {
            "a": lambda: build_stage_a(True),
            "b": build_stage_b,
            "c": build_stage_c,
            "d": build_stage_d,
            "e": build_stage_e,
            "f": build_stage_f,
            "g": build_stage_g,
            "h": build_stage_h,
        }[name]()
    return _NC_CACHE[name]

def _run(nc, maps, ncores):
    res = run_bass_kernel_spmd(nc, maps, core_ids=list(range(ncores)))
    return res.results

def kernel(**ins):
    if os.environ.get("KERNEL_UNFUSED") != "1":
        return kernel_fused(**ins)
    import ml_dtypes
    ci = const_inputs()
    NB = ins["L0_attn_u"].shape[0]
    xs = [np.ascontiguousarray(ins["L0_attn_u"][b]) for b in range(NB)]

    def with_consts(m, names):
        m = dict(m)
        m["c_scal"] = ci["c_scal"]
        for n in names:
            m[n] = ci[n]
        return m

    lng_r = repl_rows(ins["L0_ffn_ln_g"]); lnb_r = repl_rows(ins["L0_ffn_ln_b"])
    mapsA = [with_consts({"x": xs[b], "ln_g_r": lng_r, "ln_b_r": lnb_r}, ["c_id128b"])
             for b in range(NB)]
    rA = _run(_get_nc("a"), mapsA, NB)
    mb = mamba_host_prep(ins)
    mapsB = [with_consts({**mb, "xn_feat": np.asarray(rA[b]["xn_feat"])}, MAMBA_CONSTS)
             for b in range(NB)]
    rB = _run(_get_nc("b"), mapsB, NB)
    mc = attn_l0_host_prep(ins)
    mapsC = [with_consts({**mc, "h_feat": np.asarray(rA[b]["h_feat"])}, ATTN_CONSTS)
             for b in range(NB)]
    rC = _run(_get_nc("c"), mapsC, NB)
    md = moe_l0_host_prep(ins)
    mapsD = [with_consts({**md, "h2_feat": np.asarray(rC[b]["h2_feat"]),
                          "gates": np.asarray(rC[b]["gates"])}, MOE_CONSTS)
             for b in range(NB)]
    rD = _run(_get_nc("d"), mapsD, NB)
    mapsE = [with_consts({"x": xs[b], "s_in": np.asarray(rB[b]["s_out"]),
                          "g_in": np.asarray(rD[b]["g_out"])}, ["c_id128b"])
             for b in range(NB)]
    rE = _run(_get_nc("e"), mapsE, NB)
    mf = attn_l1_host_prep(ins)
    mapsF = [with_consts({**mf, "x1n_feat": np.asarray(rE[b]["x1n_feat"])}, ATTN_CONSTS)
             for b in range(NB)]
    rF = _run(_get_nc("f"), mapsF, NB)
    mg = moe_l1_host_prep(ins)
    mapsG = [with_consts({**mg, "x1n_feat": np.asarray(rE[b]["x1n_feat"])}, MOE_CONSTS)
             for b in range(NB)]
    rG = _run(_get_nc("g"), mapsG, NB)
    mapsH = [with_consts({"x1": np.asarray(rE[b]["x1_out"]), "s1_in": np.asarray(rF[b]["s1_out"]),
                          "g1_in": np.asarray(rG[b]["g1_out"])}, [])
             for b in range(NB)]
    rH = _run(_get_nc("h"), mapsH, NB)
    out = np.stack([np.asarray(rH[b]["out"], dtype=np.float32) for b in range(NB)], axis=0)
    return out



ALL_CONSTS = sorted(set(MAMBA_CONSTS + ATTN_CONSTS + MOE_CONSTS))

def build_fused():
    nc = bass.Bass("TRN2", target_bir_lowering=False)
    def ein(name, shape, dt=F32):
        return nc.dram_tensor(name, list(shape), dt, kind="ExternalInput").ap()
    def itn(name, shape, dt):
        return nc.dram_tensor(name, list(shape), dt, kind="Internal").ap()
    x = ein("x", [T, D])
    lng_r = ein("ln_g_r", [128, D]); lnb_r = ein("ln_b_r", [128, D])
    win = ein("win", [D, 4384]); wout = ein("wout", [D_IN, D])
    convw_t = ein("convw_t", [128, 72]); convb_t = ein("convb_t", [128, 18])
    dtb = ein("dtb", [32, 1]); alog = ein("alog", [32, 1]); normw_r = ein("normw_r", [128, D_IN])
    wq = ein("wq", [D, D]); wk = ein("wk", [D, D]); wv = ein("wv", [D, D]); wo = ein("wo", [D, D])
    qs = ein("qs_col", [D, 1]); ks = ein("ks_col", [D, 1])
    lngc = ein("lngc", [128, 8]); lnbc = ein("lnbc", [128, 8])
    gw = ein("gw", [D, 8]); gb = ein("gb", [1, 8])
    w1 = ein("w1", [8 * 1024, 4096]); w2 = ein("w2", [8 * 4096, 1024])
    b1_t = ein("b1_t", [128, 256]); b2 = ein("b2", [8, 1024]); b2r_all = ein("b2r_all", [128, 8 * 1024])
    l1_wq = ein("l1_wq", [D, D]); l1_wk = ein("l1_wk", [D, 64]); l1_wv = ein("l1_wv", [D, 64])
    l1_wo = ein("l1_wo", [D, D]); l1_bq = ein("l1_bq", [128, 8]); l1_bkv = ein("l1_bkv", [64, 2])
    l1_bo = ein("l1_bo", [128, D])
    l1_w1 = ein("l1_w1", [8 * 1024, 2048]); l1_w2 = ein("l1_w2", [8 * 2048, 1024])
    l1_b1t = ein("l1_b1t", [128, 128]); l1_b2 = ein("l1_b2", [8, 1024])
    l1_gwT = ein("l1_gwT", [D, 8]); l1_gb = ein("l1_gb", [1, 8]); l1_rmsw = ein("l1_rmsw", [128, 8])
    out = nc.dram_tensor("out", [T, D], F32, kind="ExternalOutput").ap()
    d_hf = itn("d_hf", [D, T], BF16)
    d_s = itn("d_s", [T, D], BF16)
    d_zs = itn("d_zs", [T, D_IN], BF16)
    d_h2 = itn("d_h2", [D, T], BF16)
    d_gates = itn("d_gates", [T, 8], F32)
    d_g = itn("d_g", [T, D], BF16)
    d_x1 = itn("d_x1", [T, D], F32)
    d_s1 = itn("d_s1", [T, D], BF16)
    d_g1 = itn("d_g1", [T, D], BF16)
    wqb = itn("wqb", [D, D], BF16); wkb = itn("wkb", [D, D], BF16)
    wvb = itn("wvb", [D, D], BF16); wob = itn("wob", [D, D], BF16)
    gwbd = itn("gwbd", [D, 8], BF16)
    w1b = itn("w1b", [8 * 1024, 4096], BF16); w2b = itn("w2b", [8 * 4096, 1024], BF16)
    l1wqb = itn("l1wqb", [D, D], BF16); l1wkb = itn("l1wkb", [D, 64], BF16)
    l1wvb = itn("l1wvb", [D, 64], BF16); l1wob = itn("l1wob", [D, D], BF16)
    l1w1b = itn("l1w1b", [8 * 1024, 2048], BF16); l1w2b = itn("l1w2b", [8 * 2048, 1024], BF16)
    l1gwTb = itn("l1gwTb", [D, 8], BF16)

    with ExitStack() as stack:
        P = Prog(nc, stack)
        bigc = {"c_blkf", "c_blkpb", "c_neglt4f"}
        load_consts(P, stack, [n for n in ALL_CONSTS if n not in bigc])
        with ExitStack() as sA_:
            load_consts(P, sA_, sorted(bigc))
            xn_feat = sbuf(nc, sA_, "f_xnf", [128, ND * T], BF16)
            with ExitStack() as sh:
                h_feat = sbuf(nc, sh, "f_hf", [128, ND * T], BF16)
                sgb = sbuf(nc, sh, "f_lng", [128, D]); sbb = sbuf(nc, sh, "f_lnb", [128, D])
                ch0 = P.ch("f0")
                P.dma("sync", sgb[:], lng_r, inc=ch0)
                P.dma("sync", sbb[:], lnb_r, inc=ch0)
                for e in Prog.ENGS:
                    P.wait(e, ch0)
                with ExitStack() as s2:
                    emit_ln_feat(P, s2, x, [(None, None, xn_feat, None),
                                            (sgb[:], sbb[:], h_feat, None)])
                chh0 = P.ch("fh0")
                for dd in range(ND):
                    P.dma("sync", d_hf[dd * 128:(dd + 1) * 128, :],
                          h_feat[:, dd * T:(dd + 1) * T], inc=chh0)
                P.wait("sync", chh0)
                P.barrier()
            emit_mamba(P, sA_, xn_feat, win, convw_t, convb_t, dtb, alog, normw_r, wout,
                       d_zs, d_s)
        P.barrier()
        with ExitStack() as s0:
            emit_cast_f32_to_bf16_dram(P, s0, wq, wqb, D, D)
            emit_cast_f32_to_bf16_dram(P, s0, wk, wkb, D, D)
            emit_cast_f32_to_bf16_dram(P, s0, wv, wvb, D, D)
            emit_cast_f32_to_bf16_dram(P, s0, wo, wob, D, D)
            emit_cast_f32_to_bf16_dram(P, s0, gw, gwbd, D, 8)
        with ExitStack() as sC_:
            out_feat = sbuf(nc, sC_, "f_atout", [128, ND * T], BF16)
            with ExitStack() as sh:
                h_feat = sbuf(nc, sh, "f_hf2", [128, ND * T], BF16)
                chh1 = P.ch("fh1")
                for dd in range(ND):
                    P.dma("sync", h_feat[:, dd * T:(dd + 1) * T],
                          d_hf[dd * 128:(dd + 1) * 128, :], inc=chh1)
                for e in Prog.ENGS:
                    P.wait(e, chh1)
                emit_attn_l0_flash(P, sC_, h_feat, out_feat, wqb, wkb, wvb, qs, ks)
            emit_attn_l0_tail(P, sC_, out_feat, wob, lngc, lnbc, gwbd, gb, d_h2, d_gates)
        P.barrier()
        with ExitStack() as s0:
            emit_cast_f32_to_bf16_dram(P, s0, w1, w1b, 8 * 1024, 4096)
        with ExitStack() as s0:
            emit_cast_f32_to_bf16_dram(P, s0, w2, w2b, 8 * 4096, 1024)
        with ExitStack() as sD_:
            emit_moe_l0(P, sD_, d_h2, w1b, w2b, b1_t, b2, b2r_all, d_gates, lng_r, lnb_r, d_g)
        P.barrier()
        emit_combine(P, stack, x, d_s, d_g, d_x1)
        with ExitStack() as sE_:
            x1nf = sbuf(nc, sE_, "f_x1nf", [128, ND * T], BF16)
            with ExitStack() as s2:
                emit_ln_feat(P, s2, d_x1, [(None, None, x1nf, None)])
            P.barrier()
            with ExitStack() as s0:
                emit_cast_f32_to_bf16_dram(P, s0, l1_wq, l1wqb, D, D)
                emit_cast_f32_to_bf16_dram(P, s0, l1_wk, l1wkb, D, 64)
                emit_cast_f32_to_bf16_dram(P, s0, l1_wv, l1wvb, D, 64)
                emit_cast_f32_to_bf16_dram(P, s0, l1_wo, l1wob, D, D)
            with ExitStack() as sF_:
                emit_attn_l1(P, sF_, x1nf, l1wqb, l1wkb, l1wvb, l1wob, l1_bq, l1_bkv,
                             l1_bo, d_s1)
            P.barrier()
            with ExitStack() as s0:
                emit_cast_f32_to_bf16_dram(P, s0, l1_w1, l1w1b, 8 * 1024, 2048)
            with ExitStack() as s0:
                emit_cast_f32_to_bf16_dram(P, s0, l1_w2, l1w2b, 8 * 2048, 1024)
            with ExitStack() as s0:
                emit_cast_f32_to_bf16_dram(P, s0, l1_gwT, l1gwTb, D, 8)
            with ExitStack() as sG_:
                emit_moe_l1(P, sG_, x1nf, l1w1b, l1w2b, l1_b1t, l1_b2, l1gwTb, l1_gb,
                            l1_rmsw, d_g1)
        P.barrier()
        emit_combine(P, stack, d_x1, d_s1, d_g1, out)
        block = stack.enter_context(nc.Block())
        P.finalize(block)
    return nc


def kernel_fused(**ins):
    import ml_dtypes
    ci = const_inputs()
    NB = ins["L0_attn_u"].shape[0]
    base = {}
    base.update({"win": np.ascontiguousarray(ins["L0_attn_W_in"]),
                 "wout": np.ascontiguousarray(ins["L0_attn_W_out"])})
    base.update({k: v for k, v in mamba_host_prep(ins).items() if k not in ("win", "wout")})
    base.update(attn_l0_host_prep(ins))
    base.update(moe_l0_host_prep(ins))
    base.update(attn_l1_host_prep(ins))
    base.update(moe_l1_host_prep(ins))
    base["ln_g_r"] = repl_rows(ins["L0_ffn_ln_g"])
    base["ln_b_r"] = repl_rows(ins["L0_ffn_ln_b"])
    base["c_scal"] = ci["c_scal"]
    for n in ALL_CONSTS:
        base[n] = ci[n]
    maps = [dict(base, x=np.ascontiguousarray(ins["L0_attn_u"][b])) for b in range(NB)]
    if "fz" not in _NC_CACHE:
        _NC_CACHE["fz"] = build_fused()
    res = run_bass_kernel_spmd(_NC_CACHE["fz"], maps, core_ids=list(range(NB)))
    return np.stack([np.asarray(res.results[b]["out"], dtype=np.float32) for b in range(NB)],
                    axis=0)


# version 04
# speedup vs baseline: 1.0657x; 1.0313x over previous
import numpy as np
from contextlib import ExitStack

import concourse.bass as bass
import concourse.mybir as mybir
from concourse.alu_op_type import AluOpType as OP
from concourse.bass_utils import run_bass_kernel_spmd

F32 = mybir.dt.float32
BF16 = mybir.dt.bfloat16
AX = mybir.AxisListType.X
AF = mybir.ActivationFunctionType

import os
DBG = os.environ.get("MAMBA_DBG", "") == "1"

NCORES = 8
T = 4096
D = 1024
NT = T // 128
ND = D // 128


class Ch:
    def __init__(self, h):
        self.h = h
        self.n = 0

class ChView:
    def __init__(self, ch):
        self._c = ch
        self.base = ch.n
    @property
    def h(self):
        return self._c.h
    @property
    def n(self):
        return self._c.n
    @n.setter
    def n(self, v):
        self._c.n = v

class Prog:
    ENGS = ("sync", "scalar", "vector", "tensor", "gpsimd")

    def __init__(self, nc, stack):
        self.nc = nc
        self.stack = stack
        self.q = {e: [] for e in self.ENGS}
        self.dirty = {e: False for e in self.ENGS}
        self._nsem = 0
        self._bar = None
        self._dma_chs = set()

    def ch(self, name=None):
        h = self.stack.enter_context(self.nc.semaphore(name or f"ch{self._nsem}"))
        self._nsem += 1
        return Ch(h)

    def emit(self, eng, f, inc=None, k=1):
        incs = []
        if inc is not None:
            inc.n += k
            incs.append((inc, k))
        ent = [f, incs, 1]
        self.q[eng].append(ent)
        self.dirty[eng] = True
        return ent

    def chv(self, ch):
        return ChView(ch)

    def dump(self, name, tile_ap, shape, dt=F32):
        if not DBG:
            return
        self.barrier()
        dram = self.nc.dram_tensor("dbg_" + name, list(shape), dt, kind="ExternalOutput").ap()
        ch = self.chp(19)
        self.dma("sync", dram, tile_ap, inc=ch)
        self.wait("sync", ch)
        self.barrier()

    def drain(self, eng):
        self.q[eng].append([("drain",), None, 0])

    def chp(self, i):
        if not hasattr(self, "_pool"):
            self._pool = []
        while len(self._pool) <= i:
            self._pool.append(self.ch(f"pool{len(self._pool)}"))
        return ChView(self._pool[i])

    def wait(self, eng, ch, n=None):
        if n is None:
            n = ch.n
        elif isinstance(ch, ChView):
            n = ch.base + n
        self.q[eng].append([("wait", ch, n), None, 0])

    def barrier(self, bar=None, engines=None):
        targets = [(c, c.n) for c in self._dma_chs]
        self._dma_chs = set()
        for e in self.ENGS:
            if not self.dirty[e]:
                continue
            for ent in reversed(self.q[e]):
                if ent[1] is not None:
                    if ent[1]:
                        c, _k = ent[1][0]
                        targets.append((c, c.n))
                    else:
                        if bar is None:
                            if self._bar is None:
                                self._bar = self.ch("bar")
                            bar = self._bar
                        ent[1].append((bar, ent[2]))
                        bar.n += ent[2]
                        targets.append((bar, bar.n))
                    break
            else:
                raise RuntimeError(f"engine {e} dirty but has no real instruction")
            self.dirty[e] = False
        seen = {}
        for c, n in targets:
            seen[id(c)] = (c, max(n, seen.get(id(c), (c, 0))[1]))
        for e in (engines or self.ENGS):
            for c, n in seen.values():
                self.q[e].append([("wait", c, n), None, 0])
        return bar

    def dma(self, eng, out, in_, inc=None, **kw):
        assert inc is not None
        incs = []
        if inc is not None:
            inc.n += 16
            incs.append((inc, 16))
            base = inc._c if isinstance(inc, ChView) else inc
            self._dma_chs.add(base)
        ent = [lambda e, out=out, in_=in_, kw=kw: e.dma_start(out=out, in_=in_, **kw), incs, 16]
        self.q[eng].append(ent)
        self.dirty[eng] = True
        return ent

    def finalize(self, block):
        for e in self.ENGS:
            ops = self.q[e]
            if not ops:
                continue
            def section(engine, ops=ops):
                for ent in ops:
                    f = ent[0]
                    if isinstance(f, tuple) and f[0] == "wait":
                        engine.wait_ge(f[1].h, f[2])
                        continue
                    if isinstance(f, tuple) and f[0] == "drain":
                        engine.drain()
                        continue
                    instr = f(engine)
                    if ent[1]:
                        for (c, k) in ent[1]:
                            instr = instr.then_inc(c.h, k)
            getattr(block, e)(section)


_NAME_N = [0]

def _uniq(name):
    _NAME_N[0] += 1
    return f"{name}_{_NAME_N[0]}"

def sbuf(nc, stack, name, shape, dt=F32):
    return stack.enter_context(nc.sbuf_tensor(_uniq(name), shape, dt))

def psum(nc, stack, name, shape, dt=F32):
    return stack.enter_context(nc.psum_tensor(_uniq(name), shape, dt))



def host_constants():
    import ml_dtypes
    c = {}
    c["id128"] = np.eye(128, dtype=np.float32)
    ut = np.triu(np.ones((128, 128), np.float32))
    c["utones"] = ut
    c["ltstrict"] = np.tril(np.ones((128, 128), np.float32), -1)
    c["neglt"] = (-1e9 * np.tril(np.ones((128, 128), np.float32), -1))
    c["ones128"] = np.ones((128, 128), np.float32)
    blk = np.zeros((32, 32 * 128), np.float32)
    for h in range(32):
        blk[h, h * 128:(h + 1) * 128] = 1.0
    c["blk"] = blk
    c["neglt_t"] = np.tile(-1e9 * np.tril(np.ones((128, 128), np.float32), -1), (1, 32))
    erep = np.zeros((32, 16 * 128), np.float32)
    for i in range(16):
        erep[2 * i, i * 128:i * 128 + 64] = 1.0
        erep[2 * i + 1, i * 128 + 64:(i + 1) * 128] = 1.0
    c["erep"] = erep
    blkp = np.zeros((32, 32 * 64), np.float32)
    for h in range(32):
        blkp[h, h * 64:(h + 1) * 64] = 1.0
    c["blkp"] = blkp
    c["negblk"] = -blk
    c["neglt4"] = np.tile(-1e9 * np.tril(np.ones((128, 128), np.float32), -1), (1, 4))
    bd2 = np.zeros((128, 2), np.float32)
    bd2[0:64, 0] = 1.0
    bd2[64:128, 1] = 1.0
    c["bd2"] = bd2
    return c



def emit_ln_feat(P, stack, x_dram, feats, eps=1e-5):
    nc = P.nc
    nf = len(feats)
    assert nf <= 2
    xb = sbuf(nc, stack, "lnf_x", [128, 2 * D])
    tokb = [sbuf(nc, stack, f"lnf_tok{i}", [128, 2 * D], BF16) for i in range(nf)]
    trash = sbuf(nc, stack, "lnf_trash", [128, D], BF16)
    trash2x = sbuf(nc, stack, "lnf_trash2", [128, D], BF16)
    tmp1 = sbuf(nc, stack, "lnf_tmp1", [128, D])
    tmp2 = sbuf(nc, stack, "lnf_tmp2", [128, D])
    st = sbuf(nc, stack, "lnf_st", [128, 16])
    pt = psum(nc, stack, "lnf_pt", [128, nf * 2 * D], BF16)
    idb = P.cid128b
    ch_ld = [P.chp(0), P.chp(7)]; ch_stat = P.chp(1); ch_var = P.chp(2); ch_rstd = P.chp(3)
    ch_norm = P.chp(4); ch_tp = P.chp(5); ch_evac = P.chp(6)

    for t in range(NT):
        par = t % 2
        x_t = xb[:, par * D:(par + 1) * D]
        c = lambda i, par=par: st[:, par * 8 + i:par * 8 + i + 1]
        if t >= 2:
            P.wait("sync", ch_norm, nf * (t - 1))
        P.dma("sync", x_t, x_dram[t * 128:(t + 1) * 128, :], inc=ch_ld[par])
        P.wait("scalar", ch_ld[par], 16 * (t // 2 + 1))
        P.emit("scalar", lambda e, x=x_t, c=c: e.activation(out=trash[:], in_=x, func=AF.Copy,
                                                            accum_out=c(0)))
        P.drain("scalar")
        P.emit("scalar", lambda e, x=x_t, c=c: e.activation(out=trash2x[:], in_=x, func=AF.Square,
                                                            accum_out=c(1)), inc=ch_stat)
        P.drain("scalar")
        P.emit("scalar", lambda e, c=c: e.activation(out=c(2), in_=c(0), func=AF.Copy, scale=1.0 / D))
        P.emit("scalar", lambda e, c=c: e.activation(out=c(3), in_=c(0), func=AF.Square, scale=1.0 / D))
        P.emit("scalar", lambda e, c=c: e.activation(out=c(4), in_=c(1), func=AF.Copy, scale=1.0 / D),
               inc=ch_stat)
        P.wait("vector", ch_stat, 2 * (t + 1))
        P.emit("vector", lambda e, c=c: e.tensor_tensor(out=c(5), in0=c(4), in1=c(3), op=OP.subtract),
               inc=ch_var)
        P.wait("scalar", ch_var, t + 1)
        P.emit("scalar", lambda e, c=c: e.activation(out=c(6), in_=c(5), func=AF.Ln, bias=P.scal("eps_ln")))
        P.drain("scalar")
        P.emit("scalar", lambda e, c=c: e.activation(out=c(7), in_=c(6), func=AF.Exp, scale=-0.5),
               inc=ch_rstd)
        P.wait("vector", ch_rstd, t + 1)
        if t >= 2:
            P.wait("vector", ch_tp, nf * (t - 1))
        outs_tok = []
        for fi, (gam, bet, feat_sb, also_tok) in enumerate(feats):
            tok_t = tokb[fi][:, par * D:(par + 1) * D]
            if gam is None:
                P.emit("vector", lambda e, x=x_t, o=tok_t, c=c: e.tensor_scalar(
                    out=o, in0=x, scalar1=c(2), scalar2=c(7),
                    op0=OP.subtract, op1=OP.mult), inc=ch_norm)
            else:
                P.emit("vector", lambda e, x=x_t, c=c: e.tensor_scalar(
                    out=tmp1[:], in0=x, scalar1=c(2), scalar2=c(7),
                    op0=OP.subtract, op1=OP.mult))
                P.drain("vector")
                P.emit("vector", lambda e, g=gam: e.tensor_tensor(
                    out=tmp2[:], in0=tmp1[:], in1=g, op=OP.mult))
                P.drain("vector")
                P.emit("vector", lambda e, o=tok_t, b=bet: e.tensor_tensor(
                    out=o, in0=tmp2[:], in1=b, op=OP.add), inc=ch_norm)
            if also_tok is not None:
                P.wait("scalar", ch_norm, nf * t + fi + 1)
                P.emit("scalar", lambda e, s=tok_t, d=also_tok, t=t: e.copy(
                    out=d[:, t * D:(t + 1) * D], in_=s))
            outs_tok.append((tok_t, feat_sb))
        P.wait("tensor", ch_norm, nf * (t + 1))
        if t >= 2:
            P.wait("tensor", ch_evac, nf * (t - 1))
        for fi, (tok_t, feat_sb) in enumerate(outs_tok):
            bank = (par * nf + fi) * D
            for d in range(ND):
                P.emit("tensor", lambda e, s=tok_t, d=d, bank=bank: e.transpose(
                    pt[:, bank + d * 128:bank + (d + 1) * 128],
                    in_=s[:, d * 128:(d + 1) * 128], identity=idb[:]),
                    inc=ch_tp if d == ND - 1 else None)
            P.wait("scalar", ch_tp, nf * t + fi + 1)
            for d in range(ND):
                P.emit("scalar", lambda e, f=feat_sb, d=d, t=t, bank=bank: e.copy(
                    out=f[:, d * T + t * 128:d * T + (t + 1) * 128],
                    in_=pt[:, bank + d * 128:bank + (d + 1) * 128]),
                    inc=ch_evac if d == ND - 1 else None)
    P.barrier()



_CONST_CACHE = None

def const_inputs():
    global _CONST_CACHE
    if _CONST_CACHE is None:
        import ml_dtypes
        c = host_constants()
        bf = lambda x: np.asarray(x, dtype=ml_dtypes.bfloat16)
        scal = np.zeros((128, 16), np.float32)
        for i, v in enumerate(_SCAL_VALS):
            scal[:, i] = v
        _CONST_CACHE = {
            "c_scal": scal,
            "c_id128b": bf(c["id128"]),
            "c_id128f": c["id128"],
            "c_utonesb": bf(c["utones"]),
            "c_ltstrictb": bf(c["ltstrict"]),
            "c_negltb": bf(c["neglt"]),
            "c_ones128b": bf(c["ones128"]),
            "c_ones128f": c["ones128"],
            "c_blkf": c["blk"],
            "c_negblkf": c["negblk"],
            "c_neglt4f": c["neglt4"],
            "c_blkpb": bf(c["blkp"]),
            "c_bd2b": bf(c["bd2"]),
            "c_bd2tb": bf(c["bd2"].T.copy()),
        }
    return _CONST_CACHE

_SCAL_NAMES = ["eps_ln", "one", "eps_l2", "eps_moe", "f32eps", "half", "neg1e9", "zero"]
_SCAL_VALS = [1e-5, 1.0, 1e-12, 1e-6, float(np.finfo(np.float32).eps), 0.5, -1e9, 0.0]

_CONST_SHAPES = {
    "c_scal": (128, 16),
    "c_id128b": (128, 128), "c_id128f": (128, 128), "c_utonesb": (128, 128),
    "c_ltstrictb": (128, 128), "c_negltb": (128, 128), "c_ones128b": (128, 128),
    "c_ones128f": (128, 128), "c_blkf": (32, 32 * 128), "c_negblkf": (32, 32 * 128),
    "c_neglt4f": (128, 4 * 128), "c_blkpb": (32, 32 * 64),
    "c_bd2b": (128, 2), "c_bd2tb": (2, 128),
}
_CONST_F32 = {"c_scal", "c_id128f", "c_ones128f", "c_blkf", "c_negblkf", "c_neglt4f"}
_CONST_ATTR = {
    "c_scal": "cscal",
    "c_id128b": "cid128b", "c_id128f": "cid128f", "c_utonesb": "cutonesb",
    "c_ltstrictb": "cltstrictb", "c_negltb": "cnegltb", "c_ones128b": "cones128b",
    "c_ones128f": "cones128f", "c_blkf": "cblkf", "c_negblkf": "cnegblkf",
    "c_neglt4f": "cneglt4f", "c_blkpb": "cblkpb",
    "c_bd2b": "cbd2b", "c_bd2tb": "cbd2tb",
}

def load_consts(P, stack, names):
    nc = P.nc
    ch = P.ch(_uniq("consts"))
    if not hasattr(P, "cscal"):
        names = ["c_scal"] + [n for n in names if n != "c_scal"]
    for name in names:
        shape = _CONST_SHAPES[name]
        dt = F32 if name in _CONST_F32 else BF16
        dram = nc.dram_tensor(name, list(shape), dt, kind="ExternalInput").ap()
        tile = sbuf(nc, stack, "s" + name, list(shape), dt)
        P.dma("sync", tile[:], dram, inc=ch)
        setattr(P, _CONST_ATTR[name], tile)
    P.scal = lambda n: P.cscal[:, _SCAL_NAMES.index(n):_SCAL_NAMES.index(n) + 1]
    P.scal32 = lambda n: P.cscal[0:32, _SCAL_NAMES.index(n):_SCAL_NAMES.index(n) + 1]
    for e in Prog.ENGS:
        P.wait(e, ch)
    return ch


def repl_rows(v, dtype=np.float32):
    return np.ascontiguousarray(np.broadcast_to(np.asarray(v, dtype).reshape(1, -1), (128, len(v))))



def build_stage_a(with_h=True):
    nc = bass.Bass("TRN2", target_bir_lowering=False)
    x = nc.dram_tensor("x", [T, D], F32, kind="ExternalInput").ap()
    o_xn = nc.dram_tensor("xn_feat", [D, T], BF16, kind="ExternalOutput").ap()
    o_h = nc.dram_tensor("h_feat", [D, T], BF16, kind="ExternalOutput").ap() if with_h else None
    g_r = nc.dram_tensor("ln_g_r", [128, D], F32, kind="ExternalInput").ap() if with_h else None
    b_r = nc.dram_tensor("ln_b_r", [128, D], F32, kind="ExternalInput").ap() if with_h else None

    with ExitStack() as stack:
        P = Prog(nc, stack)
        load_consts(P, stack, ["c_id128b"])
        xn_feat = sbuf(nc, stack, "xn_feat_sb", [128, ND * T], BF16)
        feats = [(None, None, xn_feat, None)]
        if with_h:
            sg = sbuf(nc, stack, "sg", [128, D]); sb_ = sbuf(nc, stack, "sbeta", [128, D])
            h_feat = sbuf(nc, stack, "h_feat_sb", [128, ND * T], BF16)
            ch0 = P.ch()
            P.dma("sync", sg[:], g_r, inc=ch0)
            P.dma("sync", sb_[:], b_r, inc=ch0)
            for e in Prog.ENGS:
                P.wait(e, ch0)
            feats.append((sg[:], sb_[:], h_feat, None))
        with ExitStack() as s2:
            emit_ln_feat(P, s2, x, feats)
        ch_st = P.ch("st")
        for d in range(ND):
            P.dma("sync", o_xn[d * 128:(d + 1) * 128, :], xn_feat[:, d * T:(d + 1) * T], inc=ch_st)
            if with_h:
                P.dma("sync", o_h[d * 128:(d + 1) * 128, :], h_feat[:, d * T:(d + 1) * T], inc=ch_st)
        P.wait("sync", ch_st)
        block = stack.enter_context(nc.Block())
        P.finalize(block)
    return nc



def emit_cast_f32_to_bf16_dram(P, stack, src, dst, rows, cols):
    nc = P.nc
    nr = rows // 128
    fb = sbuf(nc, stack, f"wc_f_{src.tensor.name}", [128, 2 * cols])
    bb = sbuf(nc, stack, f"wc_b_{src.tensor.name}", [128, 2 * cols], BF16)
    ch_ld = [P.chp(0), P.chp(4)]; ch_cv = P.chp(1); ch_st = [P.chp(2), P.chp(3)]
    for r in range(nr):
        par = r % 2
        if r >= 2:
            P.wait("sync", ch_cv, r - 1)
        P.dma("sync", fb[:, par * cols:(par + 1) * cols], src[r * 128:(r + 1) * 128, :],
              inc=ch_ld[par])
        P.wait("scalar", ch_ld[par], 16 * (r // 2 + 1))
        if r >= 2:
            P.wait("scalar", ch_st[par], 16 * (r // 2))
        P.emit("scalar", lambda e, par=par: e.copy(out=bb[:, par * cols:(par + 1) * cols],
                                                   in_=fb[:, par * cols:(par + 1) * cols]), inc=ch_cv)
        P.wait("sync", ch_cv, r + 1)
        P.dma("sync", dst[r * 128:(r + 1) * 128, :], bb[:, par * cols:(par + 1) * cols], inc=ch_st[par])
    P.barrier()



D_IN = 2048
NH = 32
HP = 64
DST = 128
CONV_DIM = 2304
NXT = 16
SEG = 1024
NSEG = T // SEG
SC = SEG // 128

def emit_mamba(P, stack, xn_feat, win, convw_t, convb_t, dtb, alog, normw_r, wout, d_zs, d_s):
    nc = P.nc

    winb = nc.dram_tensor("winb", [D, 4384], BF16, kind="Internal").ap()
    woutb = nc.dram_tensor("woutb", [D_IN, D], BF16, kind="Internal").ap()
    with ExitStack() as s0:
        emit_cast_f32_to_bf16_dram(P, s0, win, winb, D, 4384)
        emit_cast_f32_to_bf16_dram(P, s0, wout, woutb, D_IN, D)

    sconvw = sbuf(nc, stack, "sconvw", [128, 18 * 4])
    sconvb = sbuf(nc, stack, "sconvb", [128, 18])
    sdtb = sbuf(nc, stack, "sdtb", [32, 4])
    salog = sbuf(nc, stack, "salog", [32, 4])
    snormw = sbuf(nc, stack, "snormw", [128, D_IN])
    chp = P.chp(0)
    P.dma("sync", sconvw[:], convw_t, inc=chp)
    P.dma("sync", sconvb[:], convb_t, inc=chp)
    P.dma("sync", sdtb[:, 0:1], dtb, inc=chp)
    P.dma("sync", salog[:, 0:1], alog, inc=chp)
    P.dma("sync", snormw[:], normw_r, inc=chp)
    P.wait("scalar", chp)
    P.emit("scalar", lambda e: e.activation(out=salog[:, 1:2], in_=salog[:, 0:1], func=AF.Exp, scale=1.0))
    P.drain("scalar")
    P.emit("scalar", lambda e: e.activation(out=salog[:, 2:3], in_=salog[:, 1:2], func=AF.Copy, scale=-1.0))
    P.barrier()
    import os
    if os.environ.get("MAMBA_TRUNC") == "params":
        return

    with ExitStack() as s1:
        wz = sbuf(nc, s1, "wz", [128, 8 * D_IN], BF16)
        ch_wl = P.chp(0)
        for k in range(ND):
            P.dma("sync", wz[:, k * D_IN:(k + 1) * D_IN], winb[k * 128:(k + 1) * 128, 0:D_IN], inc=ch_wl)
        P.wait("tensor", ch_wl)
        pz = psum(nc, s1, "pz", [128, 2 * D_IN])
        zt = sbuf(nc, s1, "zt", [128, 2 * D_IN], BF16)
        ch_mm = P.chp(1); ch_ev = P.chp(2); ch_st = [P.chp(3), P.chp(4)]
        for t in range(NT):
            par = t % 2
            if t >= 2:
                P.wait("tensor", ch_ev, t - 1)
            for k in range(ND):
                for j in range(4):
                    P.emit("tensor", lambda e, par=par, t=t, k=k, j=j: e.matmul(
                        pz[:, par * D_IN + j * 512:par * D_IN + (j + 1) * 512],
                        lhsT=xn_feat[:, k * T + t * 128:k * T + (t + 1) * 128],
                        rhs=wz[:, k * D_IN + j * 512:k * D_IN + (j + 1) * 512],
                        start=(k == 0), stop=(k == ND - 1)),
                        inc=ch_mm if (k == ND - 1 and j == 3) else None)
            P.wait("scalar", ch_mm, t + 1)
            if t >= 2:
                P.wait("scalar", ch_st[par], 16 * (t // 2))
            P.emit("scalar", lambda e, par=par: e.activation(
                out=zt[:, par * D_IN:(par + 1) * D_IN], in_=pz[:, par * D_IN:(par + 1) * D_IN],
                func=AF.Silu), inc=ch_ev)
            P.wait("sync", ch_ev, t + 1)
            P.dma("sync", d_zs[t * 128:(t + 1) * 128, :], zt[:, par * D_IN:(par + 1) * D_IN],
                  inc=ch_st[par])
        P.barrier()

    sB45 = stack.enter_context(ExitStack())
    dtfb = sbuf(nc, sB45, "dtfb", [32, T], BF16)
    abf = sbuf(nc, sB45, "abf", [128, 32])
    ealc = sbuf(nc, sB45, "ealc", [32, 96])
    dtf_holder = []
    with ExitStack() as s2:
        wdt = sbuf(nc, s2, "wdt", [128, 8 * 32], BF16)
        ch_wl = P.chp(0)
        for k in range(ND):
            P.dma("sync", wdt[:, k * 32:(k + 1) * 32], winb[k * 128:(k + 1) * 128, 4352:4384], inc=ch_wl)
        P.wait("tensor", ch_wl)
        pdt = psum(nc, s2, "pdt", [32, 512])
        raw = sbuf(nc, s2, "dtraw", [32, T])
        tmp = sbuf(nc, s2, "dttmp", [32, 2 * T])
        dtf = sbuf(nc, s2, "dtf", [32, T])
        ch_mm = P.chp(1); ch_ev = P.chp(2)
        for j in range(8):
            if j >= 1:
                P.wait("tensor", ch_ev, j)
            for k in range(ND):
                P.emit("tensor", lambda e, j=j, k=k: e.matmul(
                    pdt[:, 0:512],
                    lhsT=wdt[:, k * 32:(k + 1) * 32],
                    rhs=xn_feat[:, k * T + j * 512:k * T + (j + 1) * 512],
                    start=(k == 0), stop=(k == ND - 1)),
                    inc=ch_mm if k == ND - 1 else None)
            P.wait("scalar", ch_mm, j + 1)
            P.emit("scalar", lambda e, j=j: e.copy(out=raw[:, j * 512:(j + 1) * 512], in_=pdt[:, 0:512]),
                   inc=ch_ev)
        P.wait("scalar", ch_ev, 8)
        P.barrier()
        ch_b2a = P.chp(3); ch_b2b = P.chp(4); ch_b2c = P.chp(5); ch_b2d = P.chp(6); ch_b2e = P.chp(7)
        ch_b2g = P.chp(9); ch_b2h = P.chp(10); ch_b2i = P.chp(11)
        u_t = tmp[:, 0:T]; a_t = tmp[:, T:2 * T]
        P.emit("vector", lambda e: e.tensor_scalar(out=u_t, in0=raw[:], scalar1=sdtb[:, 0:1],
                                                   scalar2=None, op0=OP.add), inc=ch_b2a)
        P.wait("scalar", ch_b2a)
        P.emit("scalar", lambda e: e.activation(out=raw[:], in_=u_t, func=AF.Relu))
        P.emit("scalar", lambda e: e.activation(out=a_t, in_=u_t, func=AF.Abs))
        P.drain("scalar")
        P.emit("scalar", lambda e: e.activation(out=a_t, in_=a_t, func=AF.Exp, scale=-1.0))
        P.drain("scalar")
        P.emit("scalar", lambda e: e.activation(out=a_t, in_=a_t, func=AF.Ln,
                                                bias=P.scal32("one")), inc=ch_b2b)
        P.wait("vector", ch_b2b)
        P.emit("vector", lambda e: e.tensor_tensor(out=dtf[:], in0=raw[:], in1=a_t, op=OP.add),
               inc=ch_b2c)
        dtf_holder.append(dtf)
        P.wait("scalar", ch_b2c)
        P.emit("scalar", lambda e: e.copy(out=dtfb[:], in_=dtf[:]))
        P.wait("vector", ch_b2c)
        for c in range(NT):
            P.emit("vector", lambda e, c=c: e.tensor_reduce(
                out=ealc[:, 64 + c:65 + c], in_=dtf[:, c * 128:(c + 1) * 128], axis=AX, op=OP.add),
                inc=ch_b2d if c == NT - 1 else None)
        P.wait("vector", ch_b2d)
        P.emit("vector", lambda e: e.tensor_scalar(out=ealc[:, 32:64], in0=ealc[:, 64:96],
                                                   scalar1=salog[:, 2:3], scalar2=None,
                                                   op0=OP.mult), inc=ch_b2e)
        P.wait("scalar", ch_b2e)
        P.emit("scalar", lambda e: e.activation(out=ealc[:, 0:32], in_=ealc[:, 32:64], func=AF.Exp))
        d_arow = nc.dram_tensor("d_arow", [32, 1], F32, kind="Internal").ap()
        ch_b2f = P.chp(8)
        P.dma("sync", d_arow, salog[:, 2:3], inc=ch_b2f)
        P.wait("sync", ch_b2f, 16)
        P.dma("sync", abf[:], d_arow.squeeze(1).unsqueeze(0).partition_broadcast(128).squeeze(1),
              inc=ch_b2f)
        P.barrier()

    P.dump("dtf", dtf_holder[0][:], [32, T])
    p_sb = sbuf(nc, sB45, "p_sb", [128, NH * HP])
    psb_bf = sbuf(nc, sB45, "psb_bf", [128, NH * HP], BF16)
    xbc = sbuf(nc, sB45, "xbc", [128, 18 * (SEG + 3)], BF16)
    tails = sbuf(nc, sB45, "tails", [128, 18 * 4], BF16)
    ch_pi = P.chp(0)
    P.emit("vector", lambda e: e.memset(p_sb[:], 0.0), inc=ch_pi)
    P.emit("gpsimd", lambda e: e.memset(xbc[:], 0.0))
    P.barrier()

    d_ygf = nc.dram_tensor("d_ygf", [NT, 128, D_IN], BF16, kind="Internal").ap()

    adtT = sbuf(nc, sB45, "adtT", [128, 32], BF16)
    dtT = sbuf(nc, sB45, "dtT", [128, 32], BF16)
    acst = sbuf(nc, sB45, "acst", [128, 32])
    acsf = sbuf(nc, sB45, "acsf", [32, 128])
    nacsf = sbuf(nc, sB45, "nacsf", [32, 128])
    odt = sbuf(nc, sB45, "odt", [128, 32], BF16)
    dect = sbuf(nc, sB45, "dect", [128, 32], BF16)
    fexp_hg = sbuf(nc, sB45, "fexp_hg", [32, 512])
    fz = sbuf(nc, sB45, "fz", [32, NH * HP], BF16)
    eg = sbuf(nc, sB45, "eg", [128, 4 * 512], BF16)
    scb = sbuf(nc, sB45, "scb", [128, 128], BF16)
    xtok = sbuf(nc, sB45, "xtok", [128, NXT * 128], BF16)
    xdt = sbuf(nc, sB45, "xdt", [128, NXT * 128], BF16)
    xdd = sbuf(nc, sB45, "xdd", [128, NXT * 128], BF16)
    ysc = sbuf(nc, sB45, "ysc", [128, D_IN])
    yf = sbuf(nc, sB45, "yf", [128, D_IN], BF16)
    ygb = sbuf(nc, sB45, "ygb", [128, D_IN], BF16)
    ygf = sbuf(nc, sB45, "ygf", [128, D_IN], BF16)
    zbuf = sbuf(nc, sB45, "zbuf", [128, D_IN], BF16)
    gst = sbuf(nc, sB45, "gst", [128, 8])
    btok = sbuf(nc, sB45, "btok", [128, 128], BF16)
    wcol = sbuf(nc, sB45, "wcol", [128, 8 * 128], BF16)
    ctmp = [sbuf(nc, sB45, f"ctmp{i}", [128, 512]) for i in range(2)]

    for half in range(NSEG):
        t0 = half * SEG
        if half >= 1:
            for m in range(18):
                P.emit("scalar", lambda e, m=m: e.copy(
                    out=xbc[:, m * (SEG + 3):m * (SEG + 3) + 3], in_=tails[:, m * 4:m * 4 + 3]))
            P.barrier()
        ch_w = P.chp(0); ch_mm = P.chp(1); ch_ev = P.chp(2)
        sxa = ExitStack()
        pxa = psum(nc, sxa, "pxa", [128, 1024])
        ntc = SEG // 512
        for m in range(18):
            if m >= 1:
                P.wait("sync", ch_mm, m * ntc)
            for k in range(ND):
                P.dma("sync", wcol[:, k * 128:(k + 1) * 128],
                      winb[k * 128:(k + 1) * 128, D_IN + m * 128:D_IN + (m + 1) * 128], inc=ch_w)
            P.wait("tensor", ch_w, 16 * 8 * (m + 1))
            if half == 0 and m == 0:
                P.dump("wcol0", wcol[:], [128, 8 * 128], BF16)
                P.dump("xnslice0", xn_feat[:, 0:512], [128, 512], BF16)
            for tc in range(ntc):
                ppar = tc % 2
                if m * ntc + tc >= 2:
                    P.wait("tensor", ch_ev, m * ntc + tc - 1)
                for k in range(ND):
                    P.emit("tensor", lambda e, m=m, tc=tc, k=k, ppar=ppar, pxa=pxa, t0=t0: e.matmul(
                        pxa[:, ppar * 512:(ppar + 1) * 512],
                        lhsT=wcol[:, k * 128:(k + 1) * 128],
                        rhs=xn_feat[:, k * T + t0 + tc * 512:k * T + t0 + (tc + 1) * 512],
                        start=(k == 0), stop=(k == ND - 1)),
                        inc=ch_mm if k == ND - 1 else None)
                P.wait("scalar", ch_mm, m * ntc + tc + 1)
                P.emit("scalar", lambda e, m=m, tc=tc, ppar=ppar, pxa=pxa: e.copy(
                    out=xbc[:, m * (SEG + 3) + 3 + tc * 512:m * (SEG + 3) + 3 + (tc + 1) * 512],
                    in_=pxa[:, ppar * 512:(ppar + 1) * 512]), inc=ch_ev)
                if half == 0 and m == 0 and tc == 0:
                    P.dump("xbc_t0", xbc[:, 0:1027], [128, 1027], BF16)
        P.barrier()
        sxa.close()
        if half == 0:
            P.dump("xbc_raw0", xbc[:], [128, 18 * (SEG + 3)], BF16)
        if half < NSEG - 1:
            for m in range(18):
                P.emit("scalar", lambda e, m=m: e.copy(
                    out=tails[:, m * 4:m * 4 + 3],
                    in_=xbc[:, m * (SEG + 3) + SEG:m * (SEG + 3) + SEG + 3]))
            P.barrier()
        ch_cv = P.chp(3); ch_cs = P.chp(4)
        npc = SEG // 512
        for m in range(18):
            for ci_idx, pc in enumerate(reversed(range(npc))):
                ci = m * npc + ci_idx
                w4 = lambda j, m=m: sconvw[:, m * 4 + j:m * 4 + j + 1]
                src = lambda j, m=m, pc=pc: xbc[:, m * (SEG + 3) + pc * 512 + j:
                                                m * (SEG + 3) + pc * 512 + j + 512]
                P.emit("vector", lambda e, w4=w4, src=src: e.tensor_scalar(
                    out=ctmp[0][:], in0=src(0), scalar1=w4(0), scalar2=None, op0=OP.mult))
                P.drain("vector")
                P.emit("vector", lambda e, w4=w4, src=src: e.scalar_tensor_tensor(
                    out=ctmp[1][:], in0=src(1), scalar=w4(1), in1=ctmp[0][:], op0=OP.mult, op1=OP.add))
                P.drain("vector")
                P.emit("vector", lambda e, w4=w4, src=src: e.scalar_tensor_tensor(
                    out=ctmp[0][:], in0=src(2), scalar=w4(2), in1=ctmp[1][:], op0=OP.mult, op1=OP.add))
                P.drain("vector")
                P.emit("vector", lambda e, w4=w4, src=src: e.scalar_tensor_tensor(
                    out=ctmp[1][:], in0=src(3), scalar=w4(3), in1=ctmp[0][:], op0=OP.mult, op1=OP.add),
                    inc=ch_cv)
                P.wait("scalar", ch_cv, ci + 1)
                P.emit("scalar", lambda e, m=m, pc=pc: e.activation(
                    out=xbc[:, m * (SEG + 3) + 3 + pc * 512:m * (SEG + 3) + 3 + (pc + 1) * 512],
                    in_=ctmp[1][:], func=AF.Silu, bias=sconvb[:, m:m + 1]), inc=ch_cs)
                P.wait("vector", ch_cs, ci + 1)
        P.barrier()
        if half == 0:
            P.dump("xbc0", xbc[:], [128, 18 * (SEG + 3)], BF16)
        for cc in range(SC):
            cg = half * SC + cc
            xo = 3 + cc * 128
            Bsl = xbc[:, 16 * (SEG + 3) + xo:16 * (SEG + 3) + xo + 128]
            Csl = xbc[:, 17 * (SEG + 3) + xo:17 * (SEG + 3) + xo + 128]
            with ExitStack() as ps1:
                pT = psum(nc, ps1, "pT", [128, 2048], BF16)
                pTf = psum(nc, ps1, "pTf", [128, 128])
                pA = psum(nc, ps1, "pA", [128, 64])
                ch1 = P.chp(5); ch1e = P.chp(6); ch1v = P.chp(7)
                P.emit("tensor", lambda e, cg=cg: e.transpose(
                    pT[:, 32:64], in_=dtfb[:, cg * 128:(cg + 1) * 128], identity=P.cid128b[0:32, 0:32]))
                P.emit("tensor", lambda e, Bsl=Bsl: e.transpose(
                    pT[:, 64:192], in_=Bsl, identity=P.cid128b[:]), inc=ch1)
                P.wait("scalar", ch1, 1)
                P.emit("scalar", lambda e: e.copy(out=dtT[:], in_=pT[:, 32:64]))
                P.emit("scalar", lambda e: e.copy(out=btok[:], in_=pT[:, 64:192]), inc=ch1e)
                P.wait("vector", ch1e, 1)
                P.emit("vector", lambda e: e.tensor_tensor(out=adtT[:], in0=dtT[:], in1=abf[:],
                                                           op=OP.mult), inc=ch1v)
                P.wait("tensor", ch1v, 1)
                P.emit("tensor", lambda e: e.matmul(pA[:, 0:32], lhsT=P.cutonesb[:], rhs=adtT[:],
                                                    start=True, stop=True))
                P.emit("tensor", lambda e: e.matmul(pA[:, 32:64], lhsT=P.cltstrictb[:],
                                                    rhs=adtT[:], start=True, stop=True), inc=ch1)
                P.wait("scalar", ch1, 2)
                P.emit("scalar", lambda e: e.activation(out=odt[:], in_=pA[:, 0:32], func=AF.Exp))
                P.emit("scalar", lambda e: e.activation(out=dect[:], in_=pA[:, 32:64], func=AF.Exp))
                P.emit("scalar", lambda e: e.copy(out=acst[:], in_=pA[:, 0:32]), inc=ch1e)
                P.wait("tensor", ch1e, 2)
                P.emit("tensor", lambda e: e.transpose(pTf[0:32, 0:128], in_=acst[:],
                                                       identity=P.cid128f[:]), inc=ch1)
                P.wait("scalar", ch1, 3)
                if cg == 0:
                    P.dump("acst0", acst[:], [128, 32])
                P.emit("scalar", lambda e: e.copy(out=acsf[:], in_=pTf[0:32, 0:128]))
                P.emit("scalar", lambda e: e.activation(out=nacsf[:], in_=pTf[0:32, 0:128],
                                                        func=AF.Copy, scale=-1.0), inc=ch1e)
                for b2 in range(2):
                    base = 1024 * (1 - b2)
                    for i in range(8):
                        xi = b2 * 8 + i
                        P.emit("tensor", lambda e, xi=xi, i=i, base=base, xo=xo: e.transpose(
                            pT[:, base + i * 128:base + (i + 1) * 128],
                            in_=xbc[:, xi * (SEG + 3) + xo:xi * (SEG + 3) + xo + 128],
                            identity=P.cid128b[:]),
                            inc=ch1 if i == 7 else None)
                    P.wait("scalar", ch1, 4 + b2)
                    for i in range(8):
                        xi = b2 * 8 + i
                        P.emit("scalar", lambda e, xi=xi, i=i, base=base: e.copy(
                            out=xtok[:, xi * 128:(xi + 1) * 128],
                            in_=pT[:, base + i * 128:base + (i + 1) * 128]),
                            inc=ch1e if i == 7 else None)
                P.barrier()
            with ExitStack() as ps2:
                pSc = psum(nc, ps2, "pSc", [128, 128])
                ch2 = P.chp(8)
                P.emit("vector", lambda e: e.tensor_tensor(
                    out=xdt[:].rearrange("p (h q) -> p h q", h=NH),
                    in0=xtok[:].rearrange("p (h q) -> p h q", h=NH),
                    in1=dtT[:].unsqueeze(2).broadcast_to((128, NH, HP)), op=OP.mult))
                P.drain("vector")
                P.emit("vector", lambda e: e.tensor_tensor(
                    out=xdd[:].rearrange("p (h q) -> p h q", h=NH),
                    in0=xdt[:].rearrange("p (h q) -> p h q", h=NH),
                    in1=dect[:].unsqueeze(2).broadcast_to((128, NH, HP)), op=OP.mult))
                P.emit("vector", lambda e, cg=cg: e.tensor_scalar(
                    out=fz[:], in0=P.cblkpb[:], scalar1=ealc[:, cg:cg + 1], scalar2=None,
                    op0=OP.mult), inc=ch2)
                P.emit("tensor", lambda e, Bsl=Bsl, Csl=Csl: e.matmul(
                    pSc[:], lhsT=Bsl, rhs=Csl, start=True, stop=True), inc=ch2)
                P.wait("scalar", ch2, 2)
                P.emit("scalar", lambda e: e.copy(out=scb[:], in_=pSc[:]))
                P.emit("scalar", lambda e: e.copy(out=psb_bf[:], in_=p_sb[:]), inc=ch2)
                P.barrier()
            with ExitStack() as ps3:
                pYo = psum(nc, ps3, "pYo", [128, D_IN])
                ch3 = P.chp(9)
                for j in range(4):
                    P.emit("tensor", lambda e, j=j, Csl=Csl: e.matmul(
                        pYo[:, j * 512:(j + 1) * 512], lhsT=Csl,
                        rhs=psb_bf[:, j * 512:(j + 1) * 512],
                        start=True, stop=True), inc=ch3 if j == 3 else None)
                P.wait("vector", ch3, 1)
                P.emit("vector", lambda e: e.tensor_tensor(
                    out=ysc[:].rearrange("p (h q) -> p h q", h=NH),
                    in0=pYo[:].rearrange("p (h q) -> p h q", h=NH),
                    in1=odt[:].unsqueeze(2).broadcast_to((128, NH, HP)), op=OP.mult), inc=ch3)
                P.barrier()
            with ExitStack() as ps4:
                pD = psum(nc, ps4, "pD", [128, 1024])
                pY = psum(nc, ps4, "pY", [128, D_IN])
                ch4m = P.chp(10); ch4x = P.chp(11); ch4g = P.chp(12); ch4d = P.chp(13); ch4y = P.chp(14)
                ch4f = P.chp(15)
                for hg in range(8):
                    dpar = hg % 2
                    if hg >= 1:
                        P.wait("vector", ch4m, hg)
                    P.emit("vector", lambda e, hg=hg: e.tensor_tensor(
                        out=fexp_hg[:].rearrange("k (h l) -> k h l", h=4),
                        in0=acsf[:].unsqueeze(1).broadcast_to((32, 4, 128)),
                        in1=P.cblkf[:, hg * 512:(hg + 1) * 512].rearrange("k (h l) -> k h l", h=4),
                        op=OP.mult), inc=ch4f)
                    P.wait("tensor", ch4f, hg + 1)
                    if hg >= 2:
                        P.wait("tensor", ch4x, hg - 1)
                    P.emit("tensor", lambda e, hg=hg, dpar=dpar: e.matmul(
                        pD[:, dpar * 512:(dpar + 1) * 512], lhsT=P.cones128f[0:32, :],
                        rhs=fexp_hg[:], start=True, stop=False), inc=ch4m)
                    P.emit("tensor", lambda e, hg=hg, dpar=dpar: e.matmul(
                        pD[:, dpar * 512:(dpar + 1) * 512], lhsT=nacsf[:],
                        rhs=P.cblkf[:, hg * 512:(hg + 1) * 512], start=False, stop=False))
                    P.emit("tensor", lambda e, hg=hg, dpar=dpar: e.matmul(
                        pD[:, dpar * 512:(dpar + 1) * 512], lhsT=P.cid128f[:],
                        rhs=P.cneglt4f[:], start=False, stop=True), inc=ch4m)
                    P.wait("scalar", ch4m, 2 * (hg + 1))
                    if hg >= 2:
                        P.wait("scalar", ch4g, hg - 1)
                    P.emit("scalar", lambda e, dpar=dpar: e.activation(
                        out=eg[:, dpar * 512:(dpar + 1) * 512],
                        in_=pD[:, dpar * 512:(dpar + 1) * 512], func=AF.Exp), inc=ch4x)
                    P.wait("vector", ch4x, hg + 1)
                    if hg >= 2:
                        P.wait("vector", ch4d, hg - 1)
                    P.emit("vector", lambda e, dpar=dpar: e.tensor_tensor(
                        out=eg[:, (2 + dpar) * 512:(3 + dpar) * 512].rearrange("p (h l) -> p h l", h=4),
                        in0=eg[:, dpar * 512:(dpar + 1) * 512].rearrange("p (h l) -> p h l", h=4),
                        in1=scb[:].unsqueeze(1).broadcast_to((128, 4, 128)),
                        op=OP.mult), inc=ch4g)
                    P.wait("tensor", ch4g, hg + 1)
                    for hh in range(4):
                        h = hg * 4 + hh
                        P.emit("tensor", lambda e, h=h, hh=hh, dpar=dpar: e.matmul(
                            pY[:, h * 64:(h + 1) * 64],
                            lhsT=eg[:, (2 + dpar) * 512 + hh * 128:(2 + dpar) * 512 + (hh + 1) * 128],
                            rhs=xdt[:, h * 64:(h + 1) * 64], start=True, stop=True),
                            inc=ch4d if hh == 3 else None)
                P.wait("vector", ch4d, 8)
                P.emit("vector", lambda e: e.tensor_tensor(out=yf[:], in0=ysc[:], in1=pY[:],
                                                           op=OP.add), inc=ch4y)
                P.barrier()
                if cg in (0, 1):
                    P.dump(f"yf{cg}", yf[:], [128, D_IN], BF16)
            with ExitStack() as ps5:
                pDz = psum(nc, ps5, "pDz", [128, D_IN])
                pS = psum(nc, ps5, "pS", [128, D_IN])
                ch5 = P.chp(16)
                for j in range(4):
                    P.emit("tensor", lambda e, j=j: e.matmul(
                        pS[:, j * 512:(j + 1) * 512], lhsT=btok[:],
                        rhs=xdd[:, j * 512:(j + 1) * 512], start=True, stop=True))
                for j in range(4):
                    P.emit("tensor", lambda e, j=j: e.matmul(
                        pDz[:, j * 512:(j + 1) * 512], lhsT=P.cones128b[0:32, :],
                        rhs=fz[:, j * 512:(j + 1) * 512], start=True, stop=True),
                        inc=ch5 if j == 3 else None)
                P.wait("vector", ch5, 1)
                P.emit("vector", lambda e: e.tensor_tensor(out=ysc[:], in0=p_sb[:], in1=pDz[:],
                                                           op=OP.mult))
                P.drain("vector")
                P.emit("vector", lambda e: e.tensor_tensor(out=p_sb[:], in0=ysc[:], in1=pS[:],
                                                           op=OP.add), inc=ch5)
                P.barrier()
                if cg == 0:
                    P.dump("p0", p_sb[:], [128, NH * HP])
            with ExitStack() as ps6:
                pTg = psum(nc, ps6, "pTg", [128, D_IN], BF16)
                ch6 = P.chp(17); ch6e = P.chp(18)
                P.dma("sync", zbuf[:], d_zs[cg * 128:(cg + 1) * 128, :], inc=ch6)
                P.emit("scalar", lambda e: e.activation(out=ygb[:], in_=yf[:], func=AF.Copy,
                                                        accum_out=gst[:, 0:1]))
                P.drain("scalar")
                P.emit("scalar", lambda e: e.activation(out=ygf[:], in_=yf[:], func=AF.Square,
                                                        accum_out=gst[:, 1:2]))
                P.drain("scalar")
                P.emit("scalar", lambda e: e.activation(out=gst[:, 2:3], in_=gst[:, 0:1],
                                                        func=AF.Copy, scale=1.0 / D_IN))
                P.emit("scalar", lambda e: e.activation(out=gst[:, 3:4], in_=gst[:, 0:1],
                                                        func=AF.Square, scale=1.0 / D_IN))
                P.emit("scalar", lambda e: e.activation(out=gst[:, 4:5], in_=gst[:, 1:2],
                                                        func=AF.Copy, scale=1.0 / D_IN), inc=ch6e)
                P.wait("vector", ch6e, 1)
                P.emit("vector", lambda e: e.tensor_tensor(out=gst[:, 5:6], in0=gst[:, 4:5],
                                                           in1=gst[:, 3:4], op=OP.subtract), inc=ch6e)
                P.wait("scalar", ch6e, 2)
                P.emit("scalar", lambda e: e.activation(out=gst[:, 6:7], in_=gst[:, 5:6],
                                                        func=AF.Ln, bias=P.scal("eps_ln")))
                P.drain("scalar")
                P.emit("scalar", lambda e: e.activation(out=gst[:, 7:8], in_=gst[:, 6:7],
                                                        func=AF.Exp, scale=-0.5), inc=ch6e)
                P.wait("vector", ch6e, 3)
                P.wait("vector", ch6, 16)
                P.emit("vector", lambda e: e.tensor_scalar(out=ysc[:], in0=yf[:],
                                                           scalar1=gst[:, 2:3], scalar2=gst[:, 7:8],
                                                           op0=OP.subtract, op1=OP.mult))
                P.drain("vector")
                P.emit("vector", lambda e: e.tensor_tensor(out=ygf[:], in0=ysc[:], in1=snormw[:],
                                                           op=OP.mult))
                P.drain("vector")
                P.emit("vector", lambda e: e.tensor_tensor(out=ygb[:], in0=ygf[:], in1=zbuf[:],
                                                           op=OP.mult), inc=ch6)
                P.wait("tensor", ch6, 17)
                for i in range(16):
                    P.emit("tensor", lambda e, i=i: e.transpose(
                        pTg[:, i * 128:(i + 1) * 128], in_=ygb[:, i * 128:(i + 1) * 128],
                        identity=P.cid128b[:]), inc=ch6 if i == 15 else None)
                P.wait("scalar", ch6, 18)
                P.emit("scalar", lambda e: e.copy(out=ygf[:], in_=pTg[:]), inc=ch6e)
                P.wait("sync", ch6e, 4)
                P.dma("sync", d_ygf[cg], ygf[:], inc=ch6)
                P.barrier()

    sB45.close()
    with ExitStack() as s6:
        wo_sb = sbuf(nc, s6, "wo_sb", [128, 16 * D], BF16)
        ch_w = P.chp(0)
        for k in range(16):
            P.dma("sync", wo_sb[:, k * D:(k + 1) * D], woutb[k * 128:(k + 1) * 128, :], inc=ch_w)
        ygl = sbuf(nc, s6, "ygl", [128, 2 * D_IN], BF16)
        sout = sbuf(nc, s6, "sout", [128, 2 * D], BF16)
        pO = psum(nc, s6, "pO", [128, 2 * D])
        P.wait("tensor", ch_w)
        ch_l = [P.chp(1), P.chp(6)]; ch_m = P.chp(2); ch_e = P.chp(3); ch_s = [P.chp(4), P.chp(5)]
        for t in range(NT):
            par = t % 2
            if t >= 2:
                P.wait("sync", ch_m, t - 1)
            P.dma("sync", ygl[:, par * D_IN:(par + 1) * D_IN], d_ygf[t], inc=ch_l[par])
            P.wait("tensor", ch_l[par], 16 * (t // 2 + 1))
            if t >= 2:
                P.wait("tensor", ch_e, t - 1)
            for k in range(16):
                for j in range(2):
                    P.emit("tensor", lambda e, t=t, par=par, k=k, j=j: e.matmul(
                        pO[:, par * D + j * 512:par * D + (j + 1) * 512],
                        lhsT=ygl[:, par * D_IN + k * 128:par * D_IN + (k + 1) * 128],
                        rhs=wo_sb[:, k * D + j * 512:k * D + (j + 1) * 512],
                        start=(k == 0), stop=(k == 15)),
                        inc=ch_m if (k == 15 and j == 1) else None)
            P.wait("scalar", ch_m, t + 1)
            if t >= 2:
                P.wait("scalar", ch_s[par], 16 * (t // 2))
            P.emit("scalar", lambda e, par=par: e.copy(out=sout[:, par * D:(par + 1) * D],
                                                       in_=pO[:, par * D:(par + 1) * D]), inc=ch_e)
            P.wait("sync", ch_e, t + 1)
            P.dma("sync", d_s[t * 128:(t + 1) * 128, :], sout[:, par * D:(par + 1) * D],
                  inc=ch_s[par])
        P.barrier()


def mamba_host_prep(ins, prefix="L0_attn_"):
    return {
        "win": np.ascontiguousarray(ins[prefix + "W_in"]),
        "wout": np.ascontiguousarray(ins[prefix + "W_out"]),
        "convw_t": np.ascontiguousarray(ins[prefix + "conv_w"].reshape(18, 128, 4)
                                        .transpose(1, 0, 2).reshape(128, 72)),
        "convb_t": np.ascontiguousarray(ins[prefix + "conv_b"].reshape(18, 128).T),
        "dtb": np.ascontiguousarray(ins[prefix + "dt_bias"].reshape(32, 1)),
        "alog": np.ascontiguousarray(ins[prefix + "A_log"].reshape(32, 1)),
        "normw_r": repl_rows(ins[prefix + "norm_w"]),
    }

MAMBA_CONSTS = ["c_id128b", "c_id128f", "c_utonesb", "c_ltstrictb", "c_ones128b",
                "c_ones128f", "c_blkf", "c_neglt4f", "c_blkpb"]

def build_stage_b():
    nc = bass.Bass("TRN2", target_bir_lowering=False)
    xnf = nc.dram_tensor("xn_feat", [D, T], BF16, kind="ExternalInput").ap()
    win = nc.dram_tensor("win", [D, 4384], F32, kind="ExternalInput").ap()
    wout = nc.dram_tensor("wout", [D_IN, D], F32, kind="ExternalInput").ap()
    convw_t = nc.dram_tensor("convw_t", [128, 72], F32, kind="ExternalInput").ap()
    convb_t = nc.dram_tensor("convb_t", [128, 18], F32, kind="ExternalInput").ap()
    dtb = nc.dram_tensor("dtb", [32, 1], F32, kind="ExternalInput").ap()
    alog = nc.dram_tensor("alog", [32, 1], F32, kind="ExternalInput").ap()
    normw_r = nc.dram_tensor("normw_r", [128, D_IN], F32, kind="ExternalInput").ap()
    d_s = nc.dram_tensor("s_out", [T, D], BF16, kind="ExternalOutput").ap()
    d_zs = nc.dram_tensor("d_zs", [T, D_IN], BF16, kind="Internal").ap()

    with ExitStack() as stack:
        P = Prog(nc, stack)
        load_consts(P, stack, MAMBA_CONSTS)
        xn_feat = sbuf(nc, stack, "xn_feat_sb", [128, ND * T], BF16)
        ch0 = P.ch()
        for dd in range(ND):
            P.dma("sync", xn_feat[:, dd * T:(dd + 1) * T], xnf[dd * 128:(dd + 1) * 128, :], inc=ch0)
        for e in Prog.ENGS:
            P.wait(e, ch0)
        emit_mamba(P, stack, xn_feat, win, convw_t, convb_t, dtb, alog, normw_r, wout,
                   d_zs, d_s)
        block = stack.enter_context(nc.Block())
        P.finalize(block)
    return nc



def emit_attn_l0_flash(P, stack, h_feat, out_feat, wqb, wkb, wvb, qs_col, ks_col):
    nc = P.nc
    with ExitStack() as sA:
        qa = [sbuf(nc, sA, f"qa{i}", [65, T], BF16) for i in range(2)]
        ka = [sbuf(nc, sA, f"ka{i}", [65, T], BF16) for i in range(2)]
        vtok = sbuf(nc, sA, "vtok", [128, NT * 130], BF16)
        qf = sbuf(nc, sA, "qf", [128, T], BF16)
        vst = sbuf(nc, sA, "vst", [128, 1024], BF16)
        sqc = sbuf(nc, sA, "sqc", [128, 512], BF16)
        wcol = sbuf(nc, sA, "wcol3", [128, 3 * 8 * 128], BF16)
        sqs = sbuf(nc, sA, "sqs", [128, 2])
        nmall = sbuf(nc, sA, "nmall", [128, 64])
        nmt = sbuf(nc, sA, "nmt", [32, 128], BF16)
        d_nm = nc.dram_tensor(_uniq("d_nm"), [32, 128], BF16, kind="Internal").ap()
        rp = sbuf(nc, sA, "rp", [65, 1024])
        rpb = sbuf(nc, sA, "rpb", [65, 1024], BF16)
        osb = sbuf(nc, sA, "osb", [64, 1024], BF16)
        ptb = [sbuf(nc, sA, f"ptb{i}", [128, 1024], BF16) for i in range(2)]
        ptd = sbuf(nc, sA, "ptd", [128, 128], BF16)

        for dt_i in range(ND):
            chw = P.chp(1); chm = P.chp(2); che = P.chp(3); chv = P.chp(4)
            for wi, w in enumerate((wqb, wkb, wvb)):
                for k in range(ND):
                    P.dma("sync", wcol[:, (wi * 8 + k) * 128:(wi * 8 + k + 1) * 128],
                          w[k * 128:(k + 1) * 128, dt_i * 128:(dt_i + 1) * 128], inc=chw)
            P.dma("sync", sqs[:, 0:1], qs_col[dt_i * 128:(dt_i + 1) * 128, :], inc=chw)
            P.dma("sync", sqs[:, 1:2], ks_col[dt_i * 128:(dt_i + 1) * 128, :], inc=chw)
            P.emit("vector", lambda e: e.memset(vtok[:], 1.0))
            P.emit("vector", lambda e, h2i=0: e.memset(ka[0][64:65, :], 1.0))
            P.emit("vector", lambda e, h2i=1: e.memset(ka[1][64:65, :], 1.0), inc=chv)
            P.barrier()
            for pss in range(2):
                with ExitStack() as sP:
                    pqkv = psum(nc, sP, "pqkv", [128, 2 * 512])
                    pT2 = psum(nc, sP, "pT2", [128, 1024], BF16)
                    nwi = 2 if pss == 0 else 1
                    wis = (0, 2) if pss == 0 else (1,)
                    for tc in range(8):
                        if tc >= 1:
                            P.wait("tensor", che, 2 * tc - 1 if pss == 0 else 16 + tc)
                        for wj, wi in enumerate(wis):
                            for k in range(ND):
                                P.emit("tensor", lambda e, tc=tc, wi=wi, wj=wj, k=k: e.matmul(
                                    pqkv[:, wj * 512:(wj + 1) * 512],
                                    lhsT=wcol[:, (wi * 8 + k) * 128:(wi * 8 + k + 1) * 128],
                                    rhs=h_feat[:, k * T + tc * 512:k * T + (tc + 1) * 512],
                                    start=(k == 0), stop=(k == ND - 1)),
                                    inc=chm if k == ND - 1 else None)
                        nmm = (3 * tc + 2) if pss == 0 else (24 + tc + 1)
                        P.wait("scalar", chm, nmm)
                        P.emit("scalar", lambda e, tc=tc: e.copy(
                            out=qf[:, tc * 512:(tc + 1) * 512], in_=pqkv[:, 0:512]),
                            inc=che if pss == 1 else None)
                        if pss == 0:
                            P.emit("scalar", lambda e, tc=tc, par=tc % 2: e.copy(
                                out=vst[:, par * 512:(par + 1) * 512], in_=pqkv[:, 512:1024]),
                                inc=che)
                            P.wait("tensor", che, 2 * tc + 1)
                            for j in range(4):
                                P.emit("tensor", lambda e, tc=tc, j=j, par=tc % 2: e.transpose(
                                    pT2[:, par * 512 + j * 128:par * 512 + (j + 1) * 128],
                                    in_=vst[:, par * 512 + j * 128:par * 512 + (j + 1) * 128],
                                    identity=P.cid128b[:]),
                                    inc=chm if j == 3 else None)
                            P.wait("scalar", chm, nmm + 1)
                            for j in range(4):
                                kj = tc * 4 + j
                                P.emit("scalar", lambda e, kj=kj, j=j, par=tc % 2: e.copy(
                                    out=vtok[:, kj * 130:kj * 130 + 64],
                                    in_=pT2[:, par * 512 + j * 128:par * 512 + j * 128 + 64]))
                                P.emit("scalar", lambda e, kj=kj, j=j, par=tc % 2: e.copy(
                                    out=vtok[:, kj * 130 + 65:kj * 130 + 129],
                                    in_=pT2[:, par * 512 + j * 128 + 64:par * 512 + (j + 1) * 128]),
                                    inc=che if j == 3 else None)
                    P.barrier()
                with ExitStack() as sL:
                    pR = psum(nc, sL, "pR", [2, 512])
                    pB = psum(nc, sL, "pB", [128, 2 * 512])
                    rrc = sbuf(nc, sL, "rrc", [2, 512])
                    rrc2 = sbuf(nc, sL, "rrc2", [2, 512])
                    rrcb = sbuf(nc, sL, "rrcb", [2, 512], BF16)
                    chl = P.chp(5); chn = P.chp(6)
                    nl = 0
                    nn_ = 0
                    dsts = qa if pss == 0 else ka
                    for tc in range(8):
                        par = tc % 2
                        P.emit("scalar", lambda e, tc=tc: e.activation(
                            out=sqc[:], in_=qf[:, tc * 512:(tc + 1) * 512], func=AF.Square),
                            inc=chl)
                        nl += 1
                        P.wait("tensor", chl, nl)
                        P.emit("tensor", lambda e: e.matmul(pR[:], lhsT=P.cbd2b[:], rhs=sqc[:],
                                                            start=True, stop=True), inc=chl)
                        nl += 1
                        P.wait("scalar", chl, nl)
                        P.emit("scalar", lambda e: e.copy(out=rrc[:], in_=pR[:]))
                        P.drain("scalar")
                        P.emit("scalar", lambda e: e.activation(out=rrc2[:], in_=rrc[:],
                                                                func=AF.Ln,
                                                                bias=P.scal("eps_l2")[0:2, :]))
                        P.drain("scalar")
                        P.emit("scalar", lambda e: e.activation(out=rrcb[:], in_=rrc2[:],
                                                                func=AF.Exp, scale=-0.5), inc=chl)
                        nl += 1
                        P.wait("tensor", chl, nl)
                        if nn_ >= 2:
                            P.wait("tensor", chn, nn_ - 1)
                        P.emit("tensor", lambda e, par=par: e.matmul(
                            pB[:, par * 512:(par + 1) * 512], lhsT=P.cbd2tb[:], rhs=rrcb[:],
                            start=True, stop=True), inc=chn)
                        nn_ += 1
                        P.wait("vector", chn, nn_)
                        sc_col = sqs[:, pss:pss + 1]
                        for hf in range(2):
                            P.emit("vector", lambda e, tc=tc, par=par, hf=hf,
                                   dsts=dsts, sc_col=sc_col: e.scalar_tensor_tensor(
                                out=dsts[hf][0:64, tc * 512:(tc + 1) * 512],
                                in0=qf[64 * hf:64 * hf + 64, tc * 512:(tc + 1) * 512],
                                scalar=sc_col[64 * hf:64 * hf + 64, :],
                                in1=pB[64 * hf:64 * hf + 64, par * 512:(par + 1) * 512],
                                op0=OP.mult, op1=OP.mult),
                                inc=chn if hf == 1 else None)
                        nn_ += 1
                    P.barrier()
            with ExitStack() as s1:
                pS = psum(nc, s1, "pS", [128, T])
                nm8 = sbuf(nc, s1, "nm8", [128, 8])
                ch1m = P.chp(5); ch1v = P.chp(6)
                nmm = 0
                for h2i in range(2):
                    for qi in range(NT):
                        if h2i * NT + qi >= 1:
                            P.wait("tensor", ch1v, h2i * NT + qi)
                        nch = (qi * 128 + 128 + 511) // 512
                        for j in range(nch):
                            n0 = j * 512
                            n1 = min((qi + 1) * 128, n0 + 512)
                            P.emit("tensor", lambda e, h2i=h2i, qi=qi, n0=n0, n1=n1: e.matmul(
                                pS[:, n0:n1], lhsT=qa[h2i][0:64, qi * 128:(qi + 1) * 128],
                                rhs=ka[h2i][0:64, n0:n1], start=True, stop=True), inc=ch1m)
                            nmm += 1
                            P.wait("vector", ch1m, nmm)
                            P.emit("vector", lambda e, j=j, n0=n0, n1=n1: e.reduce_max(
                                out=nm8[:, j:j + 1], in_=pS[:, n0:n1], axis=AX))
                        P.drain("vector")
                        P.emit("vector", lambda e, h2i=h2i, qi=qi, nch=nch: e.reduce_max(
                            out=nmall[:, h2i * 32 + qi:h2i * 32 + qi + 1],
                            in_=nm8[:, 0:nch], axis=AX, negate=True), inc=ch1v)
                        P.drain("vector")
                P.barrier()
            with ExitStack() as s1b:
                pTf = psum(nc, s1b, "pTf2", [128, 128])
                ch1b = P.chp(5)
                n1b = 0
                for h2i in range(2):
                    P.emit("tensor", lambda e, h2i=h2i: e.transpose(
                        pTf[0:32, 0:128], in_=nmall[:, h2i * 32:(h2i + 1) * 32],
                        identity=P.cid128f[:]), inc=ch1b)
                    n1b += 1
                    P.wait("scalar", ch1b, n1b)
                    P.emit("scalar", lambda e: e.copy(out=nmt[:], in_=pTf[0:32, 0:128]), inc=ch1b)
                    n1b += 1
                    P.wait("sync", ch1b, n1b)
                    P.dma("sync", d_nm, nmt[:], inc=ch1b)
                    n1b += 16
                    P.wait("sync", ch1b, n1b)
                    P.dma("sync", qa[h2i][64:65, :],
                          d_nm.rearrange("a b -> (a b)").unsqueeze(0), inc=ch1b)
                    n1b += 16
                    P.wait("tensor", ch1b, n1b)
                    P.wait("scalar", ch1b, n1b)
                P.barrier()
            for h2i in range(2):
                for ic in range(4):
                    kjmax = (ic + 1) * 8 - 1
                    with ExitStack() as s2o:
                        pOut = psum(nc, s2o, "pOut", [65, 1024])
                        with ExitStack() as s2:
                            pST = psum(nc, s2, "pST", [128, 2 * 1024])
                            chP = P.chp(5); chPV = P.chp(6); chE = P.chp(7)
                            for kj in range(kjmax + 1):
                                par = kj % 2
                                if kj >= 2:
                                    P.wait("tensor", chE, kj - 1)
                                for j2 in range(2):
                                    P.emit("tensor", lambda e, h2i=h2i, ic=ic, kj=kj, par=par, j2=j2: e.matmul(
                                        pST[:, par * 1024 + j2 * 512:par * 1024 + (j2 + 1) * 512],
                                        lhsT=ka[h2i][:, kj * 128:(kj + 1) * 128],
                                        rhs=qa[h2i][:, ic * 1024 + j2 * 512:ic * 1024 + (j2 + 1) * 512],
                                        start=True, stop=True),
                                        inc=chP if j2 == 1 else None)
                                P.wait("scalar", chP, 3 * kj + 1)
                                if kj >= 2:
                                    P.wait("scalar", chPV, kj - 1)
                                offs = (kj - ic * 8) * 128
                                if offs < 0:
                                    P.emit("scalar", lambda e, par=par: e.activation(
                                        out=ptb[par][:], in_=pST[:, par * 1024:(par + 1) * 1024],
                                        func=AF.Exp, scale=10.0), inc=chP, k=2)
                                else:
                                    if offs > 0:
                                        P.emit("vector", lambda e, par=par, offs=offs: e.memset(
                                            ptb[par][:, 0:offs], 0.0))
                                    if offs + 128 < 1024:
                                        P.emit("scalar", lambda e, par=par, offs=offs: e.activation(
                                            out=ptb[par][:, offs + 128:1024],
                                            in_=pST[:, par * 1024 + offs + 128:par * 1024 + 1024],
                                            func=AF.Exp, scale=10.0))
                                    P.emit("scalar", lambda e, par=par, offs=offs: e.activation(
                                        out=ptd[:], in_=pST[:, par * 1024 + offs:par * 1024 + offs + 128],
                                        func=AF.Exp, scale=10.0), inc=chP)
                                    P.wait("vector", chP, 3 * kj + 2)
                                    P.emit("vector", lambda e, par=par, offs=offs: e.tensor_tensor(
                                        out=ptb[par][:, offs:offs + 128], in0=ptd[:],
                                        in1=P.cutonesb[:], op=OP.mult), inc=chP)
                                P.wait("tensor", chP, 3 * kj + 3)
                                for j2 in range(2):
                                    P.emit("tensor", lambda e, h2i=h2i, kj=kj, par=par, j2=j2,
                                           pOut=pOut, kjmax=kjmax: e.matmul(
                                        pOut[:, j2 * 512:(j2 + 1) * 512],
                                        lhsT=vtok[:, kj * 130 + h2i * 65:kj * 130 + (h2i + 1) * 65],
                                        rhs=ptb[par][:, j2 * 512:(j2 + 1) * 512],
                                        start=(kj == 0), stop=(kj == kjmax)),
                                        inc=(chPV if j2 == 0 else chE) if True else None)
                            P.barrier()
                        with ExitStack() as s3:
                            pB2 = psum(nc, s3, "pB2", [64, 1024])
                            chN = P.chp(5)
                            P.emit("vector", lambda e, pOut=pOut: e.reciprocal(
                                out=rp[64:65, 0:1024], in_=pOut[64:65, :]), inc=chN)
                            P.wait("scalar", chN, 1)
                            P.emit("scalar", lambda e: e.copy(out=rpb[64:65, 0:1024],
                                                              in_=rp[64:65, 0:1024]))
                            P.emit("scalar", lambda e, pOut=pOut: e.copy(out=osb[:],
                                                                         in_=pOut[0:64, :]), inc=chN)
                            P.wait("tensor", chN, 2)
                            for j2 in range(2):
                                P.emit("tensor", lambda e, j2=j2: e.matmul(
                                    pB2[:, j2 * 512:(j2 + 1) * 512],
                                    lhsT=P.cones128b[64:65, 0:64],
                                    rhs=rpb[64:65, j2 * 512:(j2 + 1) * 512],
                                    start=True, stop=True), inc=chN if j2 == 1 else None)
                            P.wait("vector", chN, 3)
                            P.emit("vector", lambda e, dt_i=dt_i, h2i=h2i, ic=ic: e.tensor_tensor(
                                out=out_feat[64 * h2i:64 * (h2i + 1),
                                             dt_i * T + ic * 1024:dt_i * T + (ic + 1) * 1024],
                                in0=osb[:], in1=pB2[:], op=OP.mult), inc=chN)
                            P.barrier()
    P.barrier()

def emit_attn_l0_tail(P, stack, out_feat, wob, lngc, lnbc, gwbd, gb, d_h2, d_gates):
    nc = P.nc
    with ExitStack() as sW:
        af = sbuf(nc, sW, "af", [128, ND * T], BF16)
        wocol = sbuf(nc, sW, "wocol", [128, 8 * 128], BF16)
        sqc2 = sbuf(nc, sW, "sqc2", [128, 512], BF16)
        murow = sbuf(nc, sW, "murow", [1, T], BF16)
        rstrow = sbuf(nc, sW, "rstrow", [1, T], BF16)
        rc1 = sbuf(nc, sW, "rc1", [1, 512])
        rc2 = sbuf(nc, sW, "rc2", [1, 512])
        rc3 = sbuf(nc, sW, "rc3", [1, 512])
        slng = sbuf(nc, sW, "slng", [128, 8])
        slnb = sbuf(nc, sW, "slnb", [128, 8])
        gwb = sbuf(nc, sW, "gwb", [128, 8 * 8], BF16)
        gbb = sbuf(nc, sW, "gbb", [1, 8], BF16)
        h2tc = sbuf(nc, sW, "h2tc", [128, 8 * 512], BF16)
        tsc1 = sbuf(nc, sW, "tsc1", [128, 512])
        tsc2 = sbuf(nc, sW, "tsc2", [128, 512])
        gmx = sbuf(nc, sW, "gmx", [128, 4])
        gts = sbuf(nc, sW, "gts", [128, 32])
        gsm = sbuf(nc, sW, "gsm", [128, 4])
        gr = sbuf(nc, sW, "gr", [128, 4])
        gout = sbuf(nc, sW, "gout", [128, 32])
        gcols = sbuf(nc, sW, "gcols", [128, 4])
        chW = P.chp(1)
        P.dma("sync", slng[:], lngc, inc=chW)
        P.dma("sync", slnb[:], lnbc, inc=chW)
        gbf = sbuf(nc, sW, "gbf", [1, 8])
        for k in range(ND):
            P.dma("sync", gwb[:, k * 8:(k + 1) * 8], gwbd[k * 128:(k + 1) * 128, :], inc=chW)
        P.dma("sync", gbf[:], gb, inc=chW)
        P.wait("scalar", chW)
        P.emit("scalar", lambda e: e.copy(out=gbb[:], in_=gbf[:]))
        P.barrier()
        with ExitStack() as sW1:
            pW = psum(nc, sW1, "pW", [128, 2 * 512])
            chm = P.chp(2); che = P.chp(3); chw2 = P.chp(4)
            for m in range(ND):
                if m >= 1:
                    P.wait("sync", chm, m * 8)
                for k in range(ND):
                    P.dma("sync", wocol[:, k * 128:(k + 1) * 128],
                          wob[k * 128:(k + 1) * 128, m * 128:(m + 1) * 128], inc=chw2)
                P.wait("tensor", chw2, 16 * 8 * (m + 1))
                for tc in range(8):
                    par = tc % 2
                    if m * 8 + tc >= 2:
                        P.wait("tensor", che, m * 8 + tc - 1)
                    for k in range(ND):
                        P.emit("tensor", lambda e, m=m, tc=tc, k=k, par=par: e.matmul(
                            pW[:, par * 512:(par + 1) * 512],
                            lhsT=wocol[:, k * 128:(k + 1) * 128],
                            rhs=out_feat[:, k * T + tc * 512:k * T + (tc + 1) * 512],
                            start=(k == 0), stop=(k == ND - 1)),
                            inc=chm if k == ND - 1 else None)
                    P.wait("scalar", chm, m * 8 + tc + 1)
                    P.emit("scalar", lambda e, m=m, tc=tc, par=par: e.copy(
                        out=af[:, m * T + tc * 512:m * T + (tc + 1) * 512],
                        in_=pW[:, par * 512:(par + 1) * 512]), inc=che)
            P.barrier()
        with ExitStack() as sW2:
            pS0 = psum(nc, sW2, "pS0", [1, 512])
            pS1 = psum(nc, sW2, "pS1", [1, 512])
            chs = P.chp(2)
            ns = 0
            for tc in range(8):
                for k in range(ND):
                    P.emit("tensor", lambda e, k=k, tc=tc: e.matmul(
                        pS0[:], lhsT=P.cones128b[:, 0:1],
                        rhs=af[:, k * T + tc * 512:k * T + (tc + 1) * 512],
                        start=(k == 0), stop=(k == ND - 1)))
                for k in range(ND):
                    if k >= 1:
                        P.wait("scalar", chs, ns)
                    P.emit("scalar", lambda e, k=k, tc=tc: e.activation(
                        out=sqc2[:], in_=af[:, k * T + tc * 512:k * T + (tc + 1) * 512],
                        func=AF.Square), inc=chs)
                    ns += 1
                    P.wait("tensor", chs, ns)
                    P.emit("tensor", lambda e, k=k: e.matmul(
                        pS1[:], lhsT=P.cones128b[:, 1:2], rhs=sqc2[:],
                        start=(k == 0), stop=(k == ND - 1)), inc=chs)
                    ns += 1
                P.wait("scalar", chs, ns)
                P.emit("scalar", lambda e, tc=tc: e.activation(out=murow[:, tc * 512:(tc + 1) * 512],
                                                               in_=pS0[:], func=AF.Copy,
                                                               scale=1.0 / D))
                P.emit("scalar", lambda e: e.activation(out=rc1[:], in_=pS0[:], func=AF.Square,
                                                        scale=1.0 / D))
                P.emit("scalar", lambda e: e.activation(out=rc2[:], in_=pS1[:], func=AF.Copy,
                                                        scale=1.0 / D), inc=chs)
                ns += 1
                P.wait("vector", chs, ns)
                P.emit("vector", lambda e: e.tensor_tensor(out=rc3[:], in0=rc2[:], in1=rc1[:],
                                                           op=OP.subtract), inc=chs)
                ns += 1
                P.wait("scalar", chs, ns)
                P.emit("scalar", lambda e: e.activation(out=rc1[:], in_=rc3[:], func=AF.Ln,
                                                        bias=P.scal("eps_ln")[0:1, :]))
                P.drain("scalar")
                P.emit("scalar", lambda e, tc=tc: e.activation(
                    out=rstrow[:, tc * 512:(tc + 1) * 512], in_=rc1[:], func=AF.Exp,
                    scale=-0.5), inc=chs)
                ns += 1
                P.wait("tensor", chs, ns)
            P.barrier()
        with ExitStack() as sW3:
            pMu = psum(nc, sW3, "pMu", [128, 512])
            pRs = psum(nc, sW3, "pRs", [128, 512])
            pG = psum(nc, sW3, "pG", [128, 32])
            chh = P.chp(2); chd = P.chp(4)
            nh = 0
            for tc in range(8):
                P.emit("tensor", lambda e, tc=tc: e.matmul(
                    pMu[:], lhsT=P.cones128b[0:1, :], rhs=murow[:, tc * 512:(tc + 1) * 512],
                    start=True, stop=True))
                P.emit("tensor", lambda e, tc=tc: e.matmul(
                    pRs[:], lhsT=P.cones128b[0:1, :], rhs=rstrow[:, tc * 512:(tc + 1) * 512],
                    start=True, stop=True), inc=chh)
                nh += 1
                P.wait("vector", chh, nh)
                for m in range(ND):
                    P.emit("vector", lambda e, m=m, tc=tc: e.tensor_tensor(
                        out=tsc1[:], in0=af[:, m * T + tc * 512:m * T + (tc + 1) * 512],
                        in1=pMu[:], op=OP.subtract))
                    P.drain("vector")
                    P.emit("vector", lambda e, m=m: e.tensor_tensor(
                        out=tsc2[:], in0=tsc1[:], in1=pRs[:], op=OP.mult))
                    P.drain("vector")
                    P.emit("vector", lambda e, m=m: e.tensor_scalar(
                        out=h2tc[:, m * 512:(m + 1) * 512], in0=tsc2[:],
                        scalar1=slng[:, m:m + 1], scalar2=slnb[:, m:m + 1],
                        op0=OP.mult, op1=OP.add), inc=chh)
                    nh += 1
                P.wait("sync", chh, nh)
                for m in range(ND):
                    P.dma("sync", d_h2[m * 128:(m + 1) * 128, tc * 512:(tc + 1) * 512],
                          h2tc[:, m * 512:(m + 1) * 512], inc=chd)
                P.barrier()
                for tb in range(4):
                    for m in range(ND):
                        P.emit("tensor", lambda e, tb=tb, m=m: e.matmul(
                            pG[:, tb * 8:(tb + 1) * 8],
                            lhsT=h2tc[:, m * 512 + tb * 128:m * 512 + (tb + 1) * 128],
                            rhs=gwb[:, m * 8:(m + 1) * 8],
                            start=(m == 0), stop=False))
                    P.emit("tensor", lambda e, tb=tb: e.matmul(
                        pG[:, tb * 8:(tb + 1) * 8], lhsT=P.cones128b[0:1, :],
                        rhs=gbb[:], start=False, stop=True))
                P.barrier()
                for tb in range(4):
                    P.emit("vector", lambda e, tb=tb: e.reduce_max(
                        out=gmx[:, tb:tb + 1], in_=pG[:, tb * 8:(tb + 1) * 8], axis=AX,
                        negate=True))
                    P.drain("vector")
                P.barrier()
                for tb in range(4):
                    P.emit("scalar", lambda e, tb=tb: e.activation(
                        out=gts[:, tb * 8:(tb + 1) * 8], in_=pG[:, tb * 8:(tb + 1) * 8],
                        func=AF.Exp, bias=gmx[:, tb:tb + 1]))
                    P.drain("scalar")
                P.barrier()
                for tb in range(4):
                    P.emit("vector", lambda e, tb=tb: e.tensor_reduce(
                        out=gsm[:, tb:tb + 1], in_=gts[:, tb * 8:(tb + 1) * 8], axis=AX,
                        op=OP.add))
                    P.drain("vector")
                for tb in range(4):
                    P.emit("vector", lambda e, tb=tb: e.reciprocal(
                        out=gr[:, tb:tb + 1], in_=gsm[:, tb:tb + 1]))
                    P.drain("vector")
                for tb in range(4):
                    P.emit("vector", lambda e, tb=tb: e.tensor_scalar(
                        out=gout[:, tb * 8:(tb + 1) * 8], in0=gts[:, tb * 8:(tb + 1) * 8],
                        scalar1=gr[:, tb:tb + 1], scalar2=None, op0=OP.mult))
                    P.drain("vector")
                P.barrier()
                chgd = P.chp(5)
                for tb in range(4):
                    P.dma("sync", d_gates[tc * 512 + tb * 128:tc * 512 + (tb + 1) * 128, :],
                          gout[:, tb * 8:(tb + 1) * 8], inc=chgd)
                P.barrier()
    P.barrier()

ATTN_CONSTS = ["c_id128b", "c_id128f", "c_utonesb", "c_ones128b", "c_bd2b", "c_bd2tb"]

def attn_l0_host_prep(ins):
    return {
        "wq": np.ascontiguousarray(ins["L0_ffn_Wq"]),
        "wk": np.ascontiguousarray(ins["L0_ffn_Wk"]),
        "wv": np.ascontiguousarray(ins["L0_ffn_Wv"]),
        "wo": np.ascontiguousarray(ins["L0_ffn_Wo"]),
        "qs_col": np.ascontiguousarray(ins["L0_ffn_q_scale"].reshape(1024, 1)),
        "ks_col": np.ascontiguousarray(ins["L0_ffn_k_scale"].reshape(1024, 1)),
        "lngc": np.ascontiguousarray(ins["L0_ffn_ln_g"].reshape(8, 128).T),
        "lnbc": np.ascontiguousarray(ins["L0_ffn_ln_b"].reshape(8, 128).T),
        "gw": np.ascontiguousarray(ins["L0_ffn_gate_w"]),
        "gb": np.ascontiguousarray(ins["L0_ffn_gate_b"].reshape(1, 8)),
    }

def build_stage_c():
    nc = bass.Bass("TRN2", target_bir_lowering=False)
    hf = nc.dram_tensor("h_feat", [D, T], BF16, kind="ExternalInput").ap()
    wq = nc.dram_tensor("wq", [D, D], F32, kind="ExternalInput").ap()
    wk = nc.dram_tensor("wk", [D, D], F32, kind="ExternalInput").ap()
    wv = nc.dram_tensor("wv", [D, D], F32, kind="ExternalInput").ap()
    wo = nc.dram_tensor("wo", [D, D], F32, kind="ExternalInput").ap()
    qs = nc.dram_tensor("qs_col", [D, 1], F32, kind="ExternalInput").ap()
    ks = nc.dram_tensor("ks_col", [D, 1], F32, kind="ExternalInput").ap()
    lngc = nc.dram_tensor("lngc", [128, 8], F32, kind="ExternalInput").ap()
    lnbc = nc.dram_tensor("lnbc", [128, 8], F32, kind="ExternalInput").ap()
    gw = nc.dram_tensor("gw", [D, 8], F32, kind="ExternalInput").ap()
    gb = nc.dram_tensor("gb", [1, 8], F32, kind="ExternalInput").ap()
    d_h2 = nc.dram_tensor("h2_feat", [D, T], BF16, kind="ExternalOutput").ap()
    d_gates = nc.dram_tensor("gates", [T, 8], F32, kind="ExternalOutput").ap()
    with ExitStack() as stack:
        P = Prog(nc, stack)
        load_consts(P, stack, ATTN_CONSTS)
        wqb = nc.dram_tensor("wqb", [D, D], BF16, kind="Internal").ap()
        wkb = nc.dram_tensor("wkb", [D, D], BF16, kind="Internal").ap()
        wvb = nc.dram_tensor("wvb", [D, D], BF16, kind="Internal").ap()
        wob = nc.dram_tensor("wob", [D, D], BF16, kind="Internal").ap()
        gwbd = nc.dram_tensor("gwbd", [D, 8], BF16, kind="Internal").ap()
        with ExitStack() as s0:
            emit_cast_f32_to_bf16_dram(P, s0, wq, wqb, D, D)
            emit_cast_f32_to_bf16_dram(P, s0, wk, wkb, D, D)
            emit_cast_f32_to_bf16_dram(P, s0, wv, wvb, D, D)
            emit_cast_f32_to_bf16_dram(P, s0, wo, wob, D, D)
            emit_cast_f32_to_bf16_dram(P, s0, gw, gwbd, D, 8)
        out_feat = sbuf(nc, stack, "at_out", [128, ND * T], BF16)
        with ExitStack() as sh:
            h_feat = sbuf(nc, sh, "h_feat_sb", [128, ND * T], BF16)
            ch0 = P.ch()
            for dd in range(ND):
                P.dma("sync", h_feat[:, dd * T:(dd + 1) * T], hf[dd * 128:(dd + 1) * 128, :], inc=ch0)
            for e in Prog.ENGS:
                P.wait(e, ch0)
            emit_attn_l0_flash(P, stack, h_feat, out_feat, wqb, wkb, wvb, qs, ks)
        emit_attn_l0_tail(P, stack, out_feat, wob, lngc, lnbc, gwbd, gb, d_h2, d_gates)
        block = stack.enter_context(nc.Block())
        P.finalize(block)
    return nc



def emit_moe_l0(P, stack, d_h2, w1b, w2b, b1_t, b2, b2r_all, gates_d, lng_r, lnb_r, d_g, d_moe_ext=None):
    nc = P.nc
    d_moe = d_moe_ext if d_moe_ext is not None else nc.dram_tensor(_uniq("d_moe"), [T, D], F32, kind="Internal").ap()
    gs = sbuf(nc, stack, "gs", [128, NT * 8])
    sb1 = sbuf(nc, stack, "sb1", [128, 256])
    sb2 = sbuf(nc, stack, "sb2", [8, 1024])
    sb2b = sbuf(nc, stack, "sb2b", [8, 1024], BF16)

    ch0 = P.chp(0)
    for tt in range(NT):
        P.dma("sync", gs[:, tt * 8:(tt + 1) * 8], gates_d[tt * 128:(tt + 1) * 128, :], inc=ch0)
    P.dma("sync", sb1[:], b1_t, inc=ch0)
    P.dma("sync", sb2[:], b2, inc=ch0)
    P.wait("scalar", ch0)
    P.emit("scalar", lambda e: e.copy(out=sb2b[:], in_=sb2[:]))
    P.barrier()

    sE = ExitStack()
    if os.environ.get("DTRUNC") == "pre":
        sE = ExitStack()
        return
    wsb1 = sbuf(nc, sE, "wsb1", [128, 8 * T], BF16)
    wsb2 = sbuf(nc, sE, "wsb2", [128, 32 * D], BF16)
    h2c = sbuf(nc, sE, "h2c", [128, 2 * 8 * 512], BF16)
    hid = sbuf(nc, sE, "hid", [128, 32 * 512], BF16)
    mout = sbuf(nc, sE, "mout", [128, 2 * 1024])
    macc = sbuf(nc, sE, "macc", [128, 2 * 1024])
    nexp = int(os.environ.get("DNEXP", "8"))
    pHsc = ExitStack()
    ph = psum(nc, pHsc, "phX", [128, 2 * 512])
    pO = psum(nc, pHsc, "pOX", [128, 2 * 1024])
    chh = P.chp(2); chm = P.chp(3); che = P.chp(4)
    chw2 = P.chp(6); chw3 = P.chp(7)
    chdl = P.chp(5); chds = [P.chp(8), P.chp(9)]
    nm = 0; ne = 0; nw2 = 0; nw3 = 0; ndl = 0; nds = [0, 0]; nld = 0
    prev_nm = {0: 0, 1: 0}
    for e_i in range(nexp):
        if e_i > 0:
            P.wait("sync", chds[0], 16 * nds[0])
            P.wait("sync", chds[1], 16 * nds[1])
        chwl = P.ch(_uniq("wl"))
        for k in range(ND):
            P.dma("sync", wsb1[:, k * T:(k + 1) * T],
                  w1b[e_i * D + k * 128:e_i * D + (k + 1) * 128, :], inc=chwl)
        for m in range(32):
            P.dma("sync", wsb2[:, m * D:(m + 1) * D],
                  w2b[e_i * 4096 + m * 128:e_i * 4096 + (m + 1) * 128, :], inc=chwl)
        for en in ("tensor", "scalar"):
            P.wait(en, chwl)
        for tc in range(8):
            hpar = tc % 2
            if prev_nm[hpar] > 0:
                P.wait("sync", chm, prev_nm[hpar])
            for k in range(ND):
                P.dma("sync", h2c[:, hpar * 4096 + k * 512:hpar * 4096 + (k + 1) * 512],
                      d_h2[k * 128:(k + 1) * 128, tc * 512:(tc + 1) * 512], inc=chh)
            nld += 8
            P.wait("tensor", chh, 16 * nld)
            for m in range(32):
                par = m % 2
                if ne >= 2 or nw2 > 0:
                    P.wait("tensor", che, max(ne - 1, 0))
                for k in range(ND):
                    P.emit("tensor", lambda e, m=m, k=k, par=par, hpar=hpar, e_i=e_i: e.matmul(
                        ph[:, par * 512:(par + 1) * 512],
                        lhsT=wsb1[:, k * T + m * 128:k * T + (m + 1) * 128],
                        rhs=h2c[:, hpar * 4096 + k * 512:hpar * 4096 + (k + 1) * 512],
                        start=(k == 0), stop=(k == ND - 1)),
                        inc=chm if k == ND - 1 else None)
                nm += 1
                P.wait("scalar", chm, nm)
                P.emit("scalar", lambda e, m=m, e_i=e_i, par=par: e.activation(
                    out=hid[:, m * 512:(m + 1) * 512], in_=ph[:, par * 512:(par + 1) * 512],
                    func=AF.Gelu_apprx_tanh, bias=sb1[:, e_i * 32 + m:e_i * 32 + m + 1]),
                    inc=che)
                ne += 1
            prev_nm[hpar] = nm
            P.wait("tensor", che, ne)
            for t128 in range(4):
                par2 = (tc * 4 + t128) % 2
                gidx = e_i * 32 + tc * 4 + t128
                if nw3 >= 2:
                    P.wait("tensor", chw3, nw3 - 1)
                for m in range(32):
                    for j in range(2):
                        P.emit("tensor", lambda e, t128=t128, m=m, j=j, par2=par2: e.matmul(
                            pO[:, par2 * 1024 + j * 512:par2 * 1024 + (j + 1) * 512],
                            lhsT=hid[:, m * 512 + t128 * 128:m * 512 + (t128 + 1) * 128],
                            rhs=wsb2[:, m * D + j * 512:m * D + (j + 1) * 512],
                            start=(m == 0), stop=(m == 31)),
                            inc=chw2 if (m == 31 and j == 1) else None)
                nw2 += 1
                rows = slice(tc * 512 + t128 * 128, tc * 512 + (t128 + 1) * 128)
                mpar = gidx % 2
                if e_i > 0:
                    P.dma("sync", macc[:, mpar * 1024:(mpar + 1) * 1024],
                          d_moe[rows, :], inc=chdl)
                    ndl += 1
                P.wait("vector", chw2, nw2)
                if e_i > 0:
                    P.wait("vector", chdl, 16 * ndl)
                P.wait("vector", chds[mpar], 16 * nds[mpar])
                gcol = gs[:, (tc * 4 + t128) * 8 + e_i:(tc * 4 + t128) * 8 + e_i + 1]
                if e_i == 0:
                    P.emit("vector", lambda e, par2=par2, mpar=mpar, gcol=gcol: e.tensor_scalar(
                        out=mout[:, mpar * 1024:(mpar + 1) * 1024],
                        in0=pO[:, par2 * 1024:(par2 + 1) * 1024],
                        scalar1=gcol, scalar2=None, op0=OP.mult), inc=chw3)
                else:
                    P.emit("vector", lambda e, par2=par2, mpar=mpar, gcol=gcol: e.scalar_tensor_tensor(
                        out=mout[:, mpar * 1024:(mpar + 1) * 1024],
                        in0=pO[:, par2 * 1024:(par2 + 1) * 1024],
                        scalar=gcol, in1=macc[:, mpar * 1024:(mpar + 1) * 1024],
                        op0=OP.mult, op1=OP.mult if False else OP.add), inc=chw3)
                nw3 += 1
                P.wait("sync", chw3, nw3)
                P.dma("sync", d_moe[rows, :], mout[:, mpar * 1024:(mpar + 1) * 1024],
                      inc=chds[mpar])
                nds[mpar] += 1
    P.barrier()
    pHsc.close()
    sE.close()
    if os.environ.get("DTRUNC") == "experts":
        return
    with ExitStack() as sF:
        mo = sbuf(nc, sF, "mo", [128, 2 * 1024])
        mo2 = sbuf(nc, sF, "mo2", [128, 1024])
        wsum = sbuf(nc, sF, "wsum", [128, 1024])
        b2r = sbuf(nc, sF, "b2r", [128, 8 * 1024])
        gcl = sbuf(nc, sF, "gcl", [128, 8])
        slg = sbuf(nc, sF, "slg", [128, D])
        slb = sbuf(nc, sF, "slb", [128, D])
        gt = sbuf(nc, sF, "gt", [128, 3 * 1024], BF16)
        chf = P.chp(1)
        P.dma("sync", slg[:], lng_r, inc=chf)
        P.dma("sync", slb[:], lnb_r, inc=chf)
        for e8 in range(8):
            P.dma("sync", b2r[:, e8 * 1024:(e8 + 1) * 1024],
                  b2r_all[:, e8 * 1024:(e8 + 1) * 1024], inc=chf)
        P.barrier()
        chl = P.chp(1); chs = [P.chp(2), P.chp(5)]; chst = [P.chp(3), P.chp(4)]
        nl = 0
        for tt in range(NT):
            par = tt % 2
            if tt >= 2:
                P.wait("sync", chl, nl)
            P.dma("sync", mo[:, par * 1024:(par + 1) * 1024],
                  d_moe[tt * 128:(tt + 1) * 128, :], inc=chs[par])
            P.wait("vector", chs[par], 16 * (tt // 2 + 1))
            P.emit("vector", lambda e, tt=tt: e.tensor_scalar(
                out=wsum[:], in0=b2r[:, 0:1024], scalar1=gs[:, tt * 8:tt * 8 + 1],
                scalar2=None, op0=OP.mult))
            for e8 in range(1, 8):
                P.drain("vector")
                P.emit("vector", lambda e, tt=tt, e8=e8: e.scalar_tensor_tensor(
                    out=wsum[:], in0=b2r[:, e8 * 1024:(e8 + 1) * 1024],
                    scalar=gs[:, tt * 8 + e8:tt * 8 + e8 + 1], in1=wsum[:],
                    op0=OP.mult, op1=OP.add))
            P.drain("vector")
            P.emit("vector", lambda e, par=par: e.tensor_tensor(
                out=mo2[:], in0=mo[:, par * 1024:(par + 1) * 1024], in1=wsum[:], op=OP.add),
                inc=chl)
            nl += 1
            P.wait("scalar", chl, nl)
            P.emit("scalar", lambda e: e.activation(out=gt[:, 2048:3072], in_=mo2[:], func=AF.Copy,
                                                    accum_out=gcl[:, 0:1]))
            P.drain("scalar")
            P.emit("scalar", lambda e: e.activation(out=gt[:, 2048:3072], in_=mo2[:],
                                                    func=AF.Square, accum_out=gcl[:, 1:2]))
            P.drain("scalar")
            P.emit("scalar", lambda e: e.activation(out=gcl[:, 2:3], in_=gcl[:, 0:1],
                                                    func=AF.Copy, scale=1.0 / D))
            P.emit("scalar", lambda e: e.activation(out=gcl[:, 3:4], in_=gcl[:, 0:1],
                                                    func=AF.Square, scale=1.0 / D))
            P.emit("scalar", lambda e: e.activation(out=gcl[:, 4:5], in_=gcl[:, 1:2],
                                                    func=AF.Copy, scale=1.0 / D), inc=chl)
            nl += 1
            P.wait("vector", chl, nl)
            P.emit("vector", lambda e: e.tensor_tensor(out=gcl[:, 5:6], in0=gcl[:, 4:5],
                                                       in1=gcl[:, 3:4], op=OP.subtract), inc=chl)
            nl += 1
            P.wait("scalar", chl, nl)
            P.emit("scalar", lambda e: e.activation(out=gcl[:, 6:7], in_=gcl[:, 5:6],
                                                    func=AF.Ln, bias=P.scal("eps_ln")))
            P.drain("scalar")
            P.emit("scalar", lambda e: e.activation(out=gcl[:, 7:8], in_=gcl[:, 6:7],
                                                    func=AF.Exp, scale=-0.5), inc=chl)
            nl += 1
            P.wait("vector", chl, nl)
            P.emit("vector", lambda e, par=par: e.tensor_scalar(
                out=mo[:, par * 1024:(par + 1) * 1024], in0=mo2[:],
                scalar1=gcl[:, 2:3], scalar2=gcl[:, 7:8], op0=OP.subtract, op1=OP.mult))
            P.drain("vector")
            P.emit("vector", lambda e, par=par: e.tensor_tensor(
                out=mo2[:], in0=mo[:, par * 1024:(par + 1) * 1024], in1=slg[:], op=OP.mult))
            P.drain("vector")
            P.emit("vector", lambda e, par=par: e.tensor_tensor(
                out=mo[:, par * 1024:(par + 1) * 1024], in0=mo2[:], in1=slb[:], op=OP.add), inc=chl)
            nl += 1
            P.wait("scalar", chl, nl)
            if tt >= 2:
                P.wait("scalar", chst[par], 16 * (tt // 2))
            P.emit("scalar", lambda e, par=par: e.activation(
                out=gt[:, par * 1024:(par + 1) * 1024],
                in_=mo[:, par * 1024:(par + 1) * 1024], func=AF.Silu), inc=chl)
            nl += 1
            P.wait("sync", chl, nl)
            P.dma("sync", d_g[tt * 128:(tt + 1) * 128, :], gt[:, par * 1024:(par + 1) * 1024],
                  inc=chst[par])
        P.barrier()


def moe_l0_host_prep(ins):
    b1 = ins["L0_ffn_b1"]
    b1_t = b1.reshape(8, 32, 128).transpose(2, 0, 1).reshape(128, 256)
    return {
        "w1": np.ascontiguousarray(ins["L0_ffn_w1"].reshape(8 * 1024, 4096)),
        "w2": np.ascontiguousarray(ins["L0_ffn_w2"].reshape(8 * 4096, 1024)),
        "b1_t": np.ascontiguousarray(b1_t),
        "b2": np.ascontiguousarray(ins["L0_ffn_b2"]),
        "b2r_all": np.ascontiguousarray(np.broadcast_to(ins["L0_ffn_b2"].reshape(1, 8 * 1024),
                                                        (128, 8 * 1024)).copy()),
        "lng_r": repl_rows(ins["L0_ffn_ln_g"]),
        "lnb_r": repl_rows(ins["L0_ffn_ln_b"]),
    }

MOE_CONSTS = ["c_ones128b", "c_id128b"]

def build_stage_d():
    nc = bass.Bass("TRN2", target_bir_lowering=False)
    d_h2 = nc.dram_tensor("h2_feat", [D, T], BF16, kind="ExternalInput").ap()
    w1 = nc.dram_tensor("w1", [8 * 1024, 4096], F32, kind="ExternalInput").ap()
    w2 = nc.dram_tensor("w2", [8 * 4096, 1024], F32, kind="ExternalInput").ap()
    b1_t = nc.dram_tensor("b1_t", [128, 256], F32, kind="ExternalInput").ap()
    b2 = nc.dram_tensor("b2", [8, 1024], F32, kind="ExternalInput").ap()
    b2r_all = nc.dram_tensor("b2r_all", [128, 8 * 1024], F32, kind="ExternalInput").ap()
    gates_d = nc.dram_tensor("gates", [T, 8], F32, kind="ExternalInput").ap()
    lng_r = nc.dram_tensor("lng_r", [128, D], F32, kind="ExternalInput").ap()
    lnb_r = nc.dram_tensor("lnb_r", [128, D], F32, kind="ExternalInput").ap()
    d_g = nc.dram_tensor("g_out", [T, D], BF16, kind="ExternalOutput").ap()
    d_moe_ext = nc.dram_tensor("moe_dbg", [T, D], F32, kind="ExternalOutput").ap() if DBG else None
    with ExitStack() as stack:
        P = Prog(nc, stack)
        load_consts(P, stack, MOE_CONSTS)
        w1b = nc.dram_tensor("w1b", [8 * 1024, 4096], BF16, kind="Internal").ap()
        w2b = nc.dram_tensor("w2b", [8 * 4096, 1024], BF16, kind="Internal").ap()
        dt_ = os.environ.get("DTRUNC", "")
        if dt_ != "nocast1":
            with ExitStack() as s0:
                emit_cast_f32_to_bf16_dram(P, s0, w1, w1b, 8 * 1024, 4096)
        if dt_ not in ("cast1", "nocast2"):
            with ExitStack() as s0:
                emit_cast_f32_to_bf16_dram(P, s0, w2, w2b, 8 * 4096, 1024)
        if dt_ not in ("cast1", "cast2"):
            emit_moe_l0(P, stack, d_h2, w1b, w2b, b1_t, b2, b2r_all, gates_d, lng_r, lnb_r, d_g, d_moe_ext)
        else:
            ch_f = P.ch("fill")
            gtmp = sbuf(nc, stack, "gtmp", [128, 1024], BF16)
            P.emit("vector", lambda e: e.memset(gtmp[:], 0.0), inc=ch_f)
            P.wait("sync", ch_f)
            for tt in range(NT):
                P.dma("sync", d_g[tt * 128:(tt + 1) * 128, :], gtmp[:], inc=ch_f)
            P.wait("sync", ch_f)
        block = stack.enter_context(nc.Block())
        P.finalize(block)
    return nc



def emit_combine(P, stack, x_dram, s_dram, g_dram, xout_dram, out_f32=True):
    nc = P.nc
    with ExitStack() as sC:
        xb = sbuf(nc, sC, "cmb_x", [128, 2 * D])
        sb_ = sbuf(nc, sC, "cmb_s", [128, 2 * D], BF16)
        gb_ = sbuf(nc, sC, "cmb_g", [128, 2 * D], BF16)
        ob = sbuf(nc, sC, "cmb_o", [128, 2 * D])
        tb_ = sbuf(nc, sC, "cmb_t", [128, D])
        chl = [P.chp(0), P.chp(4)]; chv = P.chp(1); chst = [P.chp(2), P.chp(3)]
        for tt in range(NT):
            par = tt % 2
            if tt >= 2:
                P.wait("sync", chv, 2 * (tt - 1))
            P.dma("sync", xb[:, par * D:(par + 1) * D], x_dram[tt * 128:(tt + 1) * 128, :],
                  inc=chl[par])
            P.dma("sync", sb_[:, par * D:(par + 1) * D], s_dram[tt * 128:(tt + 1) * 128, :],
                  inc=chl[par])
            P.dma("sync", gb_[:, par * D:(par + 1) * D], g_dram[tt * 128:(tt + 1) * 128, :],
                  inc=chl[par])
            P.wait("vector", chl[par], 48 * (tt // 2 + 1))
            P.emit("vector", lambda e, par=par: e.tensor_tensor(
                out=tb_[:], in0=sb_[:, par * D:(par + 1) * D], in1=gb_[:, par * D:(par + 1) * D],
                op=OP.mult), inc=chv)
            P.drain("vector")
            if tt >= 2:
                P.wait("vector", chst[par], 16 * (tt // 2))
            P.emit("vector", lambda e, par=par: e.tensor_tensor(
                out=ob[:, par * D:(par + 1) * D], in0=xb[:, par * D:(par + 1) * D], in1=tb_[:],
                op=OP.add), inc=chv)
            P.wait("sync", chv, 2 * (tt + 1))
            P.dma("sync", xout_dram[tt * 128:(tt + 1) * 128, :], ob[:, par * D:(par + 1) * D],
                  inc=chst[par])
        P.barrier()



def emit_attn_l1(P, stack, xf, wqb, wkb, wvb, wob, bq_c, bkv_c, bo_r, d_s1):
    nc = P.nc
    out_feat = sbuf(nc, stack, "l1_out", [128, ND * T], BF16)
    with ExitStack() as sA:
        kvw = sbuf(nc, sA, "kvw", [128, 2 * 8 * 64], BF16)
        sbkv = sbuf(nc, sA, "sbkv", [64, 2])
        sbq = sbuf(nc, sA, "sbq", [128, 8])
        ka = sbuf(nc, sA, "ka1", [65, T], BF16)
        qa = sbuf(nc, sA, "qa1", [65, T], BF16)
        qf = sbuf(nc, sA, "qf1", [128, T], BF16)
        vtok = sbuf(nc, sA, "vtok1", [128, NT * 65], BF16)
        vst = sbuf(nc, sA, "vst1", [64, 2 * 512], BF16)
        wcol = sbuf(nc, sA, "wcol1", [128, 8 * 128], BF16)
        nmall = sbuf(nc, sA, "nmall1", [128, 64])
        nmt = sbuf(nc, sA, "nmt1", [32, 128], BF16)
        d_nm = nc.dram_tensor(_uniq("d_nm1"), [32, 128], BF16, kind="Internal").ap()
        rp = sbuf(nc, sA, "rp1", [65, 1024])
        rpb = sbuf(nc, sA, "rpb1", [65, 1024], BF16)
        osb = sbuf(nc, sA, "osb1", [64, 1024], BF16)
        ptb = [sbuf(nc, sA, f"ptb1{i}", [128, 1024], BF16) for i in range(2)]
        ch0 = P.chp(0)
        for k in range(ND):
            P.dma("sync", kvw[:, k * 64:(k + 1) * 64], wkb[k * 128:(k + 1) * 128, :], inc=ch0)
            P.dma("sync", kvw[:, 512 + k * 64:512 + (k + 1) * 64],
                  wvb[k * 128:(k + 1) * 128, :], inc=ch0)
        P.dma("sync", sbkv[:], bkv_c, inc=ch0)
        P.dma("sync", sbq[:, 0:8], bq_c, inc=ch0)
        P.emit("vector", lambda e: e.memset(vtok[:], 1.0))
        P.emit("vector", lambda e: e.memset(ka[64:65, :], 1.0))
        P.barrier()
        with ExitStack() as sP:
            pkv = psum(nc, sP, "pkv", [64, 2 * 512])
            pT2 = psum(nc, sP, "pT21", [128, 1024], BF16)
            chm = P.chp(1); che = P.chp(2)
            nm = 0; ne = 0
            for tc in range(8):
                par = tc % 2
                if tc >= 2:
                    P.wait("tensor", che, ne - 1)
                for wi in range(2):
                    for k in range(ND):
                        P.emit("tensor", lambda e, tc=tc, wi=wi, k=k, par=par: e.matmul(
                            pkv[:, wi * 512:(wi + 1) * 512],
                            lhsT=kvw[:, wi * 512 + k * 64:wi * 512 + (k + 1) * 64],
                            rhs=xf[:, k * T + tc * 512:k * T + (tc + 1) * 512],
                            start=(k == 0), stop=(k == ND - 1)),
                            inc=chm if k == ND - 1 else None)
                    nm += 1
                P.wait("scalar", chm, nm)
                P.emit("scalar", lambda e, tc=tc: e.activation(
                    out=ka[0:64, tc * 512:(tc + 1) * 512], in_=pkv[:, 0:512],
                    func=AF.Identity, bias=sbkv[:, 0:1]))
                P.emit("scalar", lambda e, tc=tc, par=par: e.activation(
                    out=vst[:, par * 512:(par + 1) * 512], in_=pkv[:, 512:1024],
                    func=AF.Identity, bias=sbkv[:, 1:2]), inc=che)
                ne += 1
                P.wait("tensor", che, ne)
                for j in range(4):
                    P.emit("tensor", lambda e, j=j, par=par: e.transpose(
                        pT2[:, par * 512 + j * 128:par * 512 + j * 128 + 64],
                        in_=vst[:, par * 512 + j * 128:par * 512 + (j + 1) * 128],
                        identity=P.cid128b[0:64, 0:64]), inc=chm if j == 3 else None)
                nm += 1
                P.wait("scalar", chm, nm)
                for j in range(4):
                    kj = tc * 4 + j
                    P.emit("scalar", lambda e, kj=kj, j=j, par=par: e.copy(
                        out=vtok[:, kj * 65:kj * 65 + 64],
                        in_=pT2[:, par * 512 + j * 128:par * 512 + j * 128 + 64]),
                        inc=che if j == 3 else None)
                ne += 1
            P.barrier()
        for dt_i in range(ND):
            chw = P.chp(1); chm = P.chp(2); che = P.chp(3); chv = P.chp(4)
            for k in range(ND):
                P.dma("sync", wcol[:, k * 128:(k + 1) * 128],
                      wqb[k * 128:(k + 1) * 128, dt_i * 128:(dt_i + 1) * 128], inc=chw)
            P.wait("tensor", chw)
            with ExitStack() as sQ:
                pq = psum(nc, sQ, "pq", [128, 2 * 512])
                for tc in range(8):
                    par = tc % 2
                    if tc >= 2:
                        P.wait("tensor", che, tc - 1)
                    for k in range(ND):
                        P.emit("tensor", lambda e, tc=tc, k=k, par=par: e.matmul(
                            pq[:, par * 512:(par + 1) * 512],
                            lhsT=wcol[:, k * 128:(k + 1) * 128],
                            rhs=xf[:, k * T + tc * 512:k * T + (tc + 1) * 512],
                            start=(k == 0), stop=(k == ND - 1)),
                            inc=chm if k == ND - 1 else None)
                    P.wait("scalar", chm, tc + 1)
                    P.emit("scalar", lambda e, tc=tc, par=par, dt_i=dt_i: e.activation(
                        out=qf[:, tc * 512:(tc + 1) * 512], in_=pq[:, par * 512:(par + 1) * 512],
                        func=AF.Identity, bias=sbq[:, dt_i:dt_i + 1]), inc=che)
                P.barrier()
            for h2i in range(2):
                chq = P.chp(5)
                P.emit("scalar", lambda e, h2i=h2i: e.copy(
                    out=qa[0:64, :], in_=qf[64 * h2i:64 * h2i + 64, :]), inc=chq)
                for en in Prog.ENGS:
                    P.wait(en, chq)
                with ExitStack() as s1:
                    pS = psum(nc, s1, "pS1", [128, T])
                    nm8 = sbuf(nc, s1, "nm81", [128, 8])
                    ch1m = P.chp(1); ch1v = P.chp(2)
                    nmm = 0
                    for qi in range(NT):
                        if qi >= 1:
                            P.wait("tensor", ch1v, qi)
                        for j in range(8):
                            P.emit("tensor", lambda e, qi=qi, j=j: e.matmul(
                                pS[:, j * 512:(j + 1) * 512],
                                lhsT=qa[0:64, qi * 128:(qi + 1) * 128],
                                rhs=ka[0:64, j * 512:(j + 1) * 512],
                                start=True, stop=True), inc=ch1m)
                            nmm += 1
                            P.wait("vector", ch1m, nmm)
                            P.emit("vector", lambda e, j=j: e.reduce_max(
                                out=nm8[:, j:j + 1], in_=pS[:, j * 512:(j + 1) * 512], axis=AX))
                        P.drain("vector")
                        P.emit("vector", lambda e, h2i=h2i, qi=qi: e.reduce_max(
                            out=nmall[:, h2i * 32 + qi:h2i * 32 + qi + 1],
                            in_=nm8[:], axis=AX, negate=True), inc=ch1v)
                        P.drain("vector")
                    P.barrier()
                with ExitStack() as s1b:
                    pTf = psum(nc, s1b, "pTf1", [128, 128])
                    ch1b = P.chp(1)
                    P.emit("tensor", lambda e, h2i=h2i: e.transpose(
                        pTf[0:32, 0:128], in_=nmall[:, h2i * 32:(h2i + 1) * 32],
                        identity=P.cid128f[:]), inc=ch1b)
                    P.wait("scalar", ch1b, 1)
                    P.emit("scalar", lambda e: e.copy(out=nmt[:], in_=pTf[0:32, 0:128]), inc=ch1b)
                    P.wait("sync", ch1b, 2)
                    P.dma("sync", d_nm, nmt[:], inc=ch1b)
                    P.wait("sync", ch1b, 18)
                    P.dma("sync", qa[64:65, :], d_nm.rearrange("a b -> (a b)").unsqueeze(0),
                          inc=ch1b)
                    P.barrier()
                for ic in range(4):
                    with ExitStack() as s2o:
                        pOut = psum(nc, s2o, "pOut1", [65, 1024])
                        with ExitStack() as s2:
                            pST = psum(nc, s2, "pST1", [128, 2 * 1024])
                            chP = P.chp(1); chPV = P.chp(2); chE = P.chp(3)
                            for kj in range(NT):
                                par = kj % 2
                                if kj >= 2:
                                    P.wait("tensor", chE, kj - 1)
                                for j2 in range(2):
                                    P.emit("tensor", lambda e, ic=ic, kj=kj, par=par, j2=j2: e.matmul(
                                        pST[:, par * 1024 + j2 * 512:par * 1024 + (j2 + 1) * 512],
                                        lhsT=ka[:, kj * 128:(kj + 1) * 128],
                                        rhs=qa[:, ic * 1024 + j2 * 512:ic * 1024 + (j2 + 1) * 512],
                                        start=True, stop=True),
                                        inc=chP if j2 == 1 else None)
                                P.wait("scalar", chP, 2 * kj + 1)
                                if kj >= 2:
                                    P.wait("scalar", chPV, kj - 1)
                                P.emit("scalar", lambda e, par=par: e.activation(
                                    out=ptb[par][:], in_=pST[:, par * 1024:(par + 1) * 1024],
                                    func=AF.Exp, scale=0.125), inc=chP)
                                P.wait("tensor", chP, 2 * kj + 2)
                                for j2 in range(2):
                                    P.emit("tensor", lambda e, kj=kj, par=par, j2=j2,
                                           pOut=pOut: e.matmul(
                                        pOut[:, j2 * 512:(j2 + 1) * 512],
                                        lhsT=vtok[:, kj * 65:(kj + 1) * 65],
                                        rhs=ptb[par][:, j2 * 512:(j2 + 1) * 512],
                                        start=(kj == 0), stop=(kj == NT - 1)),
                                        inc=(chPV if j2 == 0 else chE))
                            P.barrier()
                        with ExitStack() as s3:
                            pB2 = psum(nc, s3, "pB21", [64, 1024])
                            chN = P.chp(1)
                            P.emit("vector", lambda e, pOut=pOut: e.reciprocal(
                                out=rp[64:65, 0:1024], in_=pOut[64:65, :]), inc=chN)
                            P.wait("scalar", chN, 1)
                            P.emit("scalar", lambda e: e.copy(out=rpb[64:65, 0:1024],
                                                              in_=rp[64:65, 0:1024]))
                            P.emit("scalar", lambda e, pOut=pOut: e.copy(out=osb[:],
                                                                         in_=pOut[0:64, :]),
                                   inc=chN)
                            P.wait("tensor", chN, 2)
                            for j2 in range(2):
                                P.emit("tensor", lambda e, j2=j2: e.matmul(
                                    pB2[:, j2 * 512:(j2 + 1) * 512],
                                    lhsT=P.cones128b[64:65, 0:64],
                                    rhs=rpb[64:65, j2 * 512:(j2 + 1) * 512],
                                    start=True, stop=True), inc=chN if j2 == 1 else None)
                            P.wait("vector", chN, 3)
                            P.emit("vector", lambda e, dt_i=dt_i, h2i=h2i, ic=ic: e.tensor_tensor(
                                out=out_feat[64 * h2i:64 * (h2i + 1),
                                             dt_i * T + ic * 1024:dt_i * T + (ic + 1) * 1024],
                                in0=osb[:], in1=pB2[:], op=OP.mult), inc=chN)
                            P.barrier()
    with ExitStack() as sW:
        wo_sb = sbuf(nc, sW, "wo1_sb", [128, 8 * D], BF16)
        sbo = sbuf(nc, sW, "sbo", [128, D])
        so1 = sbuf(nc, sW, "so1", [128, 1024])
        so2 = sbuf(nc, sW, "so2", [128, 2 * 1024], BF16)
        chw = P.chp(0)
        for k in range(ND):
            P.dma("sync", wo_sb[:, k * D:(k + 1) * D], wob[k * 128:(k + 1) * 128, :], inc=chw)
        P.dma("sync", sbo[:], bo_r, inc=chw)
        P.barrier()
        with ExitStack() as sW1:
            pW = psum(nc, sW1, "pW1", [128, 2 * 1024])
            chm = P.chp(1); chv = P.chp(2); chst = [P.chp(3), P.chp(4)]
            for tt in range(NT):
                par = tt % 2
                if tt >= 2:
                    P.wait("tensor", chv, 2 * tt - 3)
                for k in range(ND):
                    for j in range(2):
                        P.emit("tensor", lambda e, tt=tt, k=k, j=j, par=par: e.matmul(
                            pW[:, par * 1024 + j * 512:par * 1024 + (j + 1) * 512],
                            lhsT=out_feat[:, k * T + tt * 128:k * T + (tt + 1) * 128],
                            rhs=wo_sb[:, k * D + j * 512:k * D + (j + 1) * 512],
                            start=(k == 0), stop=(k == ND - 1)),
                            inc=chm if (k == ND - 1 and j == 1) else None)
                P.wait("vector", chm, tt + 1)
                P.emit("vector", lambda e, par=par: e.tensor_tensor(
                    out=so1[:], in0=pW[:, par * 1024:(par + 1) * 1024], in1=sbo[:],
                    op=OP.add), inc=chv)
                P.drain("vector")
                if tt >= 2:
                    P.wait("vector", chst[par], 16 * (tt // 2))
                P.emit("vector", lambda e, par=par: e.tensor_copy(
                    so2[:, par * 1024:(par + 1) * 1024], so1[:]), inc=chv)
                P.drain("vector")
                P.wait("sync", chv, 2 * (tt + 1))
                P.dma("sync", d_s1[tt * 128:(tt + 1) * 128, :],
                      so2[:, par * 1024:(par + 1) * 1024], inc=chst[par])
            P.barrier()



def emit_moe_l1(P, stack, xf, w1b, w2b, b1_t, b2, gwT_b, gb, rmsw_r, d_g1):
    nc = P.nc
    d_ys = nc.dram_tensor(_uniq("d_ys"), [T, D], F32, kind="Internal").ap()
    ngs = sbuf(nc, stack, "ngs", [128, NT])
    sb1 = sbuf(nc, stack, "sb1_1", [128, 128])
    sb2 = sbuf(nc, stack, "sb2_1", [8, 1024])
    sb2b = sbuf(nc, stack, "sb2b_1", [8, 1024], BF16)
    gwb = sbuf(nc, stack, "gwb1", [128, 8 * 8], BF16)
    gbf = sbuf(nc, stack, "gbf1", [1, 8])
    gbb = sbuf(nc, stack, "gbb1", [1, 8], BF16)
    srms = sbuf(nc, stack, "srms", [128, 8])
    ch0 = P.chp(0)
    P.dma("sync", sb1[:], b1_t, inc=ch0)
    P.dma("sync", sb2[:], b2, inc=ch0)
    P.dma("sync", gbf[:], gb, inc=ch0)
    P.dma("sync", srms[:], rmsw_r, inc=ch0)
    for k in range(ND):
        P.dma("sync", gwb[:, k * 8:(k + 1) * 8], gwT_b[k * 128:(k + 1) * 128, :], inc=ch0)
    P.wait("scalar", ch0)
    P.emit("scalar", lambda e: e.copy(out=sb2b[:], in_=sb2[:]))
    P.drain("scalar")
    P.emit("scalar", lambda e: e.copy(out=gbb[:], in_=gbf[:]))
    P.barrier()
    with ExitStack() as sG:
        pG = psum(nc, sG, "pG1", [128, 32])
        sq8 = sbuf(nc, sG, "sq8", [128, 32])
        gcl = sbuf(nc, sG, "gcl1", [128, 8])
        lt8 = sbuf(nc, sG, "lt8", [128, 32])
        for tq in range(8):
            P.barrier()
            for tb in range(4):
                tt = tq * 4 + tb
                for k in range(ND):
                    P.emit("tensor", lambda e, tb=tb, tt=tt, k=k: e.matmul(
                        pG[:, tb * 8:(tb + 1) * 8],
                        lhsT=xf[:, k * T + tt * 128:k * T + (tt + 1) * 128],
                        rhs=gwb[:, k * 8:(k + 1) * 8],
                        start=(k == 0), stop=False))
                P.emit("tensor", lambda e, tb=tb: e.matmul(
                    pG[:, tb * 8:(tb + 1) * 8], lhsT=P.cones128b[0:1, :],
                    rhs=gbb[:], start=False, stop=True))
            P.barrier()
            for tb in range(4):
                P.emit("scalar", lambda e, tb=tb: e.activation(
                    out=sq8[:, tb * 8:(tb + 1) * 8], in_=pG[:, tb * 8:(tb + 1) * 8],
                    func=AF.Square))
                P.drain("scalar")
            P.barrier()
            for tb in range(4):
                P.emit("vector", lambda e, tb=tb: e.tensor_reduce(
                    out=gcl[:, 0:1], in_=sq8[:, tb * 8:(tb + 1) * 8], axis=AX, op=OP.add))
                P.drain("vector")
                P.barrier()
                P.emit("scalar", lambda e: e.activation(out=gcl[:, 1:2], in_=gcl[:, 0:1],
                                                        func=AF.Ln, scale=1.0 / 8,
                                                        bias=P.scal("f32eps")))
                P.drain("scalar")
                P.emit("scalar", lambda e: e.activation(out=gcl[:, 2:3], in_=gcl[:, 1:2],
                                                        func=AF.Exp, scale=-0.5))
                P.barrier()
                P.emit("vector", lambda e, tb=tb: e.tensor_scalar(
                    out=lt8[:, tb * 8:(tb + 1) * 8], in0=pG[:, tb * 8:(tb + 1) * 8],
                    scalar1=gcl[:, 2:3], scalar2=None, op0=OP.mult))
                P.drain("vector")
                P.emit("vector", lambda e, tb=tb: e.tensor_tensor(
                    out=lt8[:, tb * 8:(tb + 1) * 8], in0=lt8[:, tb * 8:(tb + 1) * 8],
                    in1=srms[:], op=OP.mult))
                P.drain("vector")
                P.emit("vector", lambda e, tb=tb: e.reduce_max(
                    out=gcl[:, 3:4], in_=lt8[:, tb * 8:(tb + 1) * 8], axis=AX, negate=True))
                P.drain("vector")
                P.emit("vector", lambda e: e.tensor_scalar(
                    out=gcl[:, 4:5], in0=gcl[:, 3:4], scalar1=2.0, scalar2=None, op0=OP.mult))
                P.barrier()
                P.emit("scalar", lambda e, tb=tb: e.activation(
                    out=sq8[:, tb * 8:(tb + 1) * 8], in_=lt8[:, tb * 8:(tb + 1) * 8],
                    func=AF.Exp, scale=2.0, bias=gcl[:, 4:5]))
                P.barrier()
                P.emit("vector", lambda e, tb=tb: e.tensor_reduce(
                    out=gcl[:, 5:6], in_=sq8[:, tb * 8:(tb + 1) * 8], axis=AX, op=OP.add))
                P.drain("vector")
                P.emit("vector", lambda e: e.tensor_scalar(
                    out=gcl[:, 6:7], in0=gcl[:, 5:6], scalar1=1e-6, scalar2=1.0,
                    op0=OP.mult, op1=OP.add))
                P.drain("vector")
                P.emit("vector", lambda e: e.reciprocal(out=gcl[:, 7:8], in_=gcl[:, 6:7]))
                P.drain("vector")
                P.emit("vector", lambda e, tb=tb, tq=tq: e.tensor_copy(
                    ngs[:, tq * 4 + tb:tq * 4 + tb + 1], gcl[:, 7:8]))
                P.drain("vector")
            P.barrier()
    sE = ExitStack()
    wsb1 = sbuf(nc, sE, "wsb1_1", [128, 8 * 2048], BF16)
    wsb2 = sbuf(nc, sE, "wsb2_1", [128, 16 * D], BF16)
    hid = sbuf(nc, sE, "hid_1", [128, 16 * 512], BF16)
    mout = sbuf(nc, sE, "mout_1", [128, 2 * 1024])
    macc = sbuf(nc, sE, "macc_1", [128, 2 * 1024])
    pHsc = ExitStack()
    ph = psum(nc, pHsc, "phY", [128, 2 * 512])
    pO = psum(nc, pHsc, "pOY", [128, 2 * 1024])
    chm = P.chp(3); che = P.chp(4)
    chw2 = P.chp(6); chw3 = P.chp(7)
    chdl = P.chp(5); chds = [P.chp(8), P.chp(9)]
    nm = 0; ne = 0; nw2 = 0; nw3 = 0; ndl = 0; nds = [0, 0]
    for e_i in range(8):
        if e_i > 0:
            P.wait("sync", chds[0], 16 * nds[0])
            P.wait("sync", chds[1], 16 * nds[1])
        chwl = P.ch(_uniq("wl1"))
        for k in range(ND):
            P.dma("sync", wsb1[:, k * 2048:(k + 1) * 2048],
                  w1b[e_i * D + k * 128:e_i * D + (k + 1) * 128, :], inc=chwl)
        for m in range(16):
            P.dma("sync", wsb2[:, m * D:(m + 1) * D],
                  w2b[e_i * 2048 + m * 128:e_i * 2048 + (m + 1) * 128, :], inc=chwl)
        for en in ("tensor", "scalar"):
            P.wait(en, chwl)
        for tc in range(8):
            for m in range(16):
                par = m % 2
                if ne >= 2:
                    P.wait("tensor", che, ne - 1)
                for k in range(ND):
                    P.emit("tensor", lambda e, m=m, k=k, par=par, tc=tc: e.matmul(
                        ph[:, par * 512:(par + 1) * 512],
                        lhsT=wsb1[:, k * 2048 + m * 128:k * 2048 + (m + 1) * 128],
                        rhs=xf[:, k * T + tc * 512:k * T + (tc + 1) * 512],
                        start=(k == 0), stop=(k == ND - 1)),
                        inc=chm if k == ND - 1 else None)
                nm += 1
                P.wait("scalar", chm, nm)
                P.emit("scalar", lambda e, m=m, e_i=e_i, par=par: e.activation(
                    out=hid[:, m * 512:(m + 1) * 512], in_=ph[:, par * 512:(par + 1) * 512],
                    func=AF.Gelu, bias=sb1[:, e_i * 16 + m:e_i * 16 + m + 1]),
                    inc=che)
                ne += 1
            P.wait("tensor", che, ne)
            for t128 in range(4):
                gidx = e_i * 32 + tc * 4 + t128
                par2 = (tc * 4 + t128) % 2
                mpar = gidx % 2
                if nw3 >= 2:
                    P.wait("tensor", chw3, nw3 - 1)
                for m in range(16):
                    for j in range(2):
                        P.emit("tensor", lambda e, t128=t128, m=m, j=j, par2=par2: e.matmul(
                            pO[:, par2 * 1024 + j * 512:par2 * 1024 + (j + 1) * 512],
                            lhsT=hid[:, m * 512 + t128 * 128:m * 512 + (t128 + 1) * 128],
                            rhs=wsb2[:, m * D + j * 512:m * D + (j + 1) * 512],
                            start=(m == 0), stop=(m == 15)),
                            inc=chw2 if (m == 15 and j == 1) else None)
                nw2 += 1
                rows = slice(tc * 512 + t128 * 128, tc * 512 + (t128 + 1) * 128)
                if e_i > 0:
                    P.dma("sync", macc[:, mpar * 1024:(mpar + 1) * 1024], d_ys[rows, :], inc=chdl)
                    ndl += 1
                P.wait("vector", chw2, nw2)
                if e_i > 0:
                    P.wait("vector", chdl, 16 * ndl)
                P.wait("vector", chds[mpar], 16 * nds[mpar])
                if e_i == 0:
                    P.emit("vector", lambda e, par2=par2, mpar=mpar: e.tensor_scalar(
                        out=mout[:, mpar * 1024:(mpar + 1) * 1024],
                        in0=pO[:, par2 * 1024:(par2 + 1) * 1024],
                        scalar1=1.0, scalar2=None, op0=OP.mult), inc=chw3)
                else:
                    P.emit("vector", lambda e, par2=par2, mpar=mpar: e.tensor_tensor(
                        out=mout[:, mpar * 1024:(mpar + 1) * 1024],
                        in0=pO[:, par2 * 1024:(par2 + 1) * 1024],
                        in1=macc[:, mpar * 1024:(mpar + 1) * 1024], op=OP.add), inc=chw3)
                nw3 += 1
                P.wait("sync", chw3, nw3)
                P.dma("sync", d_ys[rows, :], mout[:, mpar * 1024:(mpar + 1) * 1024],
                      inc=chds[mpar])
                nds[mpar] += 1
    P.barrier()
    pHsc.close()
    sE.close()
    with ExitStack() as sF:
        pb2r = psum(nc, sF, "pb2r1", [1, 1024])
        pb = psum(nc, sF, "pbf1", [128, 1024])
        b2row = sbuf(nc, sF, "b2row1", [1, 1024], BF16)
        mo = sbuf(nc, sF, "mo1", [128, 2 * 1024])
        mo2 = sbuf(nc, sF, "mo21", [128, 1024])
        gt = sbuf(nc, sF, "gt1", [128, 2 * 1024], BF16)
        chb = P.chp(1)
        P.emit("tensor", lambda e: e.matmul(pb2r[:, 0:512], lhsT=P.cones128b[0:8, 0:1],
                                            rhs=sb2b[:, 0:512], start=True, stop=True))
        P.emit("tensor", lambda e: e.matmul(pb2r[:, 512:1024], lhsT=P.cones128b[0:8, 0:1],
                                            rhs=sb2b[:, 512:1024], start=True, stop=True), inc=chb)
        P.wait("scalar", chb, 1)
        P.emit("scalar", lambda e: e.copy(out=b2row[:], in_=pb2r[:]), inc=chb)
        P.wait("tensor", chb, 2)
        P.emit("tensor", lambda e: e.matmul(pb[:, 0:512], lhsT=P.cones128b[0:1, :],
                                            rhs=b2row[:, 0:512], start=True, stop=True))
        P.emit("tensor", lambda e: e.matmul(pb[:, 512:1024], lhsT=P.cones128b[0:1, :],
                                            rhs=b2row[:, 512:1024], start=True, stop=True))
        P.barrier()
        chl = P.chp(1); chs = [P.chp(2), P.chp(5)]; chst = [P.chp(3), P.chp(4)]
        nl = 0
        for tt in range(NT):
            par = tt % 2
            if tt >= 2:
                P.wait("sync", chl, nl)
            P.dma("sync", mo[:, par * 1024:(par + 1) * 1024],
                  d_ys[tt * 128:(tt + 1) * 128, :], inc=chs[par])
            P.wait("vector", chs[par], 16 * (tt // 2 + 1))
            P.emit("vector", lambda e, par=par: e.tensor_tensor(
                out=mo2[:], in0=mo[:, par * 1024:(par + 1) * 1024], in1=pb[:], op=OP.add))
            P.drain("vector")
            P.emit("vector", lambda e, tt=tt, par=par: e.tensor_scalar(
                out=mo[:, par * 1024:(par + 1) * 1024], in0=mo2[:],
                scalar1=ngs[:, tt:tt + 1], scalar2=None, op0=OP.mult), inc=chl)
            nl += 1
            P.wait("scalar", chl, nl)
            if tt >= 2:
                P.wait("scalar", chst[par], 16 * (tt // 2))
            P.emit("scalar", lambda e, par=par: e.activation(
                out=gt[:, par * 1024:(par + 1) * 1024],
                in_=mo[:, par * 1024:(par + 1) * 1024], func=AF.Silu), inc=chl)
            nl += 1
            P.wait("sync", chl, nl)
            P.dma("sync", d_g1[tt * 128:(tt + 1) * 128, :], gt[:, par * 1024:(par + 1) * 1024],
                  inc=chst[par])
        P.barrier()



def attn_l1_host_prep(ins):
    return {
        "l1_wq": np.ascontiguousarray(ins["L1_attn_Wq"]),
        "l1_wk": np.ascontiguousarray(ins["L1_attn_Wk"]),
        "l1_wv": np.ascontiguousarray(ins["L1_attn_Wv"]),
        "l1_wo": np.ascontiguousarray(ins["L1_attn_Wo"]),
        "l1_bq": np.ascontiguousarray(ins["L1_attn_bq"].reshape(8, 128).T),
        "l1_bkv": np.ascontiguousarray(np.stack([ins["L1_attn_bk"], ins["L1_attn_bv"]], axis=1)),
        "l1_bo": repl_rows(ins["L1_attn_bo"]),
    }

def moe_l1_host_prep(ins):
    b1 = ins["L1_ffn_b1"]
    return {
        "l1_w1": np.ascontiguousarray(ins["L1_ffn_w1"].reshape(8 * 1024, 2048)),
        "l1_w2": np.ascontiguousarray(ins["L1_ffn_w2"].reshape(8 * 2048, 1024)),
        "l1_b1t": np.ascontiguousarray(b1.reshape(8, 16, 128).transpose(2, 0, 1).reshape(128, 128)),
        "l1_b2": np.ascontiguousarray(ins["L1_ffn_b2"]),
        "l1_gwT": np.ascontiguousarray(ins["L1_ffn_gate_w"].T),
        "l1_gb": np.ascontiguousarray(ins["L1_ffn_gate_b"].reshape(1, 8)),
        "l1_rmsw": np.ascontiguousarray(np.broadcast_to(ins["L1_ffn_rms_w"].reshape(1, 8), (128, 8)).copy()),
    }

def build_stage_f():
    nc = bass.Bass("TRN2", target_bir_lowering=False)
    xnf = nc.dram_tensor("x1n_feat", [D, T], BF16, kind="ExternalInput").ap()
    wq = nc.dram_tensor("l1_wq", [D, D], F32, kind="ExternalInput").ap()
    wk = nc.dram_tensor("l1_wk", [D, 64], F32, kind="ExternalInput").ap()
    wv = nc.dram_tensor("l1_wv", [D, 64], F32, kind="ExternalInput").ap()
    wo = nc.dram_tensor("l1_wo", [D, D], F32, kind="ExternalInput").ap()
    bq = nc.dram_tensor("l1_bq", [128, 8], F32, kind="ExternalInput").ap()
    bkv = nc.dram_tensor("l1_bkv", [64, 2], F32, kind="ExternalInput").ap()
    bo = nc.dram_tensor("l1_bo", [128, D], F32, kind="ExternalInput").ap()
    d_s1 = nc.dram_tensor("s1_out", [T, D], BF16, kind="ExternalOutput").ap()
    with ExitStack() as stack:
        P = Prog(nc, stack)
        load_consts(P, stack, ATTN_CONSTS)
        wqb = nc.dram_tensor("l1wqb", [D, D], BF16, kind="Internal").ap()
        wkb = nc.dram_tensor("l1wkb", [D, 64], BF16, kind="Internal").ap()
        wvb = nc.dram_tensor("l1wvb", [D, 64], BF16, kind="Internal").ap()
        wob = nc.dram_tensor("l1wob", [D, D], BF16, kind="Internal").ap()
        with ExitStack() as s0:
            emit_cast_f32_to_bf16_dram(P, s0, wq, wqb, D, D)
            emit_cast_f32_to_bf16_dram(P, s0, wk, wkb, D, 64)
            emit_cast_f32_to_bf16_dram(P, s0, wv, wvb, D, 64)
            emit_cast_f32_to_bf16_dram(P, s0, wo, wob, D, D)
        xf = sbuf(nc, stack, "x1nf_sb", [128, ND * T], BF16)
        ch0 = P.ch()
        for dd in range(ND):
            P.dma("sync", xf[:, dd * T:(dd + 1) * T], xnf[dd * 128:(dd + 1) * 128, :], inc=ch0)
        for e in Prog.ENGS:
            P.wait(e, ch0)
        emit_attn_l1(P, stack, xf, wqb, wkb, wvb, wob, bq, bkv, bo, d_s1)
        block = stack.enter_context(nc.Block())
        P.finalize(block)
    return nc

def build_stage_g():
    nc = bass.Bass("TRN2", target_bir_lowering=False)
    xnf = nc.dram_tensor("x1n_feat", [D, T], BF16, kind="ExternalInput").ap()
    w1 = nc.dram_tensor("l1_w1", [8 * 1024, 2048], F32, kind="ExternalInput").ap()
    w2 = nc.dram_tensor("l1_w2", [8 * 2048, 1024], F32, kind="ExternalInput").ap()
    b1t = nc.dram_tensor("l1_b1t", [128, 128], F32, kind="ExternalInput").ap()
    b2 = nc.dram_tensor("l1_b2", [8, 1024], F32, kind="ExternalInput").ap()
    gwT = nc.dram_tensor("l1_gwT", [D, 8], F32, kind="ExternalInput").ap()
    gb = nc.dram_tensor("l1_gb", [1, 8], F32, kind="ExternalInput").ap()
    rmsw = nc.dram_tensor("l1_rmsw", [128, 8], F32, kind="ExternalInput").ap()
    d_g1 = nc.dram_tensor("g1_out", [T, D], BF16, kind="ExternalOutput").ap()
    with ExitStack() as stack:
        P = Prog(nc, stack)
        load_consts(P, stack, MOE_CONSTS)
        w1b = nc.dram_tensor("l1w1b", [8 * 1024, 2048], BF16, kind="Internal").ap()
        w2b = nc.dram_tensor("l1w2b", [8 * 2048, 1024], BF16, kind="Internal").ap()
        gwTb = nc.dram_tensor("l1gwTb", [D, 8], BF16, kind="Internal").ap()
        with ExitStack() as s0:
            emit_cast_f32_to_bf16_dram(P, s0, w1, w1b, 8 * 1024, 2048)
        with ExitStack() as s0:
            emit_cast_f32_to_bf16_dram(P, s0, w2, w2b, 8 * 2048, 1024)
        with ExitStack() as s0:
            emit_cast_f32_to_bf16_dram(P, s0, gwT, gwTb, D, 8)
        xf = sbuf(nc, stack, "x1nf_sb2", [128, ND * T], BF16)
        ch0 = P.ch()
        for dd in range(ND):
            P.dma("sync", xf[:, dd * T:(dd + 1) * T], xnf[dd * 128:(dd + 1) * 128, :], inc=ch0)
        for e in Prog.ENGS:
            P.wait(e, ch0)
        emit_moe_l1(P, stack, xf, w1b, w2b, b1t, b2, gwTb, gb, rmsw, d_g1)
        block = stack.enter_context(nc.Block())
        P.finalize(block)
    return nc

def build_stage_e():
    nc = bass.Bass("TRN2", target_bir_lowering=False)
    x = nc.dram_tensor("x", [T, D], F32, kind="ExternalInput").ap()
    s_ = nc.dram_tensor("s_in", [T, D], BF16, kind="ExternalInput").ap()
    g_ = nc.dram_tensor("g_in", [T, D], BF16, kind="ExternalInput").ap()
    x1 = nc.dram_tensor("x1_out", [T, D], F32, kind="ExternalOutput").ap()
    xnf = nc.dram_tensor("x1n_feat", [D, T], BF16, kind="ExternalOutput").ap()
    with ExitStack() as stack:
        P = Prog(nc, stack)
        load_consts(P, stack, ["c_id128b"])
        emit_combine(P, stack, x, s_, g_, x1)
        xn_feat = sbuf(nc, stack, "x1n_sb", [128, ND * T], BF16)
        with ExitStack() as s2:
            emit_ln_feat(P, s2, x1, [(None, None, xn_feat, None)])
        ch_st = P.ch("ste")
        for dd in range(ND):
            P.dma("sync", xnf[dd * 128:(dd + 1) * 128, :], xn_feat[:, dd * T:(dd + 1) * T], inc=ch_st)
        P.wait("sync", ch_st)
        block = stack.enter_context(nc.Block())
        P.finalize(block)
    return nc

def build_stage_h():
    nc = bass.Bass("TRN2", target_bir_lowering=False)
    x = nc.dram_tensor("x1", [T, D], F32, kind="ExternalInput").ap()
    s_ = nc.dram_tensor("s1_in", [T, D], BF16, kind="ExternalInput").ap()
    g_ = nc.dram_tensor("g1_in", [T, D], BF16, kind="ExternalInput").ap()
    out = nc.dram_tensor("out", [T, D], F32, kind="ExternalOutput").ap()
    with ExitStack() as stack:
        P = Prog(nc, stack)
        load_consts(P, stack, [])
        emit_combine(P, stack, x, s_, g_, out)
        block = stack.enter_context(nc.Block())
        P.finalize(block)
    return nc



_NC_CACHE = {}

def _get_nc(name):
    if name not in _NC_CACHE:
        _NC_CACHE[name] = {
            "a": lambda: build_stage_a(True),
            "b": build_stage_b,
            "c": build_stage_c,
            "d": build_stage_d,
            "e": build_stage_e,
            "f": build_stage_f,
            "g": build_stage_g,
            "h": build_stage_h,
        }[name]()
    return _NC_CACHE[name]

def _run(nc, maps, ncores):
    res = run_bass_kernel_spmd(nc, maps, core_ids=list(range(ncores)))
    return res.results

def kernel(**ins):
    if os.environ.get("KERNEL_UNFUSED") != "1":
        return kernel_fused(**ins)
    import ml_dtypes
    ci = const_inputs()
    NB = ins["L0_attn_u"].shape[0]
    xs = [np.ascontiguousarray(ins["L0_attn_u"][b]) for b in range(NB)]

    def with_consts(m, names):
        m = dict(m)
        m["c_scal"] = ci["c_scal"]
        for n in names:
            m[n] = ci[n]
        return m

    lng_r = repl_rows(ins["L0_ffn_ln_g"]); lnb_r = repl_rows(ins["L0_ffn_ln_b"])
    mapsA = [with_consts({"x": xs[b], "ln_g_r": lng_r, "ln_b_r": lnb_r}, ["c_id128b"])
             for b in range(NB)]
    rA = _run(_get_nc("a"), mapsA, NB)
    mb = mamba_host_prep(ins)
    mapsB = [with_consts({**mb, "xn_feat": np.asarray(rA[b]["xn_feat"])}, MAMBA_CONSTS)
             for b in range(NB)]
    rB = _run(_get_nc("b"), mapsB, NB)
    mc = attn_l0_host_prep(ins)
    mapsC = [with_consts({**mc, "h_feat": np.asarray(rA[b]["h_feat"])}, ATTN_CONSTS)
             for b in range(NB)]
    rC = _run(_get_nc("c"), mapsC, NB)
    md = moe_l0_host_prep(ins)
    mapsD = [with_consts({**md, "h2_feat": np.asarray(rC[b]["h2_feat"]),
                          "gates": np.asarray(rC[b]["gates"])}, MOE_CONSTS)
             for b in range(NB)]
    rD = _run(_get_nc("d"), mapsD, NB)
    mapsE = [with_consts({"x": xs[b], "s_in": np.asarray(rB[b]["s_out"]),
                          "g_in": np.asarray(rD[b]["g_out"])}, ["c_id128b"])
             for b in range(NB)]
    rE = _run(_get_nc("e"), mapsE, NB)
    mf = attn_l1_host_prep(ins)
    mapsF = [with_consts({**mf, "x1n_feat": np.asarray(rE[b]["x1n_feat"])}, ATTN_CONSTS)
             for b in range(NB)]
    rF = _run(_get_nc("f"), mapsF, NB)
    mg = moe_l1_host_prep(ins)
    mapsG = [with_consts({**mg, "x1n_feat": np.asarray(rE[b]["x1n_feat"])}, MOE_CONSTS)
             for b in range(NB)]
    rG = _run(_get_nc("g"), mapsG, NB)
    mapsH = [with_consts({"x1": np.asarray(rE[b]["x1_out"]), "s1_in": np.asarray(rF[b]["s1_out"]),
                          "g1_in": np.asarray(rG[b]["g1_out"])}, [])
             for b in range(NB)]
    rH = _run(_get_nc("h"), mapsH, NB)
    out = np.stack([np.asarray(rH[b]["out"], dtype=np.float32) for b in range(NB)], axis=0)
    return out



ALL_CONSTS = sorted(set(MAMBA_CONSTS + ATTN_CONSTS + MOE_CONSTS))

def build_fused():
    nc = bass.Bass("TRN2", target_bir_lowering=False)
    def ein(name, shape, dt=F32):
        return nc.dram_tensor(name, list(shape), dt, kind="ExternalInput").ap()
    def itn(name, shape, dt):
        return nc.dram_tensor(name, list(shape), dt, kind="Internal").ap()
    x = ein("x", [T, D])
    lng_r = ein("ln_g_r", [128, D]); lnb_r = ein("ln_b_r", [128, D])
    win = ein("win", [D, 4384]); wout = ein("wout", [D_IN, D])
    convw_t = ein("convw_t", [128, 72]); convb_t = ein("convb_t", [128, 18])
    dtb = ein("dtb", [32, 1]); alog = ein("alog", [32, 1]); normw_r = ein("normw_r", [128, D_IN])
    wq = ein("wq", [D, D]); wk = ein("wk", [D, D]); wv = ein("wv", [D, D]); wo = ein("wo", [D, D])
    qs = ein("qs_col", [D, 1]); ks = ein("ks_col", [D, 1])
    lngc = ein("lngc", [128, 8]); lnbc = ein("lnbc", [128, 8])
    gw = ein("gw", [D, 8]); gb = ein("gb", [1, 8])
    w1 = ein("w1", [8 * 1024, 4096]); w2 = ein("w2", [8 * 4096, 1024])
    b1_t = ein("b1_t", [128, 256]); b2 = ein("b2", [8, 1024]); b2r_all = ein("b2r_all", [128, 8 * 1024])
    l1_wq = ein("l1_wq", [D, D]); l1_wk = ein("l1_wk", [D, 64]); l1_wv = ein("l1_wv", [D, 64])
    l1_wo = ein("l1_wo", [D, D]); l1_bq = ein("l1_bq", [128, 8]); l1_bkv = ein("l1_bkv", [64, 2])
    l1_bo = ein("l1_bo", [128, D])
    l1_w1 = ein("l1_w1", [8 * 1024, 2048]); l1_w2 = ein("l1_w2", [8 * 2048, 1024])
    l1_b1t = ein("l1_b1t", [128, 128]); l1_b2 = ein("l1_b2", [8, 1024])
    l1_gwT = ein("l1_gwT", [D, 8]); l1_gb = ein("l1_gb", [1, 8]); l1_rmsw = ein("l1_rmsw", [128, 8])
    out = nc.dram_tensor("out", [T, D], F32, kind="ExternalOutput").ap()
    d_hf = itn("d_hf", [D, T], BF16)
    d_s = itn("d_s", [T, D], BF16)
    d_zs = itn("d_zs", [T, D_IN], BF16)
    d_h2 = itn("d_h2", [D, T], BF16)
    d_gates = itn("d_gates", [T, 8], F32)
    d_g = itn("d_g", [T, D], BF16)
    d_x1 = itn("d_x1", [T, D], F32)
    d_s1 = itn("d_s1", [T, D], BF16)
    d_g1 = itn("d_g1", [T, D], BF16)
    wqb = itn("wqb", [D, D], BF16); wkb = itn("wkb", [D, D], BF16)
    wvb = itn("wvb", [D, D], BF16); wob = itn("wob", [D, D], BF16)
    gwbd = itn("gwbd", [D, 8], BF16)
    w1b = itn("w1b", [8 * 1024, 4096], BF16); w2b = itn("w2b", [8 * 4096, 1024], BF16)
    l1wqb = itn("l1wqb", [D, D], BF16); l1wkb = itn("l1wkb", [D, 64], BF16)
    l1wvb = itn("l1wvb", [D, 64], BF16); l1wob = itn("l1wob", [D, D], BF16)
    l1w1b = itn("l1w1b", [8 * 1024, 2048], BF16); l1w2b = itn("l1w2b", [8 * 2048, 1024], BF16)
    l1gwTb = itn("l1gwTb", [D, 8], BF16)

    with ExitStack() as stack:
        P = Prog(nc, stack)
        bigc = {"c_blkf", "c_blkpb", "c_neglt4f"}
        load_consts(P, stack, [n for n in ALL_CONSTS if n not in bigc])
        with ExitStack() as sA_:
            load_consts(P, sA_, sorted(bigc))
            xn_feat = sbuf(nc, sA_, "f_xnf", [128, ND * T], BF16)
            with ExitStack() as sh:
                h_feat = sbuf(nc, sh, "f_hf", [128, ND * T], BF16)
                sgb = sbuf(nc, sh, "f_lng", [128, D]); sbb = sbuf(nc, sh, "f_lnb", [128, D])
                ch0 = P.ch("f0")
                P.dma("sync", sgb[:], lng_r, inc=ch0)
                P.dma("sync", sbb[:], lnb_r, inc=ch0)
                for e in Prog.ENGS:
                    P.wait(e, ch0)
                with ExitStack() as s2:
                    emit_ln_feat(P, s2, x, [(None, None, xn_feat, None),
                                            (sgb[:], sbb[:], h_feat, None)])
                chh0 = P.ch("fh0")
                for dd in range(ND):
                    P.dma("sync", d_hf[dd * 128:(dd + 1) * 128, :],
                          h_feat[:, dd * T:(dd + 1) * T], inc=chh0)
                P.wait("sync", chh0)
                P.barrier()
            emit_mamba(P, sA_, xn_feat, win, convw_t, convb_t, dtb, alog, normw_r, wout,
                       d_zs, d_s)
        P.barrier()
        with ExitStack() as s0:
            emit_cast_f32_to_bf16_dram(P, s0, wq, wqb, D, D)
            emit_cast_f32_to_bf16_dram(P, s0, wk, wkb, D, D)
            emit_cast_f32_to_bf16_dram(P, s0, wv, wvb, D, D)
            emit_cast_f32_to_bf16_dram(P, s0, wo, wob, D, D)
            emit_cast_f32_to_bf16_dram(P, s0, gw, gwbd, D, 8)
        with ExitStack() as sC_:
            out_feat = sbuf(nc, sC_, "f_atout", [128, ND * T], BF16)
            with ExitStack() as sh:
                h_feat = sbuf(nc, sh, "f_hf2", [128, ND * T], BF16)
                chh1 = P.ch("fh1")
                for dd in range(ND):
                    P.dma("sync", h_feat[:, dd * T:(dd + 1) * T],
                          d_hf[dd * 128:(dd + 1) * 128, :], inc=chh1)
                for e in Prog.ENGS:
                    P.wait(e, chh1)
                emit_attn_l0_flash(P, sC_, h_feat, out_feat, wqb, wkb, wvb, qs, ks)
            emit_attn_l0_tail(P, sC_, out_feat, wob, lngc, lnbc, gwbd, gb, d_h2, d_gates)
        P.barrier()
        with ExitStack() as s0:
            emit_cast_f32_to_bf16_dram(P, s0, w1, w1b, 8 * 1024, 4096)
        with ExitStack() as s0:
            emit_cast_f32_to_bf16_dram(P, s0, w2, w2b, 8 * 4096, 1024)
        with ExitStack() as sD_:
            emit_moe_l0(P, sD_, d_h2, w1b, w2b, b1_t, b2, b2r_all, d_gates, lng_r, lnb_r, d_g)
        P.barrier()
        emit_combine(P, stack, x, d_s, d_g, d_x1)
        with ExitStack() as sE_:
            x1nf = sbuf(nc, sE_, "f_x1nf", [128, ND * T], BF16)
            with ExitStack() as s2:
                emit_ln_feat(P, s2, d_x1, [(None, None, x1nf, None)])
            P.barrier()
            with ExitStack() as s0:
                emit_cast_f32_to_bf16_dram(P, s0, l1_wq, l1wqb, D, D)
                emit_cast_f32_to_bf16_dram(P, s0, l1_wk, l1wkb, D, 64)
                emit_cast_f32_to_bf16_dram(P, s0, l1_wv, l1wvb, D, 64)
                emit_cast_f32_to_bf16_dram(P, s0, l1_wo, l1wob, D, D)
            with ExitStack() as sF_:
                emit_attn_l1(P, sF_, x1nf, l1wqb, l1wkb, l1wvb, l1wob, l1_bq, l1_bkv,
                             l1_bo, d_s1)
            P.barrier()
            with ExitStack() as s0:
                emit_cast_f32_to_bf16_dram(P, s0, l1_w1, l1w1b, 8 * 1024, 2048)
            with ExitStack() as s0:
                emit_cast_f32_to_bf16_dram(P, s0, l1_w2, l1w2b, 8 * 2048, 1024)
            with ExitStack() as s0:
                emit_cast_f32_to_bf16_dram(P, s0, l1_gwT, l1gwTb, D, 8)
            with ExitStack() as sG_:
                emit_moe_l1(P, sG_, x1nf, l1w1b, l1w2b, l1_b1t, l1_b2, l1gwTb, l1_gb,
                            l1_rmsw, d_g1)
        P.barrier()
        emit_combine(P, stack, d_x1, d_s1, d_g1, out)
        block = stack.enter_context(nc.Block())
        P.finalize(block)
    return nc


def kernel_fused(**ins):
    import ml_dtypes
    ci = const_inputs()
    NB = ins["L0_attn_u"].shape[0]
    base = {}
    base.update({"win": np.ascontiguousarray(ins["L0_attn_W_in"]),
                 "wout": np.ascontiguousarray(ins["L0_attn_W_out"])})
    base.update({k: v for k, v in mamba_host_prep(ins).items() if k not in ("win", "wout")})
    base.update(attn_l0_host_prep(ins))
    base.update(moe_l0_host_prep(ins))
    base.update(attn_l1_host_prep(ins))
    base.update(moe_l1_host_prep(ins))
    base["ln_g_r"] = repl_rows(ins["L0_ffn_ln_g"])
    base["ln_b_r"] = repl_rows(ins["L0_ffn_ln_b"])
    base["c_scal"] = ci["c_scal"]
    for n in ALL_CONSTS:
        base[n] = ci[n]
    maps = [dict(base, x=np.ascontiguousarray(ins["L0_attn_u"][b])) for b in range(NB)]
    if "fz" not in _NC_CACHE:
        _NC_CACHE["fz"] = build_fused()
    res = run_bass_kernel_spmd(_NC_CACHE["fz"], maps, core_ids=list(range(NB)))
    return np.stack([np.asarray(res.results[b]["out"], dtype=np.float32) for b in range(NB)],
                    axis=0)
